# version 02
# baseline (speedup 1.0000x reference)
import contextlib
import numpy as np
import ml_dtypes
import concourse.bass as bass
import concourse.mybir as mybir
from concourse.bass_utils import run_bass_kernel_spmd

F32 = mybir.dt.float32
BF16 = mybir.dt.bfloat16
AF = mybir.ActivationFunctionType
OP = mybir.AluOpType

BF = ml_dtypes.bfloat16

B, N, D = 2, 8192, 1024
NC = 8
TPC = N * B // NC
NCH = D // 128

class Seq:
    def __init__(self, nc):
        self.nc = nc
        self.ops = []

    def op(self, eng, fn, dma=False):
        self.ops.append((eng, fn, 16 if dma else 1))

    def dma(self, eng, out, in_, **kw):
        self.op(eng, lambda e, out=out, in_=in_, kw=kw: e.dma_start(out=out, in_=in_, **kw), dma=True)

    def build(self, blk, sem):
        n = 0
        sched = []
        for eng, fn, inc in self.ops:
            sched.append((eng, fn, n, inc))
            n += inc
        total = n

        def mk(engname):
            def body(e):
                for eng, fn, nwait, inc in sched:
                    if eng != engname:
                        continue
                    if nwait > 0:
                        e.wait_ge(sem, nwait)
                    fn(e).then_inc(sem, inc)
                if engname == 'sync':
                    e.wait_ge(sem, total)
            return body
        for name in ('sync', 'gpsimd', 'vector', 'scalar', 'tensor'):
            getattr(blk, name)(mk(name))


class Prog:
    def __init__(self, name):
        self.nc = bass.Bass("TRN2", target_bir_lowering=False)
        self.name = name
        self.s = Seq(self.nc)
        self.est = contextlib.ExitStack()
        self.inputs = {}
        self.outputs = {}

    def din(self, name, shape, dt=F32):
        t = self.nc.dram_tensor(name, list(shape), dt, kind="ExternalInput").ap()
        self.inputs[name] = t
        return t

    def dout(self, name, shape, dt=F32):
        t = self.nc.dram_tensor(name, list(shape), dt, kind="ExternalOutput").ap()
        self.outputs[name] = t
        return t

    def sb(self, name, shape, dt):
        return self.est.enter_context(self.nc.sbuf_tensor("s_" + name, list(shape), dt))

    def ps(self, name, shape, dt=F32):
        return self.est.enter_context(self.nc.psum_tensor("p_" + name, list(shape), dt))

    def finish(self):
        sem = self.est.enter_context(self.nc.semaphore("seqq"))
        blk = self.est.enter_context(self.nc.Block())
        self.s.build(blk, sem)
        self.est.close()
        return self.nc


_PROG_CACHE = {}

def cached_prog(key, builder):
    if key not in _PROG_CACHE:
        p = Prog(key)
        builder(p)
        p.finish()
        _PROG_CACHE[key] = p
    return _PROG_CACHE[key]


def run(p, in_maps, **kw):
    res = run_bass_kernel_spmd(p.nc, in_maps, core_ids=list(range(NC)), **kw)
    return res.results


def load_fm(p, dst, src, nch, T, eng='gpsimd'):
    v = src.rearrange("(c p) t -> p c t", p=128)
    d = dst[:, 0:nch * T].rearrange("p (c t) -> p c t", c=nch)
    p.s.dma(eng, d, v)


def load_w_fm(p, dst, src, din, dout, eng='gpsimd'):
    v = src.rearrange("(kc p) d -> p kc d", p=128)
    d = dst[:, 0:(din // 128) * dout].rearrange("p (kc d) -> p kc d", kc=din // 128)
    p.s.dma(eng, d, v)


def fm_proj(p, psum, wsb, xsb, *, dout, dc, xoff, ncols, nk, stop=True):
    for kc in range(nk):
        p.s.op('tensor', lambda t, kc=kc: t.matmul(
            psum[:, 0:ncols],
            wsb[:, kc * dout + dc * 128: kc * dout + dc * 128 + 128],
            xsb[:, xoff(kc): xoff(kc) + ncols],
            start=(kc == 0), stop=(stop and kc == nk - 1)))


def store_fm(p, dram, sbuf, nch, T):
    v = dram.rearrange("(c p) t -> p c t", p=128)
    p.s.dma('sync', v, sbuf[:, 0:nch * T].rearrange("p (c t) -> p c t", c=nch))


def store_tm(p, dram, sbuf, ntt, dout):
    v = dram.rearrange("(tt pp) d -> pp tt d", pp=128)
    p.s.dma('sync', v, sbuf[:, 0:ntt * dout].rearrange("p (tt d) -> p tt d", tt=ntt))


def build_l0_proj(p):
    T = TPC
    xfm = p.din("xfm", [D, T])
    wq = p.din("wq", [D, 512]); wk = p.din("wk", [D, 512]); wv = p.din("wv", [D, D])
    bqc = p.din("bqc", [128, 4]); bkc = p.din("bkc", [128, 4])
    bkr = p.din("bkr", [1, 512]); bvr = p.din("bvr", [1, D])
    q_fm = p.dout("q_fm", [512, T], BF16)
    k_fm = p.dout("k_fm", [512, T], BF16)
    k_tm = p.dout("k_tm", [T, 512], BF16)
    v_tm = p.dout("v_tm", [T, D], BF16)

    xb = p.sb("xb", [128, NCH * T], BF16)
    wqs = p.sb("wqs", [128, NCH * 512], BF16)
    wks = p.sb("wks", [128, NCH * 512], BF16)
    wvs = p.sb("wvs", [128, NCH * D], BF16)
    bq = p.sb("bq", [128, 4], F32); bk = p.sb("bk", [128, 4], F32)
    brk = p.sb("brk", [1, 512], BF16); brv = p.sb("brv", [1, D], BF16)
    ones1 = p.sb("ones1", [1, 128], BF16)
    of = p.sb("of", [128, 4 * T], BF16)
    ot = p.sb("ot", [128, (T // 128) * D], BF16)
    t1 = p.sb("t1", [128, 512], F32)
    t2 = p.sb("t2", [128, 512], F32)
    t3 = p.sb("t3", [128, 512], F32)
    pp = p.ps("pp", [128, 512])

    load_fm(p, xb, xfm, NCH, T)
    load_w_fm(p, wqs, wq, D, 512)
    load_w_fm(p, wks, wk, D, 512)
    load_w_fm(p, wvs, wv, D, D)
    p.s.dma('gpsimd', bq[:, :], bqc[:, :]); p.s.dma('gpsimd', bk[:, :], bkc[:, :])
    p.s.dma('gpsimd', brk[:, :], bkr[:, :]); p.s.dma('gpsimd', brv[:, :], bvr[:, :])
    p.s.op('vector', lambda v: v.memset(ones1[:, :], 1.0))

    NT = T // 512
    for (wsb, bias, dram) in ((wqs, bq, q_fm), (wks, bk, k_fm)):
        for dc in range(4):
            for tt in range(NT):
                fm_proj(p, pp, wsb, xb, dout=512, dc=dc,
                        xoff=lambda kc, tt=tt: kc * T + tt * 512, ncols=512, nk=NCH)
                p.s.op('vector', lambda v, bias=bias, dc=dc: v.tensor_scalar(
                    t1[:, :], pp[:, :], bias[:, dc:dc + 1], 0.0, OP.add, OP.min))
                p.s.op('scalar', lambda sc: sc.activation(out=t2[:, :], in_=t1[:, :], func=AF.Exp))
                p.s.op('vector', lambda v, bias=bias, dc=dc: v.tensor_scalar(
                    t3[:, :], pp[:, :], bias[:, dc:dc + 1], 0.0, OP.add, OP.max))
                p.s.op('vector', lambda v, dc=dc, tt=tt: v.tensor_add(
                    of[:, dc * T + tt * 512: dc * T + tt * 512 + 512], t2[:, :], t3[:, :]))
        store_fm(p, dram, of, 4, T)

    NT128 = T // 128
    for (wsb, dout, brow, dram, phi) in ((wks, 512, brk, k_tm, True), (wvs, D, brv, v_tm, False)):
        for tt in range(NT128):
            for qc in range(dout // 512):
                for kc in range(NCH):
                    p.s.op('tensor', lambda t, kc=kc, tt=tt, qc=qc, wsb=wsb, dout=dout: t.matmul(
                        pp[:, :], xb[:, kc * T + tt * 128: kc * T + tt * 128 + 128],
                        wsb[:, kc * dout + qc * 512: kc * dout + qc * 512 + 512],
                        start=(kc == 0), stop=False))
                p.s.op('tensor', lambda t, qc=qc, brow=brow: t.matmul(
                    pp[:, :], ones1[:, :], brow[:, qc * 512: qc * 512 + 512],
                    start=False, stop=True))
                co = tt * dout + qc * 512
                if phi:
                    p.s.op('vector', lambda v: v.tensor_scalar(t1[:, :], pp[:, :], 0.0, None, OP.min))
                    p.s.op('scalar', lambda sc: sc.activation(out=t2[:, :], in_=t1[:, :], func=AF.Exp))
                    p.s.op('vector', lambda v: v.tensor_scalar(t3[:, :], pp[:, :], 0.0, None, OP.max))
                    p.s.op('vector', lambda v, co=co: v.tensor_add(ot[:, co:co + 512], t2[:, :], t3[:, :]))
                else:
                    p.s.op('scalar', lambda sc, co=co: sc.activation(
                        out=ot[:, co:co + 512], in_=pp[:, :], func=AF.Copy))
        store_tm(p, dram, ot, NT128, dout)


CHK = 256

def build_l0_scan(p):
    S = N
    qf_d = p.din("qf", [128, S], BF16)
    kf_d = p.din("kf", [128, S], BF16)
    ktm_d = p.din("ktm", [S, 128], BF16)
    vtm_d = p.din("vtm", [S, 256], BF16)
    mask_d = p.din("mask384", [128, 384], BF16)
    o_d = p.dout("o_fm", [256, S], BF16)

    qf = p.sb("qf", [128, S], BF16)
    kf = p.sb("kf", [128, S], BF16)
    ktm = p.sb("ktm", [128, (S // 128) * 32 * 4], BF16)
    vtm = p.sb("vtm", [128, (S // 128) * 65 * 4], BF16)
    msk = p.sb("msk", [128, 384], BF16)
    pmask = p.sb("pmask", [128, 512], BF16)
    Ssb = p.sb("Ssb", [128, 65], BF16)
    numb = p.sb("numb", [64, 256], F32)
    zrow = p.sb("zrow", [1, 256], F32)
    onesf = p.sb("onesf", [1, 64], F32)
    ofm = p.sb("ofm", [128, 2 * S], BF16)
    psT = p.ps("psT", [128, 512])
    pacc = p.ps("pacc", [128, 256])
    pzb = p.ps("pzb", [64, 256])
    pst = p.ps("pst", [32, 65])

    p.s.dma('gpsimd', qf[:, :], qf_d[:, :])
    p.s.dma('gpsimd', kf[:, :], kf_d[:, :])
    for u in range(4):
        kv = ktm_d[:, u * 32:(u + 1) * 32].rearrange("(tt pp) f -> pp tt f", pp=128)
        p.s.dma('gpsimd', ktm[:, :].rearrange("p (tt u f) -> p tt u f", u=4, f=32)[:, :, u, :], kv)
        vv = vtm_d[:, u * 64:(u + 1) * 64].rearrange("(tt pp) g -> pp tt g", pp=128)
        p.s.dma('gpsimd', vtm[:, :].rearrange("p (tt u g) -> p tt u g", u=4, g=65)[:, :, u, 0:64], vv)
    p.s.dma('gpsimd', msk[:, :], mask_d[:, :])
    p.s.op('vector', lambda v: v.memset(vtm[:, :].rearrange("p (x g) -> p x g", g=65)[:, :, 64:65], 1.0))
    p.s.op('vector', lambda v: v.memset(Ssb[:, :], 0.0))
    p.s.op('vector', lambda v: v.memset(onesf[:, :], 1.0))

    nchunk = S // CHK
    for u in range(4):
        ub = u * 32
        for c in range(nchunk):
            c0 = c * CHK
            for kt in range(2):
                p.s.op('tensor', lambda t, ub=ub, c0=c0, kt=kt: t.matmul(
                    psT[:, kt * 256: kt * 256 + 256],
                    kf[ub:ub + 32, c0 + kt * 128: c0 + kt * 128 + 128],
                    qf[ub:ub + 32, c0: c0 + 256],
                    start=True, stop=True, tile_position=(ub, 0)))
            for kt in range(2):
                mv = msk[:, 128 - kt * 128: 128 - kt * 128 + 256]
                p.s.op('vector', lambda v, kt=kt, mv=mv: v.tensor_tensor(
                    pmask[:, kt * 256: kt * 256 + 256], psT[:, kt * 256: kt * 256 + 256], mv, OP.mult))
            for kt in range(2):
                tt128 = (c0 // 128) + kt
                p.s.op('tensor', lambda t, tt128=tt128, u=u, kt=kt: t.matmul(
                    pacc[0:65, :], vtm[:, (tt128 * 4 + u) * 65: (tt128 * 4 + u) * 65 + 65],
                    pmask[:, kt * 256: kt * 256 + 256],
                    start=(kt == 0), stop=False))
            p.s.op('tensor', lambda t, ub=ub, c0=c0: t.matmul(
                pacc[0:65, :], Ssb[ub:ub + 32, :], qf[ub:ub + 32, c0: c0 + 256],
                start=False, stop=True, tile_position=(ub, 0)))
            for kt in range(2):
                tt128 = (c0 // 128) + kt
                p.s.op('tensor', lambda t, tt128=tt128, u=u, c=c, kt=kt: t.matmul(
                    pst[:, :], ktm[:, (tt128 * 4 + u) * 32: (tt128 * 4 + u) * 32 + 32],
                    vtm[:, (tt128 * 4 + u) * 65: (tt128 * 4 + u) * 65 + 65],
                    start=(c == 0 and kt == 0), stop=False))
            p.s.op('vector', lambda v, ub=ub: v.tensor_copy(Ssb[ub:ub + 32, :], pst[:, :]))
            p.s.op('vector', lambda v: v.tensor_scalar_add(zrow[:, :], pacc[64:65, :], 1e-6))
            p.s.op('vector', lambda v: v.reciprocal(zrow[:, :], zrow[:, :]))
            p.s.op('tensor', lambda t: t.matmul(pzb[:, :], onesf[:, :], zrow[:, :], start=True, stop=True))
            p.s.op('scalar', lambda sc: sc.activation(out=numb[:, :], in_=pacc[0:64, :], func=AF.Copy))
            po = (u % 2) * 64
            pair = u // 2
            p.s.op('vector', lambda v, po=po, pair=pair, c0=c0: v.tensor_tensor(
                ofm[po:po + 64, pair * S + c0: pair * S + c0 + 256], numb[:, :], pzb[:, :], OP.mult))
    ov = o_d.rearrange("(pair po) t -> pair po t", po=128)
    for pair in range(2):
        p.s.dma('sync', ov[pair], ofm[:, pair * S: pair * S + S])


def emit_res_ln(p, *, u, ub, usq, onesc, onesr, srow, trow, pab, pstat, tt, T, write_ub=False, ucol=None):
    if ucol is None:
        ucol = lambda dc: dc * T + tt * 512
    """Given residual sum u (fp32, fm, all 8 chunks for token tile tt) already
    in place along with bf16 copy ub, compute LN in-place on u (and ub if write_ub)."""
    for dc in range(NCH):
        uo = dc * 512
        p.s.op('scalar', lambda sc, uo=uo: sc.activation(out=usq[:, :], in_=ub[:, uo:uo + 512], func=AF.Square))
        p.s.op('tensor', lambda t, dc=dc, uo=uo: t.matmul(
            pstat[0:1, 0:512], onesc[:, :], ub[:, uo:uo + 512], start=(dc == 0), stop=(dc == NCH - 1)))
        p.s.op('tensor', lambda t, dc=dc: t.matmul(
            pstat[0:1, 512:1024], onesc[:, :], usq[:, :], start=(dc == 0), stop=(dc == NCH - 1)))
    p.s.op('vector', lambda v: v.tensor_scalar_mul(srow[:, :], pstat[0:1, :], 1.0 / D))
    p.s.op('scalar', lambda sc: sc.activation(out=trow[0:1, :], in_=srow[0:1, 0:512], func=AF.Square))
    p.s.op('vector', lambda v: v.tensor_sub(srow[0:1, 512:1024], srow[0:1, 512:1024], trow[0:1, :]))
    p.s.op('vector', lambda v: v.tensor_scalar_add(srow[0:1, 512:1024], srow[0:1, 512:1024], 1e-5))
    p.s.op('scalar', lambda sc: sc.activation(out=srow[0:1, 512:1024], in_=srow[0:1, 512:1024], func=AF.Sqrt))
    p.s.op('vector', lambda v: v.reciprocal(srow[0:1, 512:1024], srow[0:1, 512:1024]))
    p.s.op('vector', lambda v: v.tensor_mul(trow[0:1, :], srow[0:1, 0:512], srow[0:1, 512:1024]))
    p.s.op('vector', lambda v: v.tensor_scalar_mul(trow[0:1, :], trow[0:1, :], -1.0))
    p.s.op('tensor', lambda t: t.matmul(pab[:, 0:512], onesr[:, :], srow[0:1, 512:1024], start=True, stop=True))
    p.s.op('tensor', lambda t: t.matmul(pab[:, 512:1024], onesr[:, :], trow[0:1, :], start=True, stop=True))
    for dc in range(NCH):
        co = ucol(dc)
        p.s.op('vector', lambda v, co=co: v.tensor_tensor(u[:, co:co + 512], u[:, co:co + 512], pab[:, 0:512], OP.mult))
        p.s.op('vector', lambda v, co=co: v.tensor_tensor(u[:, co:co + 512], u[:, co:co + 512], pab[:, 512:1024], OP.add))
        if write_ub:
            p.s.op('vector', lambda v, dc=dc, co=co: v.tensor_copy(ub[:, dc * 512:dc * 512 + 512], u[:, co:co + 512]))


def build_proj_res_ln(p):
    T = TPC
    afm = p.din("afm", [D, T], BF16)
    xfm = p.din("xfm", [D, T])
    wo = p.din("wo", [D, D])
    boc = p.din("boc", [128, NCH])
    y_d = p.dout("y", [D, T])

    ab = p.sb("ab", [128, NCH * T], BF16)
    xs = p.sb("xs", [128, NCH * T], F32)
    wos = p.sb("wos", [128, NCH * D], BF16)
    bo = p.sb("bo", [128, NCH], F32)
    u = p.sb("u", [128, NCH * T], F32)
    ub = p.sb("ub", [128, NCH * 512], BF16)
    usq = p.sb("usq", [128, 512], BF16)
    onesc = p.sb("onesc", [128, 1], BF16)
    onesr = p.sb("onesr", [1, 128], F32)
    srow = p.sb("srow", [1, 1024], F32)
    trow = p.sb("trow", [1, 512], F32)
    pp = p.ps("pp", [128, 512])
    pab = p.ps("pab", [128, 1024])
    pstat = p.ps("pstat", [1, 1024])

    load_fm(p, ab, afm, NCH, T)
    load_fm(p, xs, xfm, NCH, T, eng='sync')
    load_w_fm(p, wos, wo, D, D)
    p.s.dma('gpsimd', bo[:, :], boc[:, :])
    p.s.op('vector', lambda v: v.memset(onesc[:, :], 1.0))
    p.s.op('vector', lambda v: v.memset(onesr[:, :], 1.0))

    NT = T // 512
    for tt in range(NT):
        for dc in range(NCH):
            fm_proj(p, pp, wos, ab, dout=D, dc=dc,
                    xoff=lambda kc, tt=tt: kc * T + tt * 512, ncols=512, nk=NCH)
            co = dc * T + tt * 512
            p.s.op('vector', lambda v, dc=dc, co=co: v.scalar_tensor_tensor(
                u[:, co:co + 512], pp[:, :], bo[:, dc:dc + 1], xs[:, co:co + 512], OP.add, OP.add))
            p.s.op('vector', lambda v, dc=dc, co=co: v.tensor_copy(ub[:, dc * 512:dc * 512 + 512], u[:, co:co + 512]))
        emit_res_ln(p, u=u, ub=ub, usq=usq, onesc=onesc, onesr=onesr,
                    srow=srow, trow=trow, pab=pab, pstat=pstat, tt=tt, T=T)
    store_fm(p, y_d, u, NCH, T)


def build_l0_ffn(p):
    T = TPC
    T1 = T + 1
    xfm = p.din("xfm", [D, T1])
    wrT = p.din("wrT", [D, D]); wkT = p.din("wkT", [D, D]); wvT = p.din("wvT", [D, D])
    murc = p.din("murc", [128, NCH]); mukc = p.din("mukc", [128, NCH])
    y_d = p.dout("y", [D, T])

    xs = p.sb("xs", [128, NCH * T1], F32)
    wr = p.sb("wr", [128, NCH * D], BF16)
    wk = p.sb("wk", [128, NCH * D], BF16)
    wv = p.sb("wv", [128, NCH * D], BF16)
    mur = p.sb("mur", [128, NCH], F32); murM = p.sb("murM", [128, NCH], F32)
    muk = p.sb("muk", [128, NCH], F32); mukM = p.sb("mukM", [128, NCH], F32)
    xr = p.sb("xr", [128, NCH * 512], BF16)
    rr = p.sb("rr", [128, NCH * 512], BF16)
    ka = p.sb("ka", [128, NCH * 512], BF16)
    u = p.sb("u", [128, NCH * 512], F32)
    ub = p.sb("ub", [128, NCH * 512], BF16)
    usq = p.sb("usq", [128, 512], BF16)
    t1 = p.sb("t1", [128, 512], F32)
    onesc = p.sb("onesc", [128, 1], BF16)
    onesr = p.sb("onesr", [1, 128], F32)
    srow = p.sb("srow", [1, 1024], F32)
    trow = p.sb("trow", [1, 512], F32)
    pp = p.ps("pp", [128, 512])
    pab = p.ps("pab", [128, 1024])
    pstat = p.ps("pstat", [1, 1024])

    load_fm(p, xs, xfm, NCH, T1, eng='sync')
    load_w_fm(p, wr, wrT, D, D)
    load_w_fm(p, wk, wkT, D, D)
    load_w_fm(p, wv, wvT, D, D)
    p.s.dma('gpsimd', mur[:, :], murc[:, :]); p.s.dma('gpsimd', muk[:, :], mukc[:, :])
    p.s.op('vector', lambda v: v.tensor_scalar(murM[:, :], mur[:, :], -1.0, 1.0, OP.mult, OP.add))
    p.s.op('vector', lambda v: v.tensor_scalar(mukM[:, :], muk[:, :], -1.0, 1.0, OP.mult, OP.add))
    p.s.op('vector', lambda v: v.memset(onesc[:, :], 1.0))
    p.s.op('vector', lambda v: v.memset(onesr[:, :], 1.0))

    NT = T // 512
    for tt in range(NT):
        for dc in range(NCH):
            xo = dc * T1 + 1 + tt * 512
            po = dc * T1 + tt * 512
            p.s.op('vector', lambda v, dc=dc, xo=xo: v.tensor_scalar_mul(
                t1[:, :], xs[:, xo:xo + 512], mur[:, dc:dc + 1]))
            p.s.op('vector', lambda v, dc=dc, po=po: v.scalar_tensor_tensor(
                xr[:, dc * 512:dc * 512 + 512], xs[:, po:po + 512], murM[:, dc:dc + 1],
                t1[:, :], OP.mult, OP.add))
        for dc in range(NCH):
            fm_proj(p, pp, wr, xr, dout=D, dc=dc, xoff=lambda kc: kc * 512, ncols=512, nk=NCH)
            p.s.op('scalar', lambda sc, dc=dc: sc.activation(
                out=rr[:, dc * 512:dc * 512 + 512], in_=pp[:, :], func=AF.Sigmoid))
        for dc in range(NCH):
            xo = dc * T1 + 1 + tt * 512
            po = dc * T1 + tt * 512
            p.s.op('vector', lambda v, dc=dc, xo=xo: v.tensor_scalar_mul(
                t1[:, :], xs[:, xo:xo + 512], muk[:, dc:dc + 1]))
            p.s.op('vector', lambda v, dc=dc, po=po: v.scalar_tensor_tensor(
                xr[:, dc * 512:dc * 512 + 512], xs[:, po:po + 512], mukM[:, dc:dc + 1],
                t1[:, :], OP.mult, OP.add))
        for dc in range(NCH):
            fm_proj(p, pp, wk, xr, dout=D, dc=dc, xoff=lambda kc: kc * 512, ncols=512, nk=NCH)
            p.s.op('vector', lambda v: v.tensor_scalar(t1[:, :], pp[:, :], 0.0, None, OP.max))
            p.s.op('scalar', lambda sc, dc=dc: sc.activation(
                out=ka[:, dc * 512:dc * 512 + 512], in_=t1[:, :], func=AF.Square))
        for dc in range(NCH):
            fm_proj(p, pp, wv, ka, dout=D, dc=dc, xoff=lambda kc: kc * 512, ncols=512, nk=NCH)
            co = dc * 512
            xo = dc * T1 + 1 + tt * 512
            p.s.op('vector', lambda v, dc=dc: v.tensor_tensor(
                t1[:, :], rr[:, dc * 512:dc * 512 + 512], pp[:, :], OP.mult))
            p.s.op('vector', lambda v, co=co, xo=xo: v.tensor_add(
                u[:, co:co + 512], t1[:, :], xs[:, xo:xo + 512]))
            p.s.op('vector', lambda v, co=co: v.tensor_copy(
                ub[:, co:co + 512], u[:, co:co + 512]))
        emit_res_ln(p, u=u, ub=ub, usq=usq, onesc=onesc, onesr=onesr,
                    srow=srow, trow=trow, pab=pab, pstat=pstat, tt=tt, T=T,
                    ucol=lambda dc: dc * 512)
        p.s.dma('sync', y_d.rearrange("(c p) t -> p c t", p=128)[:, :, tt * 512:tt * 512 + 512],
                u[:, :].rearrange("p (c t) -> p c t", c=NCH))


def build_l1_proj(p):
    T = TPC
    xfm = p.din("xfm", [D, T])
    qkvT = p.din("qkvT", [D, 3 * D])
    cosT = p.din("cosT", [128, T], BF16)
    ssinT = p.din("ssinT", [128, T], BF16)
    qr_d = p.dout("qr_fm", [D, T], BF16)
    kr_d = p.dout("kr_fm", [D, T], BF16)
    v_d = p.dout("v_tm", [T, D], BF16)

    xb = p.sb("xb", [128, NCH * T], BF16)
    wq = p.sb("wq", [128, NCH * 2 * D], BF16)
    wv = p.sb("wv", [128, NCH * D], BF16)
    cosb = p.sb("cosb", [128, T], BF16)
    ssin = p.sb("ssin", [128, T], BF16)
    qraw = p.sb("qraw", [128, NCH * T], BF16)
    qsw = p.sb("qsw", [128, T], BF16)
    t1b = p.sb("t1b", [128, T], BF16)
    orope = p.sb("orope", [128, NCH * T], BF16)
    ot = p.sb("ot", [128, (T // 128) * D], BF16)
    pp = p.ps("pp", [128, 512])

    load_fm(p, xb, xfm, NCH, T)
    qk = qkvT.rearrange("(kc p) d -> p kc d", p=128)
    p.s.dma('gpsimd', wq[:, :].rearrange("p (kc d) -> p kc d", kc=NCH), qk[:, :, 0:2 * D])
    p.s.dma('gpsimd', wv[:, :].rearrange("p (kc d) -> p kc d", kc=NCH), qk[:, :, 2 * D:3 * D])
    p.s.dma('gpsimd', cosb[:, :], cosT[:, :])
    p.s.dma('gpsimd', ssin[:, :], ssinT[:, :])

    NT = T // 512
    for (woff, dram) in ((0, qr_d), (D, kr_d)):
        for dc in range(NCH):
            for tt in range(NT):
                for kc in range(NCH):
                    p.s.op('tensor', lambda t, kc=kc, dc=dc, tt=tt, woff=woff: t.matmul(
                        pp[:, :], wq[:, kc * 2 * D + woff + dc * 128: kc * 2 * D + woff + dc * 128 + 128],
                        xb[:, kc * T + tt * 512: kc * T + tt * 512 + 512],
                        start=(kc == 0), stop=(kc == NCH - 1)))
                p.s.op('scalar', lambda sc, dc=dc, tt=tt: sc.activation(
                    out=qraw[:, dc * T + tt * 512: dc * T + tt * 512 + 512], in_=pp[:, :], func=AF.Copy))
        for dc in range(NCH):
            qo = dc * T
            for g in range(4):
                src = (g ^ 1) * 32
                p.s.op('vector', lambda v, qo=qo, g=g, src=src: v.tensor_copy(
                    qsw[g * 32:(g + 1) * 32, :], qraw[src:src + 32, qo:qo + T]))
            p.s.op('vector', lambda v, qo=qo: v.tensor_tensor(
                t1b[:, :], qraw[:, qo:qo + T], cosb[:, :], OP.mult))
            p.s.op('vector', lambda v: v.tensor_tensor(qsw[:, :], qsw[:, :], ssin[:, :], OP.mult))
            p.s.op('vector', lambda v, qo=qo: v.tensor_add(orope[:, qo:qo + T], t1b[:, :], qsw[:, :]))
        store_fm(p, dram, orope, NCH, T)

    NT128 = T // 128
    for tt in range(NT128):
        for qc in range(2):
            for kc in range(NCH):
                p.s.op('tensor', lambda t, kc=kc, tt=tt, qc=qc: t.matmul(
                    pp[:, :], xb[:, kc * T + tt * 128: kc * T + tt * 128 + 128],
                    wv[:, kc * D + qc * 512: kc * D + qc * 512 + 512],
                    start=(kc == 0), stop=(kc == NCH - 1)))
            co = tt * D + qc * 512
            p.s.op('scalar', lambda sc, co=co: sc.activation(
                out=ot[:, co:co + 512], in_=pp[:, :], func=AF.Copy))
    store_tm(p, v_d, ot, NT128, D)


def build_l1_attn(p):
    S = N
    qf_d = p.din("qf", [256, S], BF16)
    kf_d = p.din("kf", [256, S], BF16)
    vtm_d = p.din("vtm", [S, 256], BF16)
    m896_d = p.din("m896", [128, 896], BF16)
    o_d = p.dout("o_fm", [256, S], BF16)

    qf = [p.sb("qf%d" % i, [128, S], BF16) for i in range(2)]
    kf = [p.sb("kf%d" % i, [128, S], BF16) for i in range(2)]
    vtm = p.sb("vtm", [128, (S // 128) * 65 * 4], BF16)
    m896 = p.sb("m896", [128, 896], BF16)
    psb = p.sb("psb", [128, 1024], BF16)
    numb = p.sb("numb", [64, 512], F32)
    zrow = p.sb("zrow", [1, 512], F32)
    onesf = p.sb("onesf", [1, 64], F32)
    ofm = [p.sb("ofm%d" % i, [128, S], BF16) for i in range(2)]
    psT = p.ps("psT", [128, 1024])
    pout = p.ps("pout", [128, 512])
    pzb = p.ps("pzb", [64, 512])

    qv = qf_d.rearrange("(pair po) t -> pair po t", po=128)
    kv = kf_d.rearrange("(pair po) t -> pair po t", po=128)
    for i in range(2):
        p.s.dma('gpsimd', qf[i][:, :], qv[i])
        p.s.dma('gpsimd', kf[i][:, :], kv[i])
    for u in range(4):
        vv = vtm_d[:, u * 64:(u + 1) * 64].rearrange("(tt pp) g -> pp tt g", pp=128)
        p.s.dma('gpsimd', vtm[:, :].rearrange("p (tt u g) -> p tt u g", u=4, g=65)[:, :, u, 0:64], vv)
    p.s.dma('gpsimd', m896[:, :], m896_d[:, :])
    p.s.op('vector', lambda v: v.memset(vtm[:, :].rearrange("p (x g) -> p x g", g=65)[:, :, 64:65], 1.0))
    p.s.op('vector', lambda v: v.memset(onesf[:, :], 1.0))

    NQT = S // 512
    for u in range(4):
        pair, po = u // 2, (u % 2) * 64
        qs = qf[pair]; ks = kf[pair]
        for qt in range(NQT):
            nkt = 4 * qt + 4
            first = True
            for g0 in range(0, nkt, 2):
                gn = min(2, nkt - g0)
                for gi in range(gn):
                    kt = g0 + gi
                    p.s.op('tensor', lambda t, po=po, kt=kt, qt=qt, gi=gi, qs=qs, ks=ks: t.matmul(
                        psT[:, gi * 512: gi * 512 + 512],
                        ks[po:po + 64, kt * 128: kt * 128 + 128],
                        qs[po:po + 64, qt * 512: qt * 512 + 512],
                        start=True, stop=True, tile_position=(po, 0)))
                p.s.op('scalar', lambda sc, gn=gn: sc.activation(
                    out=psb[:, 0:gn * 512], in_=psT[:, 0:gn * 512], func=AF.Exp, scale=0.125))
                for gi in range(gn):
                    kt = g0 + gi
                    off = qt * 512 - kt * 128
                    if off <= 0:
                        p.s.op('vector', lambda v, gi=gi, off=off: v.tensor_tensor(
                            psb[:, gi * 512: gi * 512 + 512], psb[:, gi * 512: gi * 512 + 512],
                            m896[:, off + 384: off + 384 + 512], OP.mult))
                for gi in range(gn):
                    kt = g0 + gi
                    p.s.op('tensor', lambda t, kt=kt, u=u, gi=gi, first=first and gi == 0: t.matmul(
                        pout[0:65, :], vtm[:, (kt * 4 + u) * 65: (kt * 4 + u) * 65 + 65],
                        psb[:, gi * 512: gi * 512 + 512],
                        start=first, stop=False))
                    first = False
            p.s.op('vector', lambda v: v.reciprocal(zrow[:, :], pout[64:65, :]))
            p.s.op('tensor', lambda t: t.matmul(pzb[:, :], onesf[:, :], zrow[:, :], start=True, stop=True))
            p.s.op('scalar', lambda sc: sc.activation(out=numb[:, :], in_=pout[0:64, :], func=AF.Copy))
            p.s.op('vector', lambda v, pair=pair, po=po, qt=qt: v.tensor_tensor(
                ofm[pair][po:po + 64, qt * 512: qt * 512 + 512], numb[:, :], pzb[:, :], OP.mult))
    ov = o_d.rearrange("(pair po) t -> pair po t", po=128)
    for pair in range(2):
        p.s.dma('sync', ov[pair], ofm[pair][:, :])


TH = 257
TKV = TPC + 256

def build_inner_attn(p):
    T = TPC
    TW = T + TH
    yfm = p.din("yfm", [D, TW])
    wqkv = p.din("wqkv", [D, 3 * D])
    wo = p.din("wo", [D, D])
    cq = p.din("cq", [128, T], BF16)
    sq = p.din("sq", [128, T], BF16)
    ck = p.din("ck", [128, TKV], BF16)
    sk = p.din("sk", [128, TKV], BF16)
    qsc = p.din("qsc", [128, 1]); qscw = p.din("qscw", [128, 1])
    ksc = p.din("ksc", [128, 1]); kscw = p.din("kscw", [128, 1])
    qsr = p.din("qsr", [1, 64]); ksr = p.din("ksr", [1, 64])
    blockpat = p.din("blockpat", [128, 2], BF16)
    blockpatT = p.din("blockpatT", [2, 128])
    mlow_d = p.din("mlow", [128, 128], F32)
    mupp_d = p.din("mupp", [128, 128], BF16)
    halom_d = p.din("halom", [128, 2])
    y_d = p.dout("y", [D, T])

    ysb = p.sb("ysb", [128, NCH * TW], BF16)
    ubig = p.sb("ubig", [128, NCH * 512], F32)
    wqb = p.sb("wqb", [128, NCH * D], BF16)
    cqs = p.sb("cqs", [128, T], BF16); sqs = p.sb("sqs", [128, T], BF16)
    cks = p.sb("cks", [128, TKV], BF16); sks = p.sb("sks", [128, TKV], BF16)
    qscs = p.sb("qscs", [128, 1], F32); qscws = p.sb("qscws", [128, 1], F32)
    kscs = p.sb("kscs", [128, 1], F32); kscws = p.sb("kscws", [128, 1], F32)
    qsrow = p.sb("qsrow", [1, 64], F32); ksrow = p.sb("ksrow", [1, 64], F32)
    bpat = p.sb("bpat", [128, 2], BF16)
    bpatT = p.sb("bpatT", [2, 128], F32)
    nrm2 = p.sb("nrm2", [2, 512], F32)
    mlow = p.sb("mlow", [128, 128], F32)
    mupp = p.sb("mupp", [128, 128], BF16)
    halom = p.sb("halom", [128, 2], F32)
    csb = p.sb("csb", [128, 1], F32)
    crow = p.sb("crow", [1, 1], F32)
    xbs = p.sb("xbs", [128, 512], BF16)
    qraw = p.sb("qraw", [128, NCH * T], BF16)
    kraw = p.sb("kraw", [128, NCH * TKV], BF16)
    qsw = p.sb("qsw", [128, TKV], BF16)
    vtm = p.sb("vtm", [128, (TKV // 128) * 16 * 65], BF16)
    psb = p.sb("psb", [128, 384], BF16)
    numb = p.sb("numb", [64, 128], F32)
    zrow = p.sb("zrow", [1, 512], F32)
    onesf = p.sb("onesf", [1, 128], F32)
    pp = p.ps("pp", [128, 512])
    pnrm = p.ps("pnrm", [2, 512])
    pbc = p.ps("pbc", [128, 512])
    psT = p.ps("psT", [128, 384])
    pacc = p.ps("pacc", [128, 128])
    pzb = p.ps("pzb", [64, 128])

    load_fm(p, ysb, yfm, NCH, TW)
    for (d_, s_) in ((cq, cqs), (sq, sqs), (ck, cks), (sk, sks), (qsc, qscs), (qscw, qscws),
                     (ksc, kscs), (kscw, kscws), (qsr, qsrow), (ksr, ksrow),
                     (blockpat, bpat), (blockpatT, bpatT), (mlow_d, mlow), (mupp_d, mupp),
                     (halom_d, halom)):
        p.s.dma('gpsimd', s_[:, :], d_[:, :])
    p.s.op('vector', lambda v: v.memset(onesf[:, :], 1.0))
    p.s.op('vector', lambda v: v.memset(vtm[:, :].rearrange("p (x g) -> p x g", g=65)[:, :, 64:65], 1.0))

    p.s.op('vector', lambda v: v.reduce_max(crow[:, :], qsrow[:, :], axis=mybir.AxisListType.X))
    p.s.op('vector', lambda v: v.reduce_max(zrow[0:1, 0:1], ksrow[:, :], axis=mybir.AxisListType.X))
    p.s.op('vector', lambda v: v.tensor_mul(crow[:, :], crow[:, :], zrow[0:1, 0:1]))
    p.s.op('vector', lambda v: v.tensor_scalar_mul(crow[:, :], crow[:, :], -8.0))
    p.s.op('tensor', lambda t: t.matmul(pbc[:, 0:1], onesf[:, :], crow[:, :], start=True, stop=True))
    p.s.op('vector', lambda v: v.tensor_copy(csb[:, :], pbc[:, 0:1]))

    def xoff(dc, st):
        return dc * TW + st + (TH if dc < 4 else TH - 1)

    wv_ = wqkv.rearrange("(kc p) d -> p kc d", p=128)
    def load_wblock(woff):
        p.s.dma('gpsimd', wqb[:, :].rearrange("p (kc d) -> p kc d", kc=NCH),
                wv_[:, :, woff:woff + D])

    for (woff, dst, ncolw, base) in ((0, qraw, T, 0), (D, kraw, TKV, -256)):
        load_wblock(woff)
        ntile = (ncolw + 511) // 512
        for ti in range(ntile):
            st = ti * 512
            w = min(512, ncolw - st)
            for dc in range(NCH):
                for kc in range(NCH):
                    p.s.op('tensor', lambda t, kc=kc, dc=dc, w=w, st=st, base=base: t.matmul(
                        pp[:, 0:w], wqb[:, kc * D + dc * 128: kc * D + dc * 128 + 128],
                        ysb[:, xoff(kc, st + base): xoff(kc, st + base) + w],
                        start=(kc == 0), stop=(kc == NCH - 1)))
                p.s.op('scalar', lambda sc, dc=dc, st=st, w=w, dst=dst, ncolw=ncolw: sc.activation(
                    out=dst[:, dc * ncolw + st: dc * ncolw + st + w], in_=pp[:, 0:w], func=AF.Copy))

    load_wblock(2 * D)
    for jt in range(TKV // 128):
        st = jt * 128 - 256
        for qc in range(2):
            for kc in range(NCH):
                p.s.op('tensor', lambda t, kc=kc, qc=qc, st=st: t.matmul(
                    pp[:, :], ysb[:, xoff(kc, st): xoff(kc, st) + 128],
                    wqb[:, kc * D + qc * 512: kc * D + qc * 512 + 512],
                    start=(kc == 0), stop=(kc == NCH - 1)))
            for hh in range(8):
                h = qc * 8 + hh
                p.s.op('scalar', lambda sc, jt=jt, h=h, hh=hh: sc.activation(
                    out=vtm[:, (jt * 16 + h) * 65: (jt * 16 + h) * 65 + 64],
                    in_=pp[:, hh * 64: hh * 64 + 64], func=AF.Copy))

    for (dst, ncolw, cosb, sinb, scl, sclw) in ((qraw, T, cqs, sqs, qscs, qscws),
                                                (kraw, TKV, cks, sks, kscs, kscws)):
        ntile = (ncolw + 511) // 512
        for dc in range(NCH):
            qo = dc * ncolw
            for ti in range(ntile):
                st = ti * 512
                w = min(512, ncolw - st)
                p.s.op('scalar', lambda sc, qo=qo, st=st, w=w, dst=dst: sc.activation(
                    out=xbs[:, 0:w], in_=dst[:, qo + st: qo + st + w], func=AF.Square))
                p.s.op('tensor', lambda t, w=w: t.matmul(
                    pnrm[:, 0:w], bpat[:, :], xbs[:, 0:w], start=True, stop=True))
                p.s.op('scalar', lambda sc, w=w: sc.activation(out=nrm2[:, 0:w], in_=pnrm[:, 0:w], func=AF.Sqrt))
                p.s.op('vector', lambda v, w=w: v.tensor_scalar_max(nrm2[:, 0:w], nrm2[:, 0:w], 1e-12))
                p.s.op('vector', lambda v, w=w: v.reciprocal(nrm2[:, 0:w], nrm2[:, 0:w]))
                p.s.op('tensor', lambda t, w=w: t.matmul(
                    pbc[:, 0:w], bpatT[:, :], nrm2[:, 0:w], start=True, stop=True))
                p.s.op('vector', lambda v, qo=qo, st=st, w=w, dst=dst: v.tensor_tensor(
                    dst[:, qo + st: qo + st + w], dst[:, qo + st: qo + st + w], pbc[:, 0:w], OP.mult))
            for g in range(4):
                src = (g ^ 1) * 32
                p.s.op('vector', lambda v, qo=qo, g=g, src=src, dst=dst, ncolw=ncolw: v.tensor_copy(
                    qsw[g * 32:(g + 1) * 32, 0:ncolw], dst[src:src + 32, qo: qo + ncolw]))
            p.s.op('vector', lambda v, qo=qo, dst=dst, ncolw=ncolw, scl=scl: v.tensor_scalar_mul(
                dst[:, qo: qo + ncolw], dst[:, qo: qo + ncolw], scl[:, :]))
            p.s.op('vector', lambda v, ncolw=ncolw, sclw=sclw: v.tensor_scalar_mul(
                qsw[:, 0:ncolw], qsw[:, 0:ncolw], sclw[:, :]))
            p.s.op('vector', lambda v, qo=qo, dst=dst, ncolw=ncolw, cosb=cosb: v.tensor_tensor(
                dst[:, qo: qo + ncolw], dst[:, qo: qo + ncolw], cosb[:, 0:ncolw], OP.mult))
            p.s.op('vector', lambda v, ncolw=ncolw, sinb=sinb: v.tensor_tensor(
                qsw[:, 0:ncolw], qsw[:, 0:ncolw], sinb[:, 0:ncolw], OP.mult))
            p.s.op('vector', lambda v, qo=qo, dst=dst, ncolw=ncolw: v.tensor_add(
                dst[:, qo: qo + ncolw], dst[:, qo: qo + ncolw], qsw[:, 0:ncolw]))

    aout = qraw
    for h in range(16):
        dc = h // 2
        hb = (h % 2) * 64
        for qt in range(T // 128):
            for d_i, kvt in enumerate((qt, qt + 1, qt + 2)):
                p.s.op('tensor', lambda t, dc=dc, hb=hb, kvt=kvt, qt=qt, d_i=d_i: t.matmul(
                    psT[:, d_i * 128: d_i * 128 + 128],
                    kraw[hb:hb + 64, dc * TKV + kvt * 128: dc * TKV + kvt * 128 + 128],
                    qraw[hb:hb + 64, dc * T + qt * 128: dc * T + qt * 128 + 128],
                    start=True, stop=True, tile_position=(hb, 0)))
            p.s.op('vector', lambda v: v.tensor_tensor(psT[:, 256:384], psT[:, 256:384], mlow[:, :], OP.add))
            p.s.op('scalar', lambda sc: sc.activation(out=psb[:, :], in_=psT[:, :], func=AF.Exp,
                                                      scale=8.0, bias=csb[:, :]))
            p.s.op('vector', lambda v: v.tensor_tensor(psb[:, 0:128], psb[:, 0:128], mupp[:, :], OP.mult))
            if qt == 0:
                p.s.op('vector', lambda v: v.tensor_scalar_mul(psb[:, 0:128], psb[:, 0:128], halom[:, 0:1]))
                p.s.op('vector', lambda v: v.tensor_scalar_mul(psb[:, 128:256], psb[:, 128:256], halom[:, 1:2]))
            elif qt == 1:
                p.s.op('vector', lambda v: v.tensor_scalar_mul(psb[:, 0:128], psb[:, 0:128], halom[:, 1:2]))
            for d_i, kvt in enumerate((qt, qt + 1, qt + 2)):
                p.s.op('tensor', lambda t, kvt=kvt, h=h, d_i=d_i: t.matmul(
                    pacc[0:65, :], vtm[:, (kvt * 16 + h) * 65: (kvt * 16 + h) * 65 + 65],
                    psb[:, d_i * 128: d_i * 128 + 128],
                    start=(d_i == 0), stop=(d_i == 2)))
            p.s.op('vector', lambda v: v.reciprocal(zrow[0:1, 0:128], pacc[64:65, :]))
            p.s.op('tensor', lambda t: t.matmul(pzb[:, :], onesf[:, 0:64], zrow[0:1, 0:128], start=True, stop=True))
            p.s.op('scalar', lambda sc: sc.activation(out=numb[:, 0:128], in_=pacc[0:64, :], func=AF.Copy))
            p.s.op('vector', lambda v, dc=dc, hb=hb, qt=qt: v.tensor_tensor(
                aout[hb:hb + 64, dc * T + qt * 128: dc * T + qt * 128 + 128], numb[:, 0:128], pzb[:, :], OP.mult))

    load_w_fm(p, wqb, wo, D, D)
    yin = yfm.rearrange("(c p) t -> p c t", p=128)
    yv = y_d.rearrange("(c p) t -> p c t", p=128)
    NT = T // 512
    for tt in range(NT):
        p.s.dma('sync', ubig[:, :].rearrange("p (c t) -> p c t", c=NCH),
                yin[:, :, TH + tt * 512: TH + tt * 512 + 512])
        for dc in range(NCH):
            fm_proj(p, pp, wqb, aout, dout=D, dc=dc,
                    xoff=lambda kc, tt=tt: kc * T + tt * 512, ncols=512, nk=NCH)
            p.s.op('vector', lambda v, dc=dc: v.tensor_add(
                ubig[:, dc * 512: dc * 512 + 512], pp[:, :], ubig[:, dc * 512: dc * 512 + 512]))
        p.s.dma('sync', yv[:, :, tt * 512: tt * 512 + 512],
                ubig[:, :].rearrange("p (c t) -> p c t", c=NCH))


def build_inner_ffn_plain(p):
    build_inner_ffn(p, final=False)

def build_inner_ffn_final(p):
    build_inner_ffn(p, final=True)

def build_inner_ffn(p, final=False):
    T = TPC
    TW = T + 1
    DI = 2730
    yfm = p.din("yfm", [D, TW])
    w1 = p.din("w1", [D, 2 * DI])
    w2 = p.din("w2", [DI, D])
    b1ac = p.din("b1ac", [128, 11])
    b1gc = p.din("b1gc", [128, 11])
    b1ac2 = p.din("b1ac2", [128, 11])
    b1gc2 = p.din("b1gc2", [128, 11])
    b2c = p.din("b2c", [128, NCH])
    gc = p.din("gc", [128, NCH]); bc = p.din("bc", [128, NCH])
    if final:
        x3fm = p.din("x3fm", [D, T])
    y_d = p.dout("y", [D, T])

    ys = p.sb("ys", [128, NCH * TW], F32)
    xn = p.sb("xn", [128, NCH * T], BF16)
    hsb = p.sb("hsb", [128, 11 * T], BF16)
    w2s = p.sb("w2s", [128, 11 * D], BF16)
    w1a = p.sb("w1a", [128, NCH * 128], BF16)
    w1g = p.sb("w1g", [128, NCH * 128], BF16)
    b1a = [p.sb("b1a0", [128, 11], F32), p.sb("b1a1", [128, 11], F32)]
    b1g = [p.sb("b1g0", [128, 11], F32), p.sb("b1g1", [128, 11], F32)]
    b2s = p.sb("b2s", [128, NCH], F32)
    gs = p.sb("gs", [128, NCH], F32); bs = p.sb("bs", [128, NCH], F32)
    gbf = p.sb("gbf", [128, 512], BF16)
    ub = p.sb("ub", [128, NCH * 512], BF16)
    usq = p.sb("usq", [128, 512], BF16)
    t1 = p.sb("t1", [128, 512], F32)
    onesc = p.sb("onesc", [128, 1], BF16)
    onesr = p.sb("onesr", [1, 128], F32)
    srow = p.sb("srow", [1, 1024], F32)
    trow = p.sb("trow", [1, 512], F32)
    if final:
        ufin = p.sb("ufin", [128, NCH * 512], F32)
    pp = p.ps("pp", [128, 512])
    pg = p.ps("pg", [128, 512])
    pout = p.ps("pout", [128, 1024])
    pab = p.ps("pab", [128, 1024])
    pstat = p.ps("pstat", [1, 1024])

    load_fm(p, ys, yfm, NCH, TW, eng='sync')
    p.s.dma('gpsimd', b1a[0][:, :], b1ac[:, :]); p.s.dma('gpsimd', b1g[0][:, :], b1gc[:, :])
    p.s.dma('gpsimd', b1a[1][:, :], b1ac2[:, :]); p.s.dma('gpsimd', b1g[1][:, :], b1gc2[:, :])
    p.s.dma('gpsimd', b2s[:, :], b2c[:, :])
    p.s.dma('gpsimd', gs[:, :], gc[:, :]); p.s.dma('gpsimd', bs[:, :], bc[:, :])
    p.s.op('vector', lambda v: v.memset(onesc[:, :], 1.0))
    p.s.op('vector', lambda v: v.memset(onesr[:, :], 1.0))

    def xoff(dc, st):
        return dc * TW + st + (1 if dc < 4 else 0)

    NT = T // 512
    for tt in range(NT):
        for dc in range(NCH):
            p.s.op('vector', lambda v, dc=dc, tt=tt: v.tensor_copy(
                ub[:, dc * 512: dc * 512 + 512], ys[:, xoff(dc, tt * 512): xoff(dc, tt * 512) + 512]))
        for dc in range(NCH):
            uo = dc * 512
            p.s.op('scalar', lambda sc, uo=uo: sc.activation(out=usq[:, :], in_=ub[:, uo:uo + 512], func=AF.Square))
            p.s.op('tensor', lambda t, dc=dc, uo=uo: t.matmul(
                pstat[0:1, 0:512], onesc[:, :], ub[:, uo:uo + 512], start=(dc == 0), stop=(dc == NCH - 1)))
            p.s.op('tensor', lambda t, dc=dc: t.matmul(
                pstat[0:1, 512:1024], onesc[:, :], usq[:, :], start=(dc == 0), stop=(dc == NCH - 1)))
        p.s.op('vector', lambda v: v.tensor_scalar_mul(srow[:, :], pstat[0:1, :], 1.0 / D))
        p.s.op('scalar', lambda sc: sc.activation(out=trow[0:1, :], in_=srow[0:1, 0:512], func=AF.Square))
        p.s.op('vector', lambda v: v.tensor_sub(srow[0:1, 512:1024], srow[0:1, 512:1024], trow[0:1, :]))
        p.s.op('vector', lambda v: v.tensor_scalar_add(srow[0:1, 512:1024], srow[0:1, 512:1024], 1e-5))
        p.s.op('scalar', lambda sc: sc.activation(out=srow[0:1, 512:1024], in_=srow[0:1, 512:1024], func=AF.Sqrt))
        p.s.op('vector', lambda v: v.reciprocal(srow[0:1, 512:1024], srow[0:1, 512:1024]))
        p.s.op('vector', lambda v: v.tensor_mul(trow[0:1, :], srow[0:1, 0:512], srow[0:1, 512:1024]))
        p.s.op('vector', lambda v: v.tensor_scalar_mul(trow[0:1, :], trow[0:1, :], -1.0))
        p.s.op('tensor', lambda t: t.matmul(pab[:, 0:512], onesr[:, :], srow[0:1, 512:1024], start=True, stop=True))
        p.s.op('tensor', lambda t: t.matmul(pab[:, 512:1024], onesr[:, :], trow[0:1, :], start=True, stop=True))
        for dc in range(NCH):
            xo = xoff(dc, tt * 512)
            co = dc * T + tt * 512
            p.s.op('vector', lambda v, xo=xo: v.tensor_tensor(t1[:, :], ys[:, xo:xo + 512], pab[:, 0:512], OP.mult))
            p.s.op('vector', lambda v: v.tensor_tensor(t1[:, :], t1[:, :], pab[:, 512:1024], OP.add))
            p.s.op('vector', lambda v, dc=dc, co=co: v.tensor_scalar(
                xn[:, co:co + 512], t1[:, :], gs[:, dc:dc + 1], bs[:, dc:dc + 1], OP.mult, OP.add))

    jblocks_half = [(i * 128, 128) for i in range(10)] + [(1280, 85)]
    wview = w1.rearrange("(kc p) d -> p kc d", p=128)
    for half in range(2):
        j0h = half * 1365
        for jbi, (jo, jw) in enumerate(jblocks_half):
            jg = j0h + jo
            p.s.dma('gpsimd', w1a[:, 0:NCH * jw].rearrange("p (kc d) -> p kc d", kc=NCH),
                    wview[:, :, jg:jg + jw])
            p.s.dma('gpsimd', w1g[:, 0:NCH * jw].rearrange("p (kc d) -> p kc d", kc=NCH),
                    wview[:, :, DI + jg:DI + jg + jw])
            for tt in range(NT):
                for (wsb, psum) in ((w1a, pp), (w1g, pg)):
                    for kc in range(NCH):
                        p.s.op('tensor', lambda t, kc=kc, tt=tt, jw=jw, wsb=wsb, psum=psum: t.matmul(
                            psum[0:jw, 0:512], wsb[:, kc * jw: kc * jw + jw],
                            xn[:, kc * T + tt * 512: kc * T + tt * 512 + 512],
                            start=(kc == 0), stop=(kc == NCH - 1)))
                p.s.op('scalar', lambda sc, jw=jw, jbi=jbi, half=half: sc.activation(
                    out=gbf[0:jw, :], in_=pg[0:jw, :], func=AF.Gelu, bias=b1g[half][0:jw, jbi:jbi + 1]))
                p.s.op('vector', lambda v, jo=jo, tt=tt, jw=jw, jbi=jbi, half=half: v.scalar_tensor_tensor(
                    hsb[0:jw, (jo // 128) * T + tt * 512: (jo // 128) * T + tt * 512 + 512],
                    pp[0:jw, 0:512], b1a[half][0:jw, jbi:jbi + 1], gbf[0:jw, :], OP.add, OP.mult))
        p.s.dma('gpsimd', w2s[:, 0:10 * D].rearrange("p (jb d) -> p jb d", jb=10),
                w2[j0h:j0h + 1280, :].rearrange("(jb p) d -> p jb d", p=128))
        p.s.dma('gpsimd', w2s[0:85, 10 * D:11 * D], w2[j0h + 1280:j0h + 1365, :])
        for t128 in range(T // 128):
            for dc in range(NCH):
                for jbi, (jo, jw) in enumerate(jblocks_half):
                    p.s.op('tensor', lambda t, jbi=jbi, jo=jo, jw=jw, dc=dc, t128=t128: t.matmul(
                        pout[:, dc * 128: dc * 128 + 128],
                        w2s[0:jw, jbi * D + dc * 128: jbi * D + dc * 128 + 128],
                        hsb[0:jw, (jo // 128) * T + t128 * 128: (jo // 128) * T + t128 * 128 + 128],
                        start=(jbi == 0), stop=(jbi == 10)))
            for dc in range(NCH):
                yo = dc * TW + 1 + t128 * 128
                if half == 0:
                    p.s.op('vector', lambda v, dc=dc, yo=yo: v.tensor_add(
                        ys[:, yo:yo + 128], pout[:, dc * 128: dc * 128 + 128], ys[:, yo:yo + 128]))
                else:
                    p.s.op('vector', lambda v, dc=dc, yo=yo: v.scalar_tensor_tensor(
                        ys[:, yo:yo + 128], pout[:, dc * 128: dc * 128 + 128], b2s[:, dc:dc + 1],
                        ys[:, yo:yo + 128], OP.add, OP.add))

    if not final:
        yv = y_d.rearrange("(c p) t -> p c t", p=128)
        p.s.dma('sync', yv, ys[:, :].rearrange("p (c t) -> p c t", c=NCH)[:, :, 1:1 + T])
    else:
        x3v = x3fm.rearrange("(c p) t -> p c t", p=128)
        yv = y_d.rearrange("(c p) t -> p c t", p=128)
        for tt in range(NT):
            p.s.dma('sync', ufin[:, :].rearrange("p (c t) -> p c t", c=NCH),
                    x3v[:, :, tt * 512:tt * 512 + 512])
            for dc in range(NCH):
                uo = dc * 512
                yo = dc * TW + 1 + tt * 512
                p.s.op('vector', lambda v, uo=uo, yo=yo: v.tensor_add(
                    ufin[:, uo:uo + 512], ufin[:, uo:uo + 512], ys[:, yo:yo + 512]))
                p.s.op('vector', lambda v, uo=uo: v.tensor_copy(ub[:, uo:uo + 512], ufin[:, uo:uo + 512]))
            emit_res_ln(p, u=ufin, ub=ub, usq=usq, onesc=onesc, onesr=onesr,
                        srow=srow, trow=trow, pab=pab, pstat=pstat, tt=tt, T=T,
                        ucol=lambda dc: dc * 512)
            p.s.dma('sync', yv[:, :, tt * 512:tt * 512 + 512],
                    ufin[:, :].rearrange("p (c t) -> p c t", c=NCH))



def _fm(x):
    return np.ascontiguousarray(np.asarray(x).T)


_IVF = 1.0 / 10000.0 ** (np.arange(0, 64, 2, dtype=np.float64) / 64)
_BS = ((np.arange(0, 64, 2, dtype=np.float64) + 0.4 * 64) / (1.4 * 64))
_SG = np.where((np.arange(128) % 64) < 32, -1.0, 1.0)[:, None]


def _module_rope(tglob):
    ang = tglob[:, None].astype(np.float64) * _IVF[None, :]
    cosT = np.tile(np.cos(ang).T, (4, 1))
    ssinT = np.tile(np.sin(ang).T, (4, 1)) * _SG
    return cosT.astype(BF), ssinT.astype(BF)


def _inner_rope(tglob):
    ang = tglob[:, None].astype(np.float64) * _IVF[None, :]
    cos, sin = np.cos(ang), np.sin(ang)
    power = (tglob.astype(np.float64) - N // 2) / 128.0
    sc = _BS[None, :] ** power[:, None]
    cq = np.tile((cos * sc).T, (4, 1)).astype(BF)
    sq = (np.tile((sin * sc).T, (4, 1)) * _SG).astype(BF)
    ci = np.tile((cos / sc).T, (4, 1)).astype(BF)
    si = (np.tile((sin / sc).T, (4, 1)) * _SG).astype(BF)
    return cq, sq, ci, si


def _colpack(v):
    return np.ascontiguousarray(np.asarray(v).reshape(NCH, 128).T)


def _gather_fm(res, key, c, rows=1024):
    b = c // 4
    t0 = (c % 4) * TPC
    blocks = []
    nh = rows // 64
    for h in range(nh):
        uid = b * 16 + h
        core_b, uu = uid // 4, uid % 4
        blocks.append(res[core_b][key][uu * 64:(uu + 1) * 64, t0:t0 + TPC])
    return np.concatenate(blocks, axis=0)


_CAP = None
_RAW = False
FUSED = True

def kernel(**I):
    if FUSED:
        return kernel_fused(I)
    x = np.asarray(I["L0_attn_x"])
    xt = x.reshape(NC, TPC, D)

    pA = cached_prog("l0proj", build_l0_proj)
    imA = []
    for c in range(NC):
        imA.append({
            "xfm": _fm(xt[c]),
            "wq": I["L0_attn_wq"], "wk": I["L0_attn_wk"], "wv": I["L0_attn_wv"],
            "bqc": _colpack(np.tile(I["L0_attn_bq"], 1)) if False else np.ascontiguousarray(I["L0_attn_bq"].reshape(4, 128).T),
            "bkc": np.ascontiguousarray(I["L0_attn_bk"].reshape(4, 128).T),
            "bkr": I["L0_attn_bk"].reshape(1, 512),
            "bvr": I["L0_attn_bv"].reshape(1, D),
        })
    rA = run(pA, imA)

    pB = cached_prog("l0scan", build_l0_scan)
    k_ = np.arange(128)[:, None]
    c_ = np.arange(384)[None, :]
    m384 = (k_ <= c_ - 128).astype(BF)
    imB = []
    for c in range(NC):
        qs_, ks_, kts, vts = [], [], [], []
        for uu in range(4):
            uid = c * 4 + uu
            b, h = uid // 16, uid % 16
            qs_.append(np.concatenate([rA[4 * b + j]["q_fm"][h * 32:(h + 1) * 32, :] for j in range(4)], axis=1))
            ks_.append(np.concatenate([rA[4 * b + j]["k_fm"][h * 32:(h + 1) * 32, :] for j in range(4)], axis=1))
            kts.append(np.concatenate([rA[4 * b + j]["k_tm"][:, h * 32:(h + 1) * 32] for j in range(4)], axis=0))
            vts.append(np.concatenate([rA[4 * b + j]["v_tm"][:, h * 64:(h + 1) * 64] for j in range(4)], axis=0))
        imB.append({"qf": np.concatenate(qs_, axis=0), "kf": np.concatenate(ks_, axis=0),
                    "ktm": np.concatenate(kts, axis=1), "vtm": np.concatenate(vts, axis=1),
                    "mask384": m384})
    rB = run(pB, imB)

    pC = cached_prog("projresln", build_proj_res_ln)
    imC = []
    for c in range(NC):
        imC.append({"afm": _gather_fm(rB, "o_fm", c), "xfm": _fm(xt[c]),
                    "wo": I["L0_attn_wo"], "boc": _colpack(I["L0_attn_bo"])})
    rC = run(pC, imC)
    x1 = [np.asarray(rC[c]["y"]) for c in range(NC)]
    if _CAP is not None: _CAP["x1"] = x1

    pD = cached_prog("l0ffn", build_l0_ffn)
    imD = []
    for c in range(NC):
        halo = np.zeros((D, 1), np.float32) if c % 4 == 0 else x1[c - 1][:, -1:]
        imD.append({"xfm": np.ascontiguousarray(np.concatenate([halo, x1[c]], axis=1)),
                    "wrT": _fm(I["L0_ffn_W_r"]), "wkT": _fm(I["L0_ffn_W_k"]),
                    "wvT": _fm(I["L0_ffn_W_v"]),
                    "murc": _colpack(I["L0_ffn_mu_r"]), "mukc": _colpack(I["L0_ffn_mu_k"])})
    rD = run(pD, imD)
    x2 = [np.asarray(rD[c]["y"]) for c in range(NC)]
    if _CAP is not None: _CAP["x2"] = x2

    pE = cached_prog("l1proj", build_l1_proj)
    imE = []
    for c in range(NC):
        t0c = (c % 4) * TPC
        cosT, ssinT = _module_rope(np.arange(t0c, t0c + TPC))
        imE.append({"xfm": x2[c], "qkvT": _fm(I["L1_attn_qkv"]),
                    "cosT": cosT, "ssinT": ssinT})
    rE = run(pE, imE)

    pF = cached_prog("l1attn", build_l1_attn)
    m896 = (np.arange(128)[:, None] <= np.arange(896)[None, :] - 384).astype(BF)
    imF = []
    for c in range(NC):
        qs_, ks_, vts = [], [], []
        for uu in range(4):
            uid = c * 4 + uu
            b, h = uid // 16, uid % 16
            qs_.append(np.concatenate([rE[4 * b + j]["qr_fm"][h * 64:(h + 1) * 64, :] for j in range(4)], axis=1))
            ks_.append(np.concatenate([rE[4 * b + j]["kr_fm"][h * 64:(h + 1) * 64, :] for j in range(4)], axis=1))
            vts.append(np.concatenate([rE[4 * b + j]["v_tm"][:, h * 64:(h + 1) * 64] for j in range(4)], axis=0))
        imF.append({"qf": np.concatenate(qs_, axis=0), "kf": np.concatenate(ks_, axis=0),
                    "vtm": np.concatenate(vts, axis=1), "m896": m896})
    rF = run(pF, imF)

    imG = []
    for c in range(NC):
        imG.append({"afm": _gather_fm(rF, "o_fm", c), "xfm": x2[c],
                    "wo": _fm(I["L1_attn_wo"]), "boc": np.zeros((128, NCH), np.float32)})
    rG = run(pC, imG)
    x3 = [np.asarray(rG[c]["y"]) for c in range(NC)]
    if _CAP is not None: _CAP["x3"] = x3

    pH = cached_prog("innerattn", build_inner_attn)
    pI = cached_prog("innerffn", build_inner_ffn_plain)
    pJ = cached_prog("innerffn_final", build_inner_ffn_final)

    kq = np.arange(128)[:, None]
    qq = np.arange(128)[None, :]
    mlow = np.where(kq <= qq, 0.0, -1e30).astype(np.float32)
    mupp = (kq >= qq).astype(BF)
    bpat = np.zeros((128, 2), np.float32)
    bpat[0:64, 0] = 1.0
    bpat[64:128, 1] = 1.0
    bpatTf = np.ascontiguousarray(bpat.T.astype(np.float32))
    bpat = bpat.astype(BF)

    y = x3
    for li in range(6):
        qs_i = np.asarray(I["L1_ffn_q_scale"][li])
        ks_i = np.asarray(I["L1_ffn_k_scale"][li])
        imH = []
        for c in range(NC):
            t0c = (c % 4) * TPC
            cq, sq, _, _ = _inner_rope(np.arange(t0c, t0c + TPC))
            _, _, ck, sk = _inner_rope(np.arange(t0c - 256, t0c + TPC))
            halo = np.zeros((D, TH), np.float32) if c % 4 == 0 else y[c - 1][:, -TH:]
            halom = np.zeros((128, 2), np.float32) if c % 4 == 0 else np.ones((128, 2), np.float32)
            imH.append({
                "yfm": np.ascontiguousarray(np.concatenate([halo, y[c]], axis=1)),
                "wqkv": I["L1_ffn_Wqkv"][li], "wo": I["L1_ffn_Wo"][li],
                "cq": cq, "sq": sq, "ck": ck, "sk": sk,
                "qsc": np.tile(qs_i, 2).reshape(128, 1),
                "qscw": np.tile(np.concatenate([qs_i[32:], qs_i[:32]]), 2).reshape(128, 1),
                "ksc": np.tile(ks_i, 2).reshape(128, 1),
                "kscw": np.tile(np.concatenate([ks_i[32:], ks_i[:32]]), 2).reshape(128, 1),
                "qsr": qs_i.reshape(1, 64), "ksr": ks_i.reshape(1, 64),
                "blockpat": bpat, "blockpatT": bpatTf,
                "mlow": mlow, "mupp": mupp, "halom": halom,
            })
        rH = run(pH, imH)
        y = [np.asarray(rH[c]["y"]) for c in range(NC)]
        if _CAP is not None: _CAP["ya%d" % li] = y

        b1_i = np.asarray(I["L1_ffn_b1"][li])
        DI = 2730
        b1cols = []
        for part0 in (0, DI):
            for half in range(2):
                m = np.zeros((128, 11), np.float32)
                for jbi in range(11):
                    jo = jbi * 128
                    jw = min(128, 1365 - jo)
                    m[0:jw, jbi] = b1_i[part0 + half * 1365 + jo: part0 + half * 1365 + jo + jw]
                b1cols.append(m)
        final = (li == 5)
        imI = []
        for c in range(NC):
            halo = np.zeros((D, 1), np.float32) if c % 4 == 0 else y[c - 1][:, -1:]
            d = {"yfm": np.ascontiguousarray(np.concatenate([halo, y[c]], axis=1)),
                 "w1": I["L1_ffn_W1"][li], "w2": I["L1_ffn_W2"][li],
                 "b1ac": b1cols[0], "b1ac2": b1cols[1],
                 "b1gc": b1cols[2], "b1gc2": b1cols[3],
                 "b2c": _colpack(I["L1_ffn_b2"][li]),
                 "gc": _colpack(I["L1_ffn_ln_g"][li]), "bc": _colpack(I["L1_ffn_ln_b"][li])}
            if final:
                d["x3fm"] = x3[c]
            imI.append(d)
        rI = run(pJ if final else pI, imI)
        y = [np.asarray(rI[c]["y"]) for c in range(NC)]
        if _CAP is not None: _CAP["yf%d" % li] = y

    out = np.concatenate([a.T for a in y], axis=0).reshape(B, N, D)
    return np.ascontiguousarray(out.astype(np.float32, copy=False))



GROUPS = [[0, 1, 2, 3], [4, 5, 6, 7]]

_UNIQ = [0]

def _sb(nc, es, name, shape, dt):
    _UNIQ[0] += 1
    return es.enter_context(nc.sbuf_tensor("s%d_%s" % (_UNIQ[0], name), list(shape), dt))

def _psn(nc, es, name, shape, dt=F32):
    _UNIQ[0] += 1
    return es.enter_context(nc.psum_tensor("p%d_%s" % (_UNIQ[0], name), list(shape), dt))


class Meta:
    def __init__(self):
        self.names = []
    def slot(self, name):
        if name not in self.names:
            self.names.append(name)
        return self.names.index(name)


def build_fused(upto=99):
    p = Prog("fused")
    nc = p.nc
    s = p.s
    T = TPC

    xfm_d = p.din("xfm", [D, T])
    meta_d = p.din("meta", [32, 1], mybir.dt.int32)
    wq_d = p.din("wq", [D, 512]); wk_d = p.din("wk", [D, 512]); wv_d = p.din("wv", [D, D])
    bqc_d = p.din("bqc", [128, 4]); bkc_d = p.din("bkc", [128, 4])
    bkr_d = p.din("bkr", [1, 512]); bvr_d = p.din("bvr", [1, D])
    wo0_d = p.din("wo0", [D, D]); boc_d = p.din("boc", [128, NCH])
    mask384_d = p.din("mask384", [128, 384], BF16)
    wrT_d = p.din("wrT", [D, D]); wkT_d = p.din("wkT", [D, D]); wvT_d = p.din("wvT", [D, D])
    murc_d = p.din("murc", [128, NCH]); mukc_d = p.din("mukc", [128, NCH])
    qkvT_d = p.din("qkvT", [D, 3 * D])
    cosT_d = p.din("cosT", [128, T], BF16); ssinT_d = p.din("ssinT", [128, T], BF16)
    m896_d = p.din("m896", [128, 896], BF16)
    woT1_d = p.din("woT1", [D, D])
    wqkv6_d = p.din("wqkv6", [6, D, 3 * D])
    wo6_d = p.din("wo6", [6, D, D])
    w16_d = p.din("w16", [6, D, 2 * 2730])
    w26_d = p.din("w26", [6, 2730, D])
    cq_d = p.din("cq", [128, T], BF16); sq_d = p.din("sq", [128, T], BF16)
    ck_d = p.din("ck", [128, TKV], BF16); sk_d = p.din("sk", [128, TKV], BF16)
    qsc6_d = p.din("qsc6", [128, 6]); qscw6_d = p.din("qscw6", [128, 6])
    ksc6_d = p.din("ksc6", [128, 6]); kscw6_d = p.din("kscw6", [128, 6])
    qsr6_d = p.din("qsr6", [6, 64]); ksr6_d = p.din("ksr6", [6, 64])
    blockpat_d = p.din("blockpat", [128, 2], BF16)
    blockpatT_d = p.din("blockpatT", [2, 128])
    mlow_d = p.din("mlow", [128, 128]); mupp_d = p.din("mupp", [128, 128], BF16)
    halom_d = p.din("halom", [128, 2])
    b1c6_d = p.din("b1c6", [128, 6 * 44])
    b2c6_d = p.din("b2c6", [128, 6 * NCH])
    zc_d = p.din("zc", [128, NCH])
    gc6_d = p.din("gc6", [128, 6 * NCH]); bc6_d = p.din("bc6", [128, 6 * NCH])
    y_d = p.dout("y", [D, T])

    ex_qf = nc.dram_tensor("ex_qf", [512, T], BF16).ap()
    ex_kf = nc.dram_tensor("ex_kf", [512, T], BF16).ap()
    ex_ktm = nc.dram_tensor("ex_ktm", [T, 512], BF16).ap()
    ex_vtm = nc.dram_tensor("ex_vtm", [T, D], BF16).ap()
    gx_qf = nc.dram_tensor("gx_qf", [4 * 512, T], BF16).ap()
    gx_kf = nc.dram_tensor("gx_kf", [4 * 512, T], BF16).ap()
    gx_ktm = nc.dram_tensor("gx_ktm", [4 * T, 512], BF16).ap()
    gx_vtm = nc.dram_tensor("gx_vtm", [4 * T, D], BF16).ap()
    ex_o = nc.dram_tensor("ex_o", [256, N], BF16).ap()
    gx_o = nc.dram_tensor("gx_o", [4 * 256, N], BF16).ap()
    ex_q1 = nc.dram_tensor("ex_q1", [D, T], BF16).ap()
    ex_k1 = nc.dram_tensor("ex_k1", [D, T], BF16).ap()
    ex_v1 = nc.dram_tensor("ex_v1", [T, D], BF16).ap()
    gx_q1 = nc.dram_tensor("gx_q1", [4 * D, T], BF16).ap()
    gx_k1 = nc.dram_tensor("gx_k1", [4 * D, T], BF16).ap()
    gx_v1 = nc.dram_tensor("gx_v1", [4 * T, D], BF16).ap()
    ex_h1 = nc.dram_tensor("ex_h1", [128, NCH], F32).ap()
    gx_h1 = nc.dram_tensor("gx_h1", [5 * 128, NCH], F32).ap()
    ex_hA = nc.dram_tensor("ex_hA", [128, NCH * TH], BF16).ap()
    gx_hA = nc.dram_tensor("gx_hA", [5 * 128, NCH * TH], BF16).ap()
    vb_i = nc.dram_tensor("vb_i", [TKV, D], BF16).ap()
    x3_i = nc.dram_tensor("x3_i", [D, T], F32).ap()

    est = p.est
    stream = _sb(nc, est, "stream", [128, NCH * T], F32)
    metas = _sb(nc, est, "metas", [32, 1], mybir.dt.int32)

    def coll(ins, outs):
        for i_, o_ in zip(ins, outs):
            s.op('gpsimd', lambda g, i_=i_, o_=o_: g.collective_compute(
                "AllGather", OP.bypass, GROUPS, [i_.opt()], [o_.opt()]))

    MS_COL, MS_HALO, MS_R128, MS_R256 = 0, 1, 2, 3

    s.dma('gpsimd', metas[:, :], meta_d[:, :])

    EX = {}
    def _preload_sync(e):
        col = e.value_load(metas[MS_COL:MS_COL + 1, 0:1])
        r128 = e.value_load(metas[MS_R128:MS_R128 + 1, 0:1])
        hrow = e.value_load(metas[MS_HALO:MS_HALO + 1, 0:1])
        EX['col'] = col
        EX['hrow'] = hrow
        EX['scan_row'] = [j * 512 + r128 for j in range(4)]
        EX['r128x'] = r128 + 0
        return e.nop()
    def _preload_g(e):
        r256g = e.value_load(metas[MS_R256:MS_R256 + 1, 0:1])
        EX['vt'] = [r256g + uu * 64 for uu in range(4)]
        EX['l1q1'] = [j * 1024 + 128 + r256g for j in range(4)]
        return e.nop()
    def _preload_sc(e):
        r256s = e.value_load(metas[MS_R256:MS_R256 + 1, 0:1])
        EX['l1q0s'] = [j * 1024 + r256s for j in range(4)]
        return e.nop()
    s.op('sync', _preload_sync)
    s.op('gpsimd', _preload_g)
    s.op('scalar', _preload_sc)
    xv = xfm_d.rearrange("(c p) t -> p c t", p=128)
    s.dma('sync', stream[:, :].rearrange("p (c t) -> p c t", c=NCH), xv)

    with contextlib.ExitStack() as es0:
        zt = _sb(nc, es0, "zt", [128, NCH * TH], BF16)
        zt2 = _sb(nc, es0, "zt2", [128, NCH], F32)
        s.op('vector', lambda v: v.memset(zt[:, :], 0.0))
        s.op('vector', lambda v: v.memset(zt2[:, :], 0.0))
        s.dma('gpsimd', gx_hA[4 * 128: 5 * 128, :], zt[:, :])
        s.dma('gpsimd', gx_h1[4 * 128: 5 * 128, :], zt2[:, :])

    def _ph1():
        with contextlib.ExitStack() as es1:
            xb = _sb(nc, es1, "xb", [128, NCH * T], BF16)
            wqs = _sb(nc, es1, "wqs", [128, NCH * 512], BF16)
            wks = _sb(nc, es1, "wks", [128, NCH * 512], BF16)
            wvs = _sb(nc, es1, "wvs", [128, NCH * D], BF16)
            bq = _sb(nc, es1, "bq", [128, 4], F32); bk = _sb(nc, es1, "bk", [128, 4], F32)
            brk = _sb(nc, es1, "brk", [1, 512], BF16); brv = _sb(nc, es1, "brv", [1, D], BF16)
            ones1 = _sb(nc, es1, "ones1", [1, 128], BF16)
            of = _sb(nc, es1, "of", [128, 4 * T], BF16)
            ot = _sb(nc, es1, "ot", [128, (T // 128) * D], BF16)
            t1 = _sb(nc, es1, "t1", [128, 512], F32)
            t2 = _sb(nc, es1, "t2", [128, 512], F32)
            t3 = _sb(nc, es1, "t3", [128, 512], F32)
            pp = _psn(nc, es1, "pp", [128, 512])

            for dc in range(NCH):
                for tt in range(4):
                    co = dc * T + tt * 512
                    s.op('vector', lambda v, co=co: v.tensor_copy(xb[:, co:co + 512], stream[:, co:co + 512]))
            load_w_fm(p, wqs, wq_d, D, 512)
            load_w_fm(p, wks, wk_d, D, 512)
            load_w_fm(p, wvs, wv_d, D, D)
            s.dma('gpsimd', bq[:, :], bqc_d[:, :]); s.dma('gpsimd', bk[:, :], bkc_d[:, :])
            s.dma('gpsimd', brk[:, :], bkr_d[:, :]); s.dma('gpsimd', brv[:, :], bvr_d[:, :])
            s.op('vector', lambda v: v.memset(ones1[:, :], 1.0))

            NT = T // 512
            for (wsb, bias, dram) in ((wqs, bq, ex_qf), (wks, bk, ex_kf)):
                for dc in range(4):
                    for tt in range(NT):
                        fm_proj(p, pp, wsb, xb, dout=512, dc=dc,
                                xoff=lambda kc, tt=tt: kc * T + tt * 512, ncols=512, nk=NCH)
                        p.s.op('vector', lambda v, bias=bias, dc=dc: v.tensor_scalar(
                            t1[:, :], pp[:, :], bias[:, dc:dc + 1], 0.0, OP.add, OP.min))
                        p.s.op('scalar', lambda sc: sc.activation(out=t2[:, :], in_=t1[:, :], func=AF.Exp))
                        p.s.op('vector', lambda v, bias=bias, dc=dc: v.tensor_scalar(
                            t3[:, :], pp[:, :], bias[:, dc:dc + 1], 0.0, OP.add, OP.max))
                        p.s.op('vector', lambda v, dc=dc, tt=tt: v.tensor_add(
                            of[:, dc * T + tt * 512: dc * T + tt * 512 + 512], t2[:, :], t3[:, :]))
                store_fm(p, dram, of, 4, T)
            NT128 = T // 128
            for (wsb, dout, brow, dram, phi) in ((wks, 512, brk, ex_ktm, True), (wvs, D, brv, ex_vtm, False)):
                for tt in range(NT128):
                    for qc in range(dout // 512):
                        for kc in range(NCH):
                            p.s.op('tensor', lambda t, kc=kc, tt=tt, qc=qc, wsb=wsb, dout=dout: t.matmul(
                                pp[:, :], xb[:, kc * T + tt * 128: kc * T + tt * 128 + 128],
                                wsb[:, kc * dout + qc * 512: kc * dout + qc * 512 + 512],
                                start=(kc == 0), stop=False))
                        p.s.op('tensor', lambda t, qc=qc, brow=brow: t.matmul(
                            pp[:, :], ones1[:, :], brow[:, qc * 512: qc * 512 + 512],
                            start=False, stop=True))
                        co = tt * dout + qc * 512
                        if phi:
                            p.s.op('vector', lambda v: v.tensor_scalar(t1[:, :], pp[:, :], 0.0, None, OP.min))
                            p.s.op('scalar', lambda sc: sc.activation(out=t2[:, :], in_=t1[:, :], func=AF.Exp))
                            p.s.op('vector', lambda v: v.tensor_scalar(t3[:, :], pp[:, :], 0.0, None, OP.max))
                            p.s.op('vector', lambda v, co=co: v.tensor_add(ot[:, co:co + 512], t2[:, :], t3[:, :]))
                        else:
                            p.s.op('scalar', lambda sc, co=co: sc.activation(
                                out=ot[:, co:co + 512], in_=pp[:, :], func=AF.Copy))
                store_tm(p, dram, ot, NT128, dout)
        coll([ex_qf, ex_kf, ex_ktm, ex_vtm], [gx_qf, gx_kf, gx_ktm, gx_vtm])
    if upto >= 1:
        _ph1()

    def _ph2():
        with contextlib.ExitStack() as es2:
            qf = _sb(nc, es2, "qf", [128, N], BF16)
            kf = _sb(nc, es2, "kf", [128, N], BF16)
            ktm = _sb(nc, es2, "ktm", [128, (N // 128) * 32 * 4], BF16)
            vtm = _sb(nc, es2, "vtm", [128, (N // 128) * 65 * 4], BF16)
            msk = _sb(nc, es2, "msk", [128, 384], BF16)
            pmask = _sb(nc, es2, "pmask", [128, 512], BF16)
            Ssb = _sb(nc, es2, "Ssb", [128, 65], BF16)
            numb = _sb(nc, es2, "numb", [64, 256], F32)
            zrow = _sb(nc, es2, "zrow", [1, 256], F32)
            onesf = _sb(nc, es2, "onesf", [1, 64], F32)
            ofm = [_sb(nc, es2, "ofm%d" % i, [128, N], BF16) for i in range(2)]
            psT = _psn(nc, es2, "psT", [128, 512])
            pacc = _psn(nc, es2, "pacc", [128, 256])
            pzb = _psn(nc, es2, "pzb", [64, 256])
            pst = _psn(nc, es2, "pst", [32, 65])

            for j in range(4):
                s.op('sync', lambda e, j=j: e.dma_start(
                    out=qf[:, j * T:(j + 1) * T],
                    in_=gx_qf[bass.ds(EX['scan_row'][j], 128), :]), dma=True)
                s.op('sync', lambda e, j=j: e.dma_start(
                    out=kf[:, j * T:(j + 1) * T],
                    in_=gx_kf[bass.ds(EX['scan_row'][j], 128), :]), dma=True)
                s.op('sync', lambda e, j=j: e.dma_start(
                    out=ktm[:, :].rearrange("p (tt x) -> p tt x", x=128)[:, j * 16:(j + 1) * 16, :],
                    in_=gx_ktm[j * T:(j + 1) * T, bass.ds(EX['r128x'], 128)].rearrange("(tt pp) f -> pp tt f", pp=128)), dma=True)
                for uu in range(4):
                    s.op('gpsimd', lambda e, j=j, uu=uu: e.dma_start(
                        out=vtm[:, :].rearrange("p (tt u g) -> p tt u g", u=4, g=65)[:, j * 16:(j + 1) * 16, uu, 0:64],
                        in_=gx_vtm[j * T:(j + 1) * T, bass.ds(EX['vt'][uu], 64)].rearrange("(tt pp) g -> pp tt g", pp=128)), dma=True)
            s.dma('gpsimd', msk[:, :], mask384_d[:, :])
            s.op('vector', lambda v: v.memset(vtm[:, :].rearrange("p (x g) -> p x g", g=65)[:, :, 64:65], 1.0))
            s.op('vector', lambda v: v.memset(Ssb[:, :], 0.0))
            s.op('vector', lambda v: v.memset(onesf[:, :], 1.0))

            nchunk = N // CHK
            for u in range(4):
                ub = u * 32
                for c in range(nchunk):
                    c0 = c * CHK
                    for kt in range(2):
                        s.op('tensor', lambda t, ub=ub, c0=c0, kt=kt: t.matmul(
                            psT[:, kt * 256: kt * 256 + 256],
                            kf[ub:ub + 32, c0 + kt * 128: c0 + kt * 128 + 128],
                            qf[ub:ub + 32, c0: c0 + 256],
                            start=True, stop=True, tile_position=(ub, 0)))
                    for kt in range(2):
                        mv = msk[:, 128 - kt * 128: 128 - kt * 128 + 256]
                        s.op('vector', lambda v, kt=kt, mv=mv: v.tensor_tensor(
                            pmask[:, kt * 256: kt * 256 + 256], psT[:, kt * 256: kt * 256 + 256], mv, OP.mult))
                    for kt in range(2):
                        tt128 = (c0 // 128) + kt
                        s.op('tensor', lambda t, tt128=tt128, u=u, kt=kt: t.matmul(
                            pacc[0:65, :], vtm[:, (tt128 * 4 + u) * 65: (tt128 * 4 + u) * 65 + 65],
                            pmask[:, kt * 256: kt * 256 + 256],
                            start=(kt == 0), stop=False))
                    s.op('tensor', lambda t, ub=ub, c0=c0: t.matmul(
                        pacc[0:65, :], Ssb[ub:ub + 32, :], qf[ub:ub + 32, c0: c0 + 256],
                        start=False, stop=True, tile_position=(ub, 0)))
                    for kt in range(2):
                        tt128 = (c0 // 128) + kt
                        s.op('tensor', lambda t, tt128=tt128, u=u, c=c, kt=kt: t.matmul(
                            pst[:, :], ktm[:, (tt128 * 4 + u) * 32: (tt128 * 4 + u) * 32 + 32],
                            vtm[:, (tt128 * 4 + u) * 65: (tt128 * 4 + u) * 65 + 65],
                            start=(c == 0 and kt == 0), stop=False))
                    s.op('vector', lambda v, ub=ub: v.tensor_copy(Ssb[ub:ub + 32, :], pst[:, :]))
                    s.op('vector', lambda v: v.tensor_scalar_add(zrow[:, :], pacc[64:65, :], 1e-6))
                    s.op('vector', lambda v: v.reciprocal(zrow[:, :], zrow[:, :]))
                    s.op('tensor', lambda t: t.matmul(pzb[:, :], onesf[:, :], zrow[:, :], start=True, stop=True))
                    s.op('scalar', lambda sc: sc.activation(out=numb[:, :], in_=pacc[0:64, :], func=AF.Copy))
                    po = (u % 2) * 64
                    pair = u // 2
                    s.op('vector', lambda v, po=po, pair=pair, c0=c0: v.tensor_tensor(
                        ofm[pair][po:po + 64, c0: c0 + 256], numb[:, :], pzb[:, :], OP.mult))
            ov = ex_o.rearrange("(pair po) t -> pair po t", po=128)
            for pair in range(2):
                s.dma('sync', ov[pair], ofm[pair][:, :])
        coll([ex_o], [gx_o])
    if upto >= 2:
        _ph2()

    def emit_oproj_res_ln(wo_dram, bias_dram, gx, nch_rows_map):
        with contextlib.ExitStack() as es3:
            ab = _sb(nc, es3, "ab", [128, NCH * T], BF16)
            wos = _sb(nc, es3, "wos", [128, NCH * D], BF16)
            bo = _sb(nc, es3, "bo", [128, NCH], F32)
            ub = _sb(nc, es3, "ub", [128, NCH * 512], BF16)
            usq = _sb(nc, es3, "usq", [128, 512], BF16)
            onesc = _sb(nc, es3, "onesc", [128, 1], BF16)
            onesr = _sb(nc, es3, "onesr", [1, 128], F32)
            srow = _sb(nc, es3, "srow", [1, 1024], F32)
            trow = _sb(nc, es3, "trow", [1, 512], F32)
            pp = _psn(nc, es3, "pp", [128, 512])
            pab = _psn(nc, es3, "pab", [128, 1024])
            pstat = _psn(nc, es3, "pstat", [1, 1024])

            for dc in range(NCH):
                r0 = nch_rows_map(dc)
                s.op('sync', lambda e, dc=dc, r0=r0: e.dma_start(
                    out=ab[:, dc * T:(dc + 1) * T],
                    in_=gx[r0:r0 + 128, bass.ds(EX['col'], T)]), dma=True)
            load_w_fm(p, wos, wo_dram, D, D)
            s.dma('gpsimd', bo[:, :], bias_dram[:, :])
            s.op('vector', lambda v: v.memset(onesc[:, :], 1.0))
            s.op('vector', lambda v: v.memset(onesr[:, :], 1.0))
            NT = T // 512
            for tt in range(NT):
                for dc in range(NCH):
                    fm_proj(p, pp, wos, ab, dout=D, dc=dc,
                            xoff=lambda kc, tt=tt: kc * T + tt * 512, ncols=512, nk=NCH)
                    co = dc * T + tt * 512
                    s.op('vector', lambda v, dc=dc, co=co: v.scalar_tensor_tensor(
                        stream[:, co:co + 512], pp[:, :], bo[:, dc:dc + 1], stream[:, co:co + 512],
                        OP.add, OP.add))
                    s.op('vector', lambda v, dc=dc, co=co: v.tensor_copy(
                        ub[:, dc * 512:dc * 512 + 512], stream[:, co:co + 512]))
                emit_res_ln(p, u=stream, ub=ub, usq=usq, onesc=onesc, onesr=onesr,
                            srow=srow, trow=trow, pab=pab, pstat=pstat, tt=tt, T=T)

    if upto >= 3:
        emit_oproj_res_ln(wo0_d, boc_d, gx_o,
                          lambda dc: ((dc * 2) // 4) * 256 + ((dc * 2) % 4) * 64)


    def emit_halo1():
        for dc in range(NCH):
            s.op('sync', lambda e, dc=dc: e.dma_start(
                out=ex_h1[:, dc:dc + 1], in_=stream[:, dc * T + T - 1: dc * T + T],
                allow_slow_non_contiguous=True), dma=True)
        coll([ex_h1], [gx_h1[0:4 * 128, :]])

    def read_halo1(hl):
        s.op('sync', lambda e: e.dma_start(
            out=hl[:, :], in_=gx_h1[bass.ds(EX['hrow'], 128), :]), dma=True)

    def _ph4():
        emit_halo1()
        with contextlib.ExitStack() as es4:
            hl = _sb(nc, es4, "hl", [128, NCH], F32)
            wr = _sb(nc, es4, "wr", [128, NCH * D], BF16)
            wk2 = _sb(nc, es4, "wk2", [128, NCH * D], BF16)
            wv2 = _sb(nc, es4, "wv2", [128, NCH * D], BF16)
            mur = _sb(nc, es4, "mur", [128, NCH], F32); murM = _sb(nc, es4, "murM", [128, NCH], F32)
            muk = _sb(nc, es4, "muk", [128, NCH], F32); mukM = _sb(nc, es4, "mukM", [128, NCH], F32)
            xr = _sb(nc, es4, "xr", [128, NCH * 512], BF16)
            rr = _sb(nc, es4, "rr", [128, NCH * 512], BF16)
            ka = _sb(nc, es4, "ka", [128, NCH * 512], BF16)
            xp0 = _sb(nc, es4, "xp0", [128, 512], F32)
            ub = _sb(nc, es4, "ub", [128, NCH * 512], BF16)
            usq = _sb(nc, es4, "usq", [128, 512], BF16)
            t1 = _sb(nc, es4, "t1", [128, 512], F32)
            onesc = _sb(nc, es4, "onesc", [128, 1], BF16)
            onesr = _sb(nc, es4, "onesr", [1, 128], F32)
            srow = _sb(nc, es4, "srow", [1, 1024], F32)
            trow = _sb(nc, es4, "trow", [1, 512], F32)
            pp = _psn(nc, es4, "pp", [128, 512])
            pab = _psn(nc, es4, "pab", [128, 1024])
            pstat = _psn(nc, es4, "pstat", [1, 1024])

            read_halo1(hl)
            load_w_fm(p, wr, wrT_d, D, D)
            load_w_fm(p, wk2, wkT_d, D, D)
            load_w_fm(p, wv2, wvT_d, D, D)
            s.dma('gpsimd', mur[:, :], murc_d[:, :]); s.dma('gpsimd', muk[:, :], mukc_d[:, :])
            s.op('vector', lambda v: v.tensor_scalar(murM[:, :], mur[:, :], -1.0, 1.0, OP.mult, OP.add))
            s.op('vector', lambda v: v.tensor_scalar(mukM[:, :], muk[:, :], -1.0, 1.0, OP.mult, OP.add))
            s.op('vector', lambda v: v.memset(onesc[:, :], 1.0))
            s.op('vector', lambda v: v.memset(onesr[:, :], 1.0))

            NT = T // 512
            for tt in range(NT):
                for which in range(2):
                    mu_, muM_ = (mur, murM) if which == 0 else (muk, mukM)
                    for dc in range(NCH):
                        co = dc * T + tt * 512
                        s.op('vector', lambda v, dc=dc, co=co, mu_=mu_: v.tensor_scalar_mul(
                            t1[:, :], stream[:, co:co + 512], mu_[:, dc:dc + 1]))
                        if tt == 0:
                            s.op('vector', lambda v, dc=dc: v.tensor_copy(
                                xp0[:, 0:1], hl[:, dc:dc + 1]))
                            s.op('vector', lambda v, dc=dc, co=co: v.tensor_copy(
                                xp0[:, 1:512], stream[:, co:co + 511]))
                            s.op('vector', lambda v, dc=dc, muM_=muM_: v.scalar_tensor_tensor(
                                xr[:, dc * 512:dc * 512 + 512], xp0[:, :], muM_[:, dc:dc + 1],
                                t1[:, :], OP.mult, OP.add))
                        else:
                            s.op('vector', lambda v, dc=dc, co=co, muM_=muM_: v.scalar_tensor_tensor(
                                xr[:, dc * 512:dc * 512 + 512], stream[:, co - 1:co + 511], muM_[:, dc:dc + 1],
                                t1[:, :], OP.mult, OP.add))
                    if which == 0:
                        for dc in range(NCH):
                            fm_proj(p, pp, wr, xr, dout=D, dc=dc, xoff=lambda kc: kc * 512, ncols=512, nk=NCH)
                            s.op('scalar', lambda sc, dc=dc: sc.activation(
                                out=rr[:, dc * 512:dc * 512 + 512], in_=pp[:, :], func=AF.Sigmoid))
                    else:
                        for dc in range(NCH):
                            fm_proj(p, pp, wk2, xr, dout=D, dc=dc, xoff=lambda kc: kc * 512, ncols=512, nk=NCH)
                            s.op('vector', lambda v: v.tensor_scalar(t1[:, :], pp[:, :], 0.0, None, OP.max))
                            s.op('scalar', lambda sc, dc=dc: sc.activation(
                                out=ka[:, dc * 512:dc * 512 + 512], in_=t1[:, :], func=AF.Square))
                for dc in range(NCH):
                    fm_proj(p, pp, wv2, ka, dout=D, dc=dc, xoff=lambda kc: kc * 512, ncols=512, nk=NCH)
                    co = dc * T + tt * 512
                    s.op('vector', lambda v, dc=dc: v.tensor_tensor(
                        t1[:, :], rr[:, dc * 512:dc * 512 + 512], pp[:, :], OP.mult))
                    s.op('vector', lambda v, co=co: v.tensor_add(
                        stream[:, co:co + 512], t1[:, :], stream[:, co:co + 512]))
                    s.op('vector', lambda v, dc=dc, co=co: v.tensor_copy(
                        ub[:, dc * 512:dc * 512 + 512], stream[:, co:co + 512]))
                emit_res_ln(p, u=stream, ub=ub, usq=usq, onesc=onesc, onesr=onesr,
                            srow=srow, trow=trow, pab=pab, pstat=pstat, tt=tt, T=T)

    if upto >= 4:
        _ph4()

    def _ph5():
        with contextlib.ExitStack() as es5:
            xb = _sb(nc, es5, "xb", [128, NCH * T], BF16)
            wqb = _sb(nc, es5, "wqb", [128, NCH * 512], BF16)
            cosb = _sb(nc, es5, "cosb", [128, T], BF16)
            ssin = _sb(nc, es5, "ssin", [128, T], BF16)
            qraw = _sb(nc, es5, "qraw", [128, NCH * T], BF16)
            qsw = _sb(nc, es5, "qsw", [128, T], BF16)
            t1b = _sb(nc, es5, "t1b", [128, T], BF16)
            och = _sb(nc, es5, "och", [128, T], BF16)
            otv = _sb(nc, es5, "otv", [128, D], BF16)
            pp = _psn(nc, es5, "pp", [128, 512])

            for dc in range(NCH):
                s.op('vector', lambda v, dc=dc: v.tensor_copy(xb[:, dc * T:(dc + 1) * T], stream[:, dc * T:(dc + 1) * T]))
            s.dma('gpsimd', cosb[:, :], cosT_d[:, :])
            s.dma('gpsimd', ssin[:, :], ssinT_d[:, :])

            qk = qkvT_d.rearrange("(kc p) d -> p kc d", p=128)
            NT = T // 512
            for (woff2, dram) in ((0, ex_q1), (D, ex_k1)):
                for half in range(2):
                    s.dma('gpsimd', wqb[:, :].rearrange("p (kc d) -> p kc d", kc=NCH),
                          qk[:, :, woff2 + half * 512: woff2 + half * 512 + 512])
                    for dch in range(4):
                        dc = half * 4 + dch
                        for tt in range(NT):
                            for kc in range(NCH):
                                s.op('tensor', lambda t, kc=kc, dch=dch, tt=tt: t.matmul(
                                    pp[:, :], wqb[:, kc * 512 + dch * 128: kc * 512 + dch * 128 + 128],
                                    xb[:, kc * T + tt * 512: kc * T + tt * 512 + 512],
                                    start=(kc == 0), stop=(kc == NCH - 1)))
                            s.op('scalar', lambda sc, dc=dc, tt=tt: sc.activation(
                                out=qraw[:, dc * T + tt * 512: dc * T + tt * 512 + 512], in_=pp[:, :], func=AF.Copy))
                for dc in range(NCH):
                    qo = dc * T
                    for g in range(4):
                        src = (g ^ 1) * 32
                        s.op('vector', lambda v, qo=qo, g=g, src=src: v.tensor_copy(
                            qsw[g * 32:(g + 1) * 32, :], qraw[src:src + 32, qo:qo + T]))
                    s.op('vector', lambda v, qo=qo: v.tensor_tensor(
                        t1b[:, :], qraw[:, qo:qo + T], cosb[:, :], OP.mult))
                    s.op('vector', lambda v: v.tensor_tensor(qsw[:, :], qsw[:, :], ssin[:, :], OP.mult))
                    s.op('vector', lambda v: v.tensor_add(och[:, :], t1b[:, :], qsw[:, :]))
                    s.op('sync', lambda e, dc=dc, dram=dram: e.dma_start(
                        out=dram[dc * 128:(dc + 1) * 128, :], in_=och[:, :]), dma=True)
            wvb2 = qraw
            s.dma('gpsimd', wqb[:, :].rearrange("p (kc d) -> p kc d", kc=NCH),
                  qk[:, :, 2 * D: 2 * D + 512])
            s.dma('gpsimd', wvb2[:, 0:NCH * 512].rearrange("p (kc d) -> p kc d", kc=NCH),
                  qk[:, :, 2 * D + 512: 2 * D + 1024])
            for tt in range(T // 128):
                for half, wb in ((0, wqb), (1, wvb2)):
                    for kc in range(NCH):
                        s.op('tensor', lambda t, kc=kc, tt=tt, wb=wb: t.matmul(
                            pp[:, :], xb[:, kc * T + tt * 128: kc * T + tt * 128 + 128],
                            wb[:, kc * 512: kc * 512 + 512],
                            start=(kc == 0), stop=(kc == NCH - 1)))
                    s.op('scalar', lambda sc, half=half: sc.activation(
                        out=otv[:, half * 512: half * 512 + 512], in_=pp[:, :], func=AF.Copy))
                s.op('sync', lambda e, tt=tt: e.dma_start(
                    out=ex_v1[tt * 128:(tt + 1) * 128, :], in_=otv[:, :]), dma=True)
        coll([ex_q1, ex_k1, ex_v1], [gx_q1, gx_k1, gx_v1])
    if upto >= 5:
        _ph5()

    def _ph6():
        with contextlib.ExitStack() as es6:
            qf = [_sb(nc, es6, "qf%d" % i, [128, N], BF16) for i in range(2)]
            kf = [_sb(nc, es6, "kf%d" % i, [128, N], BF16) for i in range(2)]
            vtm = _sb(nc, es6, "vtm", [128, (N // 128) * 65 * 4], BF16)
            m896 = _sb(nc, es6, "m896b", [128, 896], BF16)
            psb = _sb(nc, es6, "psb", [128, 1024], BF16)
            numb = _sb(nc, es6, "numb", [64, 512], F32)
            zrow = _sb(nc, es6, "zrow", [1, 512], F32)
            onesf = _sb(nc, es6, "onesf", [1, 64], F32)
            ofm = [_sb(nc, es6, "ofm%d" % i, [128, N], BF16) for i in range(2)]
            psT = _psn(nc, es6, "psT", [128, 1024])
            pout = _psn(nc, es6, "pout", [128, 512])
            pzb = _psn(nc, es6, "pzb", [64, 512])

            for j in range(4):
                for pair in range(2):
                    eng = 'scalar' if pair == 0 else 'gpsimd'
                    expr = (lambda j=j: EX['l1q0s'][j]) if pair == 0 else (lambda j=j: EX['l1q1'][j])
                    s.op(eng, lambda e, j=j, expr=expr, pair=pair: e.dma_start(
                        out=qf[pair][:, j * T:(j + 1) * T],
                        in_=gx_q1[bass.ds(expr(), 128), :]), dma=True)
                    s.op(eng, lambda e, j=j, expr=expr, pair=pair: e.dma_start(
                        out=kf[pair][:, j * T:(j + 1) * T],
                        in_=gx_k1[bass.ds(expr(), 128), :]), dma=True)
                for uu in range(4):
                    s.op('gpsimd', lambda e, j=j, uu=uu: e.dma_start(
                        out=vtm[:, :].rearrange("p (tt u g) -> p tt u g", u=4, g=65)[:, j * 16:(j + 1) * 16, uu, 0:64],
                        in_=gx_v1[j * T:(j + 1) * T, bass.ds(EX['vt'][uu], 64)].rearrange("(tt pp) g -> pp tt g", pp=128)), dma=True)
            s.dma('gpsimd', m896[:, :], m896_d[:, :])
            s.op('vector', lambda v: v.memset(vtm[:, :].rearrange("p (x g) -> p x g", g=65)[:, :, 64:65], 1.0))
            s.op('vector', lambda v: v.memset(onesf[:, :], 1.0))

            NQT = N // 512
            for u in range(4):
                pair, po = u // 2, (u % 2) * 64
                qs2 = qf[pair]; ks2 = kf[pair]
                for qt in range(NQT):
                    nkt = 4 * qt + 4
                    first = True
                    for g0 in range(0, nkt, 2):
                        gn = min(2, nkt - g0)
                        for gi in range(gn):
                            kt = g0 + gi
                            s.op('tensor', lambda t, po=po, kt=kt, qt=qt, gi=gi, qs2=qs2, ks2=ks2: t.matmul(
                                psT[:, gi * 512: gi * 512 + 512],
                                ks2[po:po + 64, kt * 128: kt * 128 + 128],
                                qs2[po:po + 64, qt * 512: qt * 512 + 512],
                                start=True, stop=True, tile_position=(po, 0)))
                        s.op('scalar', lambda sc, gn=gn: sc.activation(
                            out=psb[:, 0:gn * 512], in_=psT[:, 0:gn * 512], func=AF.Exp, scale=0.125))
                        for gi in range(gn):
                            kt = g0 + gi
                            off = qt * 512 - kt * 128
                            if off <= 0:
                                s.op('vector', lambda v, gi=gi, off=off: v.tensor_tensor(
                                    psb[:, gi * 512: gi * 512 + 512], psb[:, gi * 512: gi * 512 + 512],
                                    m896[:, off + 384: off + 384 + 512], OP.mult))
                        for gi in range(gn):
                            kt = g0 + gi
                            s.op('tensor', lambda t, kt=kt, u=u, gi=gi, first=first and gi == 0: t.matmul(
                                pout[0:65, :], vtm[:, (kt * 4 + u) * 65: (kt * 4 + u) * 65 + 65],
                                psb[:, gi * 512: gi * 512 + 512],
                                start=first, stop=False))
                            first = False
                    s.op('vector', lambda v: v.reciprocal(zrow[:, :], pout[64:65, :]))
                    s.op('tensor', lambda t: t.matmul(pzb[:, :], onesf[:, :], zrow[:, :], start=True, stop=True))
                    s.op('scalar', lambda sc: sc.activation(out=numb[:, :], in_=pout[0:64, :], func=AF.Copy))
                    s.op('vector', lambda v, pair=pair, po=po, qt=qt: v.tensor_tensor(
                        ofm[pair][po:po + 64, qt * 512: qt * 512 + 512], numb[:, :], pzb[:, :], OP.mult))
            ov = ex_o.rearrange("(pair po) t -> pair po t", po=128)
            for pair in range(2):
                s.dma('sync', ov[pair], ofm[pair][:, :])
        coll([ex_o], [gx_o])
    if upto >= 6:
        _ph6()

    if upto >= 7:
        emit_oproj_res_ln(woT1_d, zc_d, gx_o,
                          lambda dc: ((dc * 2) // 4) * 256 + ((dc * 2) % 4) * 64)
        s.dma('sync', x3_i.rearrange("(c p) t -> p c t", p=128),
              stream[:, :].rearrange("p (c t) -> p c t", c=NCH))
    def _inner_layers():
      with contextlib.ExitStack() as esT:
        bpat = _sb(nc, esT, "bpat", [128, 2], BF16)
        bpatT = _sb(nc, esT, "bpatT", [2, 128], F32)
        mlow = _sb(nc, esT, "mlow", [128, 128], F32)
        mupp = _sb(nc, esT, "mupp", [128, 128], BF16)
        halom = _sb(nc, esT, "halom", [128, 2], F32)
        onesf = _sb(nc, esT, "onesf", [1, 128], F32)
        s.dma('gpsimd', bpat[:, :], blockpat_d[:, :])
        s.dma('gpsimd', bpatT[:, :], blockpatT_d[:, :])
        s.dma('gpsimd', mlow[:, :], mlow_d[:, :])
        s.dma('gpsimd', mupp[:, :], mupp_d[:, :])
        s.dma('gpsimd', halom[:, :], halom_d[:, :])
        s.op('vector', lambda v: v.memset(onesf[:, :], 1.0))

        for li in range(6):
            if upto < 8 + 2 * li:
                break
            _inner_attn_layer(li, bpat, bpatT, mlow, mupp, halom, onesf)
            if upto < 9 + 2 * li:
                break
            _inner_ffn_layer(li, onesf, final=(li == 5))

    def _inner_attn_layer(li, bpat, bpatT, mlow, mupp, halom, onesf):
        T_ = TPC
        TW = T_ + TH
        for dc in range(NCH):
            s.op('gpsimd', lambda e, dc=dc: e.dma_start(
                out=ex_hA[:, dc * TH:(dc + 1) * TH],
                in_=stream[:, dc * T_ + T_ - TH: dc * T_ + T_]), dma=True)
        coll([ex_hA], [gx_hA[0:4 * 128, :]])
        with contextlib.ExitStack() as esA:
            wqb = _sb(nc, esA, "wqb", [128, NCH * D], BF16)
            qscs = _sb(nc, esA, "qscs", [128, 1], F32); qscws = _sb(nc, esA, "qscws", [128, 1], F32)
            kscs = _sb(nc, esA, "kscs", [128, 1], F32); kscws = _sb(nc, esA, "kscws", [128, 1], F32)
            qsrow = _sb(nc, esA, "qsrow", [1, 64], F32); ksrow = _sb(nc, esA, "ksrow", [1, 64], F32)
            csb = _sb(nc, esA, "csb", [128, 1], F32)
            crow = _sb(nc, esA, "crow", [1, 2], F32)
            xbs = _sb(nc, esA, "xbs", [128, 512], BF16)
            nrm2 = _sb(nc, esA, "nrm2", [2, 512], F32)
            qraw = _sb(nc, esA, "qraw", [128, NCH * T_], BF16)
            kraw = _sb(nc, esA, "kraw", [128, NCH * TKV], BF16)
            psb = _sb(nc, esA, "psb", [128, 384], BF16)
            numb = _sb(nc, esA, "numb", [64, 128], F32)
            zrow = _sb(nc, esA, "zrow", [1, 512], F32)
            pp = _psn(nc, esA, "pp", [128, 512])
            pnrm = _psn(nc, esA, "pnrm", [2, 512])
            pbc = _psn(nc, esA, "pbc", [128, 512])
            psT = _psn(nc, esA, "psT", [128, 384])
            pacc = _psn(nc, esA, "pacc", [128, 128])
            pzb = _psn(nc, esA, "pzb", [64, 128])

            s.dma('gpsimd', qscs[:, :], qsc6_d[:, li:li + 1], allow_slow_non_contiguous=True)
            s.dma('gpsimd', qscws[:, :], qscw6_d[:, li:li + 1], allow_slow_non_contiguous=True)
            s.dma('gpsimd', kscs[:, :], ksc6_d[:, li:li + 1], allow_slow_non_contiguous=True)
            s.dma('gpsimd', kscws[:, :], kscw6_d[:, li:li + 1], allow_slow_non_contiguous=True)
            s.dma('gpsimd', qsrow[:, :], qsr6_d[li:li + 1, :])
            s.dma('gpsimd', ksrow[:, :], ksr6_d[li:li + 1, :])
            s.op('vector', lambda v: v.reduce_max(crow[0:1, 0:1], qsrow[:, :], axis=mybir.AxisListType.X))
            s.op('vector', lambda v: v.reduce_max(crow[0:1, 1:2], ksrow[:, :], axis=mybir.AxisListType.X))
            s.op('vector', lambda v: v.tensor_mul(crow[0:1, 0:1], crow[0:1, 0:1], crow[0:1, 1:2]))
            s.op('vector', lambda v: v.tensor_scalar_mul(crow[0:1, 0:1], crow[0:1, 0:1], -8.0))
            s.op('tensor', lambda t: t.matmul(pbc[:, 0:1], onesf[:, :], crow[0:1, 0:1], start=True, stop=True))
            s.op('vector', lambda v: v.tensor_copy(csb[:, :], pbc[:, 0:1]))

            wv_ = wqkv6_d[li].rearrange("(kc p) d -> p kc d", p=128)

            with contextlib.ExitStack() as esY:
                ysb = _sb(nc, esY, "ysb", [128, NCH * TW], BF16)
                otv = _sb(nc, esY, "otv", [128, D], BF16)
                s.op('sync', lambda e: e.dma_start(
                    out=ysb[:, :].rearrange("p (c t) -> p c t", c=NCH)[:, :, 0:TH],
                    in_=gx_hA[bass.ds(EX['hrow'], 128), :].rearrange("pp (c t) -> pp c t", c=NCH)), dma=True)
                for dc in range(NCH):
                    s.op('vector', lambda v, dc=dc: v.tensor_copy(
                        ysb[:, dc * TW + TH: dc * TW + TH + T_], stream[:, dc * T_:(dc + 1) * T_]))

                def xoff(dc, st):
                    return dc * TW + st + (TH if dc < 4 else TH - 1)

                for (woff2, dst, ncolw, base) in ((0, qraw, T_, 0), (D, kraw, TKV, -256)):
                    s.dma('gpsimd', wqb[:, :].rearrange("p (kc d) -> p kc d", kc=NCH),
                          wv_[:, :, woff2: woff2 + D])
                    ntile = (ncolw + 511) // 512
                    for ti in range(ntile):
                        st = ti * 512
                        w = min(512, ncolw - st)
                        for dc in range(NCH):
                            for kc in range(NCH):
                                s.op('tensor', lambda t, kc=kc, dc=dc, w=w, st=st, base=base: t.matmul(
                                    pp[:, 0:w], wqb[:, kc * D + dc * 128: kc * D + dc * 128 + 128],
                                    ysb[:, xoff(kc, st + base): xoff(kc, st + base) + w],
                                    start=(kc == 0), stop=(kc == NCH - 1)))
                            s.op('scalar', lambda sc, dc=dc, st=st, w=w, dst=dst, ncolw=ncolw: sc.activation(
                                out=dst[:, dc * ncolw + st: dc * ncolw + st + w], in_=pp[:, 0:w], func=AF.Copy))
                s.dma('gpsimd', wqb[:, :].rearrange("p (kc d) -> p kc d", kc=NCH),
                      wv_[:, :, 2 * D: 3 * D])
                for jt in range(TKV // 128):
                    st = jt * 128 - 256
                    for qc in range(2):
                        for kc in range(NCH):
                            s.op('tensor', lambda t, kc=kc, qc=qc, st=st: t.matmul(
                                pp[:, :], ysb[:, xoff(kc, st): xoff(kc, st) + 128],
                                wqb[:, kc * D + qc * 512: kc * D + qc * 512 + 512],
                                start=(kc == 0), stop=(kc == NCH - 1)))
                        s.op('scalar', lambda sc, qc=qc: sc.activation(
                            out=otv[:, qc * 512: qc * 512 + 512], in_=pp[:, :], func=AF.Copy))
                    s.op('sync', lambda e, jt=jt: e.dma_start(
                        out=vb_i[jt * 128:(jt + 1) * 128, :], in_=otv[:, :]), dma=True)

            esR = contextlib.ExitStack()
            cqs = _sb(nc, esR, "cqs", [128, TKV], BF16)
            sqs = _sb(nc, esR, "sqs", [128, TKV], BF16)
            qsw = _sb(nc, esR, "qsw", [128, TKV], BF16)
            for (dst, ncolw, tq, ts_, scl, sclw) in ((qraw, T_, cq_d, sq_d, qscs, qscws),
                                                     (kraw, TKV, ck_d, sk_d, kscs, kscws)):
                s.dma('gpsimd', cqs[:, 0:ncolw], tq[:, 0:ncolw])
                s.dma('gpsimd', sqs[:, 0:ncolw], ts_[:, 0:ncolw])
                ntile = (ncolw + 511) // 512
                for dc in range(NCH):
                    qo = dc * ncolw
                    for ti in range(ntile):
                        st = ti * 512
                        w = min(512, ncolw - st)
                        s.op('scalar', lambda sc, qo=qo, st=st, w=w, dst=dst: sc.activation(
                            out=xbs[:, 0:w], in_=dst[:, qo + st: qo + st + w], func=AF.Square))
                        s.op('tensor', lambda t, w=w: t.matmul(
                            pnrm[:, 0:w], bpat[:, :], xbs[:, 0:w], start=True, stop=True))
                        s.op('scalar', lambda sc, w=w: sc.activation(out=nrm2[:, 0:w], in_=pnrm[:, 0:w], func=AF.Sqrt))
                        s.op('vector', lambda v, w=w: v.tensor_scalar_max(nrm2[:, 0:w], nrm2[:, 0:w], 1e-12))
                        s.op('vector', lambda v, w=w: v.reciprocal(nrm2[:, 0:w], nrm2[:, 0:w]))
                        s.op('tensor', lambda t, w=w: t.matmul(
                            pbc[:, 0:w], bpatT[:, :], nrm2[:, 0:w], start=True, stop=True))
                        s.op('vector', lambda v, qo=qo, st=st, w=w, dst=dst: v.tensor_tensor(
                            dst[:, qo + st: qo + st + w], dst[:, qo + st: qo + st + w], pbc[:, 0:w], OP.mult))
                    for g in range(4):
                        src = (g ^ 1) * 32
                        s.op('vector', lambda v, qo=qo, g=g, src=src, dst=dst, ncolw=ncolw: v.tensor_copy(
                            qsw[g * 32:(g + 1) * 32, 0:ncolw], dst[src:src + 32, qo: qo + ncolw]))
                    s.op('vector', lambda v, qo=qo, dst=dst, ncolw=ncolw, scl=scl: v.tensor_scalar_mul(
                        dst[:, qo: qo + ncolw], dst[:, qo: qo + ncolw], scl[:, :]))
                    s.op('vector', lambda v, ncolw=ncolw, sclw=sclw: v.tensor_scalar_mul(
                        qsw[:, 0:ncolw], qsw[:, 0:ncolw], sclw[:, :]))
                    s.op('vector', lambda v, qo=qo, dst=dst, ncolw=ncolw: v.tensor_tensor(
                        dst[:, qo: qo + ncolw], dst[:, qo: qo + ncolw], cqs[:, 0:ncolw], OP.mult))
                    s.op('vector', lambda v, ncolw=ncolw: v.tensor_tensor(
                        qsw[:, 0:ncolw], qsw[:, 0:ncolw], sqs[:, 0:ncolw], OP.mult))
                    s.op('vector', lambda v, qo=qo, dst=dst, ncolw=ncolw: v.tensor_add(
                        dst[:, qo: qo + ncolw], dst[:, qo: qo + ncolw], qsw[:, 0:ncolw]))

            esR.close()
            with contextlib.ExitStack() as esV:
                vtm = _sb(nc, esV, "vtm", [128, (TKV // 128) * 16 * 65], BF16)
                s.op('vector', lambda v: v.memset(
                    vtm[:, :].rearrange("p (x g) -> p x g", g=65)[:, :, 64:65], 1.0))
                for jt in range(TKV // 128):
                    s.op('gpsimd', lambda e, jt=jt: e.dma_start(
                        out=vtm[:, :].rearrange("p (j h g) -> p j h g", h=16, g=65)[:, jt, :, 0:64],
                        in_=vb_i[jt * 128:(jt + 1) * 128, :].rearrange("pp (h g) -> pp h g", g=64)), dma=True)

                aout = qraw
                for h in range(16):
                    dc = h // 2
                    hb = (h % 2) * 64
                    for qt in range(T_ // 128):
                        for d_i, kvt in enumerate((qt, qt + 1, qt + 2)):
                            s.op('tensor', lambda t, dc=dc, hb=hb, kvt=kvt, qt=qt, d_i=d_i: t.matmul(
                                psT[:, d_i * 128: d_i * 128 + 128],
                                kraw[hb:hb + 64, dc * TKV + kvt * 128: dc * TKV + kvt * 128 + 128],
                                qraw[hb:hb + 64, dc * T_ + qt * 128: dc * T_ + qt * 128 + 128],
                                start=True, stop=True, tile_position=(hb, 0)))
                        s.op('vector', lambda v: v.tensor_tensor(psT[:, 256:384], psT[:, 256:384], mlow[:, :], OP.add))
                        s.op('scalar', lambda sc: sc.activation(out=psb[:, :], in_=psT[:, :], func=AF.Exp,
                                                                scale=8.0, bias=csb[:, :]))
                        s.op('vector', lambda v: v.tensor_tensor(psb[:, 0:128], psb[:, 0:128], mupp[:, :], OP.mult))
                        if qt == 0:
                            s.op('vector', lambda v: v.tensor_scalar_mul(psb[:, 0:128], psb[:, 0:128], halom[:, 0:1]))
                            s.op('vector', lambda v: v.tensor_scalar_mul(psb[:, 128:256], psb[:, 128:256], halom[:, 1:2]))
                        elif qt == 1:
                            s.op('vector', lambda v: v.tensor_scalar_mul(psb[:, 0:128], psb[:, 0:128], halom[:, 1:2]))
                        for d_i, kvt in enumerate((qt, qt + 1, qt + 2)):
                            s.op('tensor', lambda t, kvt=kvt, h=h, d_i=d_i: t.matmul(
                                pacc[0:65, :], vtm[:, (kvt * 16 + h) * 65: (kvt * 16 + h) * 65 + 65],
                                psb[:, d_i * 128: d_i * 128 + 128],
                                start=(d_i == 0), stop=(d_i == 2)))
                        s.op('vector', lambda v: v.reciprocal(zrow[0:1, 0:128], pacc[64:65, :]))
                        s.op('tensor', lambda t: t.matmul(pzb[:, :], onesf[:, 0:64], zrow[0:1, 0:128], start=True, stop=True))
                        s.op('scalar', lambda sc: sc.activation(out=numb[:, 0:128], in_=pacc[0:64, :], func=AF.Copy))
                        s.op('vector', lambda v, dc=dc, hb=hb, qt=qt: v.tensor_tensor(
                            aout[hb:hb + 64, dc * T_ + qt * 128: dc * T_ + qt * 128 + 128],
                            numb[:, 0:128], pzb[:, :], OP.mult))

                s.dma('gpsimd', wqb[:, :].rearrange("p (kc d) -> p kc d", kc=NCH),
                      wo6_d[li].rearrange("(kc p) d -> p kc d", p=128))
                for tt in range(T_ // 512):
                    for dc in range(NCH):
                        fm_proj(p, pp, wqb, aout, dout=D, dc=dc,
                                xoff=lambda kc, tt=tt: kc * T_ + tt * 512, ncols=512, nk=NCH)
                        co = dc * T_ + tt * 512
                        s.op('vector', lambda v, co=co: v.tensor_add(
                            stream[:, co:co + 512], pp[:, :], stream[:, co:co + 512]))

    def _inner_ffn_layer(li, onesf, final=False):
        T_ = TPC
        DI = 2730
        emit_halo1()
        with contextlib.ExitStack() as esF:
            hl = _sb(nc, esF, "hl", [128, NCH], F32)
            xn = _sb(nc, esF, "xn", [128, NCH * T_], BF16)
            hsb = _sb(nc, esF, "hsb", [128, 11 * T_], BF16)
            w2s = _sb(nc, esF, "w2s", [128, 11 * D], BF16)
            w1a = _sb(nc, esF, "w1a", [128, NCH * 128], BF16)
            w1g = _sb(nc, esF, "w1g", [128, NCH * 128], BF16)
            b1s = _sb(nc, esF, "b1s", [128, 44], F32)
            b2s = _sb(nc, esF, "b2s", [128, NCH], F32)
            gs = _sb(nc, esF, "gs", [128, NCH], F32); bs = _sb(nc, esF, "bs", [128, NCH], F32)
            gbf = _sb(nc, esF, "gbf", [128, 512], BF16)
            ub = _sb(nc, esF, "ub", [128, NCH * 512], BF16)
            usq = _sb(nc, esF, "usq", [128, 512], BF16)
            t1 = _sb(nc, esF, "t1", [128, 512], F32)
            xp0 = _sb(nc, esF, "xp0", [128, 512], F32)
            onesc = _sb(nc, esF, "onesc", [128, 1], BF16)
            onesr = _sb(nc, esF, "onesr", [1, 128], F32)
            srow = _sb(nc, esF, "srow", [1, 1024], F32)
            trow = _sb(nc, esF, "trow", [1, 512], F32)
            ufin = _sb(nc, esF, "ufin", [128, NCH * 512], F32) if final else None
            pp = _psn(nc, esF, "pp", [128, 512])
            pg = _psn(nc, esF, "pg", [128, 512])
            pout = _psn(nc, esF, "pout", [128, 1024])
            pab = _psn(nc, esF, "pab", [128, 1024])
            pstat = _psn(nc, esF, "pstat", [1, 1024])

            read_halo1(hl)
            s.dma('gpsimd', b1s[:, :], b1c6_d[:, li * 44:(li + 1) * 44])
            s.dma('gpsimd', b2s[:, :], b2c6_d[:, li * NCH:(li + 1) * NCH])
            s.dma('gpsimd', gs[:, :], gc6_d[:, li * NCH:(li + 1) * NCH])
            s.dma('gpsimd', bs[:, :], bc6_d[:, li * NCH:(li + 1) * NCH])
            s.op('vector', lambda v: v.memset(onesc[:, :], 1.0))
            s.op('vector', lambda v: v.memset(onesr[:, :], 1.0))

            def sh_copy(dc, tt, dstbuf):
                co = dc * T_ + tt * 512
                if dc < 4:
                    s.op('vector', lambda v, co=co, dstbuf=dstbuf: v.tensor_copy(
                        dstbuf[:, :], stream[:, co:co + 512]))
                elif tt > 0:
                    s.op('vector', lambda v, co=co, dstbuf=dstbuf: v.tensor_copy(
                        dstbuf[:, :], stream[:, co - 1:co + 511]))
                else:
                    s.op('vector', lambda v, dc=dc, dstbuf=dstbuf: v.tensor_copy(
                        dstbuf[:, 0:1], hl[:, dc:dc + 1]))
                    s.op('vector', lambda v, co=co, dstbuf=dstbuf: v.tensor_copy(
                        dstbuf[:, 1:512], stream[:, co:co + 511]))

            NT = T_ // 512
            for tt in range(NT):
                for dc in range(NCH):
                    sh_copy(dc, tt, xp0)
                    s.op('vector', lambda v, dc=dc: v.tensor_copy(
                        ub[:, dc * 512: dc * 512 + 512], xp0[:, :]))
                for dc in range(NCH):
                    uo = dc * 512
                    s.op('scalar', lambda sc, uo=uo: sc.activation(out=usq[:, :], in_=ub[:, uo:uo + 512], func=AF.Square))
                    s.op('tensor', lambda t, dc=dc, uo=uo: t.matmul(
                        pstat[0:1, 0:512], onesc[:, :], ub[:, uo:uo + 512], start=(dc == 0), stop=(dc == NCH - 1)))
                    s.op('tensor', lambda t, dc=dc: t.matmul(
                        pstat[0:1, 512:1024], onesc[:, :], usq[:, :], start=(dc == 0), stop=(dc == NCH - 1)))
                s.op('vector', lambda v: v.tensor_scalar_mul(srow[:, :], pstat[0:1, :], 1.0 / D))
                s.op('scalar', lambda sc: sc.activation(out=trow[0:1, :], in_=srow[0:1, 0:512], func=AF.Square))
                s.op('vector', lambda v: v.tensor_sub(srow[0:1, 512:1024], srow[0:1, 512:1024], trow[0:1, :]))
                s.op('vector', lambda v: v.tensor_scalar_add(srow[0:1, 512:1024], srow[0:1, 512:1024], 1e-5))
                s.op('scalar', lambda sc: sc.activation(out=srow[0:1, 512:1024], in_=srow[0:1, 512:1024], func=AF.Sqrt))
                s.op('vector', lambda v: v.reciprocal(srow[0:1, 512:1024], srow[0:1, 512:1024]))
                s.op('vector', lambda v: v.tensor_mul(trow[0:1, :], srow[0:1, 0:512], srow[0:1, 512:1024]))
                s.op('vector', lambda v: v.tensor_scalar_mul(trow[0:1, :], trow[0:1, :], -1.0))
                s.op('tensor', lambda t: t.matmul(pab[:, 0:512], onesr[:, :], srow[0:1, 512:1024], start=True, stop=True))
                s.op('tensor', lambda t: t.matmul(pab[:, 512:1024], onesr[:, :], trow[0:1, :], start=True, stop=True))
                for dc in range(NCH):
                    co = dc * T_ + tt * 512
                    sh_copy(dc, tt, xp0)
                    s.op('vector', lambda v: v.tensor_tensor(t1[:, :], xp0[:, :], pab[:, 0:512], OP.mult))
                    s.op('vector', lambda v: v.tensor_tensor(t1[:, :], t1[:, :], pab[:, 512:1024], OP.add))
                    s.op('vector', lambda v, dc=dc, co=co: v.tensor_scalar(
                        xn[:, co:co + 512], t1[:, :], gs[:, dc:dc + 1], bs[:, dc:dc + 1], OP.mult, OP.add))

            jblocks_half = [(i * 128, 128) for i in range(10)] + [(1280, 85)]
            wview = w16_d[li].rearrange("(kc p) d -> p kc d", p=128)
            w2v = w26_d[li]
            for half in range(2):
                j0h = half * 1365
                for jbi, (jo, jw) in enumerate(jblocks_half):
                    jg = j0h + jo
                    s.dma('gpsimd', w1a[:, 0:NCH * jw].rearrange("p (kc d) -> p kc d", kc=NCH),
                          wview[:, :, jg:jg + jw])
                    s.dma('gpsimd', w1g[:, 0:NCH * jw].rearrange("p (kc d) -> p kc d", kc=NCH),
                          wview[:, :, DI + jg:DI + jg + jw])
                    for tt in range(NT):
                        for (wsb, psum) in ((w1a, pp), (w1g, pg)):
                            for kc in range(NCH):
                                s.op('tensor', lambda t, kc=kc, tt=tt, jw=jw, wsb=wsb, psum=psum: t.matmul(
                                    psum[0:jw, 0:512], wsb[:, kc * jw: kc * jw + jw],
                                    xn[:, kc * T_ + tt * 512: kc * T_ + tt * 512 + 512],
                                    start=(kc == 0), stop=(kc == NCH - 1)))
                        cga = (0 if half == 0 else 11) + jbi
                        cgg = (22 if half == 0 else 33) + jbi
                        s.op('scalar', lambda sc, jw=jw, cgg=cgg: sc.activation(
                            out=gbf[0:jw, :], in_=pg[0:jw, :], func=AF.Gelu, bias=b1s[0:jw, cgg:cgg + 1]))
                        s.op('vector', lambda v, jo=jo, tt=tt, jw=jw, cga=cga: v.scalar_tensor_tensor(
                            hsb[0:jw, (jo // 128) * T_ + tt * 512: (jo // 128) * T_ + tt * 512 + 512],
                            pp[0:jw, 0:512], b1s[0:jw, cga:cga + 1], gbf[0:jw, :], OP.add, OP.mult))
                s.dma('gpsimd', w2s[:, 0:10 * D].rearrange("p (jb d) -> p jb d", jb=10),
                      w2v[j0h:j0h + 1280, :].rearrange("(jb p) d -> p jb d", p=128))
                s.dma('gpsimd', w2s[0:85, 10 * D:11 * D], w2v[j0h + 1280:j0h + 1365, :])
                for t128 in range(T_ // 128):
                    for dc in range(NCH):
                        for jbi, (jo, jw) in enumerate(jblocks_half):
                            s.op('tensor', lambda t, jbi=jbi, jo=jo, jw=jw, dc=dc, t128=t128: t.matmul(
                                pout[:, dc * 128: dc * 128 + 128],
                                w2s[0:jw, jbi * D + dc * 128: jbi * D + dc * 128 + 128],
                                hsb[0:jw, (jo // 128) * T_ + t128 * 128: (jo // 128) * T_ + t128 * 128 + 128],
                                start=(jbi == 0), stop=(jbi == 10)))
                    for dc in range(NCH):
                        yo = dc * T_ + t128 * 128
                        if half == 0:
                            s.op('vector', lambda v, dc=dc, yo=yo: v.tensor_add(
                                stream[:, yo:yo + 128], pout[:, dc * 128: dc * 128 + 128], stream[:, yo:yo + 128]))
                        else:
                            s.op('vector', lambda v, dc=dc, yo=yo: v.scalar_tensor_tensor(
                                stream[:, yo:yo + 128], pout[:, dc * 128: dc * 128 + 128], b2s[:, dc:dc + 1],
                                stream[:, yo:yo + 128], OP.add, OP.add))

            if final:
                x3v = x3_i.rearrange("(c p) t -> p c t", p=128)
                for tt in range(NT):
                    s.dma('sync', ufin[:, :].rearrange("p (c t) -> p c t", c=NCH),
                          x3v[:, :, tt * 512:tt * 512 + 512])
                    for dc in range(NCH):
                        uo = dc * 512
                        co = dc * T_ + tt * 512
                        s.op('vector', lambda v, uo=uo, co=co: v.tensor_add(
                            ufin[:, uo:uo + 512], ufin[:, uo:uo + 512], stream[:, co:co + 512]))
                        s.op('vector', lambda v, uo=uo: v.tensor_copy(ub[:, uo:uo + 512], ufin[:, uo:uo + 512]))
                    emit_res_ln(p, u=ufin, ub=ub, usq=usq, onesc=onesc, onesr=onesr,
                                srow=srow, trow=trow, pab=pab, pstat=pstat, tt=tt, T=T_,
                                ucol=lambda dc: dc * 512)
                    for dc in range(NCH):
                        co = dc * T_ + tt * 512
                        s.op('vector', lambda v, dc=dc, co=co: v.tensor_copy(
                            stream[:, co:co + 512], ufin[:, dc * 512:dc * 512 + 512]))

    if upto >= 8:
        _inner_layers()

    if upto == 5:
        dq = p.dout("dbg_q", [D, T], BF16)
        dk = p.dout("dbg_k", [D, T], BF16)
        dv = p.dout("dbg_v", [T, D], BF16)
        with contextlib.ExitStack() as esd:
            tmp = _sb(nc, esd, "dtmp", [128, T], BF16)
            for dc in range(NCH):
                s.dma('gpsimd', tmp[:, :], ex_q1[dc * 128:(dc + 1) * 128, :])
                s.dma('sync', dq[dc * 128:(dc + 1) * 128, :], tmp[:, :])
                s.dma('gpsimd', tmp[:, :], ex_k1[dc * 128:(dc + 1) * 128, :])
                s.dma('sync', dk[dc * 128:(dc + 1) * 128, :], tmp[:, :])
            for jt in range(T // 128):
                s.dma('gpsimd', tmp[:, 0:D], ex_v1[jt * 128:(jt + 1) * 128, :])
                s.dma('sync', dv[jt * 128:(jt + 1) * 128, :], tmp[:, 0:D])

    s.dma('sync', y_d.rearrange("(c p) t -> p c t", p=128),
          stream[:, :].rearrange("p (c t) -> p c t", c=NCH))
    p.finish()
    return p


def _meta_vals(c):
    jj = c % 4
    m = np.zeros((32, 1), np.int32)
    m[0] = jj * 2048
    m[1] = (jj - 1) * 128 if jj > 0 else 4 * 128
    m[2] = jj * 128
    m[3] = jj * 256
    return m


def kernel_fused(I, upto=99):
    x = np.asarray(I["L0_attn_x"])
    xt = x.reshape(NC, TPC, D)
    pF = cached_prog("fused%d" % upto, lambda p: None) if False else None
    key = "fused%d" % upto
    if key not in _PROG_CACHE:
        _PROG_CACHE[key] = build_fused(upto)
    pF = _PROG_CACHE[key]

    bq = np.ascontiguousarray(I["L0_attn_bq"].reshape(4, 128).T)
    bk = np.ascontiguousarray(I["L0_attn_bk"].reshape(4, 128).T)
    k_ = np.arange(128)[:, None]
    m384 = (k_ <= np.arange(384)[None, :] - 128).astype(BF)
    m896 = (k_ <= np.arange(896)[None, :] - 384).astype(BF)
    kq = np.arange(128)[:, None]; qq = np.arange(128)[None, :]
    mlow = np.where(kq <= qq, 0.0, -1e30).astype(np.float32)
    mupp = (kq >= qq).astype(BF)
    bpat = np.zeros((128, 2), np.float32)
    bpat[0:64, 0] = 1.0; bpat[64:128, 1] = 1.0
    bpatTf = np.ascontiguousarray(bpat.T.astype(np.float32))
    bpatb = bpat.astype(BF)
    DI = 2730
    qsc6 = np.zeros((128, 6), np.float32); qscw6 = np.zeros((128, 6), np.float32)
    ksc6 = np.zeros((128, 6), np.float32); kscw6 = np.zeros((128, 6), np.float32)
    qsr6 = np.zeros((6, 64), np.float32); ksr6 = np.zeros((6, 64), np.float32)
    b1c6 = np.zeros((128, 6 * 44), np.float32)
    b2c6 = np.zeros((128, 6 * NCH), np.float32)
    gc6 = np.zeros((128, 6 * NCH), np.float32); bc6 = np.zeros((128, 6 * NCH), np.float32)
    for li in range(6):
        qs_i = np.asarray(I["L1_ffn_q_scale"][li]); ks_i = np.asarray(I["L1_ffn_k_scale"][li])
        qsc6[:, li] = np.tile(qs_i, 2); ksc6[:, li] = np.tile(ks_i, 2)
        qscw6[:, li] = np.tile(np.concatenate([qs_i[32:], qs_i[:32]]), 2)
        kscw6[:, li] = np.tile(np.concatenate([ks_i[32:], ks_i[:32]]), 2)
        qsr6[li] = qs_i; ksr6[li] = ks_i
        b1_i = np.asarray(I["L1_ffn_b1"][li])
        col = 0
        for part0 in (0, DI):
            for half in range(2):
                for jbi in range(11):
                    jo = jbi * 128
                    jw = min(128, 1365 - jo)
                    b1c6[0:jw, li * 44 + col] = b1_i[part0 + half * 1365 + jo: part0 + half * 1365 + jo + jw]
                    col += 1
        b2c6[:, li * NCH:(li + 1) * NCH] = I["L1_ffn_b2"][li].reshape(NCH, 128).T
        gc6[:, li * NCH:(li + 1) * NCH] = I["L1_ffn_ln_g"][li].reshape(NCH, 128).T
        bc6[:, li * NCH:(li + 1) * NCH] = I["L1_ffn_ln_b"][li].reshape(NCH, 128).T

    ims = []
    for c in range(NC):
        t0c = (c % 4) * TPC
        cosT, ssinT = _module_rope(np.arange(t0c, t0c + TPC))
        cq, sq, _, _ = _inner_rope(np.arange(t0c, t0c + TPC))
        _, _, ck, sk = _inner_rope(np.arange(t0c - 256, t0c + TPC))
        halom = np.zeros((128, 2), np.float32) if c % 4 == 0 else np.ones((128, 2), np.float32)
        ims.append({
            "xfm": _fm(xt[c]), "meta": _meta_vals(c),
            "wq": I["L0_attn_wq"], "wk": I["L0_attn_wk"], "wv": I["L0_attn_wv"],
            "bqc": bq, "bkc": bk,
            "bkr": I["L0_attn_bk"].reshape(1, 512), "bvr": I["L0_attn_bv"].reshape(1, D),
            "wo0": I["L0_attn_wo"], "boc": _colpack(I["L0_attn_bo"]),
            "mask384": m384,
            "wrT": _fm(I["L0_ffn_W_r"]), "wkT": _fm(I["L0_ffn_W_k"]), "wvT": _fm(I["L0_ffn_W_v"]),
            "murc": _colpack(I["L0_ffn_mu_r"]), "mukc": _colpack(I["L0_ffn_mu_k"]),
            "qkvT": _fm(I["L1_attn_qkv"]), "cosT": cosT, "ssinT": ssinT,
            "m896": m896, "woT1": _fm(I["L1_attn_wo"]),
            "wqkv6": I["L1_ffn_Wqkv"], "wo6": I["L1_ffn_Wo"],
            "w16": I["L1_ffn_W1"], "w26": I["L1_ffn_W2"],
            "cq": cq, "sq": sq, "ck": ck, "sk": sk,
            "qsc6": qsc6, "qscw6": qscw6, "ksc6": ksc6, "kscw6": kscw6,
            "qsr6": qsr6, "ksr6": ksr6,
            "blockpat": bpatb, "blockpatT": bpatTf,
            "mlow": mlow, "mupp": mupp, "halom": halom,
            "b1c6": b1c6, "b2c6": b2c6, "gc6": gc6, "bc6": bc6,
            "zc": np.zeros((128, NCH), np.float32),
        })
    r = run(pF, ims)
    y = [np.asarray(r[c]["y"]) for c in range(NC)]
    out = np.concatenate([a.T for a in y], axis=0).reshape(B, N, D)
    out = np.ascontiguousarray(out.astype(np.float32, copy=False))
    if _RAW:
        return out, r
    return out


# version 03
# speedup vs baseline: 1.6599x; 1.6599x over previous
import contextlib
import numpy as np
import ml_dtypes
import concourse.bass as bass
import concourse.mybir as mybir
from concourse.bass_utils import run_bass_kernel_spmd

F32 = mybir.dt.float32
BF16 = mybir.dt.bfloat16
AF = mybir.ActivationFunctionType
OP = mybir.AluOpType

BF = ml_dtypes.bfloat16

B, N, D = 2, 8192, 1024
NC = 8
TPC = N * B // NC
NCH = D // 128

ENGS = ('sync', 'gpsimd', 'vector', 'scalar', 'tensor')
SERIAL = False

class _CapDummy:
    def then_inc(self, *a, **k):
        return self

class _CapEng:
    def __init__(self):
        self.calls = []
    def __getattr__(self, name):
        def call(*a, **kw):
            self.calls.append((name, a, kw))
            return _CapDummy()
        return call

def _apname(x):
    if isinstance(x, bass.AP):
        return x.name
    n = getattr(x, 'name', None)
    if isinstance(n, str):
        return n
    return None

def _capture(fn):
    ce = _CapEng()
    fn(ce)
    reads, writes = set(), set()
    for (name, a, kw) in ce.calls:
        aps = [v for v in a if isinstance(v, bass.AP)]
        kwaps = {k: v for k, v in kw.items() if isinstance(v, bass.AP)}
        if name == 'dma_start':
            o = kwaps.get('out', aps[0] if aps else None)
            i = kwaps.get('in_', None)
            if i is None:
                rest = [v for v in aps if v is not o]
                i = rest[0] if rest else None
            if o is not None:
                writes.add(o.name)
            if i is not None:
                reads.add(i.name)
        elif name == 'matmul':
            writes.add(aps[0].name)
            for v in aps:
                reads.add(v.name)
        elif name == 'activation':
            o = kwaps.pop('out')
            writes.add(o.name)
            for v in kwaps.values():
                reads.add(v.name)
            for v in aps:
                reads.add(v.name)
        elif name == 'memset':
            writes.add(aps[0].name)
        else:
            if aps:
                writes.add(aps[0].name)
                for v in aps[1:]:
                    reads.add(v.name)
            for v in kwaps.values():
                reads.add(v.name)
    return reads, writes


class Seq:
    def __init__(self, nc):
        self.nc = nc
        self.ops = []

    def op(self, eng, fn, dma=False, reads=None, writes=None, async_=False):
        if reads is None and writes is None:
            r, w = _capture(fn)
        else:
            r = set(_apname(x) for x in (reads or []))
            w = set(_apname(x) for x in (writes or []))
            r.discard(None); w.discard(None)
        self.ops.append((eng, fn, 16 if dma else 1, r, w, dma or async_))

    def dma(self, eng, out, in_, **kw):
        self.op(eng, lambda e, out=out, in_=in_, kw=kw: e.dma_start(out=out, in_=in_, **kw),
                dma=True, reads=[in_], writes=[out])

    def barrier(self):
        self.ops.append(('BARRIER', None, 0, None, None, False))

    def build(self, blk, sems):
        if SERIAL:
            return self._build_serial(blk, sems)
        import bisect
        cnt = {e: 0 for e in ENGS}
        clock = {e: {f: 0 for f in ENGS} for e in ENGS}
        selfw = {e: 0 for e in ENGS}
        snaps = {e: ([], []) for e in ENGS}
        last_w = {}
        readers = {}
        sched = {e: [] for e in ENGS}

        def merge_from(e, f, c):
            arr, clks = snaps[f]
            i = bisect.bisect_left(arr, c)
            if i >= len(arr):
                i = len(arr) - 1
            snapc = clks[i]
            ce = clock[e]
            for g in ENGS:
                if snapc[g] > ce[g]:
                    ce[g] = snapc[g]
            if c > ce[f]:
                ce[f] = c

        for (eng, fn, inc, reads, writes, isasync) in self.ops:
            if eng == 'BARRIER':
                for e in ENGS:
                    waits = {}
                    for f in ENGS:
                        if f == e:
                            if cnt[f] > selfw[f]:
                                waits[f] = cnt[f]
                                selfw[f] = cnt[f]
                        elif cnt[f] > clock[e][f]:
                            waits[f] = cnt[f]
                    sched[e].append((None, 0, waits))
                    for f, c in waits.items():
                        if f != e:
                            merge_from(e, f, c)
                continue
            need = {}
            def register(pe, pc, pasync):
                if pe == eng and not pasync:
                    return
                if pc > need.get(pe, 0):
                    need[pe] = pc
            for r in reads:
                lw = last_w.get(r)
                if lw:
                    register(*lw)
            for w in writes:
                lw = last_w.get(w)
                if lw:
                    register(*lw)
                for pe, (pc, pasync) in readers.get(w, {}).items():
                    register(pe, pc, pasync)
            if isasync and eng in ('gpsimd', 'vector'):
                if cnt[eng] > need.get(eng, 0):
                    need[eng] = cnt[eng]
            waits = {}
            for pe, pc in need.items():
                if pe == eng:
                    if pc > selfw[eng]:
                        waits[pe] = pc
                        selfw[eng] = pc
                else:
                    if pc > clock[eng][pe]:
                        waits[pe] = pc
            for pe, pc in waits.items():
                if pe != eng:
                    merge_from(eng, pe, pc)
            cnt[eng] += inc
            clock[eng][eng] = cnt[eng]
            sched[eng].append((fn, inc, waits))
            arr, clks = snaps[eng]
            arr.append(cnt[eng])
            clks.append(dict(clock[eng]))
            for w in writes:
                last_w[w] = (eng, cnt[eng], isasync)
                readers[w] = {}
            for r in reads:
                readers.setdefault(r, {})[eng] = (cnt[eng], isasync)

        tot = dict(cnt)

        def mk(engname):
            def body(e):
                for (fn, inc, waits) in sched[engname]:
                    for f, c in waits.items():
                        e.wait_ge(sems[f], c)
                    if fn is not None:
                        fn(e).then_inc(sems[engname], inc)
                for f in ENGS:
                    if tot[f] > 0:
                        e.wait_ge(sems[f], tot[f])
            return body
        for name in ENGS:
            getattr(blk, name)(mk(name))

    def _build_serial(self, blk, sems):
        sem = sems[ENGS[0]]
        n = 0
        sched = []
        for rec in self.ops:
            if rec[0] == 'BARRIER':
                continue
            eng, fn, inc = rec[0], rec[1], rec[2]
            sched.append((eng, fn, n, inc))
            n += inc
        total = n

        def mk(engname):
            def body(e):
                for eng, fn, nwait, inc in sched:
                    if eng != engname:
                        continue
                    if nwait > 0:
                        e.wait_ge(sem, nwait)
                    fn(e).then_inc(sem, inc)
                if engname == 'sync':
                    e.wait_ge(sem, total)
            return body
        for name in ENGS:
            getattr(blk, name)(mk(name))


class Prog:
    def __init__(self, name):
        self.nc = bass.Bass("TRN2", target_bir_lowering=False)
        self.name = name
        self.s = Seq(self.nc)
        self.est = contextlib.ExitStack()
        self.inputs = {}
        self.outputs = {}

    def din(self, name, shape, dt=F32):
        t = self.nc.dram_tensor(name, list(shape), dt, kind="ExternalInput").ap()
        self.inputs[name] = t
        return t

    def dout(self, name, shape, dt=F32):
        t = self.nc.dram_tensor(name, list(shape), dt, kind="ExternalOutput").ap()
        self.outputs[name] = t
        return t

    def sb(self, name, shape, dt):
        return self.est.enter_context(self.nc.sbuf_tensor("s_" + name, list(shape), dt))

    def ps(self, name, shape, dt=F32):
        return self.est.enter_context(self.nc.psum_tensor("p_" + name, list(shape), dt))

    def finish(self):
        sems = {e: self.est.enter_context(self.nc.semaphore("sq_" + e)) for e in ENGS}
        blk = self.est.enter_context(self.nc.Block())
        self.s.build(blk, sems)
        self.est.close()
        return self.nc


_PROG_CACHE = {}

def cached_prog(key, builder):
    if key not in _PROG_CACHE:
        p = Prog(key)
        builder(p)
        p.finish()
        _PROG_CACHE[key] = p
    return _PROG_CACHE[key]


def run(p, in_maps, **kw):
    res = run_bass_kernel_spmd(p.nc, in_maps, core_ids=list(range(NC)), **kw)
    return res.results


def load_fm(p, dst, src, nch, T, eng='gpsimd'):
    v = src.rearrange("(c p) t -> p c t", p=128)
    d = dst[:, 0:nch * T].rearrange("p (c t) -> p c t", c=nch)
    p.s.dma(eng, d, v)


def load_w_fm(p, dst, src, din, dout, eng='gpsimd'):
    v = src.rearrange("(kc p) d -> p kc d", p=128)
    d = dst[:, 0:(din // 128) * dout].rearrange("p (kc d) -> p kc d", kc=din // 128)
    p.s.dma(eng, d, v)


def fm_proj(p, psum, wsb, xsb, *, dout, dc, xoff, ncols, nk, stop=True):
    for kc in range(nk):
        p.s.op('tensor', lambda t, kc=kc: t.matmul(
            psum[:, 0:ncols],
            wsb[:, kc * dout + dc * 128: kc * dout + dc * 128 + 128],
            xsb[:, xoff(kc): xoff(kc) + ncols],
            start=(kc == 0), stop=(stop and kc == nk - 1)))


def store_fm(p, dram, sbuf, nch, T):
    v = dram.rearrange("(c p) t -> p c t", p=128)
    p.s.dma('sync', v, sbuf[:, 0:nch * T].rearrange("p (c t) -> p c t", c=nch))


def store_tm(p, dram, sbuf, ntt, dout):
    v = dram.rearrange("(tt pp) d -> pp tt d", pp=128)
    p.s.dma('sync', v, sbuf[:, 0:ntt * dout].rearrange("p (tt d) -> p tt d", tt=ntt))


def build_l0_proj(p):
    T = TPC
    xfm = p.din("xfm", [D, T])
    wq = p.din("wq", [D, 512]); wk = p.din("wk", [D, 512]); wv = p.din("wv", [D, D])
    bqc = p.din("bqc", [128, 4]); bkc = p.din("bkc", [128, 4])
    bkr = p.din("bkr", [1, 512]); bvr = p.din("bvr", [1, D])
    q_fm = p.dout("q_fm", [512, T], BF16)
    k_fm = p.dout("k_fm", [512, T], BF16)
    k_tm = p.dout("k_tm", [T, 512], BF16)
    v_tm = p.dout("v_tm", [T, D], BF16)

    xb = p.sb("xb", [128, NCH * T], BF16)
    wqs = p.sb("wqs", [128, NCH * 512], BF16)
    wks = p.sb("wks", [128, NCH * 512], BF16)
    wvs = p.sb("wvs", [128, NCH * D], BF16)
    bq = p.sb("bq", [128, 4], F32); bk = p.sb("bk", [128, 4], F32)
    brk = p.sb("brk", [1, 512], BF16); brv = p.sb("brv", [1, D], BF16)
    ones1 = p.sb("ones1", [1, 128], BF16)
    of = p.sb("of", [128, 4 * T], BF16)
    ot = p.sb("ot", [128, (T // 128) * D], BF16)
    t1 = p.sb("t1", [128, 512], F32)
    t2 = p.sb("t2", [128, 512], F32)
    t3 = p.sb("t3", [128, 512], F32)
    pp = p.ps("pp", [128, 512])

    load_fm(p, xb, xfm, NCH, T)
    load_w_fm(p, wqs, wq, D, 512)
    load_w_fm(p, wks, wk, D, 512)
    load_w_fm(p, wvs, wv, D, D)
    p.s.dma('gpsimd', bq[:, :], bqc[:, :]); p.s.dma('gpsimd', bk[:, :], bkc[:, :])
    p.s.dma('gpsimd', brk[:, :], bkr[:, :]); p.s.dma('gpsimd', brv[:, :], bvr[:, :])
    p.s.op('vector', lambda v: v.memset(ones1[:, :], 1.0))

    NT = T // 512
    for (wsb, bias, dram) in ((wqs, bq, q_fm), (wks, bk, k_fm)):
        for dc in range(4):
            for tt in range(NT):
                fm_proj(p, pp, wsb, xb, dout=512, dc=dc,
                        xoff=lambda kc, tt=tt: kc * T + tt * 512, ncols=512, nk=NCH)
                p.s.op('vector', lambda v, bias=bias, dc=dc: v.tensor_scalar(
                    t1[:, :], pp[:, :], bias[:, dc:dc + 1], 0.0, OP.add, OP.min))
                p.s.op('scalar', lambda sc: sc.activation(out=t2[:, :], in_=t1[:, :], func=AF.Exp))
                p.s.op('vector', lambda v, bias=bias, dc=dc: v.tensor_scalar(
                    t3[:, :], pp[:, :], bias[:, dc:dc + 1], 0.0, OP.add, OP.max))
                p.s.op('vector', lambda v, dc=dc, tt=tt: v.tensor_add(
                    of[:, dc * T + tt * 512: dc * T + tt * 512 + 512], t2[:, :], t3[:, :]))
        store_fm(p, dram, of, 4, T)

    NT128 = T // 128
    for (wsb, dout, brow, dram, phi) in ((wks, 512, brk, k_tm, True), (wvs, D, brv, v_tm, False)):
        for tt in range(NT128):
            for qc in range(dout // 512):
                for kc in range(NCH):
                    p.s.op('tensor', lambda t, kc=kc, tt=tt, qc=qc, wsb=wsb, dout=dout: t.matmul(
                        pp[:, :], xb[:, kc * T + tt * 128: kc * T + tt * 128 + 128],
                        wsb[:, kc * dout + qc * 512: kc * dout + qc * 512 + 512],
                        start=(kc == 0), stop=False))
                p.s.op('tensor', lambda t, qc=qc, brow=brow: t.matmul(
                    pp[:, :], ones1[:, :], brow[:, qc * 512: qc * 512 + 512],
                    start=False, stop=True))
                co = tt * dout + qc * 512
                if phi:
                    p.s.op('vector', lambda v: v.tensor_scalar(t1[:, :], pp[:, :], 0.0, None, OP.min))
                    p.s.op('scalar', lambda sc: sc.activation(out=t2[:, :], in_=t1[:, :], func=AF.Exp))
                    p.s.op('vector', lambda v: v.tensor_scalar(t3[:, :], pp[:, :], 0.0, None, OP.max))
                    p.s.op('vector', lambda v, co=co: v.tensor_add(ot[:, co:co + 512], t2[:, :], t3[:, :]))
                else:
                    p.s.op('scalar', lambda sc, co=co: sc.activation(
                        out=ot[:, co:co + 512], in_=pp[:, :], func=AF.Copy))
        store_tm(p, dram, ot, NT128, dout)


CHK = 256

def build_l0_scan(p):
    S = N
    qf_d = p.din("qf", [128, S], BF16)
    kf_d = p.din("kf", [128, S], BF16)
    ktm_d = p.din("ktm", [S, 128], BF16)
    vtm_d = p.din("vtm", [S, 256], BF16)
    mask_d = p.din("mask384", [128, 384], BF16)
    o_d = p.dout("o_fm", [256, S], BF16)

    qf = p.sb("qf", [128, S], BF16)
    kf = p.sb("kf", [128, S], BF16)
    ktm = p.sb("ktm", [128, (S // 128) * 32 * 4], BF16)
    vtm = p.sb("vtm", [128, (S // 128) * 65 * 4], BF16)
    msk = p.sb("msk", [128, 384], BF16)
    pmask = p.sb("pmask", [128, 512], BF16)
    Ssb = p.sb("Ssb", [128, 65], BF16)
    numb = p.sb("numb", [64, 256], F32)
    zrow = p.sb("zrow", [1, 256], F32)
    onesf = p.sb("onesf", [1, 64], F32)
    ofm = p.sb("ofm", [128, 2 * S], BF16)
    psT = p.ps("psT", [128, 512])
    pacc = p.ps("pacc", [128, 256])
    pzb = p.ps("pzb", [64, 256])
    pst = p.ps("pst", [32, 65])

    p.s.dma('gpsimd', qf[:, :], qf_d[:, :])
    p.s.dma('gpsimd', kf[:, :], kf_d[:, :])
    for u in range(4):
        kv = ktm_d[:, u * 32:(u + 1) * 32].rearrange("(tt pp) f -> pp tt f", pp=128)
        p.s.dma('gpsimd', ktm[:, :].rearrange("p (tt u f) -> p tt u f", u=4, f=32)[:, :, u, :], kv)
        vv = vtm_d[:, u * 64:(u + 1) * 64].rearrange("(tt pp) g -> pp tt g", pp=128)
        p.s.dma('gpsimd', vtm[:, :].rearrange("p (tt u g) -> p tt u g", u=4, g=65)[:, :, u, 0:64], vv)
    p.s.dma('gpsimd', msk[:, :], mask_d[:, :])
    p.s.op('vector', lambda v: v.memset(vtm[:, :].rearrange("p (x g) -> p x g", g=65)[:, :, 64:65], 1.0))
    p.s.op('vector', lambda v: v.memset(Ssb[:, :], 0.0))
    p.s.op('vector', lambda v: v.memset(onesf[:, :], 1.0))

    nchunk = S // CHK
    for u in range(4):
        ub = u * 32
        for c in range(nchunk):
            c0 = c * CHK
            for kt in range(2):
                p.s.op('tensor', lambda t, ub=ub, c0=c0, kt=kt: t.matmul(
                    psT[:, kt * 256: kt * 256 + 256],
                    kf[ub:ub + 32, c0 + kt * 128: c0 + kt * 128 + 128],
                    qf[ub:ub + 32, c0: c0 + 256],
                    start=True, stop=True, tile_position=(ub, 0)))
            for kt in range(2):
                mv = msk[:, 128 - kt * 128: 128 - kt * 128 + 256]
                p.s.op('vector', lambda v, kt=kt, mv=mv: v.tensor_tensor(
                    pmask[:, kt * 256: kt * 256 + 256], psT[:, kt * 256: kt * 256 + 256], mv, OP.mult))
            for kt in range(2):
                tt128 = (c0 // 128) + kt
                p.s.op('tensor', lambda t, tt128=tt128, u=u, kt=kt: t.matmul(
                    pacc[0:65, :], vtm[:, (tt128 * 4 + u) * 65: (tt128 * 4 + u) * 65 + 65],
                    pmask[:, kt * 256: kt * 256 + 256],
                    start=(kt == 0), stop=False))
            p.s.op('tensor', lambda t, ub=ub, c0=c0: t.matmul(
                pacc[0:65, :], Ssb[ub:ub + 32, :], qf[ub:ub + 32, c0: c0 + 256],
                start=False, stop=True, tile_position=(ub, 0)))
            for kt in range(2):
                tt128 = (c0 // 128) + kt
                p.s.op('tensor', lambda t, tt128=tt128, u=u, c=c, kt=kt: t.matmul(
                    pst[:, :], ktm[:, (tt128 * 4 + u) * 32: (tt128 * 4 + u) * 32 + 32],
                    vtm[:, (tt128 * 4 + u) * 65: (tt128 * 4 + u) * 65 + 65],
                    start=(c == 0 and kt == 0), stop=False))
            p.s.op('vector', lambda v, ub=ub: v.tensor_copy(Ssb[ub:ub + 32, :], pst[:, :]))
            p.s.op('vector', lambda v: v.tensor_scalar_add(zrow[:, :], pacc[64:65, :], 1e-6))
            p.s.op('vector', lambda v: v.reciprocal(zrow[:, :], zrow[:, :]))
            p.s.op('tensor', lambda t: t.matmul(pzb[:, :], onesf[:, :], zrow[:, :], start=True, stop=True))
            p.s.op('scalar', lambda sc: sc.activation(out=numb[:, :], in_=pacc[0:64, :], func=AF.Copy))
            po = (u % 2) * 64
            pair = u // 2
            p.s.op('vector', lambda v, po=po, pair=pair, c0=c0: v.tensor_tensor(
                ofm[po:po + 64, pair * S + c0: pair * S + c0 + 256], numb[:, :], pzb[:, :], OP.mult))
    ov = o_d.rearrange("(pair po) t -> pair po t", po=128)
    for pair in range(2):
        p.s.dma('sync', ov[pair], ofm[:, pair * S: pair * S + S])


def emit_res_ln(p, *, u, ub, usq, onesc, onesr, srow, trow, pab, pstat, tt, T, write_ub=False, ucol=None):
    if ucol is None:
        ucol = lambda dc: dc * T + tt * 512
    """Given residual sum u (fp32, fm, all 8 chunks for token tile tt) already
    in place along with bf16 copy ub, compute LN in-place on u (and ub if write_ub)."""
    for dc in range(NCH):
        uo = dc * 512
        p.s.op('scalar', lambda sc, uo=uo: sc.activation(out=usq[:, :], in_=ub[:, uo:uo + 512], func=AF.Square))
        p.s.op('tensor', lambda t, dc=dc, uo=uo: t.matmul(
            pstat[0:1, 0:512], onesc[:, :], ub[:, uo:uo + 512], start=(dc == 0), stop=(dc == NCH - 1)))
        p.s.op('tensor', lambda t, dc=dc: t.matmul(
            pstat[0:1, 512:1024], onesc[:, :], usq[:, :], start=(dc == 0), stop=(dc == NCH - 1)))
    p.s.op('vector', lambda v: v.tensor_scalar_mul(srow[:, :], pstat[0:1, :], 1.0 / D))
    p.s.op('scalar', lambda sc: sc.activation(out=trow[0:1, :], in_=srow[0:1, 0:512], func=AF.Square))
    p.s.op('vector', lambda v: v.tensor_sub(srow[0:1, 512:1024], srow[0:1, 512:1024], trow[0:1, :]))
    p.s.op('vector', lambda v: v.tensor_scalar_add(srow[0:1, 512:1024], srow[0:1, 512:1024], 1e-5))
    p.s.op('scalar', lambda sc: sc.activation(out=srow[0:1, 512:1024], in_=srow[0:1, 512:1024], func=AF.Sqrt))
    p.s.op('vector', lambda v: v.reciprocal(srow[0:1, 512:1024], srow[0:1, 512:1024]))
    p.s.op('vector', lambda v: v.tensor_mul(trow[0:1, :], srow[0:1, 0:512], srow[0:1, 512:1024]))
    p.s.op('vector', lambda v: v.tensor_scalar_mul(trow[0:1, :], trow[0:1, :], -1.0))
    p.s.op('tensor', lambda t: t.matmul(pab[:, 0:512], onesr[:, :], srow[0:1, 512:1024], start=True, stop=True))
    p.s.op('tensor', lambda t: t.matmul(pab[:, 512:1024], onesr[:, :], trow[0:1, :], start=True, stop=True))
    for dc in range(NCH):
        co = ucol(dc)
        p.s.op('vector', lambda v, co=co: v.tensor_tensor(u[:, co:co + 512], u[:, co:co + 512], pab[:, 0:512], OP.mult))
        p.s.op('vector', lambda v, co=co: v.tensor_tensor(u[:, co:co + 512], u[:, co:co + 512], pab[:, 512:1024], OP.add))
        if write_ub:
            p.s.op('vector', lambda v, dc=dc, co=co: v.tensor_copy(ub[:, dc * 512:dc * 512 + 512], u[:, co:co + 512]))


def build_proj_res_ln(p):
    T = TPC
    afm = p.din("afm", [D, T], BF16)
    xfm = p.din("xfm", [D, T])
    wo = p.din("wo", [D, D])
    boc = p.din("boc", [128, NCH])
    y_d = p.dout("y", [D, T])

    ab = p.sb("ab", [128, NCH * T], BF16)
    xs = p.sb("xs", [128, NCH * T], F32)
    wos = p.sb("wos", [128, NCH * D], BF16)
    bo = p.sb("bo", [128, NCH], F32)
    u = p.sb("u", [128, NCH * T], F32)
    ub = p.sb("ub", [128, NCH * 512], BF16)
    usq = p.sb("usq", [128, 512], BF16)
    onesc = p.sb("onesc", [128, 1], BF16)
    onesr = p.sb("onesr", [1, 128], F32)
    srow = p.sb("srow", [1, 1024], F32)
    trow = p.sb("trow", [1, 512], F32)
    pp = p.ps("pp", [128, 512])
    pab = p.ps("pab", [128, 1024])
    pstat = p.ps("pstat", [1, 1024])

    load_fm(p, ab, afm, NCH, T)
    load_fm(p, xs, xfm, NCH, T, eng='sync')
    load_w_fm(p, wos, wo, D, D)
    p.s.dma('gpsimd', bo[:, :], boc[:, :])
    p.s.op('vector', lambda v: v.memset(onesc[:, :], 1.0))
    p.s.op('vector', lambda v: v.memset(onesr[:, :], 1.0))

    NT = T // 512
    for tt in range(NT):
        for dc in range(NCH):
            fm_proj(p, pp, wos, ab, dout=D, dc=dc,
                    xoff=lambda kc, tt=tt: kc * T + tt * 512, ncols=512, nk=NCH)
            co = dc * T + tt * 512
            p.s.op('vector', lambda v, dc=dc, co=co: v.scalar_tensor_tensor(
                u[:, co:co + 512], pp[:, :], bo[:, dc:dc + 1], xs[:, co:co + 512], OP.add, OP.add))
            p.s.op('vector', lambda v, dc=dc, co=co: v.tensor_copy(ub[:, dc * 512:dc * 512 + 512], u[:, co:co + 512]))
        emit_res_ln(p, u=u, ub=ub, usq=usq, onesc=onesc, onesr=onesr,
                    srow=srow, trow=trow, pab=pab, pstat=pstat, tt=tt, T=T)
    store_fm(p, y_d, u, NCH, T)


def build_l0_ffn(p):
    T = TPC
    T1 = T + 1
    xfm = p.din("xfm", [D, T1])
    wrT = p.din("wrT", [D, D]); wkT = p.din("wkT", [D, D]); wvT = p.din("wvT", [D, D])
    murc = p.din("murc", [128, NCH]); mukc = p.din("mukc", [128, NCH])
    y_d = p.dout("y", [D, T])

    xs = p.sb("xs", [128, NCH * T1], F32)
    wr = p.sb("wr", [128, NCH * D], BF16)
    wk = p.sb("wk", [128, NCH * D], BF16)
    wv = p.sb("wv", [128, NCH * D], BF16)
    mur = p.sb("mur", [128, NCH], F32); murM = p.sb("murM", [128, NCH], F32)
    muk = p.sb("muk", [128, NCH], F32); mukM = p.sb("mukM", [128, NCH], F32)
    xr = p.sb("xr", [128, NCH * 512], BF16)
    rr = p.sb("rr", [128, NCH * 512], BF16)
    ka = p.sb("ka", [128, NCH * 512], BF16)
    u = p.sb("u", [128, NCH * 512], F32)
    ub = p.sb("ub", [128, NCH * 512], BF16)
    usq = p.sb("usq", [128, 512], BF16)
    t1 = p.sb("t1", [128, 512], F32)
    onesc = p.sb("onesc", [128, 1], BF16)
    onesr = p.sb("onesr", [1, 128], F32)
    srow = p.sb("srow", [1, 1024], F32)
    trow = p.sb("trow", [1, 512], F32)
    pp = p.ps("pp", [128, 512])
    pab = p.ps("pab", [128, 1024])
    pstat = p.ps("pstat", [1, 1024])

    load_fm(p, xs, xfm, NCH, T1, eng='sync')
    load_w_fm(p, wr, wrT, D, D)
    load_w_fm(p, wk, wkT, D, D)
    load_w_fm(p, wv, wvT, D, D)
    p.s.dma('gpsimd', mur[:, :], murc[:, :]); p.s.dma('gpsimd', muk[:, :], mukc[:, :])
    p.s.op('vector', lambda v: v.tensor_scalar(murM[:, :], mur[:, :], -1.0, 1.0, OP.mult, OP.add))
    p.s.op('vector', lambda v: v.tensor_scalar(mukM[:, :], muk[:, :], -1.0, 1.0, OP.mult, OP.add))
    p.s.op('vector', lambda v: v.memset(onesc[:, :], 1.0))
    p.s.op('vector', lambda v: v.memset(onesr[:, :], 1.0))

    NT = T // 512
    for tt in range(NT):
        for dc in range(NCH):
            xo = dc * T1 + 1 + tt * 512
            po = dc * T1 + tt * 512
            p.s.op('vector', lambda v, dc=dc, xo=xo: v.tensor_scalar_mul(
                t1[:, :], xs[:, xo:xo + 512], mur[:, dc:dc + 1]))
            p.s.op('vector', lambda v, dc=dc, po=po: v.scalar_tensor_tensor(
                xr[:, dc * 512:dc * 512 + 512], xs[:, po:po + 512], murM[:, dc:dc + 1],
                t1[:, :], OP.mult, OP.add))
        for dc in range(NCH):
            fm_proj(p, pp, wr, xr, dout=D, dc=dc, xoff=lambda kc: kc * 512, ncols=512, nk=NCH)
            p.s.op('scalar', lambda sc, dc=dc: sc.activation(
                out=rr[:, dc * 512:dc * 512 + 512], in_=pp[:, :], func=AF.Sigmoid))
        for dc in range(NCH):
            xo = dc * T1 + 1 + tt * 512
            po = dc * T1 + tt * 512
            p.s.op('vector', lambda v, dc=dc, xo=xo: v.tensor_scalar_mul(
                t1[:, :], xs[:, xo:xo + 512], muk[:, dc:dc + 1]))
            p.s.op('vector', lambda v, dc=dc, po=po: v.scalar_tensor_tensor(
                xr[:, dc * 512:dc * 512 + 512], xs[:, po:po + 512], mukM[:, dc:dc + 1],
                t1[:, :], OP.mult, OP.add))
        for dc in range(NCH):
            fm_proj(p, pp, wk, xr, dout=D, dc=dc, xoff=lambda kc: kc * 512, ncols=512, nk=NCH)
            p.s.op('vector', lambda v: v.tensor_scalar(t1[:, :], pp[:, :], 0.0, None, OP.max))
            p.s.op('scalar', lambda sc, dc=dc: sc.activation(
                out=ka[:, dc * 512:dc * 512 + 512], in_=t1[:, :], func=AF.Square))
        for dc in range(NCH):
            fm_proj(p, pp, wv, ka, dout=D, dc=dc, xoff=lambda kc: kc * 512, ncols=512, nk=NCH)
            co = dc * 512
            xo = dc * T1 + 1 + tt * 512
            p.s.op('vector', lambda v, dc=dc: v.tensor_tensor(
                t1[:, :], rr[:, dc * 512:dc * 512 + 512], pp[:, :], OP.mult))
            p.s.op('vector', lambda v, co=co, xo=xo: v.tensor_add(
                u[:, co:co + 512], t1[:, :], xs[:, xo:xo + 512]))
            p.s.op('vector', lambda v, co=co: v.tensor_copy(
                ub[:, co:co + 512], u[:, co:co + 512]))
        emit_res_ln(p, u=u, ub=ub, usq=usq, onesc=onesc, onesr=onesr,
                    srow=srow, trow=trow, pab=pab, pstat=pstat, tt=tt, T=T,
                    ucol=lambda dc: dc * 512)
        p.s.dma('sync', y_d.rearrange("(c p) t -> p c t", p=128)[:, :, tt * 512:tt * 512 + 512],
                u[:, :].rearrange("p (c t) -> p c t", c=NCH))


def build_l1_proj(p):
    T = TPC
    xfm = p.din("xfm", [D, T])
    qkvT = p.din("qkvT", [D, 3 * D])
    cosT = p.din("cosT", [128, T], BF16)
    ssinT = p.din("ssinT", [128, T], BF16)
    qr_d = p.dout("qr_fm", [D, T], BF16)
    kr_d = p.dout("kr_fm", [D, T], BF16)
    v_d = p.dout("v_tm", [T, D], BF16)

    xb = p.sb("xb", [128, NCH * T], BF16)
    wq = p.sb("wq", [128, NCH * 2 * D], BF16)
    wv = p.sb("wv", [128, NCH * D], BF16)
    cosb = p.sb("cosb", [128, T], BF16)
    ssin = p.sb("ssin", [128, T], BF16)
    qraw = p.sb("qraw", [128, NCH * T], BF16)
    qsw = p.sb("qsw", [128, T], BF16)
    t1b = p.sb("t1b", [128, T], BF16)
    orope = p.sb("orope", [128, NCH * T], BF16)
    ot = p.sb("ot", [128, (T // 128) * D], BF16)
    pp = p.ps("pp", [128, 512])

    load_fm(p, xb, xfm, NCH, T)
    qk = qkvT.rearrange("(kc p) d -> p kc d", p=128)
    p.s.dma('gpsimd', wq[:, :].rearrange("p (kc d) -> p kc d", kc=NCH), qk[:, :, 0:2 * D])
    p.s.dma('gpsimd', wv[:, :].rearrange("p (kc d) -> p kc d", kc=NCH), qk[:, :, 2 * D:3 * D])
    p.s.dma('gpsimd', cosb[:, :], cosT[:, :])
    p.s.dma('gpsimd', ssin[:, :], ssinT[:, :])

    NT = T // 512
    for (woff, dram) in ((0, qr_d), (D, kr_d)):
        for dc in range(NCH):
            for tt in range(NT):
                for kc in range(NCH):
                    p.s.op('tensor', lambda t, kc=kc, dc=dc, tt=tt, woff=woff: t.matmul(
                        pp[:, :], wq[:, kc * 2 * D + woff + dc * 128: kc * 2 * D + woff + dc * 128 + 128],
                        xb[:, kc * T + tt * 512: kc * T + tt * 512 + 512],
                        start=(kc == 0), stop=(kc == NCH - 1)))
                p.s.op('scalar', lambda sc, dc=dc, tt=tt: sc.activation(
                    out=qraw[:, dc * T + tt * 512: dc * T + tt * 512 + 512], in_=pp[:, :], func=AF.Copy))
        for dc in range(NCH):
            qo = dc * T
            for g in range(4):
                src = (g ^ 1) * 32
                p.s.op('vector', lambda v, qo=qo, g=g, src=src: v.tensor_copy(
                    qsw[g * 32:(g + 1) * 32, :], qraw[src:src + 32, qo:qo + T]))
            p.s.op('vector', lambda v, qo=qo: v.tensor_tensor(
                t1b[:, :], qraw[:, qo:qo + T], cosb[:, :], OP.mult))
            p.s.op('vector', lambda v: v.tensor_tensor(qsw[:, :], qsw[:, :], ssin[:, :], OP.mult))
            p.s.op('vector', lambda v, qo=qo: v.tensor_add(orope[:, qo:qo + T], t1b[:, :], qsw[:, :]))
        store_fm(p, dram, orope, NCH, T)

    NT128 = T // 128
    for tt in range(NT128):
        for qc in range(2):
            for kc in range(NCH):
                p.s.op('tensor', lambda t, kc=kc, tt=tt, qc=qc: t.matmul(
                    pp[:, :], xb[:, kc * T + tt * 128: kc * T + tt * 128 + 128],
                    wv[:, kc * D + qc * 512: kc * D + qc * 512 + 512],
                    start=(kc == 0), stop=(kc == NCH - 1)))
            co = tt * D + qc * 512
            p.s.op('scalar', lambda sc, co=co: sc.activation(
                out=ot[:, co:co + 512], in_=pp[:, :], func=AF.Copy))
    store_tm(p, v_d, ot, NT128, D)


def build_l1_attn(p):
    S = N
    qf_d = p.din("qf", [256, S], BF16)
    kf_d = p.din("kf", [256, S], BF16)
    vtm_d = p.din("vtm", [S, 256], BF16)
    m896_d = p.din("m896", [128, 896], BF16)
    o_d = p.dout("o_fm", [256, S], BF16)

    qf = [p.sb("qf%d" % i, [128, S], BF16) for i in range(2)]
    kf = [p.sb("kf%d" % i, [128, S], BF16) for i in range(2)]
    vtm = p.sb("vtm", [128, (S // 128) * 65 * 4], BF16)
    m896 = p.sb("m896", [128, 896], BF16)
    psb = p.sb("psb", [128, 1024], BF16)
    numb = p.sb("numb", [64, 512], F32)
    zrow = p.sb("zrow", [1, 512], F32)
    onesf = p.sb("onesf", [1, 64], F32)
    ofm = [p.sb("ofm%d" % i, [128, S], BF16) for i in range(2)]
    psT = p.ps("psT", [128, 1024])
    pout = p.ps("pout", [128, 512])
    pzb = p.ps("pzb", [64, 512])

    qv = qf_d.rearrange("(pair po) t -> pair po t", po=128)
    kv = kf_d.rearrange("(pair po) t -> pair po t", po=128)
    for i in range(2):
        p.s.dma('gpsimd', qf[i][:, :], qv[i])
        p.s.dma('gpsimd', kf[i][:, :], kv[i])
    for u in range(4):
        vv = vtm_d[:, u * 64:(u + 1) * 64].rearrange("(tt pp) g -> pp tt g", pp=128)
        p.s.dma('gpsimd', vtm[:, :].rearrange("p (tt u g) -> p tt u g", u=4, g=65)[:, :, u, 0:64], vv)
    p.s.dma('gpsimd', m896[:, :], m896_d[:, :])
    p.s.op('vector', lambda v: v.memset(vtm[:, :].rearrange("p (x g) -> p x g", g=65)[:, :, 64:65], 1.0))
    p.s.op('vector', lambda v: v.memset(onesf[:, :], 1.0))

    NQT = S // 512
    for u in range(4):
        pair, po = u // 2, (u % 2) * 64
        qs = qf[pair]; ks = kf[pair]
        for qt in range(NQT):
            nkt = 4 * qt + 4
            first = True
            for g0 in range(0, nkt, 2):
                gn = min(2, nkt - g0)
                for gi in range(gn):
                    kt = g0 + gi
                    p.s.op('tensor', lambda t, po=po, kt=kt, qt=qt, gi=gi, qs=qs, ks=ks: t.matmul(
                        psT[:, gi * 512: gi * 512 + 512],
                        ks[po:po + 64, kt * 128: kt * 128 + 128],
                        qs[po:po + 64, qt * 512: qt * 512 + 512],
                        start=True, stop=True, tile_position=(po, 0)))
                p.s.op('scalar', lambda sc, gn=gn: sc.activation(
                    out=psb[:, 0:gn * 512], in_=psT[:, 0:gn * 512], func=AF.Exp, scale=0.125))
                for gi in range(gn):
                    kt = g0 + gi
                    off = qt * 512 - kt * 128
                    if off <= 0:
                        p.s.op('vector', lambda v, gi=gi, off=off: v.tensor_tensor(
                            psb[:, gi * 512: gi * 512 + 512], psb[:, gi * 512: gi * 512 + 512],
                            m896[:, off + 384: off + 384 + 512], OP.mult))
                for gi in range(gn):
                    kt = g0 + gi
                    p.s.op('tensor', lambda t, kt=kt, u=u, gi=gi, first=first and gi == 0: t.matmul(
                        pout[0:65, :], vtm[:, (kt * 4 + u) * 65: (kt * 4 + u) * 65 + 65],
                        psb[:, gi * 512: gi * 512 + 512],
                        start=first, stop=False))
                    first = False
            p.s.op('vector', lambda v: v.reciprocal(zrow[:, :], pout[64:65, :]))
            p.s.op('tensor', lambda t: t.matmul(pzb[:, :], onesf[:, :], zrow[:, :], start=True, stop=True))
            p.s.op('scalar', lambda sc: sc.activation(out=numb[:, :], in_=pout[0:64, :], func=AF.Copy))
            p.s.op('vector', lambda v, pair=pair, po=po, qt=qt: v.tensor_tensor(
                ofm[pair][po:po + 64, qt * 512: qt * 512 + 512], numb[:, :], pzb[:, :], OP.mult))
    ov = o_d.rearrange("(pair po) t -> pair po t", po=128)
    for pair in range(2):
        p.s.dma('sync', ov[pair], ofm[pair][:, :])


TH = 257
TKV = TPC + 256

def build_inner_attn(p):
    T = TPC
    TW = T + TH
    yfm = p.din("yfm", [D, TW])
    wqkv = p.din("wqkv", [D, 3 * D])
    wo = p.din("wo", [D, D])
    cq = p.din("cq", [128, T], BF16)
    sq = p.din("sq", [128, T], BF16)
    ck = p.din("ck", [128, TKV], BF16)
    sk = p.din("sk", [128, TKV], BF16)
    qsc = p.din("qsc", [128, 1]); qscw = p.din("qscw", [128, 1])
    ksc = p.din("ksc", [128, 1]); kscw = p.din("kscw", [128, 1])
    qsr = p.din("qsr", [1, 64]); ksr = p.din("ksr", [1, 64])
    blockpat = p.din("blockpat", [128, 2], BF16)
    blockpatT = p.din("blockpatT", [2, 128])
    mlow_d = p.din("mlow", [128, 128], F32)
    mupp_d = p.din("mupp", [128, 128], BF16)
    halom_d = p.din("halom", [128, 2])
    y_d = p.dout("y", [D, T])

    ysb = p.sb("ysb", [128, NCH * TW], BF16)
    ubig = p.sb("ubig", [128, NCH * 512], F32)
    wqb = p.sb("wqb", [128, NCH * D], BF16)
    cqs = p.sb("cqs", [128, T], BF16); sqs = p.sb("sqs", [128, T], BF16)
    cks = p.sb("cks", [128, TKV], BF16); sks = p.sb("sks", [128, TKV], BF16)
    qscs = p.sb("qscs", [128, 1], F32); qscws = p.sb("qscws", [128, 1], F32)
    kscs = p.sb("kscs", [128, 1], F32); kscws = p.sb("kscws", [128, 1], F32)
    qsrow = p.sb("qsrow", [1, 64], F32); ksrow = p.sb("ksrow", [1, 64], F32)
    bpat = p.sb("bpat", [128, 2], BF16)
    bpatT = p.sb("bpatT", [2, 128], F32)
    nrm2 = p.sb("nrm2", [2, 512], F32)
    mlow = p.sb("mlow", [128, 128], F32)
    mupp = p.sb("mupp", [128, 128], BF16)
    halom = p.sb("halom", [128, 2], F32)
    csb = p.sb("csb", [128, 1], F32)
    crow = p.sb("crow", [1, 1], F32)
    xbs = p.sb("xbs", [128, 512], BF16)
    qraw = p.sb("qraw", [128, NCH * T], BF16)
    kraw = p.sb("kraw", [128, NCH * TKV], BF16)
    qsw = p.sb("qsw", [128, TKV], BF16)
    vtm = p.sb("vtm", [128, (TKV // 128) * 16 * 65], BF16)
    psb = p.sb("psb", [128, 384], BF16)
    numb = p.sb("numb", [64, 128], F32)
    zrow = p.sb("zrow", [1, 512], F32)
    onesf = p.sb("onesf", [1, 128], F32)
    pp = p.ps("pp", [128, 512])
    pnrm = p.ps("pnrm", [2, 512])
    pbc = p.ps("pbc", [128, 512])
    psT = p.ps("psT", [128, 384])
    pacc = p.ps("pacc", [128, 128])
    pzb = p.ps("pzb", [64, 128])

    load_fm(p, ysb, yfm, NCH, TW)
    for (d_, s_) in ((cq, cqs), (sq, sqs), (ck, cks), (sk, sks), (qsc, qscs), (qscw, qscws),
                     (ksc, kscs), (kscw, kscws), (qsr, qsrow), (ksr, ksrow),
                     (blockpat, bpat), (blockpatT, bpatT), (mlow_d, mlow), (mupp_d, mupp),
                     (halom_d, halom)):
        p.s.dma('gpsimd', s_[:, :], d_[:, :])
    p.s.op('vector', lambda v: v.memset(onesf[:, :], 1.0))
    p.s.op('vector', lambda v: v.memset(vtm[:, :].rearrange("p (x g) -> p x g", g=65)[:, :, 64:65], 1.0))

    p.s.op('vector', lambda v: v.reduce_max(crow[:, :], qsrow[:, :], axis=mybir.AxisListType.X))
    p.s.op('vector', lambda v: v.reduce_max(zrow[0:1, 0:1], ksrow[:, :], axis=mybir.AxisListType.X))
    p.s.op('vector', lambda v: v.tensor_mul(crow[:, :], crow[:, :], zrow[0:1, 0:1]))
    p.s.op('vector', lambda v: v.tensor_scalar_mul(crow[:, :], crow[:, :], -8.0))
    p.s.op('tensor', lambda t: t.matmul(pbc[:, 0:1], onesf[:, :], crow[:, :], start=True, stop=True))
    p.s.op('vector', lambda v: v.tensor_copy(csb[:, :], pbc[:, 0:1]))

    def xoff(dc, st):
        return dc * TW + st + (TH if dc < 4 else TH - 1)

    wv_ = wqkv.rearrange("(kc p) d -> p kc d", p=128)
    def load_wblock(woff):
        p.s.dma('gpsimd', wqb[:, :].rearrange("p (kc d) -> p kc d", kc=NCH),
                wv_[:, :, woff:woff + D])

    for (woff, dst, ncolw, base) in ((0, qraw, T, 0), (D, kraw, TKV, -256)):
        load_wblock(woff)
        ntile = (ncolw + 511) // 512
        for ti in range(ntile):
            st = ti * 512
            w = min(512, ncolw - st)
            for dc in range(NCH):
                for kc in range(NCH):
                    p.s.op('tensor', lambda t, kc=kc, dc=dc, w=w, st=st, base=base: t.matmul(
                        pp[:, 0:w], wqb[:, kc * D + dc * 128: kc * D + dc * 128 + 128],
                        ysb[:, xoff(kc, st + base): xoff(kc, st + base) + w],
                        start=(kc == 0), stop=(kc == NCH - 1)))
                p.s.op('scalar', lambda sc, dc=dc, st=st, w=w, dst=dst, ncolw=ncolw: sc.activation(
                    out=dst[:, dc * ncolw + st: dc * ncolw + st + w], in_=pp[:, 0:w], func=AF.Copy))

    load_wblock(2 * D)
    for jt in range(TKV // 128):
        st = jt * 128 - 256
        for qc in range(2):
            for kc in range(NCH):
                p.s.op('tensor', lambda t, kc=kc, qc=qc, st=st: t.matmul(
                    pp[:, :], ysb[:, xoff(kc, st): xoff(kc, st) + 128],
                    wqb[:, kc * D + qc * 512: kc * D + qc * 512 + 512],
                    start=(kc == 0), stop=(kc == NCH - 1)))
            for hh in range(8):
                h = qc * 8 + hh
                p.s.op('scalar', lambda sc, jt=jt, h=h, hh=hh: sc.activation(
                    out=vtm[:, (jt * 16 + h) * 65: (jt * 16 + h) * 65 + 64],
                    in_=pp[:, hh * 64: hh * 64 + 64], func=AF.Copy))

    for (dst, ncolw, cosb, sinb, scl, sclw) in ((qraw, T, cqs, sqs, qscs, qscws),
                                                (kraw, TKV, cks, sks, kscs, kscws)):
        ntile = (ncolw + 511) // 512
        for dc in range(NCH):
            qo = dc * ncolw
            for ti in range(ntile):
                st = ti * 512
                w = min(512, ncolw - st)
                p.s.op('scalar', lambda sc, qo=qo, st=st, w=w, dst=dst: sc.activation(
                    out=xbs[:, 0:w], in_=dst[:, qo + st: qo + st + w], func=AF.Square))
                p.s.op('tensor', lambda t, w=w: t.matmul(
                    pnrm[:, 0:w], bpat[:, :], xbs[:, 0:w], start=True, stop=True))
                p.s.op('scalar', lambda sc, w=w: sc.activation(out=nrm2[:, 0:w], in_=pnrm[:, 0:w], func=AF.Sqrt))
                p.s.op('vector', lambda v, w=w: v.tensor_scalar_max(nrm2[:, 0:w], nrm2[:, 0:w], 1e-12))
                p.s.op('vector', lambda v, w=w: v.reciprocal(nrm2[:, 0:w], nrm2[:, 0:w]))
                p.s.op('tensor', lambda t, w=w: t.matmul(
                    pbc[:, 0:w], bpatT[:, :], nrm2[:, 0:w], start=True, stop=True))
                p.s.op('vector', lambda v, qo=qo, st=st, w=w, dst=dst: v.tensor_tensor(
                    dst[:, qo + st: qo + st + w], dst[:, qo + st: qo + st + w], pbc[:, 0:w], OP.mult))
            for g in range(4):
                src = (g ^ 1) * 32
                p.s.op('vector', lambda v, qo=qo, g=g, src=src, dst=dst, ncolw=ncolw: v.tensor_copy(
                    qsw[g * 32:(g + 1) * 32, 0:ncolw], dst[src:src + 32, qo: qo + ncolw]))
            p.s.op('vector', lambda v, qo=qo, dst=dst, ncolw=ncolw, scl=scl: v.tensor_scalar_mul(
                dst[:, qo: qo + ncolw], dst[:, qo: qo + ncolw], scl[:, :]))
            p.s.op('vector', lambda v, ncolw=ncolw, sclw=sclw: v.tensor_scalar_mul(
                qsw[:, 0:ncolw], qsw[:, 0:ncolw], sclw[:, :]))
            p.s.op('vector', lambda v, qo=qo, dst=dst, ncolw=ncolw, cosb=cosb: v.tensor_tensor(
                dst[:, qo: qo + ncolw], dst[:, qo: qo + ncolw], cosb[:, 0:ncolw], OP.mult))
            p.s.op('vector', lambda v, ncolw=ncolw, sinb=sinb: v.tensor_tensor(
                qsw[:, 0:ncolw], qsw[:, 0:ncolw], sinb[:, 0:ncolw], OP.mult))
            p.s.op('vector', lambda v, qo=qo, dst=dst, ncolw=ncolw: v.tensor_add(
                dst[:, qo: qo + ncolw], dst[:, qo: qo + ncolw], qsw[:, 0:ncolw]))

    aout = qraw
    for h in range(16):
        dc = h // 2
        hb = (h % 2) * 64
        for qt in range(T // 128):
            for d_i, kvt in enumerate((qt, qt + 1, qt + 2)):
                p.s.op('tensor', lambda t, dc=dc, hb=hb, kvt=kvt, qt=qt, d_i=d_i: t.matmul(
                    psT[:, d_i * 128: d_i * 128 + 128],
                    kraw[hb:hb + 64, dc * TKV + kvt * 128: dc * TKV + kvt * 128 + 128],
                    qraw[hb:hb + 64, dc * T + qt * 128: dc * T + qt * 128 + 128],
                    start=True, stop=True, tile_position=(hb, 0)))
            p.s.op('vector', lambda v: v.tensor_tensor(psT[:, 256:384], psT[:, 256:384], mlow[:, :], OP.add))
            p.s.op('scalar', lambda sc: sc.activation(out=psb[:, :], in_=psT[:, :], func=AF.Exp,
                                                      scale=8.0, bias=csb[:, :]))
            p.s.op('vector', lambda v: v.tensor_tensor(psb[:, 0:128], psb[:, 0:128], mupp[:, :], OP.mult))
            if qt == 0:
                p.s.op('vector', lambda v: v.tensor_scalar_mul(psb[:, 0:128], psb[:, 0:128], halom[:, 0:1]))
                p.s.op('vector', lambda v: v.tensor_scalar_mul(psb[:, 128:256], psb[:, 128:256], halom[:, 1:2]))
            elif qt == 1:
                p.s.op('vector', lambda v: v.tensor_scalar_mul(psb[:, 0:128], psb[:, 0:128], halom[:, 1:2]))
            for d_i, kvt in enumerate((qt, qt + 1, qt + 2)):
                p.s.op('tensor', lambda t, kvt=kvt, h=h, d_i=d_i: t.matmul(
                    pacc[0:65, :], vtm[:, (kvt * 16 + h) * 65: (kvt * 16 + h) * 65 + 65],
                    psb[:, d_i * 128: d_i * 128 + 128],
                    start=(d_i == 0), stop=(d_i == 2)))
            p.s.op('vector', lambda v: v.reciprocal(zrow[0:1, 0:128], pacc[64:65, :]))
            p.s.op('tensor', lambda t: t.matmul(pzb[:, :], onesf[:, 0:64], zrow[0:1, 0:128], start=True, stop=True))
            p.s.op('scalar', lambda sc: sc.activation(out=numb[:, 0:128], in_=pacc[0:64, :], func=AF.Copy))
            p.s.op('vector', lambda v, dc=dc, hb=hb, qt=qt: v.tensor_tensor(
                aout[hb:hb + 64, dc * T + qt * 128: dc * T + qt * 128 + 128], numb[:, 0:128], pzb[:, :], OP.mult))

    load_w_fm(p, wqb, wo, D, D)
    yin = yfm.rearrange("(c p) t -> p c t", p=128)
    yv = y_d.rearrange("(c p) t -> p c t", p=128)
    NT = T // 512
    for tt in range(NT):
        p.s.dma('sync', ubig[:, :].rearrange("p (c t) -> p c t", c=NCH),
                yin[:, :, TH + tt * 512: TH + tt * 512 + 512])
        for dc in range(NCH):
            fm_proj(p, pp, wqb, aout, dout=D, dc=dc,
                    xoff=lambda kc, tt=tt: kc * T + tt * 512, ncols=512, nk=NCH)
            p.s.op('vector', lambda v, dc=dc: v.tensor_add(
                ubig[:, dc * 512: dc * 512 + 512], pp[:, :], ubig[:, dc * 512: dc * 512 + 512]))
        p.s.dma('sync', yv[:, :, tt * 512: tt * 512 + 512],
                ubig[:, :].rearrange("p (c t) -> p c t", c=NCH))


def build_inner_ffn_plain(p):
    build_inner_ffn(p, final=False)

def build_inner_ffn_final(p):
    build_inner_ffn(p, final=True)

def build_inner_ffn(p, final=False):
    T = TPC
    TW = T + 1
    DI = 2730
    yfm = p.din("yfm", [D, TW])
    w1 = p.din("w1", [D, 2 * DI])
    w2 = p.din("w2", [DI, D])
    b1ac = p.din("b1ac", [128, 11])
    b1gc = p.din("b1gc", [128, 11])
    b1ac2 = p.din("b1ac2", [128, 11])
    b1gc2 = p.din("b1gc2", [128, 11])
    b2c = p.din("b2c", [128, NCH])
    gc = p.din("gc", [128, NCH]); bc = p.din("bc", [128, NCH])
    if final:
        x3fm = p.din("x3fm", [D, T])
    y_d = p.dout("y", [D, T])

    ys = p.sb("ys", [128, NCH * TW], F32)
    xn = p.sb("xn", [128, NCH * T], BF16)
    hsb = p.sb("hsb", [128, 11 * T], BF16)
    w2s = p.sb("w2s", [128, 11 * D], BF16)
    w1a = p.sb("w1a", [128, NCH * 128], BF16)
    w1g = p.sb("w1g", [128, NCH * 128], BF16)
    b1a = [p.sb("b1a0", [128, 11], F32), p.sb("b1a1", [128, 11], F32)]
    b1g = [p.sb("b1g0", [128, 11], F32), p.sb("b1g1", [128, 11], F32)]
    b2s = p.sb("b2s", [128, NCH], F32)
    gs = p.sb("gs", [128, NCH], F32); bs = p.sb("bs", [128, NCH], F32)
    gbf = p.sb("gbf", [128, 512], BF16)
    ub = p.sb("ub", [128, NCH * 512], BF16)
    usq = p.sb("usq", [128, 512], BF16)
    t1 = p.sb("t1", [128, 512], F32)
    onesc = p.sb("onesc", [128, 1], BF16)
    onesr = p.sb("onesr", [1, 128], F32)
    srow = p.sb("srow", [1, 1024], F32)
    trow = p.sb("trow", [1, 512], F32)
    if final:
        ufin = p.sb("ufin", [128, NCH * 512], F32)
    pp = p.ps("pp", [128, 512])
    pg = p.ps("pg", [128, 512])
    pout = p.ps("pout", [128, 1024])
    pab = p.ps("pab", [128, 1024])
    pstat = p.ps("pstat", [1, 1024])

    load_fm(p, ys, yfm, NCH, TW, eng='sync')
    p.s.dma('gpsimd', b1a[0][:, :], b1ac[:, :]); p.s.dma('gpsimd', b1g[0][:, :], b1gc[:, :])
    p.s.dma('gpsimd', b1a[1][:, :], b1ac2[:, :]); p.s.dma('gpsimd', b1g[1][:, :], b1gc2[:, :])
    p.s.dma('gpsimd', b2s[:, :], b2c[:, :])
    p.s.dma('gpsimd', gs[:, :], gc[:, :]); p.s.dma('gpsimd', bs[:, :], bc[:, :])
    p.s.op('vector', lambda v: v.memset(onesc[:, :], 1.0))
    p.s.op('vector', lambda v: v.memset(onesr[:, :], 1.0))

    def xoff(dc, st):
        return dc * TW + st + (1 if dc < 4 else 0)

    NT = T // 512
    for tt in range(NT):
        for dc in range(NCH):
            p.s.op('vector', lambda v, dc=dc, tt=tt: v.tensor_copy(
                ub[:, dc * 512: dc * 512 + 512], ys[:, xoff(dc, tt * 512): xoff(dc, tt * 512) + 512]))
        for dc in range(NCH):
            uo = dc * 512
            p.s.op('scalar', lambda sc, uo=uo: sc.activation(out=usq[:, :], in_=ub[:, uo:uo + 512], func=AF.Square))
            p.s.op('tensor', lambda t, dc=dc, uo=uo: t.matmul(
                pstat[0:1, 0:512], onesc[:, :], ub[:, uo:uo + 512], start=(dc == 0), stop=(dc == NCH - 1)))
            p.s.op('tensor', lambda t, dc=dc: t.matmul(
                pstat[0:1, 512:1024], onesc[:, :], usq[:, :], start=(dc == 0), stop=(dc == NCH - 1)))
        p.s.op('vector', lambda v: v.tensor_scalar_mul(srow[:, :], pstat[0:1, :], 1.0 / D))
        p.s.op('scalar', lambda sc: sc.activation(out=trow[0:1, :], in_=srow[0:1, 0:512], func=AF.Square))
        p.s.op('vector', lambda v: v.tensor_sub(srow[0:1, 512:1024], srow[0:1, 512:1024], trow[0:1, :]))
        p.s.op('vector', lambda v: v.tensor_scalar_add(srow[0:1, 512:1024], srow[0:1, 512:1024], 1e-5))
        p.s.op('scalar', lambda sc: sc.activation(out=srow[0:1, 512:1024], in_=srow[0:1, 512:1024], func=AF.Sqrt))
        p.s.op('vector', lambda v: v.reciprocal(srow[0:1, 512:1024], srow[0:1, 512:1024]))
        p.s.op('vector', lambda v: v.tensor_mul(trow[0:1, :], srow[0:1, 0:512], srow[0:1, 512:1024]))
        p.s.op('vector', lambda v: v.tensor_scalar_mul(trow[0:1, :], trow[0:1, :], -1.0))
        p.s.op('tensor', lambda t: t.matmul(pab[:, 0:512], onesr[:, :], srow[0:1, 512:1024], start=True, stop=True))
        p.s.op('tensor', lambda t: t.matmul(pab[:, 512:1024], onesr[:, :], trow[0:1, :], start=True, stop=True))
        for dc in range(NCH):
            xo = xoff(dc, tt * 512)
            co = dc * T + tt * 512
            p.s.op('vector', lambda v, xo=xo: v.tensor_tensor(t1[:, :], ys[:, xo:xo + 512], pab[:, 0:512], OP.mult))
            p.s.op('vector', lambda v: v.tensor_tensor(t1[:, :], t1[:, :], pab[:, 512:1024], OP.add))
            p.s.op('vector', lambda v, dc=dc, co=co: v.tensor_scalar(
                xn[:, co:co + 512], t1[:, :], gs[:, dc:dc + 1], bs[:, dc:dc + 1], OP.mult, OP.add))

    jblocks_half = [(i * 128, 128) for i in range(10)] + [(1280, 85)]
    wview = w1.rearrange("(kc p) d -> p kc d", p=128)
    for half in range(2):
        j0h = half * 1365
        for jbi, (jo, jw) in enumerate(jblocks_half):
            jg = j0h + jo
            p.s.dma('gpsimd', w1a[:, 0:NCH * jw].rearrange("p (kc d) -> p kc d", kc=NCH),
                    wview[:, :, jg:jg + jw])
            p.s.dma('gpsimd', w1g[:, 0:NCH * jw].rearrange("p (kc d) -> p kc d", kc=NCH),
                    wview[:, :, DI + jg:DI + jg + jw])
            for tt in range(NT):
                for (wsb, psum) in ((w1a, pp), (w1g, pg)):
                    for kc in range(NCH):
                        p.s.op('tensor', lambda t, kc=kc, tt=tt, jw=jw, wsb=wsb, psum=psum: t.matmul(
                            psum[0:jw, 0:512], wsb[:, kc * jw: kc * jw + jw],
                            xn[:, kc * T + tt * 512: kc * T + tt * 512 + 512],
                            start=(kc == 0), stop=(kc == NCH - 1)))
                p.s.op('scalar', lambda sc, jw=jw, jbi=jbi, half=half: sc.activation(
                    out=gbf[0:jw, :], in_=pg[0:jw, :], func=AF.Gelu, bias=b1g[half][0:jw, jbi:jbi + 1]))
                p.s.op('vector', lambda v, jo=jo, tt=tt, jw=jw, jbi=jbi, half=half: v.scalar_tensor_tensor(
                    hsb[0:jw, (jo // 128) * T + tt * 512: (jo // 128) * T + tt * 512 + 512],
                    pp[0:jw, 0:512], b1a[half][0:jw, jbi:jbi + 1], gbf[0:jw, :], OP.add, OP.mult))
        p.s.dma('gpsimd', w2s[:, 0:10 * D].rearrange("p (jb d) -> p jb d", jb=10),
                w2[j0h:j0h + 1280, :].rearrange("(jb p) d -> p jb d", p=128))
        p.s.dma('gpsimd', w2s[0:85, 10 * D:11 * D], w2[j0h + 1280:j0h + 1365, :])
        for t128 in range(T // 128):
            for dc in range(NCH):
                for jbi, (jo, jw) in enumerate(jblocks_half):
                    p.s.op('tensor', lambda t, jbi=jbi, jo=jo, jw=jw, dc=dc, t128=t128: t.matmul(
                        pout[:, dc * 128: dc * 128 + 128],
                        w2s[0:jw, jbi * D + dc * 128: jbi * D + dc * 128 + 128],
                        hsb[0:jw, (jo // 128) * T + t128 * 128: (jo // 128) * T + t128 * 128 + 128],
                        start=(jbi == 0), stop=(jbi == 10)))
            for dc in range(NCH):
                yo = dc * TW + 1 + t128 * 128
                if half == 0:
                    p.s.op('vector', lambda v, dc=dc, yo=yo: v.tensor_add(
                        ys[:, yo:yo + 128], pout[:, dc * 128: dc * 128 + 128], ys[:, yo:yo + 128]))
                else:
                    p.s.op('vector', lambda v, dc=dc, yo=yo: v.scalar_tensor_tensor(
                        ys[:, yo:yo + 128], pout[:, dc * 128: dc * 128 + 128], b2s[:, dc:dc + 1],
                        ys[:, yo:yo + 128], OP.add, OP.add))

    if not final:
        yv = y_d.rearrange("(c p) t -> p c t", p=128)
        p.s.dma('sync', yv, ys[:, :].rearrange("p (c t) -> p c t", c=NCH)[:, :, 1:1 + T])
    else:
        x3v = x3fm.rearrange("(c p) t -> p c t", p=128)
        yv = y_d.rearrange("(c p) t -> p c t", p=128)
        for tt in range(NT):
            p.s.dma('sync', ufin[:, :].rearrange("p (c t) -> p c t", c=NCH),
                    x3v[:, :, tt * 512:tt * 512 + 512])
            for dc in range(NCH):
                uo = dc * 512
                yo = dc * TW + 1 + tt * 512
                p.s.op('vector', lambda v, uo=uo, yo=yo: v.tensor_add(
                    ufin[:, uo:uo + 512], ufin[:, uo:uo + 512], ys[:, yo:yo + 512]))
                p.s.op('vector', lambda v, uo=uo: v.tensor_copy(ub[:, uo:uo + 512], ufin[:, uo:uo + 512]))
            emit_res_ln(p, u=ufin, ub=ub, usq=usq, onesc=onesc, onesr=onesr,
                        srow=srow, trow=trow, pab=pab, pstat=pstat, tt=tt, T=T,
                        ucol=lambda dc: dc * 512)
            p.s.dma('sync', yv[:, :, tt * 512:tt * 512 + 512],
                    ufin[:, :].rearrange("p (c t) -> p c t", c=NCH))



def _fm(x):
    return np.ascontiguousarray(np.asarray(x).T)


_IVF = 1.0 / 10000.0 ** (np.arange(0, 64, 2, dtype=np.float64) / 64)
_BS = ((np.arange(0, 64, 2, dtype=np.float64) + 0.4 * 64) / (1.4 * 64))
_SG = np.where((np.arange(128) % 64) < 32, -1.0, 1.0)[:, None]


def _module_rope(tglob):
    ang = tglob[:, None].astype(np.float64) * _IVF[None, :]
    cosT = np.tile(np.cos(ang).T, (4, 1))
    ssinT = np.tile(np.sin(ang).T, (4, 1)) * _SG
    return cosT.astype(BF), ssinT.astype(BF)


def _inner_rope(tglob):
    ang = tglob[:, None].astype(np.float64) * _IVF[None, :]
    cos, sin = np.cos(ang), np.sin(ang)
    power = (tglob.astype(np.float64) - N // 2) / 128.0
    sc = _BS[None, :] ** power[:, None]
    cq = np.tile((cos * sc).T, (4, 1)).astype(BF)
    sq = (np.tile((sin * sc).T, (4, 1)) * _SG).astype(BF)
    ci = np.tile((cos / sc).T, (4, 1)).astype(BF)
    si = (np.tile((sin / sc).T, (4, 1)) * _SG).astype(BF)
    return cq, sq, ci, si


def _colpack(v):
    return np.ascontiguousarray(np.asarray(v).reshape(NCH, 128).T)


def _gather_fm(res, key, c, rows=1024):
    b = c // 4
    t0 = (c % 4) * TPC
    blocks = []
    nh = rows // 64
    for h in range(nh):
        uid = b * 16 + h
        core_b, uu = uid // 4, uid % 4
        blocks.append(res[core_b][key][uu * 64:(uu + 1) * 64, t0:t0 + TPC])
    return np.concatenate(blocks, axis=0)


_CAP = None
_RAW = False
FUSED = True

def kernel(**I):
    if FUSED:
        return kernel_fused(I)
    x = np.asarray(I["L0_attn_x"])
    xt = x.reshape(NC, TPC, D)

    pA = cached_prog("l0proj", build_l0_proj)
    imA = []
    for c in range(NC):
        imA.append({
            "xfm": _fm(xt[c]),
            "wq": I["L0_attn_wq"], "wk": I["L0_attn_wk"], "wv": I["L0_attn_wv"],
            "bqc": _colpack(np.tile(I["L0_attn_bq"], 1)) if False else np.ascontiguousarray(I["L0_attn_bq"].reshape(4, 128).T),
            "bkc": np.ascontiguousarray(I["L0_attn_bk"].reshape(4, 128).T),
            "bkr": I["L0_attn_bk"].reshape(1, 512),
            "bvr": I["L0_attn_bv"].reshape(1, D),
        })
    rA = run(pA, imA)

    pB = cached_prog("l0scan", build_l0_scan)
    k_ = np.arange(128)[:, None]
    c_ = np.arange(384)[None, :]
    m384 = (k_ <= c_ - 128).astype(BF)
    imB = []
    for c in range(NC):
        qs_, ks_, kts, vts = [], [], [], []
        for uu in range(4):
            uid = c * 4 + uu
            b, h = uid // 16, uid % 16
            qs_.append(np.concatenate([rA[4 * b + j]["q_fm"][h * 32:(h + 1) * 32, :] for j in range(4)], axis=1))
            ks_.append(np.concatenate([rA[4 * b + j]["k_fm"][h * 32:(h + 1) * 32, :] for j in range(4)], axis=1))
            kts.append(np.concatenate([rA[4 * b + j]["k_tm"][:, h * 32:(h + 1) * 32] for j in range(4)], axis=0))
            vts.append(np.concatenate([rA[4 * b + j]["v_tm"][:, h * 64:(h + 1) * 64] for j in range(4)], axis=0))
        imB.append({"qf": np.concatenate(qs_, axis=0), "kf": np.concatenate(ks_, axis=0),
                    "ktm": np.concatenate(kts, axis=1), "vtm": np.concatenate(vts, axis=1),
                    "mask384": m384})
    rB = run(pB, imB)

    pC = cached_prog("projresln", build_proj_res_ln)
    imC = []
    for c in range(NC):
        imC.append({"afm": _gather_fm(rB, "o_fm", c), "xfm": _fm(xt[c]),
                    "wo": I["L0_attn_wo"], "boc": _colpack(I["L0_attn_bo"])})
    rC = run(pC, imC)
    x1 = [np.asarray(rC[c]["y"]) for c in range(NC)]
    if _CAP is not None: _CAP["x1"] = x1

    pD = cached_prog("l0ffn", build_l0_ffn)
    imD = []
    for c in range(NC):
        halo = np.zeros((D, 1), np.float32) if c % 4 == 0 else x1[c - 1][:, -1:]
        imD.append({"xfm": np.ascontiguousarray(np.concatenate([halo, x1[c]], axis=1)),
                    "wrT": _fm(I["L0_ffn_W_r"]), "wkT": _fm(I["L0_ffn_W_k"]),
                    "wvT": _fm(I["L0_ffn_W_v"]),
                    "murc": _colpack(I["L0_ffn_mu_r"]), "mukc": _colpack(I["L0_ffn_mu_k"])})
    rD = run(pD, imD)
    x2 = [np.asarray(rD[c]["y"]) for c in range(NC)]
    if _CAP is not None: _CAP["x2"] = x2

    pE = cached_prog("l1proj", build_l1_proj)
    imE = []
    for c in range(NC):
        t0c = (c % 4) * TPC
        cosT, ssinT = _module_rope(np.arange(t0c, t0c + TPC))
        imE.append({"xfm": x2[c], "qkvT": _fm(I["L1_attn_qkv"]),
                    "cosT": cosT, "ssinT": ssinT})
    rE = run(pE, imE)

    pF = cached_prog("l1attn", build_l1_attn)
    m896 = (np.arange(128)[:, None] <= np.arange(896)[None, :] - 384).astype(BF)
    imF = []
    for c in range(NC):
        qs_, ks_, vts = [], [], []
        for uu in range(4):
            uid = c * 4 + uu
            b, h = uid // 16, uid % 16
            qs_.append(np.concatenate([rE[4 * b + j]["qr_fm"][h * 64:(h + 1) * 64, :] for j in range(4)], axis=1))
            ks_.append(np.concatenate([rE[4 * b + j]["kr_fm"][h * 64:(h + 1) * 64, :] for j in range(4)], axis=1))
            vts.append(np.concatenate([rE[4 * b + j]["v_tm"][:, h * 64:(h + 1) * 64] for j in range(4)], axis=0))
        imF.append({"qf": np.concatenate(qs_, axis=0), "kf": np.concatenate(ks_, axis=0),
                    "vtm": np.concatenate(vts, axis=1), "m896": m896})
    rF = run(pF, imF)

    imG = []
    for c in range(NC):
        imG.append({"afm": _gather_fm(rF, "o_fm", c), "xfm": x2[c],
                    "wo": _fm(I["L1_attn_wo"]), "boc": np.zeros((128, NCH), np.float32)})
    rG = run(pC, imG)
    x3 = [np.asarray(rG[c]["y"]) for c in range(NC)]
    if _CAP is not None: _CAP["x3"] = x3

    pH = cached_prog("innerattn", build_inner_attn)
    pI = cached_prog("innerffn", build_inner_ffn_plain)
    pJ = cached_prog("innerffn_final", build_inner_ffn_final)

    kq = np.arange(128)[:, None]
    qq = np.arange(128)[None, :]
    mlow = np.where(kq <= qq, 0.0, -1e30).astype(np.float32)
    mupp = (kq >= qq).astype(BF)
    bpat = np.zeros((128, 2), np.float32)
    bpat[0:64, 0] = 1.0
    bpat[64:128, 1] = 1.0
    bpatTf = np.ascontiguousarray(bpat.T.astype(np.float32))
    bpat = bpat.astype(BF)

    y = x3
    for li in range(6):
        qs_i = np.asarray(I["L1_ffn_q_scale"][li])
        ks_i = np.asarray(I["L1_ffn_k_scale"][li])
        imH = []
        for c in range(NC):
            t0c = (c % 4) * TPC
            cq, sq, _, _ = _inner_rope(np.arange(t0c, t0c + TPC))
            _, _, ck, sk = _inner_rope(np.arange(t0c - 256, t0c + TPC))
            halo = np.zeros((D, TH), np.float32) if c % 4 == 0 else y[c - 1][:, -TH:]
            halom = np.zeros((128, 2), np.float32) if c % 4 == 0 else np.ones((128, 2), np.float32)
            imH.append({
                "yfm": np.ascontiguousarray(np.concatenate([halo, y[c]], axis=1)),
                "wqkv": I["L1_ffn_Wqkv"][li], "wo": I["L1_ffn_Wo"][li],
                "cq": cq, "sq": sq, "ck": ck, "sk": sk,
                "qsc": np.tile(qs_i, 2).reshape(128, 1),
                "qscw": np.tile(np.concatenate([qs_i[32:], qs_i[:32]]), 2).reshape(128, 1),
                "ksc": np.tile(ks_i, 2).reshape(128, 1),
                "kscw": np.tile(np.concatenate([ks_i[32:], ks_i[:32]]), 2).reshape(128, 1),
                "qsr": qs_i.reshape(1, 64), "ksr": ks_i.reshape(1, 64),
                "blockpat": bpat, "blockpatT": bpatTf,
                "mlow": mlow, "mupp": mupp, "halom": halom,
            })
        rH = run(pH, imH)
        y = [np.asarray(rH[c]["y"]) for c in range(NC)]
        if _CAP is not None: _CAP["ya%d" % li] = y

        b1_i = np.asarray(I["L1_ffn_b1"][li])
        DI = 2730
        b1cols = []
        for part0 in (0, DI):
            for half in range(2):
                m = np.zeros((128, 11), np.float32)
                for jbi in range(11):
                    jo = jbi * 128
                    jw = min(128, 1365 - jo)
                    m[0:jw, jbi] = b1_i[part0 + half * 1365 + jo: part0 + half * 1365 + jo + jw]
                b1cols.append(m)
        final = (li == 5)
        imI = []
        for c in range(NC):
            halo = np.zeros((D, 1), np.float32) if c % 4 == 0 else y[c - 1][:, -1:]
            d = {"yfm": np.ascontiguousarray(np.concatenate([halo, y[c]], axis=1)),
                 "w1": I["L1_ffn_W1"][li], "w2": I["L1_ffn_W2"][li],
                 "b1ac": b1cols[0], "b1ac2": b1cols[1],
                 "b1gc": b1cols[2], "b1gc2": b1cols[3],
                 "b2c": _colpack(I["L1_ffn_b2"][li]),
                 "gc": _colpack(I["L1_ffn_ln_g"][li]), "bc": _colpack(I["L1_ffn_ln_b"][li])}
            if final:
                d["x3fm"] = x3[c]
            imI.append(d)
        rI = run(pJ if final else pI, imI)
        y = [np.asarray(rI[c]["y"]) for c in range(NC)]
        if _CAP is not None: _CAP["yf%d" % li] = y

    out = np.concatenate([a.T for a in y], axis=0).reshape(B, N, D)
    return np.ascontiguousarray(out.astype(np.float32, copy=False))



GROUPS = [[0, 1, 2, 3], [4, 5, 6, 7]]

_UNIQ = [0]

def _sb(nc, es, name, shape, dt):
    _UNIQ[0] += 1
    return es.enter_context(nc.sbuf_tensor("s%d_%s" % (_UNIQ[0], name), list(shape), dt))

def _psn(nc, es, name, shape, dt=F32):
    _UNIQ[0] += 1
    return es.enter_context(nc.psum_tensor("p%d_%s" % (_UNIQ[0], name), list(shape), dt))


class Meta:
    def __init__(self):
        self.names = []
    def slot(self, name):
        if name not in self.names:
            self.names.append(name)
        return self.names.index(name)


def build_fused(upto=99):
    p = Prog("fused")
    nc = p.nc
    s = p.s
    T = TPC

    xfm_d = p.din("xfm", [D, T])
    meta_d = p.din("meta", [32, 1], mybir.dt.int32)
    wq_d = p.din("wq", [D, 512]); wk_d = p.din("wk", [D, 512]); wv_d = p.din("wv", [D, D])
    bqc_d = p.din("bqc", [128, 4]); bkc_d = p.din("bkc", [128, 4])
    bkr_d = p.din("bkr", [1, 512]); bvr_d = p.din("bvr", [1, D])
    wo0_d = p.din("wo0", [D, D]); boc_d = p.din("boc", [128, NCH])
    mask384_d = p.din("mask384", [128, 384], BF16)
    wrT_d = p.din("wrT", [D, D]); wkT_d = p.din("wkT", [D, D]); wvT_d = p.din("wvT", [D, D])
    murc_d = p.din("murc", [128, NCH]); mukc_d = p.din("mukc", [128, NCH])
    qkvT_d = p.din("qkvT", [D, 3 * D])
    cosT_d = p.din("cosT", [128, T], BF16); ssinT_d = p.din("ssinT", [128, T], BF16)
    m896_d = p.din("m896", [128, 896], BF16)
    woT1_d = p.din("woT1", [D, D])
    wqkv6_d = p.din("wqkv6", [6, D, 3 * D])
    wo6_d = p.din("wo6", [6, D, D])
    w16_d = p.din("w16", [6, D, 2 * 2730])
    w26_d = p.din("w26", [6, 2730, D])
    cq_d = p.din("cq", [128, T], BF16); sq_d = p.din("sq", [128, T], BF16)
    ck_d = p.din("ck", [128, TKV], BF16); sk_d = p.din("sk", [128, TKV], BF16)
    qsc6_d = p.din("qsc6", [128, 6]); qscw6_d = p.din("qscw6", [128, 6])
    ksc6_d = p.din("ksc6", [128, 6]); kscw6_d = p.din("kscw6", [128, 6])
    qsr6_d = p.din("qsr6", [6, 64]); ksr6_d = p.din("ksr6", [6, 64])
    blockpat_d = p.din("blockpat", [128, 2], BF16)
    blockpatT_d = p.din("blockpatT", [2, 128])
    mlow_d = p.din("mlow", [128, 128]); mupp_d = p.din("mupp", [128, 128], BF16)
    halom_d = p.din("halom", [128, 2])
    b1c6_d = p.din("b1c6", [128, 6 * 44])
    b2c6_d = p.din("b2c6", [128, 6 * NCH])
    zc_d = p.din("zc", [128, NCH])
    gc6_d = p.din("gc6", [128, 6 * NCH]); bc6_d = p.din("bc6", [128, 6 * NCH])
    y_d = p.dout("y", [D, T])

    ex_qf = nc.dram_tensor("ex_qf", [512, T], BF16).ap()
    ex_kf = nc.dram_tensor("ex_kf", [512, T], BF16).ap()
    ex_ktm = nc.dram_tensor("ex_ktm", [T, 512], BF16).ap()
    ex_vtm = nc.dram_tensor("ex_vtm", [T, D], BF16).ap()
    gx_qf = nc.dram_tensor("gx_qf", [4 * 512, T], BF16).ap()
    gx_kf = nc.dram_tensor("gx_kf", [4 * 512, T], BF16).ap()
    gx_ktm = nc.dram_tensor("gx_ktm", [4 * T, 512], BF16).ap()
    gx_vtm = nc.dram_tensor("gx_vtm", [4 * T, D], BF16).ap()
    ex_o = nc.dram_tensor("ex_o", [256, N], BF16).ap()
    gx_o = nc.dram_tensor("gx_o", [4 * 256, N], BF16).ap()
    ex_q1 = nc.dram_tensor("ex_q1", [D, T], BF16).ap()
    ex_k1 = nc.dram_tensor("ex_k1", [D, T], BF16).ap()
    ex_v1 = nc.dram_tensor("ex_v1", [T, D], BF16).ap()
    gx_q1 = nc.dram_tensor("gx_q1", [4 * D, T], BF16).ap()
    gx_k1 = nc.dram_tensor("gx_k1", [4 * D, T], BF16).ap()
    gx_v1 = nc.dram_tensor("gx_v1", [4 * T, D], BF16).ap()
    ex_h1 = nc.dram_tensor("ex_h1", [128, NCH], F32).ap()
    gx_h1 = nc.dram_tensor("gx_h1", [5 * 128, NCH], F32).ap()
    ex_hA = nc.dram_tensor("ex_hA", [128, NCH * TH], BF16).ap()
    gx_hA = nc.dram_tensor("gx_hA", [5 * 128, NCH * TH], BF16).ap()
    vb_i = nc.dram_tensor("vb_i", [TKV, D], BF16).ap()
    x3_i = nc.dram_tensor("x3_i", [D, T], F32).ap()

    est = p.est
    stream = _sb(nc, est, "stream", [128, NCH * T], F32)
    metas = _sb(nc, est, "metas", [32, 1], mybir.dt.int32)

    def coll(ins, outs):
        for i_, o_ in zip(ins, outs):
            s.op('gpsimd', lambda g, i_=i_, o_=o_: g.collective_compute(
                "AllGather", OP.bypass, GROUPS, [i_.opt()], [o_.opt()]),
                reads=[i_], writes=[o_], async_=True)
        s.barrier()

    MS_COL, MS_HALO, MS_R128, MS_R256 = 0, 1, 2, 3

    s.dma('gpsimd', metas[:, :], meta_d[:, :])

    EX = {}
    def _preload_sync(e):
        col = e.value_load(metas[MS_COL:MS_COL + 1, 0:1])
        r128 = e.value_load(metas[MS_R128:MS_R128 + 1, 0:1])
        hrow = e.value_load(metas[MS_HALO:MS_HALO + 1, 0:1])
        EX['col'] = col
        EX['hrow'] = hrow
        EX['scan_row'] = [j * 512 + r128 for j in range(4)]
        EX['r128x'] = r128 + 0
        return e.nop()
    def _preload_g(e):
        r256g = e.value_load(metas[MS_R256:MS_R256 + 1, 0:1])
        EX['vt'] = [r256g + uu * 64 for uu in range(4)]
        EX['l1q1'] = [j * 1024 + 128 + r256g for j in range(4)]
        return e.nop()
    def _preload_sc(e):
        r256s = e.value_load(metas[MS_R256:MS_R256 + 1, 0:1])
        EX['l1q0s'] = [j * 1024 + r256s for j in range(4)]
        return e.nop()
    s.op('sync', _preload_sync, reads=[metas[:, :]], writes=[])
    s.op('gpsimd', _preload_g, reads=[metas[:, :]], writes=[])
    s.op('scalar', _preload_sc, reads=[metas[:, :]], writes=[])
    xv = xfm_d.rearrange("(c p) t -> p c t", p=128)
    s.dma('sync', stream[:, :].rearrange("p (c t) -> p c t", c=NCH), xv)

    with contextlib.ExitStack() as es0:
        zt = _sb(nc, es0, "zt", [128, NCH * TH], BF16)
        zt2 = _sb(nc, es0, "zt2", [128, NCH], F32)
        s.op('vector', lambda v: v.memset(zt[:, :], 0.0))
        s.op('vector', lambda v: v.memset(zt2[:, :], 0.0))
        s.dma('gpsimd', gx_hA[4 * 128: 5 * 128, :], zt[:, :])
        s.dma('gpsimd', gx_h1[4 * 128: 5 * 128, :], zt2[:, :])
    s.barrier()

    def _ph1():
        with contextlib.ExitStack() as es1:
            xb = _sb(nc, es1, "xb", [128, NCH * T], BF16)
            wqs = _sb(nc, es1, "wqs", [128, NCH * 512], BF16)
            wks = _sb(nc, es1, "wks", [128, NCH * 512], BF16)
            wvs = _sb(nc, es1, "wvs", [128, NCH * D], BF16)
            bq = _sb(nc, es1, "bq", [128, 4], F32); bk = _sb(nc, es1, "bk", [128, 4], F32)
            brk = _sb(nc, es1, "brk", [1, 512], BF16); brv = _sb(nc, es1, "brv", [1, D], BF16)
            ones1 = _sb(nc, es1, "ones1", [1, 128], BF16)
            of = _sb(nc, es1, "of", [128, 4 * T], BF16)
            ot = _sb(nc, es1, "ot", [128, (T // 128) * D], BF16)
            t1 = _sb(nc, es1, "t1", [128, 512], F32)
            t2 = _sb(nc, es1, "t2", [128, 512], F32)
            t3 = _sb(nc, es1, "t3", [128, 512], F32)
            pp = _psn(nc, es1, "pp", [128, 512])

            for dc in range(NCH):
                for tt in range(4):
                    co = dc * T + tt * 512
                    s.op('vector', lambda v, co=co: v.tensor_copy(xb[:, co:co + 512], stream[:, co:co + 512]))
            load_w_fm(p, wqs, wq_d, D, 512)
            load_w_fm(p, wks, wk_d, D, 512)
            load_w_fm(p, wvs, wv_d, D, D)
            s.dma('gpsimd', bq[:, :], bqc_d[:, :]); s.dma('gpsimd', bk[:, :], bkc_d[:, :])
            s.dma('gpsimd', brk[:, :], bkr_d[:, :]); s.dma('gpsimd', brv[:, :], bvr_d[:, :])
            s.op('vector', lambda v: v.memset(ones1[:, :], 1.0))

            NT = T // 512
            for (wsb, bias, dram) in ((wqs, bq, ex_qf), (wks, bk, ex_kf)):
                for dc in range(4):
                    for tt in range(NT):
                        fm_proj(p, pp, wsb, xb, dout=512, dc=dc,
                                xoff=lambda kc, tt=tt: kc * T + tt * 512, ncols=512, nk=NCH)
                        p.s.op('vector', lambda v, bias=bias, dc=dc: v.tensor_scalar(
                            t1[:, :], pp[:, :], bias[:, dc:dc + 1], 0.0, OP.add, OP.min))
                        p.s.op('scalar', lambda sc: sc.activation(out=t2[:, :], in_=t1[:, :], func=AF.Exp))
                        p.s.op('vector', lambda v, bias=bias, dc=dc: v.tensor_scalar(
                            t3[:, :], pp[:, :], bias[:, dc:dc + 1], 0.0, OP.add, OP.max))
                        p.s.op('vector', lambda v, dc=dc, tt=tt: v.tensor_add(
                            of[:, dc * T + tt * 512: dc * T + tt * 512 + 512], t2[:, :], t3[:, :]))
                store_fm(p, dram, of, 4, T)
            NT128 = T // 128
            for (wsb, dout, brow, dram, phi) in ((wks, 512, brk, ex_ktm, True), (wvs, D, brv, ex_vtm, False)):
                for tt in range(NT128):
                    for qc in range(dout // 512):
                        for kc in range(NCH):
                            p.s.op('tensor', lambda t, kc=kc, tt=tt, qc=qc, wsb=wsb, dout=dout: t.matmul(
                                pp[:, :], xb[:, kc * T + tt * 128: kc * T + tt * 128 + 128],
                                wsb[:, kc * dout + qc * 512: kc * dout + qc * 512 + 512],
                                start=(kc == 0), stop=False))
                        p.s.op('tensor', lambda t, qc=qc, brow=brow: t.matmul(
                            pp[:, :], ones1[:, :], brow[:, qc * 512: qc * 512 + 512],
                            start=False, stop=True))
                        co = tt * dout + qc * 512
                        if phi:
                            p.s.op('vector', lambda v: v.tensor_scalar(t1[:, :], pp[:, :], 0.0, None, OP.min))
                            p.s.op('scalar', lambda sc: sc.activation(out=t2[:, :], in_=t1[:, :], func=AF.Exp))
                            p.s.op('vector', lambda v: v.tensor_scalar(t3[:, :], pp[:, :], 0.0, None, OP.max))
                            p.s.op('vector', lambda v, co=co: v.tensor_add(ot[:, co:co + 512], t2[:, :], t3[:, :]))
                        else:
                            p.s.op('scalar', lambda sc, co=co: sc.activation(
                                out=ot[:, co:co + 512], in_=pp[:, :], func=AF.Copy))
                store_tm(p, dram, ot, NT128, dout)
        coll([ex_qf, ex_kf, ex_ktm, ex_vtm], [gx_qf, gx_kf, gx_ktm, gx_vtm])
    if upto >= 1:
        _ph1()

    def _ph2():
        with contextlib.ExitStack() as es2:
            qf = _sb(nc, es2, "qf", [128, N], BF16)
            kf = _sb(nc, es2, "kf", [128, N], BF16)
            ktm = _sb(nc, es2, "ktm", [128, (N // 128) * 32 * 4], BF16)
            vtm = _sb(nc, es2, "vtm", [128, (N // 128) * 65 * 4], BF16)
            msk = _sb(nc, es2, "msk", [128, 384], BF16)
            pmask = _sb(nc, es2, "pmask", [128, 512], BF16)
            Ssb = _sb(nc, es2, "Ssb", [128, 65], BF16)
            numb = _sb(nc, es2, "numb", [64, 256], F32)
            zrow = _sb(nc, es2, "zrow", [1, 256], F32)
            onesf = _sb(nc, es2, "onesf", [1, 64], F32)
            ofm = [_sb(nc, es2, "ofm%d" % i, [128, N], BF16) for i in range(2)]
            psT = _psn(nc, es2, "psT", [128, 512])
            pacc = _psn(nc, es2, "pacc", [128, 256])
            pzb = _psn(nc, es2, "pzb", [64, 256])
            pst = _psn(nc, es2, "pst", [32, 65])

            for j in range(4):
                s.op('sync', lambda e, j=j: e.dma_start(
                    out=qf[:, j * T:(j + 1) * T],
                    in_=gx_qf[bass.ds(EX['scan_row'][j], 128), :]), dma=True,
                    reads=[gx_qf], writes=[qf[:, 0:1]])
                s.op('sync', lambda e, j=j: e.dma_start(
                    out=kf[:, j * T:(j + 1) * T],
                    in_=gx_kf[bass.ds(EX['scan_row'][j], 128), :]), dma=True,
                    reads=[gx_kf], writes=[kf[:, 0:1]])
                s.op('sync', lambda e, j=j: e.dma_start(
                    out=ktm[:, :].rearrange("p (tt x) -> p tt x", x=128)[:, j * 16:(j + 1) * 16, :],
                    in_=gx_ktm[j * T:(j + 1) * T, bass.ds(EX['r128x'], 128)].rearrange("(tt pp) f -> pp tt f", pp=128)), dma=True,
                    reads=[gx_ktm], writes=[ktm[:, 0:1]])
                for uu in range(4):
                    s.op('gpsimd', lambda e, j=j, uu=uu: e.dma_start(
                        out=vtm[:, :].rearrange("p (tt u g) -> p tt u g", u=4, g=65)[:, j * 16:(j + 1) * 16, uu, 0:64],
                        in_=gx_vtm[j * T:(j + 1) * T, bass.ds(EX['vt'][uu], 64)].rearrange("(tt pp) g -> pp tt g", pp=128)), dma=True,
                        reads=[gx_vtm], writes=[vtm[:, 0:1]])
            s.dma('gpsimd', msk[:, :], mask384_d[:, :])
            s.op('vector', lambda v: v.memset(vtm[:, :].rearrange("p (x g) -> p x g", g=65)[:, :, 64:65], 1.0))
            s.op('vector', lambda v: v.memset(Ssb[:, :], 0.0))
            s.op('vector', lambda v: v.memset(onesf[:, :], 1.0))

            nchunk = N // CHK
            for u in range(4):
                ub = u * 32
                for c in range(nchunk):
                    c0 = c * CHK
                    for kt in range(2):
                        s.op('tensor', lambda t, ub=ub, c0=c0, kt=kt: t.matmul(
                            psT[:, kt * 256: kt * 256 + 256],
                            kf[ub:ub + 32, c0 + kt * 128: c0 + kt * 128 + 128],
                            qf[ub:ub + 32, c0: c0 + 256],
                            start=True, stop=True, tile_position=(ub, 0)))
                    for kt in range(2):
                        mv = msk[:, 128 - kt * 128: 128 - kt * 128 + 256]
                        s.op('vector', lambda v, kt=kt, mv=mv: v.tensor_tensor(
                            pmask[:, kt * 256: kt * 256 + 256], psT[:, kt * 256: kt * 256 + 256], mv, OP.mult))
                    for kt in range(2):
                        tt128 = (c0 // 128) + kt
                        s.op('tensor', lambda t, tt128=tt128, u=u, kt=kt: t.matmul(
                            pacc[0:65, :], vtm[:, (tt128 * 4 + u) * 65: (tt128 * 4 + u) * 65 + 65],
                            pmask[:, kt * 256: kt * 256 + 256],
                            start=(kt == 0), stop=False))
                    s.op('tensor', lambda t, ub=ub, c0=c0: t.matmul(
                        pacc[0:65, :], Ssb[ub:ub + 32, :], qf[ub:ub + 32, c0: c0 + 256],
                        start=False, stop=True, tile_position=(ub, 0)))
                    for kt in range(2):
                        tt128 = (c0 // 128) + kt
                        s.op('tensor', lambda t, tt128=tt128, u=u, c=c, kt=kt: t.matmul(
                            pst[:, :], ktm[:, (tt128 * 4 + u) * 32: (tt128 * 4 + u) * 32 + 32],
                            vtm[:, (tt128 * 4 + u) * 65: (tt128 * 4 + u) * 65 + 65],
                            start=(c == 0 and kt == 0), stop=False))
                    s.op('vector', lambda v, ub=ub: v.tensor_copy(Ssb[ub:ub + 32, :], pst[:, :]))
                    s.op('vector', lambda v: v.tensor_scalar_add(zrow[:, :], pacc[64:65, :], 1e-6))
                    s.op('vector', lambda v: v.reciprocal(zrow[:, :], zrow[:, :]))
                    s.op('tensor', lambda t: t.matmul(pzb[:, :], onesf[:, :], zrow[:, :], start=True, stop=True))
                    s.op('scalar', lambda sc: sc.activation(out=numb[:, :], in_=pacc[0:64, :], func=AF.Copy))
                    po = (u % 2) * 64
                    pair = u // 2
                    s.op('vector', lambda v, po=po, pair=pair, c0=c0: v.tensor_tensor(
                        ofm[pair][po:po + 64, c0: c0 + 256], numb[:, :], pzb[:, :], OP.mult))
            ov = ex_o.rearrange("(pair po) t -> pair po t", po=128)
            for pair in range(2):
                s.dma('sync', ov[pair], ofm[pair][:, :])
        coll([ex_o], [gx_o])
    if upto >= 2:
        _ph2()

    def emit_oproj_res_ln(wo_dram, bias_dram, gx, nch_rows_map):
        with contextlib.ExitStack() as es3:
            ab = _sb(nc, es3, "ab", [128, NCH * T], BF16)
            wos = _sb(nc, es3, "wos", [128, NCH * D], BF16)
            bo = _sb(nc, es3, "bo", [128, NCH], F32)
            ub = _sb(nc, es3, "ub", [128, NCH * 512], BF16)
            usq = _sb(nc, es3, "usq", [128, 512], BF16)
            onesc = _sb(nc, es3, "onesc", [128, 1], BF16)
            onesr = _sb(nc, es3, "onesr", [1, 128], F32)
            srow = _sb(nc, es3, "srow", [1, 1024], F32)
            trow = _sb(nc, es3, "trow", [1, 512], F32)
            pp = _psn(nc, es3, "pp", [128, 512])
            pab = _psn(nc, es3, "pab", [128, 1024])
            pstat = _psn(nc, es3, "pstat", [1, 1024])

            for dc in range(NCH):
                r0 = nch_rows_map(dc)
                s.op('sync', lambda e, dc=dc, r0=r0: e.dma_start(
                    out=ab[:, dc * T:(dc + 1) * T],
                    in_=gx[r0:r0 + 128, bass.ds(EX['col'], T)]), dma=True,
                    reads=[gx], writes=[ab[:, 0:1]])
            load_w_fm(p, wos, wo_dram, D, D)
            s.dma('gpsimd', bo[:, :], bias_dram[:, :])
            s.op('vector', lambda v: v.memset(onesc[:, :], 1.0))
            s.op('vector', lambda v: v.memset(onesr[:, :], 1.0))
            NT = T // 512
            for tt in range(NT):
                for dc in range(NCH):
                    fm_proj(p, pp, wos, ab, dout=D, dc=dc,
                            xoff=lambda kc, tt=tt: kc * T + tt * 512, ncols=512, nk=NCH)
                    co = dc * T + tt * 512
                    s.op('vector', lambda v, dc=dc, co=co: v.scalar_tensor_tensor(
                        stream[:, co:co + 512], pp[:, :], bo[:, dc:dc + 1], stream[:, co:co + 512],
                        OP.add, OP.add))
                    s.op('vector', lambda v, dc=dc, co=co: v.tensor_copy(
                        ub[:, dc * 512:dc * 512 + 512], stream[:, co:co + 512]))
                emit_res_ln(p, u=stream, ub=ub, usq=usq, onesc=onesc, onesr=onesr,
                            srow=srow, trow=trow, pab=pab, pstat=pstat, tt=tt, T=T)
        s.barrier()

    if upto >= 3:
        emit_oproj_res_ln(wo0_d, boc_d, gx_o,
                          lambda dc: ((dc * 2) // 4) * 256 + ((dc * 2) % 4) * 64)


    def emit_halo1():
        for dc in range(NCH):
            s.op('sync', lambda e, dc=dc: e.dma_start(
                out=ex_h1[:, dc:dc + 1], in_=stream[:, dc * T + T - 1: dc * T + T],
                allow_slow_non_contiguous=True), dma=True)
        coll([ex_h1], [gx_h1[0:4 * 128, :]])

    def read_halo1(hl):
        s.op('sync', lambda e, hl=hl: e.dma_start(
            out=hl[:, :], in_=gx_h1[bass.ds(EX['hrow'], 128), :]), dma=True,
            reads=[gx_h1], writes=[hl[:, 0:1]])

    def _ph4():
        emit_halo1()
        with contextlib.ExitStack() as es4:
            hl = _sb(nc, es4, "hl", [128, NCH], F32)
            wr = _sb(nc, es4, "wr", [128, NCH * D], BF16)
            wk2 = _sb(nc, es4, "wk2", [128, NCH * D], BF16)
            wv2 = _sb(nc, es4, "wv2", [128, NCH * D], BF16)
            mur = _sb(nc, es4, "mur", [128, NCH], F32); murM = _sb(nc, es4, "murM", [128, NCH], F32)
            muk = _sb(nc, es4, "muk", [128, NCH], F32); mukM = _sb(nc, es4, "mukM", [128, NCH], F32)
            xr = _sb(nc, es4, "xr", [128, NCH * 512], BF16)
            rr = _sb(nc, es4, "rr", [128, NCH * 512], BF16)
            ka = _sb(nc, es4, "ka", [128, NCH * 512], BF16)
            xp0 = _sb(nc, es4, "xp0", [128, 512], F32)
            ub = _sb(nc, es4, "ub", [128, NCH * 512], BF16)
            usq = _sb(nc, es4, "usq", [128, 512], BF16)
            t1 = _sb(nc, es4, "t1", [128, 512], F32)
            onesc = _sb(nc, es4, "onesc", [128, 1], BF16)
            onesr = _sb(nc, es4, "onesr", [1, 128], F32)
            srow = _sb(nc, es4, "srow", [1, 1024], F32)
            trow = _sb(nc, es4, "trow", [1, 512], F32)
            pp = _psn(nc, es4, "pp", [128, 512])
            pab = _psn(nc, es4, "pab", [128, 1024])
            pstat = _psn(nc, es4, "pstat", [1, 1024])

            read_halo1(hl)
            load_w_fm(p, wr, wrT_d, D, D)
            load_w_fm(p, wk2, wkT_d, D, D)
            load_w_fm(p, wv2, wvT_d, D, D)
            s.dma('gpsimd', mur[:, :], murc_d[:, :]); s.dma('gpsimd', muk[:, :], mukc_d[:, :])
            s.op('vector', lambda v: v.tensor_scalar(murM[:, :], mur[:, :], -1.0, 1.0, OP.mult, OP.add))
            s.op('vector', lambda v: v.tensor_scalar(mukM[:, :], muk[:, :], -1.0, 1.0, OP.mult, OP.add))
            s.op('vector', lambda v: v.memset(onesc[:, :], 1.0))
            s.op('vector', lambda v: v.memset(onesr[:, :], 1.0))

            NT = T // 512
            for tt in range(NT):
                for which in range(2):
                    mu_, muM_ = (mur, murM) if which == 0 else (muk, mukM)
                    for dc in range(NCH):
                        co = dc * T + tt * 512
                        s.op('vector', lambda v, dc=dc, co=co, mu_=mu_: v.tensor_scalar_mul(
                            t1[:, :], stream[:, co:co + 512], mu_[:, dc:dc + 1]))
                        if tt == 0:
                            s.op('vector', lambda v, dc=dc: v.tensor_copy(
                                xp0[:, 0:1], hl[:, dc:dc + 1]))
                            s.op('vector', lambda v, dc=dc, co=co: v.tensor_copy(
                                xp0[:, 1:512], stream[:, co:co + 511]))
                            s.op('vector', lambda v, dc=dc, muM_=muM_: v.scalar_tensor_tensor(
                                xr[:, dc * 512:dc * 512 + 512], xp0[:, :], muM_[:, dc:dc + 1],
                                t1[:, :], OP.mult, OP.add))
                        else:
                            s.op('vector', lambda v, dc=dc, co=co, muM_=muM_: v.scalar_tensor_tensor(
                                xr[:, dc * 512:dc * 512 + 512], stream[:, co - 1:co + 511], muM_[:, dc:dc + 1],
                                t1[:, :], OP.mult, OP.add))
                    if which == 0:
                        for dc in range(NCH):
                            fm_proj(p, pp, wr, xr, dout=D, dc=dc, xoff=lambda kc: kc * 512, ncols=512, nk=NCH)
                            s.op('scalar', lambda sc, dc=dc: sc.activation(
                                out=rr[:, dc * 512:dc * 512 + 512], in_=pp[:, :], func=AF.Sigmoid))
                    else:
                        for dc in range(NCH):
                            fm_proj(p, pp, wk2, xr, dout=D, dc=dc, xoff=lambda kc: kc * 512, ncols=512, nk=NCH)
                            s.op('vector', lambda v: v.tensor_scalar(t1[:, :], pp[:, :], 0.0, None, OP.max))
                            s.op('scalar', lambda sc, dc=dc: sc.activation(
                                out=ka[:, dc * 512:dc * 512 + 512], in_=t1[:, :], func=AF.Square))
                for dc in range(NCH):
                    fm_proj(p, pp, wv2, ka, dout=D, dc=dc, xoff=lambda kc: kc * 512, ncols=512, nk=NCH)
                    co = dc * T + tt * 512
                    s.op('vector', lambda v, dc=dc: v.tensor_tensor(
                        t1[:, :], rr[:, dc * 512:dc * 512 + 512], pp[:, :], OP.mult))
                    s.op('vector', lambda v, co=co: v.tensor_add(
                        stream[:, co:co + 512], t1[:, :], stream[:, co:co + 512]))
                    s.op('vector', lambda v, dc=dc, co=co: v.tensor_copy(
                        ub[:, dc * 512:dc * 512 + 512], stream[:, co:co + 512]))
                emit_res_ln(p, u=stream, ub=ub, usq=usq, onesc=onesc, onesr=onesr,
                            srow=srow, trow=trow, pab=pab, pstat=pstat, tt=tt, T=T)
        s.barrier()

    if upto >= 4:
        _ph4()

    def _ph5():
        with contextlib.ExitStack() as es5:
            xb = _sb(nc, es5, "xb", [128, NCH * T], BF16)
            wqb = _sb(nc, es5, "wqb", [128, NCH * 512], BF16)
            cosb = _sb(nc, es5, "cosb", [128, T], BF16)
            ssin = _sb(nc, es5, "ssin", [128, T], BF16)
            qraw = _sb(nc, es5, "qraw", [128, NCH * T], BF16)
            qsw = _sb(nc, es5, "qsw", [128, T], BF16)
            t1b = _sb(nc, es5, "t1b", [128, T], BF16)
            och = _sb(nc, es5, "och", [128, T], BF16)
            otv = _sb(nc, es5, "otv", [128, D], BF16)
            pp = _psn(nc, es5, "pp", [128, 512])

            for dc in range(NCH):
                s.op('vector', lambda v, dc=dc: v.tensor_copy(xb[:, dc * T:(dc + 1) * T], stream[:, dc * T:(dc + 1) * T]))
            s.dma('gpsimd', cosb[:, :], cosT_d[:, :])
            s.dma('gpsimd', ssin[:, :], ssinT_d[:, :])

            qk = qkvT_d.rearrange("(kc p) d -> p kc d", p=128)
            NT = T // 512
            for (woff2, dram) in ((0, ex_q1), (D, ex_k1)):
                for half in range(2):
                    s.dma('gpsimd', wqb[:, :].rearrange("p (kc d) -> p kc d", kc=NCH),
                          qk[:, :, woff2 + half * 512: woff2 + half * 512 + 512])
                    for dch in range(4):
                        dc = half * 4 + dch
                        for tt in range(NT):
                            for kc in range(NCH):
                                s.op('tensor', lambda t, kc=kc, dch=dch, tt=tt: t.matmul(
                                    pp[:, :], wqb[:, kc * 512 + dch * 128: kc * 512 + dch * 128 + 128],
                                    xb[:, kc * T + tt * 512: kc * T + tt * 512 + 512],
                                    start=(kc == 0), stop=(kc == NCH - 1)))
                            s.op('scalar', lambda sc, dc=dc, tt=tt: sc.activation(
                                out=qraw[:, dc * T + tt * 512: dc * T + tt * 512 + 512], in_=pp[:, :], func=AF.Copy))
                for dc in range(NCH):
                    qo = dc * T
                    for g in range(4):
                        src = (g ^ 1) * 32
                        s.op('vector', lambda v, qo=qo, g=g, src=src: v.tensor_copy(
                            qsw[g * 32:(g + 1) * 32, :], qraw[src:src + 32, qo:qo + T]))
                    s.op('vector', lambda v, qo=qo: v.tensor_tensor(
                        t1b[:, :], qraw[:, qo:qo + T], cosb[:, :], OP.mult))
                    s.op('vector', lambda v: v.tensor_tensor(qsw[:, :], qsw[:, :], ssin[:, :], OP.mult))
                    s.op('vector', lambda v: v.tensor_add(och[:, :], t1b[:, :], qsw[:, :]))
                    s.op('sync', lambda e, dc=dc, dram=dram: e.dma_start(
                        out=dram[dc * 128:(dc + 1) * 128, :], in_=och[:, :]), dma=True)
            wvb2 = qraw
            s.dma('gpsimd', wqb[:, :].rearrange("p (kc d) -> p kc d", kc=NCH),
                  qk[:, :, 2 * D: 2 * D + 512])
            s.dma('gpsimd', wvb2[:, 0:NCH * 512].rearrange("p (kc d) -> p kc d", kc=NCH),
                  qk[:, :, 2 * D + 512: 2 * D + 1024])
            for tt in range(T // 128):
                for half, wb in ((0, wqb), (1, wvb2)):
                    for kc in range(NCH):
                        s.op('tensor', lambda t, kc=kc, tt=tt, wb=wb: t.matmul(
                            pp[:, :], xb[:, kc * T + tt * 128: kc * T + tt * 128 + 128],
                            wb[:, kc * 512: kc * 512 + 512],
                            start=(kc == 0), stop=(kc == NCH - 1)))
                    s.op('scalar', lambda sc, half=half: sc.activation(
                        out=otv[:, half * 512: half * 512 + 512], in_=pp[:, :], func=AF.Copy))
                s.op('sync', lambda e, tt=tt: e.dma_start(
                    out=ex_v1[tt * 128:(tt + 1) * 128, :], in_=otv[:, :]), dma=True)
        coll([ex_q1, ex_k1, ex_v1], [gx_q1, gx_k1, gx_v1])
    if upto >= 5:
        _ph5()

    def _ph6():
        with contextlib.ExitStack() as es6:
            qf = [_sb(nc, es6, "qf%d" % i, [128, N], BF16) for i in range(2)]
            kf = [_sb(nc, es6, "kf%d" % i, [128, N], BF16) for i in range(2)]
            vtm = _sb(nc, es6, "vtm", [128, (N // 128) * 65 * 4], BF16)
            m896 = _sb(nc, es6, "m896b", [128, 896], BF16)
            psb = _sb(nc, es6, "psb", [128, 1024], BF16)
            numb = _sb(nc, es6, "numb", [64, 512], F32)
            zrow = _sb(nc, es6, "zrow", [1, 512], F32)
            onesf = _sb(nc, es6, "onesf", [1, 64], F32)
            ofm = [_sb(nc, es6, "ofm%d" % i, [128, N], BF16) for i in range(2)]
            psT = _psn(nc, es6, "psT", [128, 1024])
            pout = _psn(nc, es6, "pout", [128, 512])
            pzb = _psn(nc, es6, "pzb", [64, 512])

            for j in range(4):
                for pair in range(2):
                    eng = 'scalar' if pair == 0 else 'gpsimd'
                    expr = (lambda j=j: EX['l1q0s'][j]) if pair == 0 else (lambda j=j: EX['l1q1'][j])
                    s.op(eng, lambda e, j=j, expr=expr, pair=pair: e.dma_start(
                        out=qf[pair][:, j * T:(j + 1) * T],
                        in_=gx_q1[bass.ds(expr(), 128), :]), dma=True,
                        reads=[gx_q1], writes=[qf[pair][:, 0:1]])
                    s.op(eng, lambda e, j=j, expr=expr, pair=pair: e.dma_start(
                        out=kf[pair][:, j * T:(j + 1) * T],
                        in_=gx_k1[bass.ds(expr(), 128), :]), dma=True,
                        reads=[gx_k1], writes=[kf[pair][:, 0:1]])
                for uu in range(4):
                    s.op('gpsimd', lambda e, j=j, uu=uu: e.dma_start(
                        out=vtm[:, :].rearrange("p (tt u g) -> p tt u g", u=4, g=65)[:, j * 16:(j + 1) * 16, uu, 0:64],
                        in_=gx_v1[j * T:(j + 1) * T, bass.ds(EX['vt'][uu], 64)].rearrange("(tt pp) g -> pp tt g", pp=128)), dma=True,
                        reads=[gx_v1], writes=[vtm[:, 0:1]])
            s.dma('gpsimd', m896[:, :], m896_d[:, :])
            s.op('vector', lambda v: v.memset(vtm[:, :].rearrange("p (x g) -> p x g", g=65)[:, :, 64:65], 1.0))
            s.op('vector', lambda v: v.memset(onesf[:, :], 1.0))

            NQT = N // 512
            for u in range(4):
                pair, po = u // 2, (u % 2) * 64
                qs2 = qf[pair]; ks2 = kf[pair]
                for qt in range(NQT):
                    nkt = 4 * qt + 4
                    first = True
                    for g0 in range(0, nkt, 2):
                        gn = min(2, nkt - g0)
                        for gi in range(gn):
                            kt = g0 + gi
                            s.op('tensor', lambda t, po=po, kt=kt, qt=qt, gi=gi, qs2=qs2, ks2=ks2: t.matmul(
                                psT[:, gi * 512: gi * 512 + 512],
                                ks2[po:po + 64, kt * 128: kt * 128 + 128],
                                qs2[po:po + 64, qt * 512: qt * 512 + 512],
                                start=True, stop=True, tile_position=(po, 0)))
                        s.op('scalar', lambda sc, gn=gn: sc.activation(
                            out=psb[:, 0:gn * 512], in_=psT[:, 0:gn * 512], func=AF.Exp, scale=0.125))
                        for gi in range(gn):
                            kt = g0 + gi
                            off = qt * 512 - kt * 128
                            if off <= 0:
                                s.op('vector', lambda v, gi=gi, off=off: v.tensor_tensor(
                                    psb[:, gi * 512: gi * 512 + 512], psb[:, gi * 512: gi * 512 + 512],
                                    m896[:, off + 384: off + 384 + 512], OP.mult))
                        for gi in range(gn):
                            kt = g0 + gi
                            s.op('tensor', lambda t, kt=kt, u=u, gi=gi, first=first and gi == 0: t.matmul(
                                pout[0:65, :], vtm[:, (kt * 4 + u) * 65: (kt * 4 + u) * 65 + 65],
                                psb[:, gi * 512: gi * 512 + 512],
                                start=first, stop=False))
                            first = False
                    s.op('vector', lambda v: v.reciprocal(zrow[:, :], pout[64:65, :]))
                    s.op('tensor', lambda t: t.matmul(pzb[:, :], onesf[:, :], zrow[:, :], start=True, stop=True))
                    s.op('scalar', lambda sc: sc.activation(out=numb[:, :], in_=pout[0:64, :], func=AF.Copy))
                    s.op('vector', lambda v, pair=pair, po=po, qt=qt: v.tensor_tensor(
                        ofm[pair][po:po + 64, qt * 512: qt * 512 + 512], numb[:, :], pzb[:, :], OP.mult))
            ov = ex_o.rearrange("(pair po) t -> pair po t", po=128)
            for pair in range(2):
                s.dma('sync', ov[pair], ofm[pair][:, :])
        coll([ex_o], [gx_o])
    if upto >= 6:
        _ph6()

    if upto >= 7:
        emit_oproj_res_ln(woT1_d, zc_d, gx_o,
                          lambda dc: ((dc * 2) // 4) * 256 + ((dc * 2) % 4) * 64)
        s.dma('sync', x3_i.rearrange("(c p) t -> p c t", p=128),
              stream[:, :].rearrange("p (c t) -> p c t", c=NCH))
    def _inner_layers():
      with contextlib.ExitStack() as esT:
        bpat = _sb(nc, esT, "bpat", [128, 2], BF16)
        bpatT = _sb(nc, esT, "bpatT", [2, 128], F32)
        mlow = _sb(nc, esT, "mlow", [128, 128], F32)
        mupp = _sb(nc, esT, "mupp", [128, 128], BF16)
        halom = _sb(nc, esT, "halom", [128, 2], F32)
        onesf = _sb(nc, esT, "onesf", [1, 128], F32)
        s.dma('gpsimd', bpat[:, :], blockpat_d[:, :])
        s.dma('gpsimd', bpatT[:, :], blockpatT_d[:, :])
        s.dma('gpsimd', mlow[:, :], mlow_d[:, :])
        s.dma('gpsimd', mupp[:, :], mupp_d[:, :])
        s.dma('gpsimd', halom[:, :], halom_d[:, :])
        s.op('vector', lambda v: v.memset(onesf[:, :], 1.0))

        for li in range(6):
            if upto < 8 + 2 * li:
                break
            _inner_attn_layer(li, bpat, bpatT, mlow, mupp, halom, onesf)
            if upto < 9 + 2 * li:
                break
            _inner_ffn_layer(li, onesf, final=(li == 5))

    def _inner_attn_layer(li, bpat, bpatT, mlow, mupp, halom, onesf):
        T_ = TPC
        TW = T_ + TH
        for dc in range(NCH):
            s.op('gpsimd', lambda e, dc=dc: e.dma_start(
                out=ex_hA[:, dc * TH:(dc + 1) * TH],
                in_=stream[:, dc * T_ + T_ - TH: dc * T_ + T_]), dma=True)
        coll([ex_hA], [gx_hA[0:4 * 128, :]])
        with contextlib.ExitStack() as esA:
            wqb = _sb(nc, esA, "wqb", [128, NCH * D], BF16)
            qscs = _sb(nc, esA, "qscs", [128, 1], F32); qscws = _sb(nc, esA, "qscws", [128, 1], F32)
            kscs = _sb(nc, esA, "kscs", [128, 1], F32); kscws = _sb(nc, esA, "kscws", [128, 1], F32)
            qsrow = _sb(nc, esA, "qsrow", [1, 64], F32); ksrow = _sb(nc, esA, "ksrow", [1, 64], F32)
            csb = _sb(nc, esA, "csb", [128, 1], F32)
            crow = _sb(nc, esA, "crow", [1, 2], F32)
            xbs = _sb(nc, esA, "xbs", [128, 512], BF16)
            nrm2 = _sb(nc, esA, "nrm2", [2, 512], F32)
            qraw = _sb(nc, esA, "qraw", [128, NCH * T_], BF16)
            kraw = _sb(nc, esA, "kraw", [128, NCH * TKV], BF16)
            psb = _sb(nc, esA, "psb", [128, 384], BF16)
            numb = _sb(nc, esA, "numb", [64, 128], F32)
            zrow = _sb(nc, esA, "zrow", [1, 512], F32)
            pp = _psn(nc, esA, "pp", [128, 512])
            pnrm = _psn(nc, esA, "pnrm", [2, 512])
            pbc = _psn(nc, esA, "pbc", [128, 512])
            psT = _psn(nc, esA, "psT", [128, 384])
            pacc = _psn(nc, esA, "pacc", [128, 128])
            pzb = _psn(nc, esA, "pzb", [64, 128])

            s.dma('gpsimd', qscs[:, :], qsc6_d[:, li:li + 1], allow_slow_non_contiguous=True)
            s.dma('gpsimd', qscws[:, :], qscw6_d[:, li:li + 1], allow_slow_non_contiguous=True)
            s.dma('gpsimd', kscs[:, :], ksc6_d[:, li:li + 1], allow_slow_non_contiguous=True)
            s.dma('gpsimd', kscws[:, :], kscw6_d[:, li:li + 1], allow_slow_non_contiguous=True)
            s.dma('gpsimd', qsrow[:, :], qsr6_d[li:li + 1, :])
            s.dma('gpsimd', ksrow[:, :], ksr6_d[li:li + 1, :])
            s.op('vector', lambda v: v.reduce_max(crow[0:1, 0:1], qsrow[:, :], axis=mybir.AxisListType.X))
            s.op('vector', lambda v: v.reduce_max(crow[0:1, 1:2], ksrow[:, :], axis=mybir.AxisListType.X))
            s.op('vector', lambda v: v.tensor_mul(crow[0:1, 0:1], crow[0:1, 0:1], crow[0:1, 1:2]))
            s.op('vector', lambda v: v.tensor_scalar_mul(crow[0:1, 0:1], crow[0:1, 0:1], -8.0))
            s.op('tensor', lambda t: t.matmul(pbc[:, 0:1], onesf[:, :], crow[0:1, 0:1], start=True, stop=True))
            s.op('vector', lambda v: v.tensor_copy(csb[:, :], pbc[:, 0:1]))

            wv_ = wqkv6_d[li].rearrange("(kc p) d -> p kc d", p=128)

            with contextlib.ExitStack() as esY:
                ysb = _sb(nc, esY, "ysb", [128, NCH * TW], BF16)
                otv = _sb(nc, esY, "otv", [128, D], BF16)
                s.op('sync', lambda e, ysb=ysb: e.dma_start(
                    out=ysb[:, :].rearrange("p (c t) -> p c t", c=NCH)[:, :, 0:TH],
                    in_=gx_hA[bass.ds(EX['hrow'], 128), :].rearrange("pp (c t) -> pp c t", c=NCH)), dma=True,
                    reads=[gx_hA], writes=[ysb[:, 0:1]])
                for dc in range(NCH):
                    s.op('vector', lambda v, dc=dc: v.tensor_copy(
                        ysb[:, dc * TW + TH: dc * TW + TH + T_], stream[:, dc * T_:(dc + 1) * T_]))

                def xoff(dc, st):
                    return dc * TW + st + (TH if dc < 4 else TH - 1)

                for (woff2, dst, ncolw, base) in ((0, qraw, T_, 0), (D, kraw, TKV, -256)):
                    s.dma('gpsimd', wqb[:, :].rearrange("p (kc d) -> p kc d", kc=NCH),
                          wv_[:, :, woff2: woff2 + D])
                    ntile = (ncolw + 511) // 512
                    for ti in range(ntile):
                        st = ti * 512
                        w = min(512, ncolw - st)
                        for dc in range(NCH):
                            for kc in range(NCH):
                                s.op('tensor', lambda t, kc=kc, dc=dc, w=w, st=st, base=base: t.matmul(
                                    pp[:, 0:w], wqb[:, kc * D + dc * 128: kc * D + dc * 128 + 128],
                                    ysb[:, xoff(kc, st + base): xoff(kc, st + base) + w],
                                    start=(kc == 0), stop=(kc == NCH - 1)))
                            s.op('scalar', lambda sc, dc=dc, st=st, w=w, dst=dst, ncolw=ncolw: sc.activation(
                                out=dst[:, dc * ncolw + st: dc * ncolw + st + w], in_=pp[:, 0:w], func=AF.Copy))
                s.dma('gpsimd', wqb[:, :].rearrange("p (kc d) -> p kc d", kc=NCH),
                      wv_[:, :, 2 * D: 3 * D])
                for jt in range(TKV // 128):
                    st = jt * 128 - 256
                    for qc in range(2):
                        for kc in range(NCH):
                            s.op('tensor', lambda t, kc=kc, qc=qc, st=st: t.matmul(
                                pp[:, :], ysb[:, xoff(kc, st): xoff(kc, st) + 128],
                                wqb[:, kc * D + qc * 512: kc * D + qc * 512 + 512],
                                start=(kc == 0), stop=(kc == NCH - 1)))
                        s.op('scalar', lambda sc, qc=qc: sc.activation(
                            out=otv[:, qc * 512: qc * 512 + 512], in_=pp[:, :], func=AF.Copy))
                    s.op('sync', lambda e, jt=jt: e.dma_start(
                        out=vb_i[jt * 128:(jt + 1) * 128, :], in_=otv[:, :]), dma=True)
            s.barrier()

            esR = contextlib.ExitStack()
            cqs = _sb(nc, esR, "cqs", [128, TKV], BF16)
            sqs = _sb(nc, esR, "sqs", [128, TKV], BF16)
            qsw = _sb(nc, esR, "qsw", [128, TKV], BF16)
            for (dst, ncolw, tq, ts_, scl, sclw) in ((qraw, T_, cq_d, sq_d, qscs, qscws),
                                                     (kraw, TKV, ck_d, sk_d, kscs, kscws)):
                s.dma('gpsimd', cqs[:, 0:ncolw], tq[:, 0:ncolw])
                s.dma('gpsimd', sqs[:, 0:ncolw], ts_[:, 0:ncolw])
                ntile = (ncolw + 511) // 512
                for dc in range(NCH):
                    qo = dc * ncolw
                    for ti in range(ntile):
                        st = ti * 512
                        w = min(512, ncolw - st)
                        s.op('scalar', lambda sc, qo=qo, st=st, w=w, dst=dst: sc.activation(
                            out=xbs[:, 0:w], in_=dst[:, qo + st: qo + st + w], func=AF.Square))
                        s.op('tensor', lambda t, w=w: t.matmul(
                            pnrm[:, 0:w], bpat[:, :], xbs[:, 0:w], start=True, stop=True))
                        s.op('scalar', lambda sc, w=w: sc.activation(out=nrm2[:, 0:w], in_=pnrm[:, 0:w], func=AF.Sqrt))
                        s.op('vector', lambda v, w=w: v.tensor_scalar_max(nrm2[:, 0:w], nrm2[:, 0:w], 1e-12))
                        s.op('vector', lambda v, w=w: v.reciprocal(nrm2[:, 0:w], nrm2[:, 0:w]))
                        s.op('tensor', lambda t, w=w: t.matmul(
                            pbc[:, 0:w], bpatT[:, :], nrm2[:, 0:w], start=True, stop=True))
                        s.op('vector', lambda v, qo=qo, st=st, w=w, dst=dst: v.tensor_tensor(
                            dst[:, qo + st: qo + st + w], dst[:, qo + st: qo + st + w], pbc[:, 0:w], OP.mult))
                    for g in range(4):
                        src = (g ^ 1) * 32
                        s.op('vector', lambda v, qo=qo, g=g, src=src, dst=dst, ncolw=ncolw: v.tensor_copy(
                            qsw[g * 32:(g + 1) * 32, 0:ncolw], dst[src:src + 32, qo: qo + ncolw]))
                    s.op('vector', lambda v, qo=qo, dst=dst, ncolw=ncolw, scl=scl: v.tensor_scalar_mul(
                        dst[:, qo: qo + ncolw], dst[:, qo: qo + ncolw], scl[:, :]))
                    s.op('vector', lambda v, ncolw=ncolw, sclw=sclw: v.tensor_scalar_mul(
                        qsw[:, 0:ncolw], qsw[:, 0:ncolw], sclw[:, :]))
                    s.op('vector', lambda v, qo=qo, dst=dst, ncolw=ncolw: v.tensor_tensor(
                        dst[:, qo: qo + ncolw], dst[:, qo: qo + ncolw], cqs[:, 0:ncolw], OP.mult))
                    s.op('vector', lambda v, ncolw=ncolw: v.tensor_tensor(
                        qsw[:, 0:ncolw], qsw[:, 0:ncolw], sqs[:, 0:ncolw], OP.mult))
                    s.op('vector', lambda v, qo=qo, dst=dst, ncolw=ncolw: v.tensor_add(
                        dst[:, qo: qo + ncolw], dst[:, qo: qo + ncolw], qsw[:, 0:ncolw]))

            esR.close()
            s.barrier()
            with contextlib.ExitStack() as esV:
                vtm = _sb(nc, esV, "vtm", [128, (TKV // 128) * 16 * 65], BF16)
                s.op('vector', lambda v: v.memset(
                    vtm[:, :].rearrange("p (x g) -> p x g", g=65)[:, :, 64:65], 1.0))
                for jt in range(TKV // 128):
                    s.op('gpsimd', lambda e, jt=jt: e.dma_start(
                        out=vtm[:, :].rearrange("p (j h g) -> p j h g", h=16, g=65)[:, jt, :, 0:64],
                        in_=vb_i[jt * 128:(jt + 1) * 128, :].rearrange("pp (h g) -> pp h g", g=64)), dma=True)

                aout = qraw
                for h in range(16):
                    dc = h // 2
                    hb = (h % 2) * 64
                    for qt in range(T_ // 128):
                        for d_i, kvt in enumerate((qt, qt + 1, qt + 2)):
                            s.op('tensor', lambda t, dc=dc, hb=hb, kvt=kvt, qt=qt, d_i=d_i: t.matmul(
                                psT[:, d_i * 128: d_i * 128 + 128],
                                kraw[hb:hb + 64, dc * TKV + kvt * 128: dc * TKV + kvt * 128 + 128],
                                qraw[hb:hb + 64, dc * T_ + qt * 128: dc * T_ + qt * 128 + 128],
                                start=True, stop=True, tile_position=(hb, 0)))
                        s.op('vector', lambda v: v.tensor_tensor(psT[:, 256:384], psT[:, 256:384], mlow[:, :], OP.add))
                        s.op('scalar', lambda sc: sc.activation(out=psb[:, :], in_=psT[:, :], func=AF.Exp,
                                                                scale=8.0, bias=csb[:, :]))
                        s.op('vector', lambda v: v.tensor_tensor(psb[:, 0:128], psb[:, 0:128], mupp[:, :], OP.mult))
                        if qt == 0:
                            s.op('vector', lambda v: v.tensor_scalar_mul(psb[:, 0:128], psb[:, 0:128], halom[:, 0:1]))
                            s.op('vector', lambda v: v.tensor_scalar_mul(psb[:, 128:256], psb[:, 128:256], halom[:, 1:2]))
                        elif qt == 1:
                            s.op('vector', lambda v: v.tensor_scalar_mul(psb[:, 0:128], psb[:, 0:128], halom[:, 1:2]))
                        for d_i, kvt in enumerate((qt, qt + 1, qt + 2)):
                            s.op('tensor', lambda t, kvt=kvt, h=h, d_i=d_i: t.matmul(
                                pacc[0:65, :], vtm[:, (kvt * 16 + h) * 65: (kvt * 16 + h) * 65 + 65],
                                psb[:, d_i * 128: d_i * 128 + 128],
                                start=(d_i == 0), stop=(d_i == 2)))
                        s.op('vector', lambda v: v.reciprocal(zrow[0:1, 0:128], pacc[64:65, :]))
                        s.op('tensor', lambda t: t.matmul(pzb[:, :], onesf[:, 0:64], zrow[0:1, 0:128], start=True, stop=True))
                        s.op('scalar', lambda sc: sc.activation(out=numb[:, 0:128], in_=pacc[0:64, :], func=AF.Copy))
                        s.op('vector', lambda v, dc=dc, hb=hb, qt=qt: v.tensor_tensor(
                            aout[hb:hb + 64, dc * T_ + qt * 128: dc * T_ + qt * 128 + 128],
                            numb[:, 0:128], pzb[:, :], OP.mult))

                s.dma('gpsimd', wqb[:, :].rearrange("p (kc d) -> p kc d", kc=NCH),
                      wo6_d[li].rearrange("(kc p) d -> p kc d", p=128))
                for tt in range(T_ // 512):
                    for dc in range(NCH):
                        fm_proj(p, pp, wqb, aout, dout=D, dc=dc,
                                xoff=lambda kc, tt=tt: kc * T_ + tt * 512, ncols=512, nk=NCH)
                        co = dc * T_ + tt * 512
                        s.op('vector', lambda v, co=co: v.tensor_add(
                            stream[:, co:co + 512], pp[:, :], stream[:, co:co + 512]))
        s.barrier()

    def _inner_ffn_layer(li, onesf, final=False):
        T_ = TPC
        DI = 2730
        emit_halo1()
        with contextlib.ExitStack() as esF:
            hl = _sb(nc, esF, "hl", [128, NCH], F32)
            xn = _sb(nc, esF, "xn", [128, NCH * T_], BF16)
            hsb = _sb(nc, esF, "hsb", [128, 11 * T_], BF16)
            w2s = _sb(nc, esF, "w2s", [128, 11 * D], BF16)
            w1a = _sb(nc, esF, "w1a", [128, NCH * 128], BF16)
            w1g = _sb(nc, esF, "w1g", [128, NCH * 128], BF16)
            b1s = _sb(nc, esF, "b1s", [128, 44], F32)
            b2s = _sb(nc, esF, "b2s", [128, NCH], F32)
            gs = _sb(nc, esF, "gs", [128, NCH], F32); bs = _sb(nc, esF, "bs", [128, NCH], F32)
            gbf = _sb(nc, esF, "gbf", [128, 512], BF16)
            ub = _sb(nc, esF, "ub", [128, NCH * 512], BF16)
            usq = _sb(nc, esF, "usq", [128, 512], BF16)
            t1 = _sb(nc, esF, "t1", [128, 512], F32)
            xp0 = _sb(nc, esF, "xp0", [128, 512], F32)
            onesc = _sb(nc, esF, "onesc", [128, 1], BF16)
            onesr = _sb(nc, esF, "onesr", [1, 128], F32)
            srow = _sb(nc, esF, "srow", [1, 1024], F32)
            trow = _sb(nc, esF, "trow", [1, 512], F32)
            ufin = _sb(nc, esF, "ufin", [128, NCH * 512], F32) if final else None
            pp = _psn(nc, esF, "pp", [128, 512])
            pg = _psn(nc, esF, "pg", [128, 512])
            pout = _psn(nc, esF, "pout", [128, 1024])
            pab = _psn(nc, esF, "pab", [128, 1024])
            pstat = _psn(nc, esF, "pstat", [1, 1024])

            read_halo1(hl)
            s.dma('gpsimd', b1s[:, :], b1c6_d[:, li * 44:(li + 1) * 44])
            s.dma('gpsimd', b2s[:, :], b2c6_d[:, li * NCH:(li + 1) * NCH])
            s.dma('gpsimd', gs[:, :], gc6_d[:, li * NCH:(li + 1) * NCH])
            s.dma('gpsimd', bs[:, :], bc6_d[:, li * NCH:(li + 1) * NCH])
            s.op('vector', lambda v: v.memset(onesc[:, :], 1.0))
            s.op('vector', lambda v: v.memset(onesr[:, :], 1.0))

            def sh_copy(dc, tt, dstbuf):
                co = dc * T_ + tt * 512
                if dc < 4:
                    s.op('vector', lambda v, co=co, dstbuf=dstbuf: v.tensor_copy(
                        dstbuf[:, :], stream[:, co:co + 512]))
                elif tt > 0:
                    s.op('vector', lambda v, co=co, dstbuf=dstbuf: v.tensor_copy(
                        dstbuf[:, :], stream[:, co - 1:co + 511]))
                else:
                    s.op('vector', lambda v, dc=dc, dstbuf=dstbuf: v.tensor_copy(
                        dstbuf[:, 0:1], hl[:, dc:dc + 1]))
                    s.op('vector', lambda v, co=co, dstbuf=dstbuf: v.tensor_copy(
                        dstbuf[:, 1:512], stream[:, co:co + 511]))

            NT = T_ // 512
            for tt in range(NT):
                for dc in range(NCH):
                    sh_copy(dc, tt, xp0)
                    s.op('vector', lambda v, dc=dc: v.tensor_copy(
                        ub[:, dc * 512: dc * 512 + 512], xp0[:, :]))
                for dc in range(NCH):
                    uo = dc * 512
                    s.op('scalar', lambda sc, uo=uo: sc.activation(out=usq[:, :], in_=ub[:, uo:uo + 512], func=AF.Square))
                    s.op('tensor', lambda t, dc=dc, uo=uo: t.matmul(
                        pstat[0:1, 0:512], onesc[:, :], ub[:, uo:uo + 512], start=(dc == 0), stop=(dc == NCH - 1)))
                    s.op('tensor', lambda t, dc=dc: t.matmul(
                        pstat[0:1, 512:1024], onesc[:, :], usq[:, :], start=(dc == 0), stop=(dc == NCH - 1)))
                s.op('vector', lambda v: v.tensor_scalar_mul(srow[:, :], pstat[0:1, :], 1.0 / D))
                s.op('scalar', lambda sc: sc.activation(out=trow[0:1, :], in_=srow[0:1, 0:512], func=AF.Square))
                s.op('vector', lambda v: v.tensor_sub(srow[0:1, 512:1024], srow[0:1, 512:1024], trow[0:1, :]))
                s.op('vector', lambda v: v.tensor_scalar_add(srow[0:1, 512:1024], srow[0:1, 512:1024], 1e-5))
                s.op('scalar', lambda sc: sc.activation(out=srow[0:1, 512:1024], in_=srow[0:1, 512:1024], func=AF.Sqrt))
                s.op('vector', lambda v: v.reciprocal(srow[0:1, 512:1024], srow[0:1, 512:1024]))
                s.op('vector', lambda v: v.tensor_mul(trow[0:1, :], srow[0:1, 0:512], srow[0:1, 512:1024]))
                s.op('vector', lambda v: v.tensor_scalar_mul(trow[0:1, :], trow[0:1, :], -1.0))
                s.op('tensor', lambda t: t.matmul(pab[:, 0:512], onesr[:, :], srow[0:1, 512:1024], start=True, stop=True))
                s.op('tensor', lambda t: t.matmul(pab[:, 512:1024], onesr[:, :], trow[0:1, :], start=True, stop=True))
                for dc in range(NCH):
                    co = dc * T_ + tt * 512
                    sh_copy(dc, tt, xp0)
                    s.op('vector', lambda v: v.tensor_tensor(t1[:, :], xp0[:, :], pab[:, 0:512], OP.mult))
                    s.op('vector', lambda v: v.tensor_tensor(t1[:, :], t1[:, :], pab[:, 512:1024], OP.add))
                    s.op('vector', lambda v, dc=dc, co=co: v.tensor_scalar(
                        xn[:, co:co + 512], t1[:, :], gs[:, dc:dc + 1], bs[:, dc:dc + 1], OP.mult, OP.add))

            jblocks_half = [(i * 128, 128) for i in range(10)] + [(1280, 85)]
            wview = w16_d[li].rearrange("(kc p) d -> p kc d", p=128)
            w2v = w26_d[li]
            for half in range(2):
                j0h = half * 1365
                for jbi, (jo, jw) in enumerate(jblocks_half):
                    jg = j0h + jo
                    s.dma('gpsimd', w1a[:, 0:NCH * jw].rearrange("p (kc d) -> p kc d", kc=NCH),
                          wview[:, :, jg:jg + jw])
                    s.dma('gpsimd', w1g[:, 0:NCH * jw].rearrange("p (kc d) -> p kc d", kc=NCH),
                          wview[:, :, DI + jg:DI + jg + jw])
                    for tt in range(NT):
                        for (wsb, psum) in ((w1a, pp), (w1g, pg)):
                            for kc in range(NCH):
                                s.op('tensor', lambda t, kc=kc, tt=tt, jw=jw, wsb=wsb, psum=psum: t.matmul(
                                    psum[0:jw, 0:512], wsb[:, kc * jw: kc * jw + jw],
                                    xn[:, kc * T_ + tt * 512: kc * T_ + tt * 512 + 512],
                                    start=(kc == 0), stop=(kc == NCH - 1)))
                        cga = (0 if half == 0 else 11) + jbi
                        cgg = (22 if half == 0 else 33) + jbi
                        s.op('scalar', lambda sc, jw=jw, cgg=cgg: sc.activation(
                            out=gbf[0:jw, :], in_=pg[0:jw, :], func=AF.Gelu, bias=b1s[0:jw, cgg:cgg + 1]))
                        s.op('vector', lambda v, jo=jo, tt=tt, jw=jw, cga=cga: v.scalar_tensor_tensor(
                            hsb[0:jw, (jo // 128) * T_ + tt * 512: (jo // 128) * T_ + tt * 512 + 512],
                            pp[0:jw, 0:512], b1s[0:jw, cga:cga + 1], gbf[0:jw, :], OP.add, OP.mult))
                s.dma('gpsimd', w2s[:, 0:10 * D].rearrange("p (jb d) -> p jb d", jb=10),
                      w2v[j0h:j0h + 1280, :].rearrange("(jb p) d -> p jb d", p=128))
                s.dma('gpsimd', w2s[0:85, 10 * D:11 * D], w2v[j0h + 1280:j0h + 1365, :])
                for t128 in range(T_ // 128):
                    for dc in range(NCH):
                        for jbi, (jo, jw) in enumerate(jblocks_half):
                            s.op('tensor', lambda t, jbi=jbi, jo=jo, jw=jw, dc=dc, t128=t128: t.matmul(
                                pout[:, dc * 128: dc * 128 + 128],
                                w2s[0:jw, jbi * D + dc * 128: jbi * D + dc * 128 + 128],
                                hsb[0:jw, (jo // 128) * T_ + t128 * 128: (jo // 128) * T_ + t128 * 128 + 128],
                                start=(jbi == 0), stop=(jbi == 10)))
                    for dc in range(NCH):
                        yo = dc * T_ + t128 * 128
                        if half == 0:
                            s.op('vector', lambda v, dc=dc, yo=yo: v.tensor_add(
                                stream[:, yo:yo + 128], pout[:, dc * 128: dc * 128 + 128], stream[:, yo:yo + 128]))
                        else:
                            s.op('vector', lambda v, dc=dc, yo=yo: v.scalar_tensor_tensor(
                                stream[:, yo:yo + 128], pout[:, dc * 128: dc * 128 + 128], b2s[:, dc:dc + 1],
                                stream[:, yo:yo + 128], OP.add, OP.add))

            if final:
                x3v = x3_i.rearrange("(c p) t -> p c t", p=128)
                for tt in range(NT):
                    s.dma('sync', ufin[:, :].rearrange("p (c t) -> p c t", c=NCH),
                          x3v[:, :, tt * 512:tt * 512 + 512])
                    for dc in range(NCH):
                        uo = dc * 512
                        co = dc * T_ + tt * 512
                        s.op('vector', lambda v, uo=uo, co=co: v.tensor_add(
                            ufin[:, uo:uo + 512], ufin[:, uo:uo + 512], stream[:, co:co + 512]))
                        s.op('vector', lambda v, uo=uo: v.tensor_copy(ub[:, uo:uo + 512], ufin[:, uo:uo + 512]))
                    emit_res_ln(p, u=ufin, ub=ub, usq=usq, onesc=onesc, onesr=onesr,
                                srow=srow, trow=trow, pab=pab, pstat=pstat, tt=tt, T=T_,
                                ucol=lambda dc: dc * 512)
                    for dc in range(NCH):
                        co = dc * T_ + tt * 512
                        s.op('vector', lambda v, dc=dc, co=co: v.tensor_copy(
                            stream[:, co:co + 512], ufin[:, dc * 512:dc * 512 + 512]))
        s.barrier()

    if upto >= 8:
        _inner_layers()

    if upto == 5:
        dq = p.dout("dbg_q", [D, T], BF16)
        dk = p.dout("dbg_k", [D, T], BF16)
        dv = p.dout("dbg_v", [T, D], BF16)
        with contextlib.ExitStack() as esd:
            tmp = _sb(nc, esd, "dtmp", [128, T], BF16)
            for dc in range(NCH):
                s.dma('gpsimd', tmp[:, :], ex_q1[dc * 128:(dc + 1) * 128, :])
                s.dma('sync', dq[dc * 128:(dc + 1) * 128, :], tmp[:, :])
                s.dma('gpsimd', tmp[:, :], ex_k1[dc * 128:(dc + 1) * 128, :])
                s.dma('sync', dk[dc * 128:(dc + 1) * 128, :], tmp[:, :])
            for jt in range(T // 128):
                s.dma('gpsimd', tmp[:, 0:D], ex_v1[jt * 128:(jt + 1) * 128, :])
                s.dma('sync', dv[jt * 128:(jt + 1) * 128, :], tmp[:, 0:D])

    s.dma('sync', y_d.rearrange("(c p) t -> p c t", p=128),
          stream[:, :].rearrange("p (c t) -> p c t", c=NCH))
    p.finish()
    return p


def _meta_vals(c):
    jj = c % 4
    m = np.zeros((32, 1), np.int32)
    m[0] = jj * 2048
    m[1] = (jj - 1) * 128 if jj > 0 else 4 * 128
    m[2] = jj * 128
    m[3] = jj * 256
    return m


def kernel_fused(I, upto=99):
    x = np.asarray(I["L0_attn_x"])
    xt = x.reshape(NC, TPC, D)
    pF = cached_prog("fused%d" % upto, lambda p: None) if False else None
    key = "fused%d" % upto
    if key not in _PROG_CACHE:
        _PROG_CACHE[key] = build_fused(upto)
    pF = _PROG_CACHE[key]

    bq = np.ascontiguousarray(I["L0_attn_bq"].reshape(4, 128).T)
    bk = np.ascontiguousarray(I["L0_attn_bk"].reshape(4, 128).T)
    k_ = np.arange(128)[:, None]
    m384 = (k_ <= np.arange(384)[None, :] - 128).astype(BF)
    m896 = (k_ <= np.arange(896)[None, :] - 384).astype(BF)
    kq = np.arange(128)[:, None]; qq = np.arange(128)[None, :]
    mlow = np.where(kq <= qq, 0.0, -1e30).astype(np.float32)
    mupp = (kq >= qq).astype(BF)
    bpat = np.zeros((128, 2), np.float32)
    bpat[0:64, 0] = 1.0; bpat[64:128, 1] = 1.0
    bpatTf = np.ascontiguousarray(bpat.T.astype(np.float32))
    bpatb = bpat.astype(BF)
    DI = 2730
    qsc6 = np.zeros((128, 6), np.float32); qscw6 = np.zeros((128, 6), np.float32)
    ksc6 = np.zeros((128, 6), np.float32); kscw6 = np.zeros((128, 6), np.float32)
    qsr6 = np.zeros((6, 64), np.float32); ksr6 = np.zeros((6, 64), np.float32)
    b1c6 = np.zeros((128, 6 * 44), np.float32)
    b2c6 = np.zeros((128, 6 * NCH), np.float32)
    gc6 = np.zeros((128, 6 * NCH), np.float32); bc6 = np.zeros((128, 6 * NCH), np.float32)
    for li in range(6):
        qs_i = np.asarray(I["L1_ffn_q_scale"][li]); ks_i = np.asarray(I["L1_ffn_k_scale"][li])
        qsc6[:, li] = np.tile(qs_i, 2); ksc6[:, li] = np.tile(ks_i, 2)
        qscw6[:, li] = np.tile(np.concatenate([qs_i[32:], qs_i[:32]]), 2)
        kscw6[:, li] = np.tile(np.concatenate([ks_i[32:], ks_i[:32]]), 2)
        qsr6[li] = qs_i; ksr6[li] = ks_i
        b1_i = np.asarray(I["L1_ffn_b1"][li])
        col = 0
        for part0 in (0, DI):
            for half in range(2):
                for jbi in range(11):
                    jo = jbi * 128
                    jw = min(128, 1365 - jo)
                    b1c6[0:jw, li * 44 + col] = b1_i[part0 + half * 1365 + jo: part0 + half * 1365 + jo + jw]
                    col += 1
        b2c6[:, li * NCH:(li + 1) * NCH] = I["L1_ffn_b2"][li].reshape(NCH, 128).T
        gc6[:, li * NCH:(li + 1) * NCH] = I["L1_ffn_ln_g"][li].reshape(NCH, 128).T
        bc6[:, li * NCH:(li + 1) * NCH] = I["L1_ffn_ln_b"][li].reshape(NCH, 128).T

    ims = []
    for c in range(NC):
        t0c = (c % 4) * TPC
        cosT, ssinT = _module_rope(np.arange(t0c, t0c + TPC))
        cq, sq, _, _ = _inner_rope(np.arange(t0c, t0c + TPC))
        _, _, ck, sk = _inner_rope(np.arange(t0c - 256, t0c + TPC))
        halom = np.zeros((128, 2), np.float32) if c % 4 == 0 else np.ones((128, 2), np.float32)
        ims.append({
            "xfm": _fm(xt[c]), "meta": _meta_vals(c),
            "wq": I["L0_attn_wq"], "wk": I["L0_attn_wk"], "wv": I["L0_attn_wv"],
            "bqc": bq, "bkc": bk,
            "bkr": I["L0_attn_bk"].reshape(1, 512), "bvr": I["L0_attn_bv"].reshape(1, D),
            "wo0": I["L0_attn_wo"], "boc": _colpack(I["L0_attn_bo"]),
            "mask384": m384,
            "wrT": _fm(I["L0_ffn_W_r"]), "wkT": _fm(I["L0_ffn_W_k"]), "wvT": _fm(I["L0_ffn_W_v"]),
            "murc": _colpack(I["L0_ffn_mu_r"]), "mukc": _colpack(I["L0_ffn_mu_k"]),
            "qkvT": _fm(I["L1_attn_qkv"]), "cosT": cosT, "ssinT": ssinT,
            "m896": m896, "woT1": _fm(I["L1_attn_wo"]),
            "wqkv6": I["L1_ffn_Wqkv"], "wo6": I["L1_ffn_Wo"],
            "w16": I["L1_ffn_W1"], "w26": I["L1_ffn_W2"],
            "cq": cq, "sq": sq, "ck": ck, "sk": sk,
            "qsc6": qsc6, "qscw6": qscw6, "ksc6": ksc6, "kscw6": kscw6,
            "qsr6": qsr6, "ksr6": ksr6,
            "blockpat": bpatb, "blockpatT": bpatTf,
            "mlow": mlow, "mupp": mupp, "halom": halom,
            "b1c6": b1c6, "b2c6": b2c6, "gc6": gc6, "bc6": bc6,
            "zc": np.zeros((128, NCH), np.float32),
        })
    r = run(pF, ims)
    y = [np.asarray(r[c]["y"]) for c in range(NC)]
    out = np.concatenate([a.T for a in y], axis=0).reshape(B, N, D)
    out = np.ascontiguousarray(out.astype(np.float32, copy=False))
    if _RAW:
        return out, r
    return out


# version 05 (final)
# speedup vs baseline: 1.8038x; 1.0867x over previous
import contextlib
import numpy as np
import ml_dtypes
import concourse.bass as bass
import concourse.mybir as mybir
from concourse.bass_utils import run_bass_kernel_spmd

F32 = mybir.dt.float32
BF16 = mybir.dt.bfloat16
AF = mybir.ActivationFunctionType
OP = mybir.AluOpType

BF = ml_dtypes.bfloat16

B, N, D = 2, 8192, 1024
NC = 8
TPC = N * B // NC
NCH = D // 128

ENGS = ('sync', 'gpsimd', 'vector', 'scalar', 'tensor')
SERIAL = False

class _CapDummy:
    def then_inc(self, *a, **k):
        return self

class _CapEng:
    def __init__(self):
        self.calls = []
    def __getattr__(self, name):
        def call(*a, **kw):
            self.calls.append((name, a, kw))
            return _CapDummy()
        return call

_INF = 1 << 40

def _key(x):
    if not isinstance(x, bass.AP):
        n = getattr(x, 'name', None)
        return (n, (0, _INF, 0, _INF)) if isinstance(n, str) else None
    if str(x.space) != 'MemorySpace.SBUF':
        return (x.name, (0, _INF, 0, _INF))
    pat = x.ap
    off = x.offset
    if not pat:
        return (x.name, (0, _INF, 0, _INF))
    pitch, npart = pat[0]
    if pitch <= 0:
        return (x.name, (0, _INF, 0, _INF))
    p0 = off // pitch
    c0 = off % pitch
    span = 1
    for (st, sz) in pat[1:]:
        span += (sz - 1) * abs(st)
    return (x.name, (p0, p0 + npart, c0, c0 + span))

def _capture(fn):
    ce = _CapEng()
    fn(ce)
    reads, writes = [], []
    for (name, a, kw) in ce.calls:
        aps = [v for v in a if isinstance(v, bass.AP)]
        kwaps = {k: v for k, v in kw.items() if isinstance(v, bass.AP)}
        if name == 'dma_start':
            o = kwaps.get('out', aps[0] if aps else None)
            i = kwaps.get('in_', None)
            if i is None:
                rest = [v for v in aps if v is not o]
                i = rest[0] if rest else None
            if o is not None:
                writes.append(o)
            if i is not None:
                reads.append(i)
        elif name == 'matmul':
            writes.append(aps[0])
            reads.extend(aps)
        elif name == 'activation':
            o = kwaps.pop('out')
            writes.append(o)
            reads.extend(kwaps.values())
            reads.extend(aps)
        elif name == 'memset':
            writes.append(aps[0])
        else:
            if aps:
                writes.append(aps[0])
                reads.extend(aps[1:])
            reads.extend(kwaps.values())
    return reads, writes


class Seq:
    def __init__(self, nc):
        self.nc = nc
        self.ops = []

    def op(self, eng, fn, dma=False, reads=None, writes=None, async_=False):
        if reads is None and writes is None:
            r, w = _capture(fn)
        else:
            r, w = list(reads or []), list(writes or [])
        rk = [k for k in (_key(x) for x in r) if k is not None]
        wk = [k for k in (_key(x) for x in w) if k is not None]
        self.ops.append((eng, fn, 16 if dma else 1, rk, wk, dma or async_))

    def dma(self, eng, out, in_, **kw):
        self.op(eng, lambda e, out=out, in_=in_, kw=kw: e.dma_start(out=out, in_=in_, **kw),
                dma=True, reads=[in_], writes=[out])

    def barrier(self):
        self.ops.append(('BARRIER', None, 0, None, None, False))

    def build(self, blk, sems):
        if SERIAL:
            return self._build_serial(blk, sems)
        import bisect
        cnt = {e: 0 for e in ENGS}
        clock = {e: {f: 0 for f in ENGS} for e in ENGS}
        selfw = {e: 0 for e in ENGS}
        snaps = {e: ([], []) for e in ENGS}
        wmap = {}
        rmap = {}
        inflight = {e: [] for e in ENGS}
        sched = {e: [] for e in ENGS}

        def overlap(a, b):
            return a[0] < b[1] and b[0] < a[1] and a[2] < b[3] and b[2] < a[3]

        def covers(a, b):
            return a[0] <= b[0] and a[1] >= b[1] and a[2] <= b[2] and a[3] >= b[3]

        def merge_from(e, f, c):
            arr, clks = snaps[f]
            i = bisect.bisect_left(arr, c)
            if i >= len(arr):
                i = len(arr) - 1
            snapc = clks[i]
            ce = clock[e]
            for g in ENGS:
                if snapc[g] > ce[g]:
                    ce[g] = snapc[g]
            if c > ce[f]:
                ce[f] = c

        for (eng, fn, inc, reads, writes, isasync) in self.ops:
            if eng == 'BARRIER':
                for e in ENGS:
                    waits = {}
                    for f in ENGS:
                        if f == e:
                            if cnt[f] > selfw[f]:
                                waits[f] = cnt[f]
                                selfw[f] = cnt[f]
                        elif cnt[f] > clock[e][f]:
                            waits[f] = cnt[f]
                    sched[e].append((None, 0, waits))
                    for f, c in waits.items():
                        if f != e:
                            merge_from(e, f, c)
                wmap.clear()
                rmap.clear()
                continue
            need = {}
            def register(pe, pc, pasync):
                if pe == eng and not pasync:
                    return
                if pc > need.get(pe, 0):
                    need[pe] = pc
            for (nm, rc) in reads:
                for ent in wmap.get(nm, ()):
                    if overlap(ent[0], rc):
                        register(ent[1], ent[2], ent[3])
            for (nm, rc) in writes:
                for ent in wmap.get(nm, ()):
                    if overlap(ent[0], rc):
                        register(ent[1], ent[2], ent[3])
                for ent in rmap.get(nm, ()):
                    if overlap(ent[0], rc):
                        register(ent[1], ent[2], ent[3])
            if isasync:
                if eng in ('gpsimd', 'vector'):
                    if cnt[eng] > need.get(eng, 0):
                        need[eng] = cnt[eng]
                else:
                    q = inflight[eng]
                    if len(q) >= 4 and q[-4] > need.get(eng, 0):
                        need[eng] = q[-4]
            waits = {}
            for pe, pc in need.items():
                if pe == eng:
                    if pc > selfw[eng]:
                        waits[pe] = pc
                        selfw[eng] = pc
                else:
                    if pc > clock[eng][pe]:
                        waits[pe] = pc
            for pe, pc in waits.items():
                if pe != eng:
                    merge_from(eng, pe, pc)
            cnt[eng] += inc
            clock[eng][eng] = cnt[eng]
            if isasync:
                inflight[eng].append(cnt[eng])
            sched[eng].append((fn, inc, waits))
            arr, clks = snaps[eng]
            arr.append(cnt[eng])
            clks.append(dict(clock[eng]))
            for (nm, rc) in writes:
                lst = wmap.get(nm)
                if lst is None:
                    wmap[nm] = [[rc, eng, cnt[eng], isasync]]
                else:
                    lst[:] = [ent for ent in lst if not covers(rc, ent[0])]
                    lst.append([rc, eng, cnt[eng], isasync])
                rl = rmap.get(nm)
                if rl:
                    rl[:] = [ent for ent in rl if not covers(rc, ent[0])]
            for (nm, rc) in reads:
                rmap.setdefault(nm, []).append([rc, eng, cnt[eng], isasync])

        tot = dict(cnt)

        def mk(engname):
            def body(e):
                for (fn, inc, waits) in sched[engname]:
                    for f, c in waits.items():
                        e.wait_ge(sems[f], c)
                    if fn is not None:
                        fn(e).then_inc(sems[engname], inc)
                for f in ENGS:
                    if tot[f] > 0:
                        e.wait_ge(sems[f], tot[f])
            return body
        for name in ENGS:
            getattr(blk, name)(mk(name))

    def _build_serial(self, blk, sems):
        sem = sems[ENGS[0]]
        n = 0
        sched = []
        for rec in self.ops:
            if rec[0] == 'BARRIER':
                continue
            eng, fn, inc = rec[0], rec[1], rec[2]
            sched.append((eng, fn, n, inc))
            n += inc
        total = n

        def mk(engname):
            def body(e):
                for eng, fn, nwait, inc in sched:
                    if eng != engname:
                        continue
                    if nwait > 0:
                        e.wait_ge(sem, nwait)
                    fn(e).then_inc(sem, inc)
                if engname == 'sync':
                    e.wait_ge(sem, total)
            return body
        for name in ENGS:
            getattr(blk, name)(mk(name))


class Prog:
    def __init__(self, name):
        self.nc = bass.Bass("TRN2", target_bir_lowering=False)
        self.name = name
        self.s = Seq(self.nc)
        self.est = contextlib.ExitStack()
        self.inputs = {}
        self.outputs = {}

    def din(self, name, shape, dt=F32):
        t = self.nc.dram_tensor(name, list(shape), dt, kind="ExternalInput").ap()
        self.inputs[name] = t
        return t

    def dout(self, name, shape, dt=F32):
        t = self.nc.dram_tensor(name, list(shape), dt, kind="ExternalOutput").ap()
        self.outputs[name] = t
        return t

    def sb(self, name, shape, dt):
        return self.est.enter_context(self.nc.sbuf_tensor("s_" + name, list(shape), dt))

    def ps(self, name, shape, dt=F32):
        return self.est.enter_context(self.nc.psum_tensor("p_" + name, list(shape), dt))

    def finish(self):
        sems = {e: self.est.enter_context(self.nc.semaphore("sq_" + e)) for e in ENGS}
        blk = self.est.enter_context(self.nc.Block())
        self.s.build(blk, sems)
        self.est.close()
        return self.nc


_PROG_CACHE = {}

def cached_prog(key, builder):
    if key not in _PROG_CACHE:
        p = Prog(key)
        builder(p)
        p.finish()
        _PROG_CACHE[key] = p
    return _PROG_CACHE[key]


def run(p, in_maps, **kw):
    res = run_bass_kernel_spmd(p.nc, in_maps, core_ids=list(range(NC)), **kw)
    return res.results


def load_fm(p, dst, src, nch, T, eng='gpsimd'):
    v = src.rearrange("(c p) t -> p c t", p=128)
    d = dst[:, 0:nch * T].rearrange("p (c t) -> p c t", c=nch)
    p.s.dma(eng, d, v)


def load_w_fm(p, dst, src, din, dout, eng='gpsimd'):
    v = src.rearrange("(kc p) d -> p kc d", p=128)
    d = dst[:, 0:(din // 128) * dout].rearrange("p (kc d) -> p kc d", kc=din // 128)
    p.s.dma(eng, d, v)


def fm_proj(p, psum, wsb, xsb, *, dout, dc, xoff, ncols, nk, stop=True):
    for kc in range(nk):
        p.s.op('tensor', lambda t, kc=kc: t.matmul(
            psum[:, 0:ncols],
            wsb[:, kc * dout + dc * 128: kc * dout + dc * 128 + 128],
            xsb[:, xoff(kc): xoff(kc) + ncols],
            start=(kc == 0), stop=(stop and kc == nk - 1)))


def store_fm(p, dram, sbuf, nch, T):
    v = dram.rearrange("(c p) t -> p c t", p=128)
    p.s.dma('sync', v, sbuf[:, 0:nch * T].rearrange("p (c t) -> p c t", c=nch))


def store_tm(p, dram, sbuf, ntt, dout):
    v = dram.rearrange("(tt pp) d -> pp tt d", pp=128)
    p.s.dma('sync', v, sbuf[:, 0:ntt * dout].rearrange("p (tt d) -> p tt d", tt=ntt))


def build_l0_proj(p):
    T = TPC
    xfm = p.din("xfm", [D, T])
    wq = p.din("wq", [D, 512]); wk = p.din("wk", [D, 512]); wv = p.din("wv", [D, D])
    bqc = p.din("bqc", [128, 4]); bkc = p.din("bkc", [128, 4])
    bkr = p.din("bkr", [1, 512]); bvr = p.din("bvr", [1, D])
    q_fm = p.dout("q_fm", [512, T], BF16)
    k_fm = p.dout("k_fm", [512, T], BF16)
    k_tm = p.dout("k_tm", [T, 512], BF16)
    v_tm = p.dout("v_tm", [T, D], BF16)

    xb = p.sb("xb", [128, NCH * T], BF16)
    wqs = p.sb("wqs", [128, NCH * 512], BF16)
    wks = p.sb("wks", [128, NCH * 512], BF16)
    wvs = p.sb("wvs", [128, NCH * D], BF16)
    bq = p.sb("bq", [128, 4], F32); bk = p.sb("bk", [128, 4], F32)
    brk = p.sb("brk", [1, 512], BF16); brv = p.sb("brv", [1, D], BF16)
    ones1 = p.sb("ones1", [1, 128], BF16)
    of = p.sb("of", [128, 4 * T], BF16)
    ot = p.sb("ot", [128, (T // 128) * D], BF16)
    t1 = p.sb("t1", [128, 512], F32)
    t2 = p.sb("t2", [128, 512], F32)
    t3 = p.sb("t3", [128, 512], F32)
    pp = p.ps("pp", [128, 512])

    load_fm(p, xb, xfm, NCH, T)
    load_w_fm(p, wqs, wq, D, 512)
    load_w_fm(p, wks, wk, D, 512)
    load_w_fm(p, wvs, wv, D, D)
    p.s.dma('gpsimd', bq[:, :], bqc[:, :]); p.s.dma('gpsimd', bk[:, :], bkc[:, :])
    p.s.dma('gpsimd', brk[:, :], bkr[:, :]); p.s.dma('gpsimd', brv[:, :], bvr[:, :])
    p.s.op('vector', lambda v: v.memset(ones1[:, :], 1.0))

    NT = T // 512
    for (wsb, bias, dram) in ((wqs, bq, q_fm), (wks, bk, k_fm)):
        for dc in range(4):
            for tt in range(NT):
                fm_proj(p, pp, wsb, xb, dout=512, dc=dc,
                        xoff=lambda kc, tt=tt: kc * T + tt * 512, ncols=512, nk=NCH)
                p.s.op('vector', lambda v, bias=bias, dc=dc: v.tensor_scalar(
                    t1[:, :], pp[:, :], bias[:, dc:dc + 1], 0.0, OP.add, OP.min))
                p.s.op('scalar', lambda sc: sc.activation(out=t2[:, :], in_=t1[:, :], func=AF.Exp))
                p.s.op('vector', lambda v, bias=bias, dc=dc: v.tensor_scalar(
                    t3[:, :], pp[:, :], bias[:, dc:dc + 1], 0.0, OP.add, OP.max))
                p.s.op('vector', lambda v, dc=dc, tt=tt: v.tensor_add(
                    of[:, dc * T + tt * 512: dc * T + tt * 512 + 512], t2[:, :], t3[:, :]))
        store_fm(p, dram, of, 4, T)

    NT128 = T // 128
    for (wsb, dout, brow, dram, phi) in ((wks, 512, brk, k_tm, True), (wvs, D, brv, v_tm, False)):
        for tt in range(NT128):
            for qc in range(dout // 512):
                for kc in range(NCH):
                    p.s.op('tensor', lambda t, kc=kc, tt=tt, qc=qc, wsb=wsb, dout=dout: t.matmul(
                        pp[:, :], xb[:, kc * T + tt * 128: kc * T + tt * 128 + 128],
                        wsb[:, kc * dout + qc * 512: kc * dout + qc * 512 + 512],
                        start=(kc == 0), stop=False))
                p.s.op('tensor', lambda t, qc=qc, brow=brow: t.matmul(
                    pp[:, :], ones1[:, :], brow[:, qc * 512: qc * 512 + 512],
                    start=False, stop=True))
                co = tt * dout + qc * 512
                if phi:
                    p.s.op('vector', lambda v: v.tensor_scalar(t1[:, :], pp[:, :], 0.0, None, OP.min))
                    p.s.op('scalar', lambda sc: sc.activation(out=t2[:, :], in_=t1[:, :], func=AF.Exp))
                    p.s.op('vector', lambda v: v.tensor_scalar(t3[:, :], pp[:, :], 0.0, None, OP.max))
                    p.s.op('vector', lambda v, co=co: v.tensor_add(ot[:, co:co + 512], t2[:, :], t3[:, :]))
                else:
                    p.s.op('scalar', lambda sc, co=co: sc.activation(
                        out=ot[:, co:co + 512], in_=pp[:, :], func=AF.Copy))
        store_tm(p, dram, ot, NT128, dout)


CHK = 256

def build_l0_scan(p):
    S = N
    qf_d = p.din("qf", [128, S], BF16)
    kf_d = p.din("kf", [128, S], BF16)
    ktm_d = p.din("ktm", [S, 128], BF16)
    vtm_d = p.din("vtm", [S, 256], BF16)
    mask_d = p.din("mask384", [128, 384], BF16)
    o_d = p.dout("o_fm", [256, S], BF16)

    qf = p.sb("qf", [128, S], BF16)
    kf = p.sb("kf", [128, S], BF16)
    ktm = p.sb("ktm", [128, (S // 128) * 32 * 4], BF16)
    vtm = p.sb("vtm", [128, (S // 128) * 65 * 4], BF16)
    msk = p.sb("msk", [128, 384], BF16)
    pmask = p.sb("pmask", [128, 512], BF16)
    Ssb = p.sb("Ssb", [128, 65], BF16)
    numb = p.sb("numb", [64, 256], F32)
    zrow = p.sb("zrow", [1, 256], F32)
    onesf = p.sb("onesf", [1, 64], F32)
    ofm = p.sb("ofm", [128, 2 * S], BF16)
    psT = p.ps("psT", [128, 512])
    pacc = p.ps("pacc", [128, 256])
    pzb = p.ps("pzb", [64, 256])
    pst = p.ps("pst", [32, 65])

    p.s.dma('gpsimd', qf[:, :], qf_d[:, :])
    p.s.dma('gpsimd', kf[:, :], kf_d[:, :])
    for u in range(4):
        kv = ktm_d[:, u * 32:(u + 1) * 32].rearrange("(tt pp) f -> pp tt f", pp=128)
        p.s.dma('gpsimd', ktm[:, :].rearrange("p (tt u f) -> p tt u f", u=4, f=32)[:, :, u, :], kv)
        vv = vtm_d[:, u * 64:(u + 1) * 64].rearrange("(tt pp) g -> pp tt g", pp=128)
        p.s.dma('gpsimd', vtm[:, :].rearrange("p (tt u g) -> p tt u g", u=4, g=65)[:, :, u, 0:64], vv)
    p.s.dma('gpsimd', msk[:, :], mask_d[:, :])
    p.s.op('vector', lambda v: v.memset(vtm[:, :].rearrange("p (x g) -> p x g", g=65)[:, :, 64:65], 1.0))
    p.s.op('vector', lambda v: v.memset(Ssb[:, :], 0.0))
    p.s.op('vector', lambda v: v.memset(onesf[:, :], 1.0))

    nchunk = S // CHK
    for u in range(4):
        ub = u * 32
        for c in range(nchunk):
            c0 = c * CHK
            for kt in range(2):
                p.s.op('tensor', lambda t, ub=ub, c0=c0, kt=kt: t.matmul(
                    psT[:, kt * 256: kt * 256 + 256],
                    kf[ub:ub + 32, c0 + kt * 128: c0 + kt * 128 + 128],
                    qf[ub:ub + 32, c0: c0 + 256],
                    start=True, stop=True, tile_position=(ub, 0)))
            for kt in range(2):
                mv = msk[:, 128 - kt * 128: 128 - kt * 128 + 256]
                p.s.op('vector', lambda v, kt=kt, mv=mv: v.tensor_tensor(
                    pmask[:, kt * 256: kt * 256 + 256], psT[:, kt * 256: kt * 256 + 256], mv, OP.mult))
            for kt in range(2):
                tt128 = (c0 // 128) + kt
                p.s.op('tensor', lambda t, tt128=tt128, u=u, kt=kt: t.matmul(
                    pacc[0:65, :], vtm[:, (tt128 * 4 + u) * 65: (tt128 * 4 + u) * 65 + 65],
                    pmask[:, kt * 256: kt * 256 + 256],
                    start=(kt == 0), stop=False))
            p.s.op('tensor', lambda t, ub=ub, c0=c0: t.matmul(
                pacc[0:65, :], Ssb[ub:ub + 32, :], qf[ub:ub + 32, c0: c0 + 256],
                start=False, stop=True, tile_position=(ub, 0)))
            for kt in range(2):
                tt128 = (c0 // 128) + kt
                p.s.op('tensor', lambda t, tt128=tt128, u=u, c=c, kt=kt: t.matmul(
                    pst[:, :], ktm[:, (tt128 * 4 + u) * 32: (tt128 * 4 + u) * 32 + 32],
                    vtm[:, (tt128 * 4 + u) * 65: (tt128 * 4 + u) * 65 + 65],
                    start=(c == 0 and kt == 0), stop=False))
            p.s.op('vector', lambda v, ub=ub: v.tensor_copy(Ssb[ub:ub + 32, :], pst[:, :]))
            p.s.op('vector', lambda v: v.tensor_scalar_add(zrow[:, :], pacc[64:65, :], 1e-6))
            p.s.op('vector', lambda v: v.reciprocal(zrow[:, :], zrow[:, :]))
            p.s.op('tensor', lambda t: t.matmul(pzb[:, :], onesf[:, :], zrow[:, :], start=True, stop=True))
            p.s.op('scalar', lambda sc: sc.activation(out=numb[:, :], in_=pacc[0:64, :], func=AF.Copy))
            po = (u % 2) * 64
            pair = u // 2
            p.s.op('vector', lambda v, po=po, pair=pair, c0=c0: v.tensor_tensor(
                ofm[po:po + 64, pair * S + c0: pair * S + c0 + 256], numb[:, :], pzb[:, :], OP.mult))
    ov = o_d.rearrange("(pair po) t -> pair po t", po=128)
    for pair in range(2):
        p.s.dma('sync', ov[pair], ofm[:, pair * S: pair * S + S])


def emit_res_ln(p, *, u, ub, usq, onesc, onesr, srow, trow, pab, pstat, tt, T, write_ub=False, ucol=None):
    if ucol is None:
        ucol = lambda dc: dc * T + tt * 512
    """Given residual sum u (fp32, fm, all 8 chunks for token tile tt) already
    in place along with bf16 copy ub, compute LN in-place on u (and ub if write_ub)."""
    for dc in range(NCH):
        uo = dc * 512
        p.s.op('scalar', lambda sc, uo=uo: sc.activation(out=usq[:, :], in_=ub[:, uo:uo + 512], func=AF.Square))
        p.s.op('tensor', lambda t, dc=dc, uo=uo: t.matmul(
            pstat[0:1, 0:512], onesc[:, :], ub[:, uo:uo + 512], start=(dc == 0), stop=(dc == NCH - 1)))
        p.s.op('tensor', lambda t, dc=dc: t.matmul(
            pstat[0:1, 512:1024], onesc[:, :], usq[:, :], start=(dc == 0), stop=(dc == NCH - 1)))
    p.s.op('vector', lambda v: v.tensor_scalar_mul(srow[:, :], pstat[0:1, :], 1.0 / D))
    p.s.op('scalar', lambda sc: sc.activation(out=trow[0:1, :], in_=srow[0:1, 0:512], func=AF.Square))
    p.s.op('vector', lambda v: v.tensor_sub(srow[0:1, 512:1024], srow[0:1, 512:1024], trow[0:1, :]))
    p.s.op('vector', lambda v: v.tensor_scalar_add(srow[0:1, 512:1024], srow[0:1, 512:1024], 1e-5))
    p.s.op('scalar', lambda sc: sc.activation(out=srow[0:1, 512:1024], in_=srow[0:1, 512:1024], func=AF.Sqrt))
    p.s.op('vector', lambda v: v.reciprocal(srow[0:1, 512:1024], srow[0:1, 512:1024]))
    p.s.op('vector', lambda v: v.tensor_mul(trow[0:1, :], srow[0:1, 0:512], srow[0:1, 512:1024]))
    p.s.op('vector', lambda v: v.tensor_scalar_mul(trow[0:1, :], trow[0:1, :], -1.0))
    p.s.op('tensor', lambda t: t.matmul(pab[:, 0:512], onesr[:, :], srow[0:1, 512:1024], start=True, stop=True))
    p.s.op('tensor', lambda t: t.matmul(pab[:, 512:1024], onesr[:, :], trow[0:1, :], start=True, stop=True))
    for dc in range(NCH):
        co = ucol(dc)
        p.s.op('vector', lambda v, co=co: v.tensor_tensor(u[:, co:co + 512], u[:, co:co + 512], pab[:, 0:512], OP.mult))
        p.s.op('vector', lambda v, co=co: v.tensor_tensor(u[:, co:co + 512], u[:, co:co + 512], pab[:, 512:1024], OP.add))
        if write_ub:
            p.s.op('vector', lambda v, dc=dc, co=co: v.tensor_copy(ub[:, dc * 512:dc * 512 + 512], u[:, co:co + 512]))


def build_proj_res_ln(p):
    T = TPC
    afm = p.din("afm", [D, T], BF16)
    xfm = p.din("xfm", [D, T])
    wo = p.din("wo", [D, D])
    boc = p.din("boc", [128, NCH])
    y_d = p.dout("y", [D, T])

    ab = p.sb("ab", [128, NCH * T], BF16)
    xs = p.sb("xs", [128, NCH * T], F32)
    wos = p.sb("wos", [128, NCH * D], BF16)
    bo = p.sb("bo", [128, NCH], F32)
    u = p.sb("u", [128, NCH * T], F32)
    ub = p.sb("ub", [128, NCH * 512], BF16)
    usq = p.sb("usq", [128, 512], BF16)
    onesc = p.sb("onesc", [128, 1], BF16)
    onesr = p.sb("onesr", [1, 128], F32)
    srow = p.sb("srow", [1, 1024], F32)
    trow = p.sb("trow", [1, 512], F32)
    pp = p.ps("pp", [128, 512])
    pab = p.ps("pab", [128, 1024])
    pstat = p.ps("pstat", [1, 1024])

    load_fm(p, ab, afm, NCH, T)
    load_fm(p, xs, xfm, NCH, T, eng='sync')
    load_w_fm(p, wos, wo, D, D)
    p.s.dma('gpsimd', bo[:, :], boc[:, :])
    p.s.op('vector', lambda v: v.memset(onesc[:, :], 1.0))
    p.s.op('vector', lambda v: v.memset(onesr[:, :], 1.0))

    NT = T // 512
    for tt in range(NT):
        for dc in range(NCH):
            fm_proj(p, pp, wos, ab, dout=D, dc=dc,
                    xoff=lambda kc, tt=tt: kc * T + tt * 512, ncols=512, nk=NCH)
            co = dc * T + tt * 512
            p.s.op('vector', lambda v, dc=dc, co=co: v.scalar_tensor_tensor(
                u[:, co:co + 512], pp[:, :], bo[:, dc:dc + 1], xs[:, co:co + 512], OP.add, OP.add))
            p.s.op('vector', lambda v, dc=dc, co=co: v.tensor_copy(ub[:, dc * 512:dc * 512 + 512], u[:, co:co + 512]))
        emit_res_ln(p, u=u, ub=ub, usq=usq, onesc=onesc, onesr=onesr,
                    srow=srow, trow=trow, pab=pab, pstat=pstat, tt=tt, T=T)
    store_fm(p, y_d, u, NCH, T)


def build_l0_ffn(p):
    T = TPC
    T1 = T + 1
    xfm = p.din("xfm", [D, T1])
    wrT = p.din("wrT", [D, D]); wkT = p.din("wkT", [D, D]); wvT = p.din("wvT", [D, D])
    murc = p.din("murc", [128, NCH]); mukc = p.din("mukc", [128, NCH])
    y_d = p.dout("y", [D, T])

    xs = p.sb("xs", [128, NCH * T1], F32)
    wr = p.sb("wr", [128, NCH * D], BF16)
    wk = p.sb("wk", [128, NCH * D], BF16)
    wv = p.sb("wv", [128, NCH * D], BF16)
    mur = p.sb("mur", [128, NCH], F32); murM = p.sb("murM", [128, NCH], F32)
    muk = p.sb("muk", [128, NCH], F32); mukM = p.sb("mukM", [128, NCH], F32)
    xr = p.sb("xr", [128, NCH * 512], BF16)
    rr = p.sb("rr", [128, NCH * 512], BF16)
    ka = p.sb("ka", [128, NCH * 512], BF16)
    u = p.sb("u", [128, NCH * 512], F32)
    ub = p.sb("ub", [128, NCH * 512], BF16)
    usq = p.sb("usq", [128, 512], BF16)
    t1 = p.sb("t1", [128, 512], F32)
    onesc = p.sb("onesc", [128, 1], BF16)
    onesr = p.sb("onesr", [1, 128], F32)
    srow = p.sb("srow", [1, 1024], F32)
    trow = p.sb("trow", [1, 512], F32)
    pp = p.ps("pp", [128, 512])
    pab = p.ps("pab", [128, 1024])
    pstat = p.ps("pstat", [1, 1024])

    load_fm(p, xs, xfm, NCH, T1, eng='sync')
    load_w_fm(p, wr, wrT, D, D)
    load_w_fm(p, wk, wkT, D, D)
    load_w_fm(p, wv, wvT, D, D)
    p.s.dma('gpsimd', mur[:, :], murc[:, :]); p.s.dma('gpsimd', muk[:, :], mukc[:, :])
    p.s.op('vector', lambda v: v.tensor_scalar(murM[:, :], mur[:, :], -1.0, 1.0, OP.mult, OP.add))
    p.s.op('vector', lambda v: v.tensor_scalar(mukM[:, :], muk[:, :], -1.0, 1.0, OP.mult, OP.add))
    p.s.op('vector', lambda v: v.memset(onesc[:, :], 1.0))
    p.s.op('vector', lambda v: v.memset(onesr[:, :], 1.0))

    NT = T // 512
    for tt in range(NT):
        for dc in range(NCH):
            xo = dc * T1 + 1 + tt * 512
            po = dc * T1 + tt * 512
            p.s.op('vector', lambda v, dc=dc, xo=xo: v.tensor_scalar_mul(
                t1[:, :], xs[:, xo:xo + 512], mur[:, dc:dc + 1]))
            p.s.op('vector', lambda v, dc=dc, po=po: v.scalar_tensor_tensor(
                xr[:, dc * 512:dc * 512 + 512], xs[:, po:po + 512], murM[:, dc:dc + 1],
                t1[:, :], OP.mult, OP.add))
        for dc in range(NCH):
            fm_proj(p, pp, wr, xr, dout=D, dc=dc, xoff=lambda kc: kc * 512, ncols=512, nk=NCH)
            p.s.op('scalar', lambda sc, dc=dc: sc.activation(
                out=rr[:, dc * 512:dc * 512 + 512], in_=pp[:, :], func=AF.Sigmoid))
        for dc in range(NCH):
            xo = dc * T1 + 1 + tt * 512
            po = dc * T1 + tt * 512
            p.s.op('vector', lambda v, dc=dc, xo=xo: v.tensor_scalar_mul(
                t1[:, :], xs[:, xo:xo + 512], muk[:, dc:dc + 1]))
            p.s.op('vector', lambda v, dc=dc, po=po: v.scalar_tensor_tensor(
                xr[:, dc * 512:dc * 512 + 512], xs[:, po:po + 512], mukM[:, dc:dc + 1],
                t1[:, :], OP.mult, OP.add))
        for dc in range(NCH):
            fm_proj(p, pp, wk, xr, dout=D, dc=dc, xoff=lambda kc: kc * 512, ncols=512, nk=NCH)
            p.s.op('vector', lambda v: v.tensor_scalar(t1[:, :], pp[:, :], 0.0, None, OP.max))
            p.s.op('scalar', lambda sc, dc=dc: sc.activation(
                out=ka[:, dc * 512:dc * 512 + 512], in_=t1[:, :], func=AF.Square))
        for dc in range(NCH):
            fm_proj(p, pp, wv, ka, dout=D, dc=dc, xoff=lambda kc: kc * 512, ncols=512, nk=NCH)
            co = dc * 512
            xo = dc * T1 + 1 + tt * 512
            p.s.op('vector', lambda v, dc=dc: v.tensor_tensor(
                t1[:, :], rr[:, dc * 512:dc * 512 + 512], pp[:, :], OP.mult))
            p.s.op('vector', lambda v, co=co, xo=xo: v.tensor_add(
                u[:, co:co + 512], t1[:, :], xs[:, xo:xo + 512]))
            p.s.op('vector', lambda v, co=co: v.tensor_copy(
                ub[:, co:co + 512], u[:, co:co + 512]))
        emit_res_ln(p, u=u, ub=ub, usq=usq, onesc=onesc, onesr=onesr,
                    srow=srow, trow=trow, pab=pab, pstat=pstat, tt=tt, T=T,
                    ucol=lambda dc: dc * 512)
        p.s.dma('sync', y_d.rearrange("(c p) t -> p c t", p=128)[:, :, tt * 512:tt * 512 + 512],
                u[:, :].rearrange("p (c t) -> p c t", c=NCH))


def build_l1_proj(p):
    T = TPC
    xfm = p.din("xfm", [D, T])
    qkvT = p.din("qkvT", [D, 3 * D])
    cosT = p.din("cosT", [128, T], BF16)
    ssinT = p.din("ssinT", [128, T], BF16)
    qr_d = p.dout("qr_fm", [D, T], BF16)
    kr_d = p.dout("kr_fm", [D, T], BF16)
    v_d = p.dout("v_tm", [T, D], BF16)

    xb = p.sb("xb", [128, NCH * T], BF16)
    wq = p.sb("wq", [128, NCH * 2 * D], BF16)
    wv = p.sb("wv", [128, NCH * D], BF16)
    cosb = p.sb("cosb", [128, T], BF16)
    ssin = p.sb("ssin", [128, T], BF16)
    qraw = p.sb("qraw", [128, NCH * T], BF16)
    qsw = p.sb("qsw", [128, T], BF16)
    t1b = p.sb("t1b", [128, T], BF16)
    orope = p.sb("orope", [128, NCH * T], BF16)
    ot = p.sb("ot", [128, (T // 128) * D], BF16)
    pp = p.ps("pp", [128, 512])

    load_fm(p, xb, xfm, NCH, T)
    qk = qkvT.rearrange("(kc p) d -> p kc d", p=128)
    p.s.dma('gpsimd', wq[:, :].rearrange("p (kc d) -> p kc d", kc=NCH), qk[:, :, 0:2 * D])
    p.s.dma('gpsimd', wv[:, :].rearrange("p (kc d) -> p kc d", kc=NCH), qk[:, :, 2 * D:3 * D])
    p.s.dma('gpsimd', cosb[:, :], cosT[:, :])
    p.s.dma('gpsimd', ssin[:, :], ssinT[:, :])

    NT = T // 512
    for (woff, dram) in ((0, qr_d), (D, kr_d)):
        for dc in range(NCH):
            for tt in range(NT):
                for kc in range(NCH):
                    p.s.op('tensor', lambda t, kc=kc, dc=dc, tt=tt, woff=woff: t.matmul(
                        pp[:, :], wq[:, kc * 2 * D + woff + dc * 128: kc * 2 * D + woff + dc * 128 + 128],
                        xb[:, kc * T + tt * 512: kc * T + tt * 512 + 512],
                        start=(kc == 0), stop=(kc == NCH - 1)))
                p.s.op('scalar', lambda sc, dc=dc, tt=tt: sc.activation(
                    out=qraw[:, dc * T + tt * 512: dc * T + tt * 512 + 512], in_=pp[:, :], func=AF.Copy))
        for dc in range(NCH):
            qo = dc * T
            for g in range(4):
                src = (g ^ 1) * 32
                p.s.op('vector', lambda v, qo=qo, g=g, src=src: v.tensor_copy(
                    qsw[g * 32:(g + 1) * 32, :], qraw[src:src + 32, qo:qo + T]))
            p.s.op('vector', lambda v, qo=qo: v.tensor_tensor(
                t1b[:, :], qraw[:, qo:qo + T], cosb[:, :], OP.mult))
            p.s.op('vector', lambda v: v.tensor_tensor(qsw[:, :], qsw[:, :], ssin[:, :], OP.mult))
            p.s.op('vector', lambda v, qo=qo: v.tensor_add(orope[:, qo:qo + T], t1b[:, :], qsw[:, :]))
        store_fm(p, dram, orope, NCH, T)

    NT128 = T // 128
    for tt in range(NT128):
        for qc in range(2):
            for kc in range(NCH):
                p.s.op('tensor', lambda t, kc=kc, tt=tt, qc=qc: t.matmul(
                    pp[:, :], xb[:, kc * T + tt * 128: kc * T + tt * 128 + 128],
                    wv[:, kc * D + qc * 512: kc * D + qc * 512 + 512],
                    start=(kc == 0), stop=(kc == NCH - 1)))
            co = tt * D + qc * 512
            p.s.op('scalar', lambda sc, co=co: sc.activation(
                out=ot[:, co:co + 512], in_=pp[:, :], func=AF.Copy))
    store_tm(p, v_d, ot, NT128, D)


def build_l1_attn(p):
    S = N
    qf_d = p.din("qf", [256, S], BF16)
    kf_d = p.din("kf", [256, S], BF16)
    vtm_d = p.din("vtm", [S, 256], BF16)
    m896_d = p.din("m896", [128, 896], BF16)
    o_d = p.dout("o_fm", [256, S], BF16)

    qf = [p.sb("qf%d" % i, [128, S], BF16) for i in range(2)]
    kf = [p.sb("kf%d" % i, [128, S], BF16) for i in range(2)]
    vtm = p.sb("vtm", [128, (S // 128) * 65 * 4], BF16)
    m896 = p.sb("m896", [128, 896], BF16)
    psb = p.sb("psb", [128, 1024], BF16)
    numb = p.sb("numb", [64, 512], F32)
    zrow = p.sb("zrow", [1, 512], F32)
    onesf = p.sb("onesf", [1, 64], F32)
    ofm = [p.sb("ofm%d" % i, [128, S], BF16) for i in range(2)]
    psT = p.ps("psT", [128, 1024])
    pout = p.ps("pout", [128, 512])
    pzb = p.ps("pzb", [64, 512])

    qv = qf_d.rearrange("(pair po) t -> pair po t", po=128)
    kv = kf_d.rearrange("(pair po) t -> pair po t", po=128)
    for i in range(2):
        p.s.dma('gpsimd', qf[i][:, :], qv[i])
        p.s.dma('gpsimd', kf[i][:, :], kv[i])
    for u in range(4):
        vv = vtm_d[:, u * 64:(u + 1) * 64].rearrange("(tt pp) g -> pp tt g", pp=128)
        p.s.dma('gpsimd', vtm[:, :].rearrange("p (tt u g) -> p tt u g", u=4, g=65)[:, :, u, 0:64], vv)
    p.s.dma('gpsimd', m896[:, :], m896_d[:, :])
    p.s.op('vector', lambda v: v.memset(vtm[:, :].rearrange("p (x g) -> p x g", g=65)[:, :, 64:65], 1.0))
    p.s.op('vector', lambda v: v.memset(onesf[:, :], 1.0))

    NQT = S // 512
    for u in range(4):
        pair, po = u // 2, (u % 2) * 64
        qs = qf[pair]; ks = kf[pair]
        for qt in range(NQT):
            nkt = 4 * qt + 4
            first = True
            for g0 in range(0, nkt, 2):
                gn = min(2, nkt - g0)
                for gi in range(gn):
                    kt = g0 + gi
                    p.s.op('tensor', lambda t, po=po, kt=kt, qt=qt, gi=gi, qs=qs, ks=ks: t.matmul(
                        psT[:, gi * 512: gi * 512 + 512],
                        ks[po:po + 64, kt * 128: kt * 128 + 128],
                        qs[po:po + 64, qt * 512: qt * 512 + 512],
                        start=True, stop=True, tile_position=(po, 0)))
                p.s.op('scalar', lambda sc, gn=gn: sc.activation(
                    out=psb[:, 0:gn * 512], in_=psT[:, 0:gn * 512], func=AF.Exp, scale=0.125))
                for gi in range(gn):
                    kt = g0 + gi
                    off = qt * 512 - kt * 128
                    if off <= 0:
                        p.s.op('vector', lambda v, gi=gi, off=off: v.tensor_tensor(
                            psb[:, gi * 512: gi * 512 + 512], psb[:, gi * 512: gi * 512 + 512],
                            m896[:, off + 384: off + 384 + 512], OP.mult))
                for gi in range(gn):
                    kt = g0 + gi
                    p.s.op('tensor', lambda t, kt=kt, u=u, gi=gi, first=first and gi == 0: t.matmul(
                        pout[0:65, :], vtm[:, (kt * 4 + u) * 65: (kt * 4 + u) * 65 + 65],
                        psb[:, gi * 512: gi * 512 + 512],
                        start=first, stop=False))
                    first = False
            p.s.op('vector', lambda v: v.reciprocal(zrow[:, :], pout[64:65, :]))
            p.s.op('tensor', lambda t: t.matmul(pzb[:, :], onesf[:, :], zrow[:, :], start=True, stop=True))
            p.s.op('scalar', lambda sc: sc.activation(out=numb[:, :], in_=pout[0:64, :], func=AF.Copy))
            p.s.op('vector', lambda v, pair=pair, po=po, qt=qt: v.tensor_tensor(
                ofm[pair][po:po + 64, qt * 512: qt * 512 + 512], numb[:, :], pzb[:, :], OP.mult))
    ov = o_d.rearrange("(pair po) t -> pair po t", po=128)
    for pair in range(2):
        p.s.dma('sync', ov[pair], ofm[pair][:, :])


TH = 257
TKV = TPC + 256

def build_inner_attn(p):
    T = TPC
    TW = T + TH
    yfm = p.din("yfm", [D, TW])
    wqkv = p.din("wqkv", [D, 3 * D])
    wo = p.din("wo", [D, D])
    cq = p.din("cq", [128, T], BF16)
    sq = p.din("sq", [128, T], BF16)
    ck = p.din("ck", [128, TKV], BF16)
    sk = p.din("sk", [128, TKV], BF16)
    qsc = p.din("qsc", [128, 1]); qscw = p.din("qscw", [128, 1])
    ksc = p.din("ksc", [128, 1]); kscw = p.din("kscw", [128, 1])
    qsr = p.din("qsr", [1, 64]); ksr = p.din("ksr", [1, 64])
    blockpat = p.din("blockpat", [128, 2], BF16)
    blockpatT = p.din("blockpatT", [2, 128])
    mlow_d = p.din("mlow", [128, 128], F32)
    mupp_d = p.din("mupp", [128, 128], BF16)
    halom_d = p.din("halom", [128, 2])
    y_d = p.dout("y", [D, T])

    ysb = p.sb("ysb", [128, NCH * TW], BF16)
    ubig = p.sb("ubig", [128, NCH * 512], F32)
    wqb = p.sb("wqb", [128, NCH * D], BF16)
    cqs = p.sb("cqs", [128, T], BF16); sqs = p.sb("sqs", [128, T], BF16)
    cks = p.sb("cks", [128, TKV], BF16); sks = p.sb("sks", [128, TKV], BF16)
    qscs = p.sb("qscs", [128, 1], F32); qscws = p.sb("qscws", [128, 1], F32)
    kscs = p.sb("kscs", [128, 1], F32); kscws = p.sb("kscws", [128, 1], F32)
    qsrow = p.sb("qsrow", [1, 64], F32); ksrow = p.sb("ksrow", [1, 64], F32)
    bpat = p.sb("bpat", [128, 2], BF16)
    bpatT = p.sb("bpatT", [2, 128], F32)
    nrm2 = p.sb("nrm2", [2, 512], F32)
    mlow = p.sb("mlow", [128, 128], F32)
    mupp = p.sb("mupp", [128, 128], BF16)
    halom = p.sb("halom", [128, 2], F32)
    csb = p.sb("csb", [128, 1], F32)
    crow = p.sb("crow", [1, 1], F32)
    xbs = p.sb("xbs", [128, 512], BF16)
    qraw = p.sb("qraw", [128, NCH * T], BF16)
    kraw = p.sb("kraw", [128, NCH * TKV], BF16)
    qsw = p.sb("qsw", [128, TKV], BF16)
    vtm = p.sb("vtm", [128, (TKV // 128) * 16 * 65], BF16)
    psb = p.sb("psb", [128, 384], BF16)
    numb = p.sb("numb", [64, 128], F32)
    zrow = p.sb("zrow", [1, 512], F32)
    onesf = p.sb("onesf", [1, 128], F32)
    pp = p.ps("pp", [128, 512])
    pnrm = p.ps("pnrm", [2, 512])
    pbc = p.ps("pbc", [128, 512])
    psT = p.ps("psT", [128, 384])
    pacc = p.ps("pacc", [128, 128])
    pzb = p.ps("pzb", [64, 128])

    load_fm(p, ysb, yfm, NCH, TW)
    for (d_, s_) in ((cq, cqs), (sq, sqs), (ck, cks), (sk, sks), (qsc, qscs), (qscw, qscws),
                     (ksc, kscs), (kscw, kscws), (qsr, qsrow), (ksr, ksrow),
                     (blockpat, bpat), (blockpatT, bpatT), (mlow_d, mlow), (mupp_d, mupp),
                     (halom_d, halom)):
        p.s.dma('gpsimd', s_[:, :], d_[:, :])
    p.s.op('vector', lambda v: v.memset(onesf[:, :], 1.0))
    p.s.op('vector', lambda v: v.memset(vtm[:, :].rearrange("p (x g) -> p x g", g=65)[:, :, 64:65], 1.0))

    p.s.op('vector', lambda v: v.reduce_max(crow[:, :], qsrow[:, :], axis=mybir.AxisListType.X))
    p.s.op('vector', lambda v: v.reduce_max(zrow[0:1, 0:1], ksrow[:, :], axis=mybir.AxisListType.X))
    p.s.op('vector', lambda v: v.tensor_mul(crow[:, :], crow[:, :], zrow[0:1, 0:1]))
    p.s.op('vector', lambda v: v.tensor_scalar_mul(crow[:, :], crow[:, :], -8.0))
    p.s.op('tensor', lambda t: t.matmul(pbc[:, 0:1], onesf[:, :], crow[:, :], start=True, stop=True))
    p.s.op('vector', lambda v: v.tensor_copy(csb[:, :], pbc[:, 0:1]))

    def xoff(dc, st):
        return dc * TW + st + (TH if dc < 4 else TH - 1)

    wv_ = wqkv.rearrange("(kc p) d -> p kc d", p=128)
    def load_wblock(woff):
        p.s.dma('gpsimd', wqb[:, :].rearrange("p (kc d) -> p kc d", kc=NCH),
                wv_[:, :, woff:woff + D])

    for (woff, dst, ncolw, base) in ((0, qraw, T, 0), (D, kraw, TKV, -256)):
        load_wblock(woff)
        ntile = (ncolw + 511) // 512
        for ti in range(ntile):
            st = ti * 512
            w = min(512, ncolw - st)
            for dc in range(NCH):
                for kc in range(NCH):
                    p.s.op('tensor', lambda t, kc=kc, dc=dc, w=w, st=st, base=base: t.matmul(
                        pp[:, 0:w], wqb[:, kc * D + dc * 128: kc * D + dc * 128 + 128],
                        ysb[:, xoff(kc, st + base): xoff(kc, st + base) + w],
                        start=(kc == 0), stop=(kc == NCH - 1)))
                p.s.op('scalar', lambda sc, dc=dc, st=st, w=w, dst=dst, ncolw=ncolw: sc.activation(
                    out=dst[:, dc * ncolw + st: dc * ncolw + st + w], in_=pp[:, 0:w], func=AF.Copy))

    load_wblock(2 * D)
    for jt in range(TKV // 128):
        st = jt * 128 - 256
        for qc in range(2):
            for kc in range(NCH):
                p.s.op('tensor', lambda t, kc=kc, qc=qc, st=st: t.matmul(
                    pp[:, :], ysb[:, xoff(kc, st): xoff(kc, st) + 128],
                    wqb[:, kc * D + qc * 512: kc * D + qc * 512 + 512],
                    start=(kc == 0), stop=(kc == NCH - 1)))
            for hh in range(8):
                h = qc * 8 + hh
                p.s.op('scalar', lambda sc, jt=jt, h=h, hh=hh: sc.activation(
                    out=vtm[:, (jt * 16 + h) * 65: (jt * 16 + h) * 65 + 64],
                    in_=pp[:, hh * 64: hh * 64 + 64], func=AF.Copy))

    for (dst, ncolw, cosb, sinb, scl, sclw) in ((qraw, T, cqs, sqs, qscs, qscws),
                                                (kraw, TKV, cks, sks, kscs, kscws)):
        ntile = (ncolw + 511) // 512
        for dc in range(NCH):
            qo = dc * ncolw
            for ti in range(ntile):
                st = ti * 512
                w = min(512, ncolw - st)
                p.s.op('scalar', lambda sc, qo=qo, st=st, w=w, dst=dst: sc.activation(
                    out=xbs[:, 0:w], in_=dst[:, qo + st: qo + st + w], func=AF.Square))
                p.s.op('tensor', lambda t, w=w: t.matmul(
                    pnrm[:, 0:w], bpat[:, :], xbs[:, 0:w], start=True, stop=True))
                p.s.op('scalar', lambda sc, w=w: sc.activation(out=nrm2[:, 0:w], in_=pnrm[:, 0:w], func=AF.Sqrt))
                p.s.op('vector', lambda v, w=w: v.tensor_scalar_max(nrm2[:, 0:w], nrm2[:, 0:w], 1e-12))
                p.s.op('vector', lambda v, w=w: v.reciprocal(nrm2[:, 0:w], nrm2[:, 0:w]))
                p.s.op('tensor', lambda t, w=w: t.matmul(
                    pbc[:, 0:w], bpatT[:, :], nrm2[:, 0:w], start=True, stop=True))
                p.s.op('vector', lambda v, qo=qo, st=st, w=w, dst=dst: v.tensor_tensor(
                    dst[:, qo + st: qo + st + w], dst[:, qo + st: qo + st + w], pbc[:, 0:w], OP.mult))
            for g in range(4):
                src = (g ^ 1) * 32
                p.s.op('vector', lambda v, qo=qo, g=g, src=src, dst=dst, ncolw=ncolw: v.tensor_copy(
                    qsw[g * 32:(g + 1) * 32, 0:ncolw], dst[src:src + 32, qo: qo + ncolw]))
            p.s.op('vector', lambda v, qo=qo, dst=dst, ncolw=ncolw, scl=scl: v.tensor_scalar_mul(
                dst[:, qo: qo + ncolw], dst[:, qo: qo + ncolw], scl[:, :]))
            p.s.op('vector', lambda v, ncolw=ncolw, sclw=sclw: v.tensor_scalar_mul(
                qsw[:, 0:ncolw], qsw[:, 0:ncolw], sclw[:, :]))
            p.s.op('vector', lambda v, qo=qo, dst=dst, ncolw=ncolw, cosb=cosb: v.tensor_tensor(
                dst[:, qo: qo + ncolw], dst[:, qo: qo + ncolw], cosb[:, 0:ncolw], OP.mult))
            p.s.op('vector', lambda v, ncolw=ncolw, sinb=sinb: v.tensor_tensor(
                qsw[:, 0:ncolw], qsw[:, 0:ncolw], sinb[:, 0:ncolw], OP.mult))
            p.s.op('vector', lambda v, qo=qo, dst=dst, ncolw=ncolw: v.tensor_add(
                dst[:, qo: qo + ncolw], dst[:, qo: qo + ncolw], qsw[:, 0:ncolw]))

    aout = qraw
    for h in range(16):
        dc = h // 2
        hb = (h % 2) * 64
        for qt in range(T // 128):
            for d_i, kvt in enumerate((qt, qt + 1, qt + 2)):
                p.s.op('tensor', lambda t, dc=dc, hb=hb, kvt=kvt, qt=qt, d_i=d_i: t.matmul(
                    psT[:, d_i * 128: d_i * 128 + 128],
                    kraw[hb:hb + 64, dc * TKV + kvt * 128: dc * TKV + kvt * 128 + 128],
                    qraw[hb:hb + 64, dc * T + qt * 128: dc * T + qt * 128 + 128],
                    start=True, stop=True, tile_position=(hb, 0)))
            p.s.op('vector', lambda v: v.tensor_tensor(psT[:, 256:384], psT[:, 256:384], mlow[:, :], OP.add))
            p.s.op('scalar', lambda sc: sc.activation(out=psb[:, :], in_=psT[:, :], func=AF.Exp,
                                                      scale=8.0, bias=csb[:, :]))
            p.s.op('vector', lambda v: v.tensor_tensor(psb[:, 0:128], psb[:, 0:128], mupp[:, :], OP.mult))
            if qt == 0:
                p.s.op('vector', lambda v: v.tensor_scalar_mul(psb[:, 0:128], psb[:, 0:128], halom[:, 0:1]))
                p.s.op('vector', lambda v: v.tensor_scalar_mul(psb[:, 128:256], psb[:, 128:256], halom[:, 1:2]))
            elif qt == 1:
                p.s.op('vector', lambda v: v.tensor_scalar_mul(psb[:, 0:128], psb[:, 0:128], halom[:, 1:2]))
            for d_i, kvt in enumerate((qt, qt + 1, qt + 2)):
                p.s.op('tensor', lambda t, kvt=kvt, h=h, d_i=d_i: t.matmul(
                    pacc[0:65, :], vtm[:, (kvt * 16 + h) * 65: (kvt * 16 + h) * 65 + 65],
                    psb[:, d_i * 128: d_i * 128 + 128],
                    start=(d_i == 0), stop=(d_i == 2)))
            p.s.op('vector', lambda v: v.reciprocal(zrow[0:1, 0:128], pacc[64:65, :]))
            p.s.op('tensor', lambda t: t.matmul(pzb[:, :], onesf[:, 0:64], zrow[0:1, 0:128], start=True, stop=True))
            p.s.op('scalar', lambda sc: sc.activation(out=numb[:, 0:128], in_=pacc[0:64, :], func=AF.Copy))
            p.s.op('vector', lambda v, dc=dc, hb=hb, qt=qt: v.tensor_tensor(
                aout[hb:hb + 64, dc * T + qt * 128: dc * T + qt * 128 + 128], numb[:, 0:128], pzb[:, :], OP.mult))

    load_w_fm(p, wqb, wo, D, D)
    yin = yfm.rearrange("(c p) t -> p c t", p=128)
    yv = y_d.rearrange("(c p) t -> p c t", p=128)
    NT = T // 512
    for tt in range(NT):
        p.s.dma('sync', ubig[:, :].rearrange("p (c t) -> p c t", c=NCH),
                yin[:, :, TH + tt * 512: TH + tt * 512 + 512])
        for dc in range(NCH):
            fm_proj(p, pp, wqb, aout, dout=D, dc=dc,
                    xoff=lambda kc, tt=tt: kc * T + tt * 512, ncols=512, nk=NCH)
            p.s.op('vector', lambda v, dc=dc: v.tensor_add(
                ubig[:, dc * 512: dc * 512 + 512], pp[:, :], ubig[:, dc * 512: dc * 512 + 512]))
        p.s.dma('sync', yv[:, :, tt * 512: tt * 512 + 512],
                ubig[:, :].rearrange("p (c t) -> p c t", c=NCH))


def build_inner_ffn_plain(p):
    build_inner_ffn(p, final=False)

def build_inner_ffn_final(p):
    build_inner_ffn(p, final=True)

def build_inner_ffn(p, final=False):
    T = TPC
    TW = T + 1
    DI = 2730
    yfm = p.din("yfm", [D, TW])
    w1 = p.din("w1", [D, 2 * DI])
    w2 = p.din("w2", [DI, D])
    b1ac = p.din("b1ac", [128, 11])
    b1gc = p.din("b1gc", [128, 11])
    b1ac2 = p.din("b1ac2", [128, 11])
    b1gc2 = p.din("b1gc2", [128, 11])
    b2c = p.din("b2c", [128, NCH])
    gc = p.din("gc", [128, NCH]); bc = p.din("bc", [128, NCH])
    if final:
        x3fm = p.din("x3fm", [D, T])
    y_d = p.dout("y", [D, T])

    ys = p.sb("ys", [128, NCH * TW], F32)
    xn = p.sb("xn", [128, NCH * T], BF16)
    hsb = p.sb("hsb", [128, 11 * T], BF16)
    w2s = p.sb("w2s", [128, 11 * D], BF16)
    w1a = p.sb("w1a", [128, NCH * 128], BF16)
    w1g = p.sb("w1g", [128, NCH * 128], BF16)
    b1a = [p.sb("b1a0", [128, 11], F32), p.sb("b1a1", [128, 11], F32)]
    b1g = [p.sb("b1g0", [128, 11], F32), p.sb("b1g1", [128, 11], F32)]
    b2s = p.sb("b2s", [128, NCH], F32)
    gs = p.sb("gs", [128, NCH], F32); bs = p.sb("bs", [128, NCH], F32)
    gbf = p.sb("gbf", [128, 512], BF16)
    ub = p.sb("ub", [128, NCH * 512], BF16)
    usq = p.sb("usq", [128, 512], BF16)
    t1 = p.sb("t1", [128, 512], F32)
    onesc = p.sb("onesc", [128, 1], BF16)
    onesr = p.sb("onesr", [1, 128], F32)
    srow = p.sb("srow", [1, 1024], F32)
    trow = p.sb("trow", [1, 512], F32)
    if final:
        ufin = p.sb("ufin", [128, NCH * 512], F32)
    pp = p.ps("pp", [128, 512])
    pg = p.ps("pg", [128, 512])
    pout = p.ps("pout", [128, 1024])
    pab = p.ps("pab", [128, 1024])
    pstat = p.ps("pstat", [1, 1024])

    load_fm(p, ys, yfm, NCH, TW, eng='sync')
    p.s.dma('gpsimd', b1a[0][:, :], b1ac[:, :]); p.s.dma('gpsimd', b1g[0][:, :], b1gc[:, :])
    p.s.dma('gpsimd', b1a[1][:, :], b1ac2[:, :]); p.s.dma('gpsimd', b1g[1][:, :], b1gc2[:, :])
    p.s.dma('gpsimd', b2s[:, :], b2c[:, :])
    p.s.dma('gpsimd', gs[:, :], gc[:, :]); p.s.dma('gpsimd', bs[:, :], bc[:, :])
    p.s.op('vector', lambda v: v.memset(onesc[:, :], 1.0))
    p.s.op('vector', lambda v: v.memset(onesr[:, :], 1.0))

    def xoff(dc, st):
        return dc * TW + st + (1 if dc < 4 else 0)

    NT = T // 512
    for tt in range(NT):
        for dc in range(NCH):
            p.s.op('vector', lambda v, dc=dc, tt=tt: v.tensor_copy(
                ub[:, dc * 512: dc * 512 + 512], ys[:, xoff(dc, tt * 512): xoff(dc, tt * 512) + 512]))
        for dc in range(NCH):
            uo = dc * 512
            p.s.op('scalar', lambda sc, uo=uo: sc.activation(out=usq[:, :], in_=ub[:, uo:uo + 512], func=AF.Square))
            p.s.op('tensor', lambda t, dc=dc, uo=uo: t.matmul(
                pstat[0:1, 0:512], onesc[:, :], ub[:, uo:uo + 512], start=(dc == 0), stop=(dc == NCH - 1)))
            p.s.op('tensor', lambda t, dc=dc: t.matmul(
                pstat[0:1, 512:1024], onesc[:, :], usq[:, :], start=(dc == 0), stop=(dc == NCH - 1)))
        p.s.op('vector', lambda v: v.tensor_scalar_mul(srow[:, :], pstat[0:1, :], 1.0 / D))
        p.s.op('scalar', lambda sc: sc.activation(out=trow[0:1, :], in_=srow[0:1, 0:512], func=AF.Square))
        p.s.op('vector', lambda v: v.tensor_sub(srow[0:1, 512:1024], srow[0:1, 512:1024], trow[0:1, :]))
        p.s.op('vector', lambda v: v.tensor_scalar_add(srow[0:1, 512:1024], srow[0:1, 512:1024], 1e-5))
        p.s.op('scalar', lambda sc: sc.activation(out=srow[0:1, 512:1024], in_=srow[0:1, 512:1024], func=AF.Sqrt))
        p.s.op('vector', lambda v: v.reciprocal(srow[0:1, 512:1024], srow[0:1, 512:1024]))
        p.s.op('vector', lambda v: v.tensor_mul(trow[0:1, :], srow[0:1, 0:512], srow[0:1, 512:1024]))
        p.s.op('vector', lambda v: v.tensor_scalar_mul(trow[0:1, :], trow[0:1, :], -1.0))
        p.s.op('tensor', lambda t: t.matmul(pab[:, 0:512], onesr[:, :], srow[0:1, 512:1024], start=True, stop=True))
        p.s.op('tensor', lambda t: t.matmul(pab[:, 512:1024], onesr[:, :], trow[0:1, :], start=True, stop=True))
        for dc in range(NCH):
            xo = xoff(dc, tt * 512)
            co = dc * T + tt * 512
            p.s.op('vector', lambda v, xo=xo: v.tensor_tensor(t1[:, :], ys[:, xo:xo + 512], pab[:, 0:512], OP.mult))
            p.s.op('vector', lambda v: v.tensor_tensor(t1[:, :], t1[:, :], pab[:, 512:1024], OP.add))
            p.s.op('vector', lambda v, dc=dc, co=co: v.tensor_scalar(
                xn[:, co:co + 512], t1[:, :], gs[:, dc:dc + 1], bs[:, dc:dc + 1], OP.mult, OP.add))

    jblocks_half = [(i * 128, 128) for i in range(10)] + [(1280, 85)]
    wview = w1.rearrange("(kc p) d -> p kc d", p=128)
    for half in range(2):
        j0h = half * 1365
        for jbi, (jo, jw) in enumerate(jblocks_half):
            jg = j0h + jo
            p.s.dma('gpsimd', w1a[:, 0:NCH * jw].rearrange("p (kc d) -> p kc d", kc=NCH),
                    wview[:, :, jg:jg + jw])
            p.s.dma('gpsimd', w1g[:, 0:NCH * jw].rearrange("p (kc d) -> p kc d", kc=NCH),
                    wview[:, :, DI + jg:DI + jg + jw])
            for tt in range(NT):
                for (wsb, psum) in ((w1a, pp), (w1g, pg)):
                    for kc in range(NCH):
                        p.s.op('tensor', lambda t, kc=kc, tt=tt, jw=jw, wsb=wsb, psum=psum: t.matmul(
                            psum[0:jw, 0:512], wsb[:, kc * jw: kc * jw + jw],
                            xn[:, kc * T + tt * 512: kc * T + tt * 512 + 512],
                            start=(kc == 0), stop=(kc == NCH - 1)))
                p.s.op('scalar', lambda sc, jw=jw, jbi=jbi, half=half: sc.activation(
                    out=gbf[0:jw, :], in_=pg[0:jw, :], func=AF.Gelu, bias=b1g[half][0:jw, jbi:jbi + 1]))
                p.s.op('vector', lambda v, jo=jo, tt=tt, jw=jw, jbi=jbi, half=half: v.scalar_tensor_tensor(
                    hsb[0:jw, (jo // 128) * T + tt * 512: (jo // 128) * T + tt * 512 + 512],
                    pp[0:jw, 0:512], b1a[half][0:jw, jbi:jbi + 1], gbf[0:jw, :], OP.add, OP.mult))
        p.s.dma('gpsimd', w2s[:, 0:10 * D].rearrange("p (jb d) -> p jb d", jb=10),
                w2[j0h:j0h + 1280, :].rearrange("(jb p) d -> p jb d", p=128))
        p.s.dma('gpsimd', w2s[0:85, 10 * D:11 * D], w2[j0h + 1280:j0h + 1365, :])
        for t128 in range(T // 128):
            for dc in range(NCH):
                for jbi, (jo, jw) in enumerate(jblocks_half):
                    p.s.op('tensor', lambda t, jbi=jbi, jo=jo, jw=jw, dc=dc, t128=t128: t.matmul(
                        pout[:, dc * 128: dc * 128 + 128],
                        w2s[0:jw, jbi * D + dc * 128: jbi * D + dc * 128 + 128],
                        hsb[0:jw, (jo // 128) * T + t128 * 128: (jo // 128) * T + t128 * 128 + 128],
                        start=(jbi == 0), stop=(jbi == 10)))
            for dc in range(NCH):
                yo = dc * TW + 1 + t128 * 128
                if half == 0:
                    p.s.op('vector', lambda v, dc=dc, yo=yo: v.tensor_add(
                        ys[:, yo:yo + 128], pout[:, dc * 128: dc * 128 + 128], ys[:, yo:yo + 128]))
                else:
                    p.s.op('vector', lambda v, dc=dc, yo=yo: v.scalar_tensor_tensor(
                        ys[:, yo:yo + 128], pout[:, dc * 128: dc * 128 + 128], b2s[:, dc:dc + 1],
                        ys[:, yo:yo + 128], OP.add, OP.add))

    if not final:
        yv = y_d.rearrange("(c p) t -> p c t", p=128)
        p.s.dma('sync', yv, ys[:, :].rearrange("p (c t) -> p c t", c=NCH)[:, :, 1:1 + T])
    else:
        x3v = x3fm.rearrange("(c p) t -> p c t", p=128)
        yv = y_d.rearrange("(c p) t -> p c t", p=128)
        for tt in range(NT):
            p.s.dma('sync', ufin[:, :].rearrange("p (c t) -> p c t", c=NCH),
                    x3v[:, :, tt * 512:tt * 512 + 512])
            for dc in range(NCH):
                uo = dc * 512
                yo = dc * TW + 1 + tt * 512
                p.s.op('vector', lambda v, uo=uo, yo=yo: v.tensor_add(
                    ufin[:, uo:uo + 512], ufin[:, uo:uo + 512], ys[:, yo:yo + 512]))
                p.s.op('vector', lambda v, uo=uo: v.tensor_copy(ub[:, uo:uo + 512], ufin[:, uo:uo + 512]))
            emit_res_ln(p, u=ufin, ub=ub, usq=usq, onesc=onesc, onesr=onesr,
                        srow=srow, trow=trow, pab=pab, pstat=pstat, tt=tt, T=T,
                        ucol=lambda dc: dc * 512)
            p.s.dma('sync', yv[:, :, tt * 512:tt * 512 + 512],
                    ufin[:, :].rearrange("p (c t) -> p c t", c=NCH))



def _fm(x):
    return np.ascontiguousarray(np.asarray(x).T)


_IVF = 1.0 / 10000.0 ** (np.arange(0, 64, 2, dtype=np.float64) / 64)
_BS = ((np.arange(0, 64, 2, dtype=np.float64) + 0.4 * 64) / (1.4 * 64))
_SG = np.where((np.arange(128) % 64) < 32, -1.0, 1.0)[:, None]


def _module_rope(tglob):
    ang = tglob[:, None].astype(np.float64) * _IVF[None, :]
    cosT = np.tile(np.cos(ang).T, (4, 1))
    ssinT = np.tile(np.sin(ang).T, (4, 1)) * _SG
    return cosT.astype(BF), ssinT.astype(BF)


def _inner_rope(tglob):
    ang = tglob[:, None].astype(np.float64) * _IVF[None, :]
    cos, sin = np.cos(ang), np.sin(ang)
    power = (tglob.astype(np.float64) - N // 2) / 128.0
    sc = _BS[None, :] ** power[:, None]
    cq = np.tile((cos * sc).T, (4, 1)).astype(BF)
    sq = (np.tile((sin * sc).T, (4, 1)) * _SG).astype(BF)
    ci = np.tile((cos / sc).T, (4, 1)).astype(BF)
    si = (np.tile((sin / sc).T, (4, 1)) * _SG).astype(BF)
    return cq, sq, ci, si


def _colpack(v):
    return np.ascontiguousarray(np.asarray(v).reshape(NCH, 128).T)


def _gather_fm(res, key, c, rows=1024):
    b = c // 4
    t0 = (c % 4) * TPC
    blocks = []
    nh = rows // 64
    for h in range(nh):
        uid = b * 16 + h
        core_b, uu = uid // 4, uid % 4
        blocks.append(res[core_b][key][uu * 64:(uu + 1) * 64, t0:t0 + TPC])
    return np.concatenate(blocks, axis=0)


_CAP = None
_RAW = False
FUSED = True

def kernel(**I):
    if FUSED:
        return kernel_fused(I)
    x = np.asarray(I["L0_attn_x"])
    xt = x.reshape(NC, TPC, D)

    pA = cached_prog("l0proj", build_l0_proj)
    imA = []
    for c in range(NC):
        imA.append({
            "xfm": _fm(xt[c]),
            "wq": I["L0_attn_wq"], "wk": I["L0_attn_wk"], "wv": I["L0_attn_wv"],
            "bqc": _colpack(np.tile(I["L0_attn_bq"], 1)) if False else np.ascontiguousarray(I["L0_attn_bq"].reshape(4, 128).T),
            "bkc": np.ascontiguousarray(I["L0_attn_bk"].reshape(4, 128).T),
            "bkr": I["L0_attn_bk"].reshape(1, 512),
            "bvr": I["L0_attn_bv"].reshape(1, D),
        })
    rA = run(pA, imA)

    pB = cached_prog("l0scan", build_l0_scan)
    k_ = np.arange(128)[:, None]
    c_ = np.arange(384)[None, :]
    m384 = (k_ <= c_ - 128).astype(BF)
    imB = []
    for c in range(NC):
        qs_, ks_, kts, vts = [], [], [], []
        for uu in range(4):
            uid = c * 4 + uu
            b, h = uid // 16, uid % 16
            qs_.append(np.concatenate([rA[4 * b + j]["q_fm"][h * 32:(h + 1) * 32, :] for j in range(4)], axis=1))
            ks_.append(np.concatenate([rA[4 * b + j]["k_fm"][h * 32:(h + 1) * 32, :] for j in range(4)], axis=1))
            kts.append(np.concatenate([rA[4 * b + j]["k_tm"][:, h * 32:(h + 1) * 32] for j in range(4)], axis=0))
            vts.append(np.concatenate([rA[4 * b + j]["v_tm"][:, h * 64:(h + 1) * 64] for j in range(4)], axis=0))
        imB.append({"qf": np.concatenate(qs_, axis=0), "kf": np.concatenate(ks_, axis=0),
                    "ktm": np.concatenate(kts, axis=1), "vtm": np.concatenate(vts, axis=1),
                    "mask384": m384})
    rB = run(pB, imB)

    pC = cached_prog("projresln", build_proj_res_ln)
    imC = []
    for c in range(NC):
        imC.append({"afm": _gather_fm(rB, "o_fm", c), "xfm": _fm(xt[c]),
                    "wo": I["L0_attn_wo"], "boc": _colpack(I["L0_attn_bo"])})
    rC = run(pC, imC)
    x1 = [np.asarray(rC[c]["y"]) for c in range(NC)]
    if _CAP is not None: _CAP["x1"] = x1

    pD = cached_prog("l0ffn", build_l0_ffn)
    imD = []
    for c in range(NC):
        halo = np.zeros((D, 1), np.float32) if c % 4 == 0 else x1[c - 1][:, -1:]
        imD.append({"xfm": np.ascontiguousarray(np.concatenate([halo, x1[c]], axis=1)),
                    "wrT": _fm(I["L0_ffn_W_r"]), "wkT": _fm(I["L0_ffn_W_k"]),
                    "wvT": _fm(I["L0_ffn_W_v"]),
                    "murc": _colpack(I["L0_ffn_mu_r"]), "mukc": _colpack(I["L0_ffn_mu_k"])})
    rD = run(pD, imD)
    x2 = [np.asarray(rD[c]["y"]) for c in range(NC)]
    if _CAP is not None: _CAP["x2"] = x2

    pE = cached_prog("l1proj", build_l1_proj)
    imE = []
    for c in range(NC):
        t0c = (c % 4) * TPC
        cosT, ssinT = _module_rope(np.arange(t0c, t0c + TPC))
        imE.append({"xfm": x2[c], "qkvT": _fm(I["L1_attn_qkv"]),
                    "cosT": cosT, "ssinT": ssinT})
    rE = run(pE, imE)

    pF = cached_prog("l1attn", build_l1_attn)
    m896 = (np.arange(128)[:, None] <= np.arange(896)[None, :] - 384).astype(BF)
    imF = []
    for c in range(NC):
        qs_, ks_, vts = [], [], []
        for uu in range(4):
            uid = c * 4 + uu
            b, h = uid // 16, uid % 16
            qs_.append(np.concatenate([rE[4 * b + j]["qr_fm"][h * 64:(h + 1) * 64, :] for j in range(4)], axis=1))
            ks_.append(np.concatenate([rE[4 * b + j]["kr_fm"][h * 64:(h + 1) * 64, :] for j in range(4)], axis=1))
            vts.append(np.concatenate([rE[4 * b + j]["v_tm"][:, h * 64:(h + 1) * 64] for j in range(4)], axis=0))
        imF.append({"qf": np.concatenate(qs_, axis=0), "kf": np.concatenate(ks_, axis=0),
                    "vtm": np.concatenate(vts, axis=1), "m896": m896})
    rF = run(pF, imF)

    imG = []
    for c in range(NC):
        imG.append({"afm": _gather_fm(rF, "o_fm", c), "xfm": x2[c],
                    "wo": _fm(I["L1_attn_wo"]), "boc": np.zeros((128, NCH), np.float32)})
    rG = run(pC, imG)
    x3 = [np.asarray(rG[c]["y"]) for c in range(NC)]
    if _CAP is not None: _CAP["x3"] = x3

    pH = cached_prog("innerattn", build_inner_attn)
    pI = cached_prog("innerffn", build_inner_ffn_plain)
    pJ = cached_prog("innerffn_final", build_inner_ffn_final)

    kq = np.arange(128)[:, None]
    qq = np.arange(128)[None, :]
    mlow = np.where(kq <= qq, 0.0, -1e30).astype(np.float32)
    mupp = (kq >= qq).astype(BF)
    bpat = np.zeros((128, 2), np.float32)
    bpat[0:64, 0] = 1.0
    bpat[64:128, 1] = 1.0
    bpatTf = np.ascontiguousarray(bpat.T.astype(np.float32))
    bpat = bpat.astype(BF)

    y = x3
    for li in range(6):
        qs_i = np.asarray(I["L1_ffn_q_scale"][li])
        ks_i = np.asarray(I["L1_ffn_k_scale"][li])
        imH = []
        for c in range(NC):
            t0c = (c % 4) * TPC
            cq, sq, _, _ = _inner_rope(np.arange(t0c, t0c + TPC))
            _, _, ck, sk = _inner_rope(np.arange(t0c - 256, t0c + TPC))
            halo = np.zeros((D, TH), np.float32) if c % 4 == 0 else y[c - 1][:, -TH:]
            halom = np.zeros((128, 2), np.float32) if c % 4 == 0 else np.ones((128, 2), np.float32)
            imH.append({
                "yfm": np.ascontiguousarray(np.concatenate([halo, y[c]], axis=1)),
                "wqkv": I["L1_ffn_Wqkv"][li], "wo": I["L1_ffn_Wo"][li],
                "cq": cq, "sq": sq, "ck": ck, "sk": sk,
                "qsc": np.tile(qs_i, 2).reshape(128, 1),
                "qscw": np.tile(np.concatenate([qs_i[32:], qs_i[:32]]), 2).reshape(128, 1),
                "ksc": np.tile(ks_i, 2).reshape(128, 1),
                "kscw": np.tile(np.concatenate([ks_i[32:], ks_i[:32]]), 2).reshape(128, 1),
                "qsr": qs_i.reshape(1, 64), "ksr": ks_i.reshape(1, 64),
                "blockpat": bpat, "blockpatT": bpatTf,
                "mlow": mlow, "mupp": mupp, "halom": halom,
            })
        rH = run(pH, imH)
        y = [np.asarray(rH[c]["y"]) for c in range(NC)]
        if _CAP is not None: _CAP["ya%d" % li] = y

        b1_i = np.asarray(I["L1_ffn_b1"][li])
        DI = 2730
        b1cols = []
        for part0 in (0, DI):
            for half in range(2):
                m = np.zeros((128, 11), np.float32)
                for jbi in range(11):
                    jo = jbi * 128
                    jw = min(128, 1365 - jo)
                    m[0:jw, jbi] = b1_i[part0 + half * 1365 + jo: part0 + half * 1365 + jo + jw]
                b1cols.append(m)
        final = (li == 5)
        imI = []
        for c in range(NC):
            halo = np.zeros((D, 1), np.float32) if c % 4 == 0 else y[c - 1][:, -1:]
            d = {"yfm": np.ascontiguousarray(np.concatenate([halo, y[c]], axis=1)),
                 "w1": I["L1_ffn_W1"][li], "w2": I["L1_ffn_W2"][li],
                 "b1ac": b1cols[0], "b1ac2": b1cols[1],
                 "b1gc": b1cols[2], "b1gc2": b1cols[3],
                 "b2c": _colpack(I["L1_ffn_b2"][li]),
                 "gc": _colpack(I["L1_ffn_ln_g"][li]), "bc": _colpack(I["L1_ffn_ln_b"][li])}
            if final:
                d["x3fm"] = x3[c]
            imI.append(d)
        rI = run(pJ if final else pI, imI)
        y = [np.asarray(rI[c]["y"]) for c in range(NC)]
        if _CAP is not None: _CAP["yf%d" % li] = y

    out = np.concatenate([a.T for a in y], axis=0).reshape(B, N, D)
    return np.ascontiguousarray(out.astype(np.float32, copy=False))



GROUPS = [[0, 1, 2, 3], [4, 5, 6, 7]]

_UNIQ = [0]

def _sb(nc, es, name, shape, dt):
    _UNIQ[0] += 1
    return es.enter_context(nc.sbuf_tensor("s%d_%s" % (_UNIQ[0], name), list(shape), dt))

def _psn(nc, es, name, shape, dt=F32):
    _UNIQ[0] += 1
    return es.enter_context(nc.psum_tensor("p%d_%s" % (_UNIQ[0], name), list(shape), dt))


class Meta:
    def __init__(self):
        self.names = []
    def slot(self, name):
        if name not in self.names:
            self.names.append(name)
        return self.names.index(name)


def build_fused(upto=99):
    p = Prog("fused")
    nc = p.nc
    s = p.s
    T = TPC

    xfm_d = p.din("xfm", [D, T])
    meta_d = p.din("meta", [32, 1], mybir.dt.int32)
    wq_d = p.din("wq", [D, 512]); wk_d = p.din("wk", [D, 512]); wv_d = p.din("wv", [D, D])
    bqc_d = p.din("bqc", [128, 4]); bkc_d = p.din("bkc", [128, 4])
    bkr_d = p.din("bkr", [1, 512]); bvr_d = p.din("bvr", [1, D])
    wo0_d = p.din("wo0", [D, D]); boc_d = p.din("boc", [128, NCH])
    mask384_d = p.din("mask384", [128, 384], BF16)
    wrT_d = p.din("wrT", [D, D]); wkT_d = p.din("wkT", [D, D]); wvT_d = p.din("wvT", [D, D])
    murc_d = p.din("murc", [128, NCH]); mukc_d = p.din("mukc", [128, NCH])
    qkvT_d = p.din("qkvT", [D, 3 * D])
    cosT_d = p.din("cosT", [128, T], BF16); ssinT_d = p.din("ssinT", [128, T], BF16)
    m896_d = p.din("m896", [128, 896], BF16)
    woT1_d = p.din("woT1", [D, D])
    wqkv6_d = p.din("wqkv6", [6, D, 3 * D])
    wo6_d = p.din("wo6", [6, D, D])
    w16_d = p.din("w16", [6, D, 2 * 2730])
    w26_d = p.din("w26", [6, 2730, D])
    cq_d = p.din("cq", [128, T], BF16); sq_d = p.din("sq", [128, T], BF16)
    ck_d = p.din("ck", [128, TKV], BF16); sk_d = p.din("sk", [128, TKV], BF16)
    qsc6_d = p.din("qsc6", [128, 6]); qscw6_d = p.din("qscw6", [128, 6])
    ksc6_d = p.din("ksc6", [128, 6]); kscw6_d = p.din("kscw6", [128, 6])
    qsr6_d = p.din("qsr6", [6, 64]); ksr6_d = p.din("ksr6", [6, 64])
    blockpat_d = p.din("blockpat", [128, 2], BF16)
    blockpatT_d = p.din("blockpatT", [2, 128])
    mlow_d = p.din("mlow", [128, 128]); mupp_d = p.din("mupp", [128, 128], BF16)
    halom_d = p.din("halom", [128, 2])
    b1c6_d = p.din("b1c6", [128, 6 * 44])
    b2c6_d = p.din("b2c6", [128, 6 * NCH])
    zc_d = p.din("zc", [128, NCH])
    gc6_d = p.din("gc6", [128, 6 * NCH]); bc6_d = p.din("bc6", [128, 6 * NCH])
    y_d = p.dout("y", [D, T])

    ex_qf = nc.dram_tensor("ex_qf", [512, T], BF16).ap()
    ex_kf = nc.dram_tensor("ex_kf", [512, T], BF16).ap()
    ex_ktm = nc.dram_tensor("ex_ktm", [T, 512], BF16).ap()
    ex_vtm = nc.dram_tensor("ex_vtm", [T, D], BF16).ap()
    gx_qf = nc.dram_tensor("gx_qf", [4 * 512, T], BF16).ap()
    gx_kf = nc.dram_tensor("gx_kf", [4 * 512, T], BF16).ap()
    gx_ktm = nc.dram_tensor("gx_ktm", [4 * T, 512], BF16).ap()
    gx_vtm = nc.dram_tensor("gx_vtm", [4 * T, D], BF16).ap()
    ex_o = nc.dram_tensor("ex_o", [256, N], BF16).ap()
    gx_o = nc.dram_tensor("gx_o", [4 * 256, N], BF16).ap()
    ex_q1 = nc.dram_tensor("ex_q1", [D, T], BF16).ap()
    ex_k1 = nc.dram_tensor("ex_k1", [D, T], BF16).ap()
    ex_v1 = nc.dram_tensor("ex_v1", [T, D], BF16).ap()
    gx_q1 = nc.dram_tensor("gx_q1", [4 * D, T], BF16).ap()
    gx_k1 = nc.dram_tensor("gx_k1", [4 * D, T], BF16).ap()
    gx_v1 = nc.dram_tensor("gx_v1", [4 * T, D], BF16).ap()
    ex_h1 = nc.dram_tensor("ex_h1", [128, NCH], F32).ap()
    gx_h1 = nc.dram_tensor("gx_h1", [5 * 128, NCH], F32).ap()
    ex_hA = nc.dram_tensor("ex_hA", [128, NCH * TH], BF16).ap()
    gx_hA = nc.dram_tensor("gx_hA", [5 * 128, NCH * TH], BF16).ap()
    vb_i = nc.dram_tensor("vb_i", [TKV, D], BF16).ap()
    x3_i = nc.dram_tensor("x3_i", [D, T], F32).ap()

    est = p.est
    stream = _sb(nc, est, "stream", [128, NCH * T], F32)
    metas = _sb(nc, est, "metas", [32, 1], mybir.dt.int32)

    def coll(ins, outs):
        for i_, o_ in zip(ins, outs):
            s.op('gpsimd', lambda g, i_=i_, o_=o_: g.collective_compute(
                "AllGather", OP.bypass, GROUPS, [i_.opt()], [o_.opt()]),
                reads=[i_], writes=[o_], async_=True)
        s.barrier()

    MS_COL, MS_HALO, MS_R128, MS_R256 = 0, 1, 2, 3

    s.dma('gpsimd', metas[:, :], meta_d[:, :])

    EX = {}
    def _preload_sync(e):
        col = e.value_load(metas[MS_COL:MS_COL + 1, 0:1])
        r128 = e.value_load(metas[MS_R128:MS_R128 + 1, 0:1])
        hrow = e.value_load(metas[MS_HALO:MS_HALO + 1, 0:1])
        EX['col'] = col
        EX['hrow'] = hrow
        EX['scan_row'] = [j * 512 + r128 for j in range(4)]
        EX['r128x'] = r128 + 0
        return e.nop()
    def _preload_g(e):
        r256g = e.value_load(metas[MS_R256:MS_R256 + 1, 0:1])
        EX['vt'] = [r256g + uu * 64 for uu in range(4)]
        EX['l1q1'] = [j * 1024 + 128 + r256g for j in range(4)]
        return e.nop()
    def _preload_sc(e):
        r256s = e.value_load(metas[MS_R256:MS_R256 + 1, 0:1])
        EX['l1q0s'] = [j * 1024 + r256s for j in range(4)]
        return e.nop()
    s.op('sync', _preload_sync, reads=[metas[:, :]], writes=[])
    s.op('gpsimd', _preload_g, reads=[metas[:, :]], writes=[])
    s.op('scalar', _preload_sc, reads=[metas[:, :]], writes=[])
    xv = xfm_d.rearrange("(c p) t -> p c t", p=128)
    s.dma('sync', stream[:, :].rearrange("p (c t) -> p c t", c=NCH), xv)

    with contextlib.ExitStack() as es0:
        zt = _sb(nc, es0, "zt", [128, NCH * TH], BF16)
        zt2 = _sb(nc, es0, "zt2", [128, NCH], F32)
        s.op('vector', lambda v: v.memset(zt[:, :], 0.0))
        s.op('vector', lambda v: v.memset(zt2[:, :], 0.0))
        s.dma('gpsimd', gx_hA[4 * 128: 5 * 128, :], zt[:, :])
        s.dma('gpsimd', gx_h1[4 * 128: 5 * 128, :], zt2[:, :])
    s.barrier()

    def _ph1():
        with contextlib.ExitStack() as es1:
            xb = _sb(nc, es1, "xb", [128, NCH * T], BF16)
            wqs = _sb(nc, es1, "wqs", [128, NCH * 512], BF16)
            wks = _sb(nc, es1, "wks", [128, NCH * 512], BF16)
            wvs = _sb(nc, es1, "wvs", [128, NCH * D], BF16)
            bq = _sb(nc, es1, "bq", [128, 4], F32); bk = _sb(nc, es1, "bk", [128, 4], F32)
            brk = _sb(nc, es1, "brk", [1, 512], BF16); brv = _sb(nc, es1, "brv", [1, D], BF16)
            ones1 = _sb(nc, es1, "ones1", [1, 128], BF16)
            of = _sb(nc, es1, "of", [128, 4 * T], BF16)
            ot = _sb(nc, es1, "ot", [128, (T // 128) * D], BF16)
            t1 = _sb(nc, es1, "t1", [128, 512], F32)
            t2 = _sb(nc, es1, "t2", [128, 512], F32)
            t3 = _sb(nc, es1, "t3", [128, 512], F32)
            pp = _psn(nc, es1, "pp", [128, 512])

            for dc in range(NCH):
                for tt in range(4):
                    co = dc * T + tt * 512
                    s.op('vector', lambda v, co=co: v.tensor_copy(xb[:, co:co + 512], stream[:, co:co + 512]))
            load_w_fm(p, wqs, wq_d, D, 512)
            load_w_fm(p, wks, wk_d, D, 512)
            load_w_fm(p, wvs, wv_d, D, D)
            s.dma('gpsimd', bq[:, :], bqc_d[:, :]); s.dma('gpsimd', bk[:, :], bkc_d[:, :])
            s.dma('gpsimd', brk[:, :], bkr_d[:, :]); s.dma('gpsimd', brv[:, :], bvr_d[:, :])
            s.op('vector', lambda v: v.memset(ones1[:, :], 1.0))

            NT = T // 512
            for (wsb, bias, dram) in ((wqs, bq, ex_qf), (wks, bk, ex_kf)):
                for dc in range(4):
                    for tt in range(NT):
                        fm_proj(p, pp, wsb, xb, dout=512, dc=dc,
                                xoff=lambda kc, tt=tt: kc * T + tt * 512, ncols=512, nk=NCH)
                        p.s.op('vector', lambda v, bias=bias, dc=dc: v.tensor_scalar(
                            t1[:, :], pp[:, :], bias[:, dc:dc + 1], 0.0, OP.add, OP.min))
                        p.s.op('scalar', lambda sc: sc.activation(out=t2[:, :], in_=t1[:, :], func=AF.Exp))
                        p.s.op('vector', lambda v, bias=bias, dc=dc: v.tensor_scalar(
                            t3[:, :], pp[:, :], bias[:, dc:dc + 1], 0.0, OP.add, OP.max))
                        p.s.op('vector', lambda v, dc=dc, tt=tt: v.tensor_add(
                            of[:, dc * T + tt * 512: dc * T + tt * 512 + 512], t2[:, :], t3[:, :]))
                store_fm(p, dram, of, 4, T)
            NT128 = T // 128
            for (wsb, dout, brow, dram, phi) in ((wks, 512, brk, ex_ktm, True), (wvs, D, brv, ex_vtm, False)):
                for tt in range(NT128):
                    for qc in range(dout // 512):
                        for kc in range(NCH):
                            p.s.op('tensor', lambda t, kc=kc, tt=tt, qc=qc, wsb=wsb, dout=dout: t.matmul(
                                pp[:, :], xb[:, kc * T + tt * 128: kc * T + tt * 128 + 128],
                                wsb[:, kc * dout + qc * 512: kc * dout + qc * 512 + 512],
                                start=(kc == 0), stop=False))
                        p.s.op('tensor', lambda t, qc=qc, brow=brow: t.matmul(
                            pp[:, :], ones1[:, :], brow[:, qc * 512: qc * 512 + 512],
                            start=False, stop=True))
                        co = tt * dout + qc * 512
                        if phi:
                            p.s.op('vector', lambda v: v.tensor_scalar(t1[:, :], pp[:, :], 0.0, None, OP.min))
                            p.s.op('scalar', lambda sc: sc.activation(out=t2[:, :], in_=t1[:, :], func=AF.Exp))
                            p.s.op('vector', lambda v: v.tensor_scalar(t3[:, :], pp[:, :], 0.0, None, OP.max))
                            p.s.op('vector', lambda v, co=co: v.tensor_add(ot[:, co:co + 512], t2[:, :], t3[:, :]))
                        else:
                            p.s.op('scalar', lambda sc, co=co: sc.activation(
                                out=ot[:, co:co + 512], in_=pp[:, :], func=AF.Copy))
                store_tm(p, dram, ot, NT128, dout)
        coll([ex_qf, ex_kf, ex_ktm, ex_vtm], [gx_qf, gx_kf, gx_ktm, gx_vtm])
    if upto >= 1:
        _ph1()

    def _ph2():
        with contextlib.ExitStack() as es2:
            qf = _sb(nc, es2, "qf", [128, N], BF16)
            kf = _sb(nc, es2, "kf", [128, N], BF16)
            ktm = _sb(nc, es2, "ktm", [128, (N // 128) * 32 * 4], BF16)
            vtm = _sb(nc, es2, "vtm", [128, (N // 128) * 65 * 4], BF16)
            msk = _sb(nc, es2, "msk", [128, 384], BF16)
            pmask = _sb(nc, es2, "pmask", [128, 512], BF16)
            Ssb = _sb(nc, es2, "Ssb", [128, 65], BF16)
            numb = _sb(nc, es2, "numb", [64, 256], F32)
            zrow = _sb(nc, es2, "zrow", [1, 256], F32)
            onesf = _sb(nc, es2, "onesf", [1, 64], F32)
            ofm = [_sb(nc, es2, "ofm%d" % i, [128, N], BF16) for i in range(2)]
            psT = _psn(nc, es2, "psT", [128, 512])
            pacc = _psn(nc, es2, "pacc", [128, 256])
            pzb = _psn(nc, es2, "pzb", [64, 256])
            pst = _psn(nc, es2, "pst", [32, 65])

            for j in range(4):
                s.op('sync', lambda e, j=j: e.dma_start(
                    out=qf[:, j * T:(j + 1) * T],
                    in_=gx_qf[bass.ds(EX['scan_row'][j], 128), :]), dma=True,
                    reads=[gx_qf], writes=[qf[:, :]])
                s.op('sync', lambda e, j=j: e.dma_start(
                    out=kf[:, j * T:(j + 1) * T],
                    in_=gx_kf[bass.ds(EX['scan_row'][j], 128), :]), dma=True,
                    reads=[gx_kf], writes=[kf[:, :]])
                s.op('sync', lambda e, j=j: e.dma_start(
                    out=ktm[:, :].rearrange("p (tt x) -> p tt x", x=128)[:, j * 16:(j + 1) * 16, :],
                    in_=gx_ktm[j * T:(j + 1) * T, bass.ds(EX['r128x'], 128)].rearrange("(tt pp) f -> pp tt f", pp=128)), dma=True,
                    reads=[gx_ktm], writes=[ktm[:, :]])
                for uu in range(4):
                    s.op('gpsimd', lambda e, j=j, uu=uu: e.dma_start(
                        out=vtm[:, :].rearrange("p (tt u g) -> p tt u g", u=4, g=65)[:, j * 16:(j + 1) * 16, uu, 0:64],
                        in_=gx_vtm[j * T:(j + 1) * T, bass.ds(EX['vt'][uu], 64)].rearrange("(tt pp) g -> pp tt g", pp=128)), dma=True,
                        reads=[gx_vtm], writes=[vtm[:, :]])
            s.dma('gpsimd', msk[:, :], mask384_d[:, :])
            s.op('vector', lambda v: v.memset(vtm[:, :].rearrange("p (x g) -> p x g", g=65)[:, :, 64:65], 1.0))
            s.op('vector', lambda v: v.memset(Ssb[:, :], 0.0))
            s.op('vector', lambda v: v.memset(onesf[:, :], 1.0))

            nchunk = N // CHK
            for u in range(4):
                ub = u * 32
                for c in range(nchunk):
                    c0 = c * CHK
                    for kt in range(2):
                        s.op('tensor', lambda t, ub=ub, c0=c0, kt=kt: t.matmul(
                            psT[:, kt * 256: kt * 256 + 256],
                            kf[ub:ub + 32, c0 + kt * 128: c0 + kt * 128 + 128],
                            qf[ub:ub + 32, c0: c0 + 256],
                            start=True, stop=True, tile_position=(ub, 0)))
                    for kt in range(2):
                        mv = msk[:, 128 - kt * 128: 128 - kt * 128 + 256]
                        s.op('vector', lambda v, kt=kt, mv=mv: v.tensor_tensor(
                            pmask[:, kt * 256: kt * 256 + 256], psT[:, kt * 256: kt * 256 + 256], mv, OP.mult))
                    for kt in range(2):
                        tt128 = (c0 // 128) + kt
                        s.op('tensor', lambda t, tt128=tt128, u=u, kt=kt: t.matmul(
                            pacc[0:65, :], vtm[:, (tt128 * 4 + u) * 65: (tt128 * 4 + u) * 65 + 65],
                            pmask[:, kt * 256: kt * 256 + 256],
                            start=(kt == 0), stop=False))
                    s.op('tensor', lambda t, ub=ub, c0=c0: t.matmul(
                        pacc[0:65, :], Ssb[ub:ub + 32, :], qf[ub:ub + 32, c0: c0 + 256],
                        start=False, stop=True, tile_position=(ub, 0)))
                    for kt in range(2):
                        tt128 = (c0 // 128) + kt
                        s.op('tensor', lambda t, tt128=tt128, u=u, c=c, kt=kt: t.matmul(
                            pst[:, :], ktm[:, (tt128 * 4 + u) * 32: (tt128 * 4 + u) * 32 + 32],
                            vtm[:, (tt128 * 4 + u) * 65: (tt128 * 4 + u) * 65 + 65],
                            start=(c == 0 and kt == 0), stop=False))
                    s.op('vector', lambda v, ub=ub: v.tensor_copy(Ssb[ub:ub + 32, :], pst[:, :]))
                    s.op('vector', lambda v: v.tensor_scalar_add(zrow[:, :], pacc[64:65, :], 1e-6))
                    s.op('vector', lambda v: v.reciprocal(zrow[:, :], zrow[:, :]))
                    s.op('tensor', lambda t: t.matmul(pzb[:, :], onesf[:, :], zrow[:, :], start=True, stop=True))
                    s.op('scalar', lambda sc: sc.activation(out=numb[:, :], in_=pacc[0:64, :], func=AF.Copy))
                    po = (u % 2) * 64
                    pair = u // 2
                    s.op('vector', lambda v, po=po, pair=pair, c0=c0: v.tensor_tensor(
                        ofm[pair][po:po + 64, c0: c0 + 256], numb[:, :], pzb[:, :], OP.mult))
            ov = ex_o.rearrange("(pair po) t -> pair po t", po=128)
            for pair in range(2):
                s.dma('sync', ov[pair], ofm[pair][:, :])
        coll([ex_o], [gx_o])
    if upto >= 2:
        _ph2()

    def emit_oproj_res_ln(wo_dram, bias_dram, gx, nch_rows_map):
        with contextlib.ExitStack() as es3:
            ab = _sb(nc, es3, "ab", [128, NCH * T], BF16)
            wos = _sb(nc, es3, "wos", [128, NCH * D], BF16)
            bo = _sb(nc, es3, "bo", [128, NCH], F32)
            ub = _sb(nc, es3, "ub", [128, NCH * 512], BF16)
            usq = _sb(nc, es3, "usq", [128, 512], BF16)
            onesc = _sb(nc, es3, "onesc", [128, 1], BF16)
            onesr = _sb(nc, es3, "onesr", [1, 128], F32)
            srow = _sb(nc, es3, "srow", [1, 1024], F32)
            trow = _sb(nc, es3, "trow", [1, 512], F32)
            pp = _psn(nc, es3, "pp", [128, 512])
            pab = _psn(nc, es3, "pab", [128, 1024])
            pstat = _psn(nc, es3, "pstat", [1, 1024])

            for dc in range(NCH):
                r0 = nch_rows_map(dc)
                s.op('sync', lambda e, dc=dc, r0=r0: e.dma_start(
                    out=ab[:, dc * T:(dc + 1) * T],
                    in_=gx[r0:r0 + 128, bass.ds(EX['col'], T)]), dma=True,
                    reads=[gx], writes=[ab[:, :]])
            load_w_fm(p, wos, wo_dram, D, D)
            s.dma('gpsimd', bo[:, :], bias_dram[:, :])
            s.op('vector', lambda v: v.memset(onesc[:, :], 1.0))
            s.op('vector', lambda v: v.memset(onesr[:, :], 1.0))
            NT = T // 512
            for tt in range(NT):
                for dc in range(NCH):
                    fm_proj(p, pp, wos, ab, dout=D, dc=dc,
                            xoff=lambda kc, tt=tt: kc * T + tt * 512, ncols=512, nk=NCH)
                    co = dc * T + tt * 512
                    s.op('vector', lambda v, dc=dc, co=co: v.scalar_tensor_tensor(
                        stream[:, co:co + 512], pp[:, :], bo[:, dc:dc + 1], stream[:, co:co + 512],
                        OP.add, OP.add))
                    s.op('vector', lambda v, dc=dc, co=co: v.tensor_copy(
                        ub[:, dc * 512:dc * 512 + 512], stream[:, co:co + 512]))
                emit_res_ln(p, u=stream, ub=ub, usq=usq, onesc=onesc, onesr=onesr,
                            srow=srow, trow=trow, pab=pab, pstat=pstat, tt=tt, T=T)
        s.barrier()

    if upto >= 3:
        emit_oproj_res_ln(wo0_d, boc_d, gx_o,
                          lambda dc: ((dc * 2) // 4) * 256 + ((dc * 2) % 4) * 64)


    def emit_halo1():
        for dc in range(NCH):
            s.op('sync', lambda e, dc=dc: e.dma_start(
                out=ex_h1[:, dc:dc + 1], in_=stream[:, dc * T + T - 1: dc * T + T],
                allow_slow_non_contiguous=True), dma=True)
        coll([ex_h1], [gx_h1[0:4 * 128, :]])

    def read_halo1(hl):
        s.op('sync', lambda e, hl=hl: e.dma_start(
            out=hl[:, :], in_=gx_h1[bass.ds(EX['hrow'], 128), :]), dma=True,
            reads=[gx_h1], writes=[hl[:, :]])

    def _ph4():
        emit_halo1()
        with contextlib.ExitStack() as es4:
            hl = _sb(nc, es4, "hl", [128, NCH], F32)
            wr = _sb(nc, es4, "wr", [128, NCH * D], BF16)
            wk2 = _sb(nc, es4, "wk2", [128, NCH * D], BF16)
            wv2 = _sb(nc, es4, "wv2", [128, NCH * D], BF16)
            mur = _sb(nc, es4, "mur", [128, NCH], F32); murM = _sb(nc, es4, "murM", [128, NCH], F32)
            muk = _sb(nc, es4, "muk", [128, NCH], F32); mukM = _sb(nc, es4, "mukM", [128, NCH], F32)
            xr = _sb(nc, es4, "xr", [128, NCH * 512], BF16)
            rr = _sb(nc, es4, "rr", [128, NCH * 512], BF16)
            ka = _sb(nc, es4, "ka", [128, NCH * 512], BF16)
            xp0 = _sb(nc, es4, "xp0", [128, 512], F32)
            ub = _sb(nc, es4, "ub", [128, NCH * 512], BF16)
            usq = _sb(nc, es4, "usq", [128, 512], BF16)
            t1 = _sb(nc, es4, "t1", [128, 512], F32)
            onesc = _sb(nc, es4, "onesc", [128, 1], BF16)
            onesr = _sb(nc, es4, "onesr", [1, 128], F32)
            srow = _sb(nc, es4, "srow", [1, 1024], F32)
            trow = _sb(nc, es4, "trow", [1, 512], F32)
            pp = _psn(nc, es4, "pp", [128, 512])
            pab = _psn(nc, es4, "pab", [128, 1024])
            pstat = _psn(nc, es4, "pstat", [1, 1024])

            read_halo1(hl)
            load_w_fm(p, wr, wrT_d, D, D)
            load_w_fm(p, wk2, wkT_d, D, D)
            load_w_fm(p, wv2, wvT_d, D, D)
            s.dma('gpsimd', mur[:, :], murc_d[:, :]); s.dma('gpsimd', muk[:, :], mukc_d[:, :])
            s.op('vector', lambda v: v.tensor_scalar(murM[:, :], mur[:, :], -1.0, 1.0, OP.mult, OP.add))
            s.op('vector', lambda v: v.tensor_scalar(mukM[:, :], muk[:, :], -1.0, 1.0, OP.mult, OP.add))
            s.op('vector', lambda v: v.memset(onesc[:, :], 1.0))
            s.op('vector', lambda v: v.memset(onesr[:, :], 1.0))

            NT = T // 512
            for tt in range(NT):
                for which in range(2):
                    mu_, muM_ = (mur, murM) if which == 0 else (muk, mukM)
                    for dc in range(NCH):
                        co = dc * T + tt * 512
                        s.op('vector', lambda v, dc=dc, co=co, mu_=mu_: v.tensor_scalar_mul(
                            t1[:, :], stream[:, co:co + 512], mu_[:, dc:dc + 1]))
                        if tt == 0:
                            s.op('vector', lambda v, dc=dc: v.tensor_copy(
                                xp0[:, 0:1], hl[:, dc:dc + 1]))
                            s.op('vector', lambda v, dc=dc, co=co: v.tensor_copy(
                                xp0[:, 1:512], stream[:, co:co + 511]))
                            s.op('vector', lambda v, dc=dc, muM_=muM_: v.scalar_tensor_tensor(
                                xr[:, dc * 512:dc * 512 + 512], xp0[:, :], muM_[:, dc:dc + 1],
                                t1[:, :], OP.mult, OP.add))
                        else:
                            s.op('vector', lambda v, dc=dc, co=co, muM_=muM_: v.scalar_tensor_tensor(
                                xr[:, dc * 512:dc * 512 + 512], stream[:, co - 1:co + 511], muM_[:, dc:dc + 1],
                                t1[:, :], OP.mult, OP.add))
                    if which == 0:
                        for dc in range(NCH):
                            fm_proj(p, pp, wr, xr, dout=D, dc=dc, xoff=lambda kc: kc * 512, ncols=512, nk=NCH)
                            s.op('scalar', lambda sc, dc=dc: sc.activation(
                                out=rr[:, dc * 512:dc * 512 + 512], in_=pp[:, :], func=AF.Sigmoid))
                    else:
                        for dc in range(NCH):
                            fm_proj(p, pp, wk2, xr, dout=D, dc=dc, xoff=lambda kc: kc * 512, ncols=512, nk=NCH)
                            s.op('vector', lambda v: v.tensor_scalar(t1[:, :], pp[:, :], 0.0, None, OP.max))
                            s.op('scalar', lambda sc, dc=dc: sc.activation(
                                out=ka[:, dc * 512:dc * 512 + 512], in_=t1[:, :], func=AF.Square))
                for dc in range(NCH):
                    fm_proj(p, pp, wv2, ka, dout=D, dc=dc, xoff=lambda kc: kc * 512, ncols=512, nk=NCH)
                    co = dc * T + tt * 512
                    s.op('vector', lambda v, dc=dc: v.tensor_tensor(
                        t1[:, :], rr[:, dc * 512:dc * 512 + 512], pp[:, :], OP.mult))
                    s.op('vector', lambda v, co=co: v.tensor_add(
                        stream[:, co:co + 512], t1[:, :], stream[:, co:co + 512]))
                    s.op('vector', lambda v, dc=dc, co=co: v.tensor_copy(
                        ub[:, dc * 512:dc * 512 + 512], stream[:, co:co + 512]))
                emit_res_ln(p, u=stream, ub=ub, usq=usq, onesc=onesc, onesr=onesr,
                            srow=srow, trow=trow, pab=pab, pstat=pstat, tt=tt, T=T)
        s.barrier()

    if upto >= 4:
        _ph4()

    def _ph5():
        with contextlib.ExitStack() as es5:
            xb = _sb(nc, es5, "xb", [128, NCH * T], BF16)
            wqb = _sb(nc, es5, "wqb", [128, NCH * 512], BF16)
            cosb = _sb(nc, es5, "cosb", [128, T], BF16)
            ssin = _sb(nc, es5, "ssin", [128, T], BF16)
            qraw = _sb(nc, es5, "qraw", [128, NCH * T], BF16)
            qsw = _sb(nc, es5, "qsw", [128, T], BF16)
            t1b = _sb(nc, es5, "t1b", [128, T], BF16)
            och = _sb(nc, es5, "och", [128, T], BF16)
            otv = _sb(nc, es5, "otv", [128, D], BF16)
            pp = _psn(nc, es5, "pp", [128, 512])

            for dc in range(NCH):
                s.op('vector', lambda v, dc=dc: v.tensor_copy(xb[:, dc * T:(dc + 1) * T], stream[:, dc * T:(dc + 1) * T]))
            s.dma('gpsimd', cosb[:, :], cosT_d[:, :])
            s.dma('gpsimd', ssin[:, :], ssinT_d[:, :])

            qk = qkvT_d.rearrange("(kc p) d -> p kc d", p=128)
            NT = T // 512
            for (woff2, dram) in ((0, ex_q1), (D, ex_k1)):
                for half in range(2):
                    s.dma('gpsimd', wqb[:, :].rearrange("p (kc d) -> p kc d", kc=NCH),
                          qk[:, :, woff2 + half * 512: woff2 + half * 512 + 512])
                    for dch in range(4):
                        dc = half * 4 + dch
                        for tt in range(NT):
                            for kc in range(NCH):
                                s.op('tensor', lambda t, kc=kc, dch=dch, tt=tt: t.matmul(
                                    pp[:, :], wqb[:, kc * 512 + dch * 128: kc * 512 + dch * 128 + 128],
                                    xb[:, kc * T + tt * 512: kc * T + tt * 512 + 512],
                                    start=(kc == 0), stop=(kc == NCH - 1)))
                            s.op('scalar', lambda sc, dc=dc, tt=tt: sc.activation(
                                out=qraw[:, dc * T + tt * 512: dc * T + tt * 512 + 512], in_=pp[:, :], func=AF.Copy))
                for dc in range(NCH):
                    qo = dc * T
                    for g in range(4):
                        src = (g ^ 1) * 32
                        s.op('vector', lambda v, qo=qo, g=g, src=src: v.tensor_copy(
                            qsw[g * 32:(g + 1) * 32, :], qraw[src:src + 32, qo:qo + T]))
                    s.op('vector', lambda v, qo=qo: v.tensor_tensor(
                        t1b[:, :], qraw[:, qo:qo + T], cosb[:, :], OP.mult))
                    s.op('vector', lambda v: v.tensor_tensor(qsw[:, :], qsw[:, :], ssin[:, :], OP.mult))
                    s.op('vector', lambda v: v.tensor_add(och[:, :], t1b[:, :], qsw[:, :]))
                    s.op('sync', lambda e, dc=dc, dram=dram: e.dma_start(
                        out=dram[dc * 128:(dc + 1) * 128, :], in_=och[:, :]), dma=True)
            wvb2 = qraw
            s.dma('gpsimd', wqb[:, :].rearrange("p (kc d) -> p kc d", kc=NCH),
                  qk[:, :, 2 * D: 2 * D + 512])
            s.dma('gpsimd', wvb2[:, 0:NCH * 512].rearrange("p (kc d) -> p kc d", kc=NCH),
                  qk[:, :, 2 * D + 512: 2 * D + 1024])
            for tt in range(T // 128):
                for half, wb in ((0, wqb), (1, wvb2)):
                    for kc in range(NCH):
                        s.op('tensor', lambda t, kc=kc, tt=tt, wb=wb: t.matmul(
                            pp[:, :], xb[:, kc * T + tt * 128: kc * T + tt * 128 + 128],
                            wb[:, kc * 512: kc * 512 + 512],
                            start=(kc == 0), stop=(kc == NCH - 1)))
                    s.op('scalar', lambda sc, half=half: sc.activation(
                        out=otv[:, half * 512: half * 512 + 512], in_=pp[:, :], func=AF.Copy))
                s.op('sync', lambda e, tt=tt: e.dma_start(
                    out=ex_v1[tt * 128:(tt + 1) * 128, :], in_=otv[:, :]), dma=True)
        coll([ex_q1, ex_k1, ex_v1], [gx_q1, gx_k1, gx_v1])
    if upto >= 5:
        _ph5()

    def _ph6():
        with contextlib.ExitStack() as es6:
            qf = [_sb(nc, es6, "qf%d" % i, [128, N], BF16) for i in range(2)]
            kf = [_sb(nc, es6, "kf%d" % i, [128, N], BF16) for i in range(2)]
            vtm = _sb(nc, es6, "vtm", [128, (N // 128) * 65 * 4], BF16)
            m896 = _sb(nc, es6, "m896b", [128, 896], BF16)
            psb = _sb(nc, es6, "psb", [128, 1024], BF16)
            numb = _sb(nc, es6, "numb", [64, 512], F32)
            zrow = _sb(nc, es6, "zrow", [1, 512], F32)
            onesf = _sb(nc, es6, "onesf", [1, 64], F32)
            ofm = [_sb(nc, es6, "ofm%d" % i, [128, N], BF16) for i in range(2)]
            psT = _psn(nc, es6, "psT", [128, 1024])
            pout = _psn(nc, es6, "pout", [128, 512])
            pzb = _psn(nc, es6, "pzb", [64, 512])

            for j in range(4):
                for pair in range(2):
                    eng = 'scalar' if pair == 0 else 'gpsimd'
                    expr = (lambda j=j: EX['l1q0s'][j]) if pair == 0 else (lambda j=j: EX['l1q1'][j])
                    s.op(eng, lambda e, j=j, expr=expr, pair=pair: e.dma_start(
                        out=qf[pair][:, j * T:(j + 1) * T],
                        in_=gx_q1[bass.ds(expr(), 128), :]), dma=True,
                        reads=[gx_q1], writes=[qf[pair][:, :]])
                    s.op(eng, lambda e, j=j, expr=expr, pair=pair: e.dma_start(
                        out=kf[pair][:, j * T:(j + 1) * T],
                        in_=gx_k1[bass.ds(expr(), 128), :]), dma=True,
                        reads=[gx_k1], writes=[kf[pair][:, :]])
                for uu in range(4):
                    s.op('gpsimd', lambda e, j=j, uu=uu: e.dma_start(
                        out=vtm[:, :].rearrange("p (tt u g) -> p tt u g", u=4, g=65)[:, j * 16:(j + 1) * 16, uu, 0:64],
                        in_=gx_v1[j * T:(j + 1) * T, bass.ds(EX['vt'][uu], 64)].rearrange("(tt pp) g -> pp tt g", pp=128)), dma=True,
                        reads=[gx_v1], writes=[vtm[:, :]])
            s.dma('gpsimd', m896[:, :], m896_d[:, :])
            s.op('vector', lambda v: v.memset(vtm[:, :].rearrange("p (x g) -> p x g", g=65)[:, :, 64:65], 1.0))
            s.op('vector', lambda v: v.memset(onesf[:, :], 1.0))

            NQT = N // 512
            for u in range(4):
                pair, po = u // 2, (u % 2) * 64
                qs2 = qf[pair]; ks2 = kf[pair]
                for qt in range(NQT):
                    nkt = 4 * qt + 4
                    first = True
                    for g0 in range(0, nkt, 2):
                        gn = min(2, nkt - g0)
                        for gi in range(gn):
                            kt = g0 + gi
                            s.op('tensor', lambda t, po=po, kt=kt, qt=qt, gi=gi, qs2=qs2, ks2=ks2: t.matmul(
                                psT[:, gi * 512: gi * 512 + 512],
                                ks2[po:po + 64, kt * 128: kt * 128 + 128],
                                qs2[po:po + 64, qt * 512: qt * 512 + 512],
                                start=True, stop=True, tile_position=(po, 0)))
                        s.op('scalar', lambda sc, gn=gn: sc.activation(
                            out=psb[:, 0:gn * 512], in_=psT[:, 0:gn * 512], func=AF.Exp, scale=0.125))
                        for gi in range(gn):
                            kt = g0 + gi
                            off = qt * 512 - kt * 128
                            if off <= 0:
                                s.op('vector', lambda v, gi=gi, off=off: v.tensor_tensor(
                                    psb[:, gi * 512: gi * 512 + 512], psb[:, gi * 512: gi * 512 + 512],
                                    m896[:, off + 384: off + 384 + 512], OP.mult))
                        for gi in range(gn):
                            kt = g0 + gi
                            s.op('tensor', lambda t, kt=kt, u=u, gi=gi, first=first and gi == 0: t.matmul(
                                pout[0:65, :], vtm[:, (kt * 4 + u) * 65: (kt * 4 + u) * 65 + 65],
                                psb[:, gi * 512: gi * 512 + 512],
                                start=first, stop=False))
                            first = False
                    s.op('vector', lambda v: v.reciprocal(zrow[:, :], pout[64:65, :]))
                    s.op('tensor', lambda t: t.matmul(pzb[:, :], onesf[:, :], zrow[:, :], start=True, stop=True))
                    s.op('scalar', lambda sc: sc.activation(out=numb[:, :], in_=pout[0:64, :], func=AF.Copy))
                    s.op('vector', lambda v, pair=pair, po=po, qt=qt: v.tensor_tensor(
                        ofm[pair][po:po + 64, qt * 512: qt * 512 + 512], numb[:, :], pzb[:, :], OP.mult))
            ov = ex_o.rearrange("(pair po) t -> pair po t", po=128)
            for pair in range(2):
                s.dma('sync', ov[pair], ofm[pair][:, :])
        coll([ex_o], [gx_o])
    if upto >= 6:
        _ph6()

    if upto >= 7:
        emit_oproj_res_ln(woT1_d, zc_d, gx_o,
                          lambda dc: ((dc * 2) // 4) * 256 + ((dc * 2) % 4) * 64)
        s.dma('sync', x3_i.rearrange("(c p) t -> p c t", p=128),
              stream[:, :].rearrange("p (c t) -> p c t", c=NCH))
    def _inner_layers():
      with contextlib.ExitStack() as esT:
        bpat = _sb(nc, esT, "bpat", [128, 2], BF16)
        bpatT = _sb(nc, esT, "bpatT", [2, 128], F32)
        mlow = _sb(nc, esT, "mlow", [128, 128], F32)
        mupp = _sb(nc, esT, "mupp", [128, 128], BF16)
        halom = _sb(nc, esT, "halom", [128, 2], F32)
        onesf = _sb(nc, esT, "onesf", [1, 128], F32)
        s.dma('gpsimd', bpat[:, :], blockpat_d[:, :])
        s.dma('gpsimd', bpatT[:, :], blockpatT_d[:, :])
        s.dma('gpsimd', mlow[:, :], mlow_d[:, :])
        s.dma('gpsimd', mupp[:, :], mupp_d[:, :])
        s.dma('gpsimd', halom[:, :], halom_d[:, :])
        s.op('vector', lambda v: v.memset(onesf[:, :], 1.0))

        for li in range(6):
            if upto < 8 + 2 * li:
                break
            _inner_attn_layer(li, bpat, bpatT, mlow, mupp, halom, onesf)
            if upto < 9 + 2 * li:
                break
            _inner_ffn_layer(li, onesf, final=(li == 5))

    def _inner_attn_layer(li, bpat, bpatT, mlow, mupp, halom, onesf):
        T_ = TPC
        TW = T_ + TH
        for dc in range(NCH):
            s.op('gpsimd', lambda e, dc=dc: e.dma_start(
                out=ex_hA[:, dc * TH:(dc + 1) * TH],
                in_=stream[:, dc * T_ + T_ - TH: dc * T_ + T_]), dma=True)
        coll([ex_hA], [gx_hA[0:4 * 128, :]])
        with contextlib.ExitStack() as esA:
            wqb = _sb(nc, esA, "wqb", [128, NCH * D], BF16)
            qscs = _sb(nc, esA, "qscs", [128, 1], F32); qscws = _sb(nc, esA, "qscws", [128, 1], F32)
            kscs = _sb(nc, esA, "kscs", [128, 1], F32); kscws = _sb(nc, esA, "kscws", [128, 1], F32)
            qsrow = _sb(nc, esA, "qsrow", [1, 64], F32); ksrow = _sb(nc, esA, "ksrow", [1, 64], F32)
            csb = _sb(nc, esA, "csb", [128, 1], F32)
            crow = _sb(nc, esA, "crow", [1, 2], F32)
            xbs = _sb(nc, esA, "xbs", [128, 512], BF16)
            nrm2 = _sb(nc, esA, "nrm2", [2, 512], F32)
            qraw = _sb(nc, esA, "qraw", [128, NCH * T_], BF16)
            kraw = _sb(nc, esA, "kraw", [128, NCH * TKV], BF16)
            psb = _sb(nc, esA, "psb", [128, 384], BF16)
            numb = _sb(nc, esA, "numb", [64, 128], F32)
            zrow = _sb(nc, esA, "zrow", [1, 512], F32)
            pp = _psn(nc, esA, "pp", [128, 512])
            pnrm = _psn(nc, esA, "pnrm", [2, 512])
            pbc = _psn(nc, esA, "pbc", [128, 512])
            psT = _psn(nc, esA, "psT", [128, 384])
            pacc = _psn(nc, esA, "pacc", [128, 128])
            pzb = _psn(nc, esA, "pzb", [64, 128])

            s.dma('gpsimd', qscs[:, :], qsc6_d[:, li:li + 1], allow_slow_non_contiguous=True)
            s.dma('gpsimd', qscws[:, :], qscw6_d[:, li:li + 1], allow_slow_non_contiguous=True)
            s.dma('gpsimd', kscs[:, :], ksc6_d[:, li:li + 1], allow_slow_non_contiguous=True)
            s.dma('gpsimd', kscws[:, :], kscw6_d[:, li:li + 1], allow_slow_non_contiguous=True)
            s.dma('gpsimd', qsrow[:, :], qsr6_d[li:li + 1, :])
            s.dma('gpsimd', ksrow[:, :], ksr6_d[li:li + 1, :])
            s.op('vector', lambda v: v.reduce_max(crow[0:1, 0:1], qsrow[:, :], axis=mybir.AxisListType.X))
            s.op('vector', lambda v: v.reduce_max(crow[0:1, 1:2], ksrow[:, :], axis=mybir.AxisListType.X))
            s.op('vector', lambda v: v.tensor_mul(crow[0:1, 0:1], crow[0:1, 0:1], crow[0:1, 1:2]))
            s.op('vector', lambda v: v.tensor_scalar_mul(crow[0:1, 0:1], crow[0:1, 0:1], -8.0))
            s.op('tensor', lambda t: t.matmul(pbc[:, 0:1], onesf[:, :], crow[0:1, 0:1], start=True, stop=True))
            s.op('vector', lambda v: v.tensor_copy(csb[:, :], pbc[:, 0:1]))

            wv_ = wqkv6_d[li].rearrange("(kc p) d -> p kc d", p=128)

            with contextlib.ExitStack() as esY:
                ysb = _sb(nc, esY, "ysb", [128, NCH * TW], BF16)
                otv = _sb(nc, esY, "otv", [128, D], BF16)
                s.op('sync', lambda e, ysb=ysb: e.dma_start(
                    out=ysb[:, :].rearrange("p (c t) -> p c t", c=NCH)[:, :, 0:TH],
                    in_=gx_hA[bass.ds(EX['hrow'], 128), :].rearrange("pp (c t) -> pp c t", c=NCH)), dma=True,
                    reads=[gx_hA], writes=[ysb[:, :]])
                for dc in range(NCH):
                    s.op('vector', lambda v, dc=dc: v.tensor_copy(
                        ysb[:, dc * TW + TH: dc * TW + TH + T_], stream[:, dc * T_:(dc + 1) * T_]))

                def xoff(dc, st):
                    return dc * TW + st + (TH if dc < 4 else TH - 1)

                for (woff2, dst, ncolw, base) in ((0, qraw, T_, 0), (D, kraw, TKV, -256)):
                    s.dma('gpsimd', wqb[:, :].rearrange("p (kc d) -> p kc d", kc=NCH),
                          wv_[:, :, woff2: woff2 + D])
                    ntile = (ncolw + 511) // 512
                    for ti in range(ntile):
                        st = ti * 512
                        w = min(512, ncolw - st)
                        for dc in range(NCH):
                            for kc in range(NCH):
                                s.op('tensor', lambda t, kc=kc, dc=dc, w=w, st=st, base=base: t.matmul(
                                    pp[:, 0:w], wqb[:, kc * D + dc * 128: kc * D + dc * 128 + 128],
                                    ysb[:, xoff(kc, st + base): xoff(kc, st + base) + w],
                                    start=(kc == 0), stop=(kc == NCH - 1)))
                            s.op('scalar', lambda sc, dc=dc, st=st, w=w, dst=dst, ncolw=ncolw: sc.activation(
                                out=dst[:, dc * ncolw + st: dc * ncolw + st + w], in_=pp[:, 0:w], func=AF.Copy))
                s.dma('gpsimd', wqb[:, :].rearrange("p (kc d) -> p kc d", kc=NCH),
                      wv_[:, :, 2 * D: 3 * D])
                for jt in range(TKV // 128):
                    st = jt * 128 - 256
                    for qc in range(2):
                        for kc in range(NCH):
                            s.op('tensor', lambda t, kc=kc, qc=qc, st=st: t.matmul(
                                pp[:, :], ysb[:, xoff(kc, st): xoff(kc, st) + 128],
                                wqb[:, kc * D + qc * 512: kc * D + qc * 512 + 512],
                                start=(kc == 0), stop=(kc == NCH - 1)))
                        s.op('scalar', lambda sc, qc=qc: sc.activation(
                            out=otv[:, qc * 512: qc * 512 + 512], in_=pp[:, :], func=AF.Copy))
                    s.op('sync', lambda e, jt=jt: e.dma_start(
                        out=vb_i[jt * 128:(jt + 1) * 128, :], in_=otv[:, :]), dma=True)
            s.barrier()

            esR = contextlib.ExitStack()
            cqs = _sb(nc, esR, "cqs", [128, TKV], BF16)
            sqs = _sb(nc, esR, "sqs", [128, TKV], BF16)
            qsw = _sb(nc, esR, "qsw", [128, TKV], BF16)
            for (dst, ncolw, tq, ts_, scl, sclw) in ((qraw, T_, cq_d, sq_d, qscs, qscws),
                                                     (kraw, TKV, ck_d, sk_d, kscs, kscws)):
                s.dma('gpsimd', cqs[:, 0:ncolw], tq[:, 0:ncolw])
                s.dma('gpsimd', sqs[:, 0:ncolw], ts_[:, 0:ncolw])
                ntile = (ncolw + 511) // 512
                for dc in range(NCH):
                    qo = dc * ncolw
                    for ti in range(ntile):
                        st = ti * 512
                        w = min(512, ncolw - st)
                        s.op('scalar', lambda sc, qo=qo, st=st, w=w, dst=dst: sc.activation(
                            out=xbs[:, 0:w], in_=dst[:, qo + st: qo + st + w], func=AF.Square))
                        s.op('tensor', lambda t, w=w: t.matmul(
                            pnrm[:, 0:w], bpat[:, :], xbs[:, 0:w], start=True, stop=True))
                        s.op('scalar', lambda sc, w=w: sc.activation(out=nrm2[:, 0:w], in_=pnrm[:, 0:w], func=AF.Sqrt))
                        s.op('vector', lambda v, w=w: v.tensor_scalar_max(nrm2[:, 0:w], nrm2[:, 0:w], 1e-12))
                        s.op('vector', lambda v, w=w: v.reciprocal(nrm2[:, 0:w], nrm2[:, 0:w]))
                        s.op('tensor', lambda t, w=w: t.matmul(
                            pbc[:, 0:w], bpatT[:, :], nrm2[:, 0:w], start=True, stop=True))
                        s.op('vector', lambda v, qo=qo, st=st, w=w, dst=dst: v.tensor_tensor(
                            dst[:, qo + st: qo + st + w], dst[:, qo + st: qo + st + w], pbc[:, 0:w], OP.mult))
                    for g in range(4):
                        src = (g ^ 1) * 32
                        s.op('vector', lambda v, qo=qo, g=g, src=src, dst=dst, ncolw=ncolw: v.tensor_copy(
                            qsw[g * 32:(g + 1) * 32, 0:ncolw], dst[src:src + 32, qo: qo + ncolw]))
                    s.op('vector', lambda v, qo=qo, dst=dst, ncolw=ncolw, scl=scl: v.tensor_scalar_mul(
                        dst[:, qo: qo + ncolw], dst[:, qo: qo + ncolw], scl[:, :]))
                    s.op('vector', lambda v, ncolw=ncolw, sclw=sclw: v.tensor_scalar_mul(
                        qsw[:, 0:ncolw], qsw[:, 0:ncolw], sclw[:, :]))
                    s.op('vector', lambda v, qo=qo, dst=dst, ncolw=ncolw: v.tensor_tensor(
                        dst[:, qo: qo + ncolw], dst[:, qo: qo + ncolw], cqs[:, 0:ncolw], OP.mult))
                    s.op('vector', lambda v, ncolw=ncolw: v.tensor_tensor(
                        qsw[:, 0:ncolw], qsw[:, 0:ncolw], sqs[:, 0:ncolw], OP.mult))
                    s.op('vector', lambda v, qo=qo, dst=dst, ncolw=ncolw: v.tensor_add(
                        dst[:, qo: qo + ncolw], dst[:, qo: qo + ncolw], qsw[:, 0:ncolw]))

            esR.close()
            s.barrier()
            with contextlib.ExitStack() as esV:
                vtm = _sb(nc, esV, "vtm", [128, (TKV // 128) * 16 * 65], BF16)
                s.op('vector', lambda v: v.memset(
                    vtm[:, :].rearrange("p (x g) -> p x g", g=65)[:, :, 64:65], 1.0))
                for jt in range(TKV // 128):
                    s.op('gpsimd', lambda e, jt=jt: e.dma_start(
                        out=vtm[:, :].rearrange("p (j h g) -> p j h g", h=16, g=65)[:, jt, :, 0:64],
                        in_=vb_i[jt * 128:(jt + 1) * 128, :].rearrange("pp (h g) -> pp h g", g=64)), dma=True)

                aout = qraw
                for h in range(16):
                    dc = h // 2
                    hb = (h % 2) * 64
                    for qt in range(T_ // 128):
                        for d_i, kvt in enumerate((qt, qt + 1, qt + 2)):
                            s.op('tensor', lambda t, dc=dc, hb=hb, kvt=kvt, qt=qt, d_i=d_i: t.matmul(
                                psT[:, d_i * 128: d_i * 128 + 128],
                                kraw[hb:hb + 64, dc * TKV + kvt * 128: dc * TKV + kvt * 128 + 128],
                                qraw[hb:hb + 64, dc * T_ + qt * 128: dc * T_ + qt * 128 + 128],
                                start=True, stop=True, tile_position=(hb, 0)))
                        s.op('vector', lambda v: v.tensor_tensor(psT[:, 256:384], psT[:, 256:384], mlow[:, :], OP.add))
                        s.op('scalar', lambda sc: sc.activation(out=psb[:, :], in_=psT[:, :], func=AF.Exp,
                                                                scale=8.0, bias=csb[:, :]))
                        s.op('vector', lambda v: v.tensor_tensor(psb[:, 0:128], psb[:, 0:128], mupp[:, :], OP.mult))
                        if qt == 0:
                            s.op('vector', lambda v: v.tensor_scalar_mul(psb[:, 0:128], psb[:, 0:128], halom[:, 0:1]))
                            s.op('vector', lambda v: v.tensor_scalar_mul(psb[:, 128:256], psb[:, 128:256], halom[:, 1:2]))
                        elif qt == 1:
                            s.op('vector', lambda v: v.tensor_scalar_mul(psb[:, 0:128], psb[:, 0:128], halom[:, 1:2]))
                        for d_i, kvt in enumerate((qt, qt + 1, qt + 2)):
                            s.op('tensor', lambda t, kvt=kvt, h=h, d_i=d_i: t.matmul(
                                pacc[0:65, :], vtm[:, (kvt * 16 + h) * 65: (kvt * 16 + h) * 65 + 65],
                                psb[:, d_i * 128: d_i * 128 + 128],
                                start=(d_i == 0), stop=(d_i == 2)))
                        s.op('vector', lambda v: v.reciprocal(zrow[0:1, 0:128], pacc[64:65, :]))
                        s.op('tensor', lambda t: t.matmul(pzb[:, :], onesf[:, 0:64], zrow[0:1, 0:128], start=True, stop=True))
                        s.op('scalar', lambda sc: sc.activation(out=numb[:, 0:128], in_=pacc[0:64, :], func=AF.Copy))
                        s.op('vector', lambda v, dc=dc, hb=hb, qt=qt: v.tensor_tensor(
                            aout[hb:hb + 64, dc * T_ + qt * 128: dc * T_ + qt * 128 + 128],
                            numb[:, 0:128], pzb[:, :], OP.mult))

                s.dma('gpsimd', wqb[:, :].rearrange("p (kc d) -> p kc d", kc=NCH),
                      wo6_d[li].rearrange("(kc p) d -> p kc d", p=128))
                for tt in range(T_ // 512):
                    for dc in range(NCH):
                        fm_proj(p, pp, wqb, aout, dout=D, dc=dc,
                                xoff=lambda kc, tt=tt: kc * T_ + tt * 512, ncols=512, nk=NCH)
                        co = dc * T_ + tt * 512
                        s.op('vector', lambda v, co=co: v.tensor_add(
                            stream[:, co:co + 512], pp[:, :], stream[:, co:co + 512]))
        s.barrier()

    def _inner_ffn_layer(li, onesf, final=False):
        T_ = TPC
        DI = 2730
        emit_halo1()
        with contextlib.ExitStack() as esF:
            hl = _sb(nc, esF, "hl", [128, NCH], F32)
            xn = _sb(nc, esF, "xn", [128, NCH * T_], BF16)
            hsb = _sb(nc, esF, "hsb", [128, 11 * T_], BF16)
            w2s = _sb(nc, esF, "w2s", [128, 11 * D], BF16)
            w1a = _sb(nc, esF, "w1a", [128, NCH * 128], BF16)
            w1g = _sb(nc, esF, "w1g", [128, NCH * 128], BF16)
            b1s = _sb(nc, esF, "b1s", [128, 44], F32)
            b2s = _sb(nc, esF, "b2s", [128, NCH], F32)
            gs = _sb(nc, esF, "gs", [128, NCH], F32); bs = _sb(nc, esF, "bs", [128, NCH], F32)
            gbf = _sb(nc, esF, "gbf", [128, 512], BF16)
            ub = _sb(nc, esF, "ub", [128, NCH * 512], BF16)
            usq = _sb(nc, esF, "usq", [128, 512], BF16)
            t1 = _sb(nc, esF, "t1", [128, 512], F32)
            xp0 = _sb(nc, esF, "xp0", [128, 512], F32)
            onesc = _sb(nc, esF, "onesc", [128, 1], BF16)
            onesr = _sb(nc, esF, "onesr", [1, 128], F32)
            srow = _sb(nc, esF, "srow", [1, 1024], F32)
            trow = _sb(nc, esF, "trow", [1, 512], F32)
            ufin = _sb(nc, esF, "ufin", [128, NCH * 512], F32) if final else None
            pp = _psn(nc, esF, "pp", [128, 512])
            pg = _psn(nc, esF, "pg", [128, 512])
            pout = _psn(nc, esF, "pout", [128, 1024])
            pab = _psn(nc, esF, "pab", [128, 1024])
            pstat = _psn(nc, esF, "pstat", [1, 1024])

            read_halo1(hl)
            s.dma('gpsimd', b1s[:, :], b1c6_d[:, li * 44:(li + 1) * 44])
            s.dma('gpsimd', b2s[:, :], b2c6_d[:, li * NCH:(li + 1) * NCH])
            s.dma('gpsimd', gs[:, :], gc6_d[:, li * NCH:(li + 1) * NCH])
            s.dma('gpsimd', bs[:, :], bc6_d[:, li * NCH:(li + 1) * NCH])
            s.op('vector', lambda v: v.memset(onesc[:, :], 1.0))
            s.op('vector', lambda v: v.memset(onesr[:, :], 1.0))

            def sh_copy(dc, tt, dstbuf):
                co = dc * T_ + tt * 512
                if dc < 4:
                    s.op('vector', lambda v, co=co, dstbuf=dstbuf: v.tensor_copy(
                        dstbuf[:, :], stream[:, co:co + 512]))
                elif tt > 0:
                    s.op('vector', lambda v, co=co, dstbuf=dstbuf: v.tensor_copy(
                        dstbuf[:, :], stream[:, co - 1:co + 511]))
                else:
                    s.op('vector', lambda v, dc=dc, dstbuf=dstbuf: v.tensor_copy(
                        dstbuf[:, 0:1], hl[:, dc:dc + 1]))
                    s.op('vector', lambda v, co=co, dstbuf=dstbuf: v.tensor_copy(
                        dstbuf[:, 1:512], stream[:, co:co + 511]))

            NT = T_ // 512
            for tt in range(NT):
                for dc in range(NCH):
                    sh_copy(dc, tt, xp0)
                    s.op('vector', lambda v, dc=dc: v.tensor_copy(
                        ub[:, dc * 512: dc * 512 + 512], xp0[:, :]))
                for dc in range(NCH):
                    uo = dc * 512
                    s.op('scalar', lambda sc, uo=uo: sc.activation(out=usq[:, :], in_=ub[:, uo:uo + 512], func=AF.Square))
                    s.op('tensor', lambda t, dc=dc, uo=uo: t.matmul(
                        pstat[0:1, 0:512], onesc[:, :], ub[:, uo:uo + 512], start=(dc == 0), stop=(dc == NCH - 1)))
                    s.op('tensor', lambda t, dc=dc: t.matmul(
                        pstat[0:1, 512:1024], onesc[:, :], usq[:, :], start=(dc == 0), stop=(dc == NCH - 1)))
                s.op('vector', lambda v: v.tensor_scalar_mul(srow[:, :], pstat[0:1, :], 1.0 / D))
                s.op('scalar', lambda sc: sc.activation(out=trow[0:1, :], in_=srow[0:1, 0:512], func=AF.Square))
                s.op('vector', lambda v: v.tensor_sub(srow[0:1, 512:1024], srow[0:1, 512:1024], trow[0:1, :]))
                s.op('vector', lambda v: v.tensor_scalar_add(srow[0:1, 512:1024], srow[0:1, 512:1024], 1e-5))
                s.op('scalar', lambda sc: sc.activation(out=srow[0:1, 512:1024], in_=srow[0:1, 512:1024], func=AF.Sqrt))
                s.op('vector', lambda v: v.reciprocal(srow[0:1, 512:1024], srow[0:1, 512:1024]))
                s.op('vector', lambda v: v.tensor_mul(trow[0:1, :], srow[0:1, 0:512], srow[0:1, 512:1024]))
                s.op('vector', lambda v: v.tensor_scalar_mul(trow[0:1, :], trow[0:1, :], -1.0))
                s.op('tensor', lambda t: t.matmul(pab[:, 0:512], onesr[:, :], srow[0:1, 512:1024], start=True, stop=True))
                s.op('tensor', lambda t: t.matmul(pab[:, 512:1024], onesr[:, :], trow[0:1, :], start=True, stop=True))
                for dc in range(NCH):
                    co = dc * T_ + tt * 512
                    sh_copy(dc, tt, xp0)
                    s.op('vector', lambda v: v.tensor_tensor(t1[:, :], xp0[:, :], pab[:, 0:512], OP.mult))
                    s.op('vector', lambda v: v.tensor_tensor(t1[:, :], t1[:, :], pab[:, 512:1024], OP.add))
                    s.op('vector', lambda v, dc=dc, co=co: v.tensor_scalar(
                        xn[:, co:co + 512], t1[:, :], gs[:, dc:dc + 1], bs[:, dc:dc + 1], OP.mult, OP.add))

            jblocks_half = [(i * 128, 128) for i in range(10)] + [(1280, 85)]
            wview = w16_d[li].rearrange("(kc p) d -> p kc d", p=128)
            w2v = w26_d[li]
            for half in range(2):
                j0h = half * 1365
                for jbi, (jo, jw) in enumerate(jblocks_half):
                    jg = j0h + jo
                    s.dma('gpsimd', w1a[:, 0:NCH * jw].rearrange("p (kc d) -> p kc d", kc=NCH),
                          wview[:, :, jg:jg + jw])
                    s.dma('gpsimd', w1g[:, 0:NCH * jw].rearrange("p (kc d) -> p kc d", kc=NCH),
                          wview[:, :, DI + jg:DI + jg + jw])
                    for tt in range(NT):
                        for (wsb, psum) in ((w1a, pp), (w1g, pg)):
                            for kc in range(NCH):
                                s.op('tensor', lambda t, kc=kc, tt=tt, jw=jw, wsb=wsb, psum=psum: t.matmul(
                                    psum[0:jw, 0:512], wsb[:, kc * jw: kc * jw + jw],
                                    xn[:, kc * T_ + tt * 512: kc * T_ + tt * 512 + 512],
                                    start=(kc == 0), stop=(kc == NCH - 1)))
                        cga = (0 if half == 0 else 11) + jbi
                        cgg = (22 if half == 0 else 33) + jbi
                        s.op('scalar', lambda sc, jw=jw, cgg=cgg: sc.activation(
                            out=gbf[0:jw, :], in_=pg[0:jw, :], func=AF.Gelu, bias=b1s[0:jw, cgg:cgg + 1]))
                        s.op('vector', lambda v, jo=jo, tt=tt, jw=jw, cga=cga: v.scalar_tensor_tensor(
                            hsb[0:jw, (jo // 128) * T_ + tt * 512: (jo // 128) * T_ + tt * 512 + 512],
                            pp[0:jw, 0:512], b1s[0:jw, cga:cga + 1], gbf[0:jw, :], OP.add, OP.mult))
                s.dma('gpsimd', w2s[:, 0:10 * D].rearrange("p (jb d) -> p jb d", jb=10),
                      w2v[j0h:j0h + 1280, :].rearrange("(jb p) d -> p jb d", p=128))
                s.dma('gpsimd', w2s[0:85, 10 * D:11 * D], w2v[j0h + 1280:j0h + 1365, :])
                for t128 in range(T_ // 128):
                    for dc in range(NCH):
                        for jbi, (jo, jw) in enumerate(jblocks_half):
                            s.op('tensor', lambda t, jbi=jbi, jo=jo, jw=jw, dc=dc, t128=t128: t.matmul(
                                pout[:, dc * 128: dc * 128 + 128],
                                w2s[0:jw, jbi * D + dc * 128: jbi * D + dc * 128 + 128],
                                hsb[0:jw, (jo // 128) * T_ + t128 * 128: (jo // 128) * T_ + t128 * 128 + 128],
                                start=(jbi == 0), stop=(jbi == 10)))
                    for dc in range(NCH):
                        yo = dc * T_ + t128 * 128
                        if half == 0:
                            s.op('vector', lambda v, dc=dc, yo=yo: v.tensor_add(
                                stream[:, yo:yo + 128], pout[:, dc * 128: dc * 128 + 128], stream[:, yo:yo + 128]))
                        else:
                            s.op('vector', lambda v, dc=dc, yo=yo: v.scalar_tensor_tensor(
                                stream[:, yo:yo + 128], pout[:, dc * 128: dc * 128 + 128], b2s[:, dc:dc + 1],
                                stream[:, yo:yo + 128], OP.add, OP.add))

            if final:
                x3v = x3_i.rearrange("(c p) t -> p c t", p=128)
                for tt in range(NT):
                    s.dma('sync', ufin[:, :].rearrange("p (c t) -> p c t", c=NCH),
                          x3v[:, :, tt * 512:tt * 512 + 512])
                    for dc in range(NCH):
                        uo = dc * 512
                        co = dc * T_ + tt * 512
                        s.op('vector', lambda v, uo=uo, co=co: v.tensor_add(
                            ufin[:, uo:uo + 512], ufin[:, uo:uo + 512], stream[:, co:co + 512]))
                        s.op('vector', lambda v, uo=uo: v.tensor_copy(ub[:, uo:uo + 512], ufin[:, uo:uo + 512]))
                    emit_res_ln(p, u=ufin, ub=ub, usq=usq, onesc=onesc, onesr=onesr,
                                srow=srow, trow=trow, pab=pab, pstat=pstat, tt=tt, T=T_,
                                ucol=lambda dc: dc * 512)
                    for dc in range(NCH):
                        co = dc * T_ + tt * 512
                        s.op('vector', lambda v, dc=dc, co=co: v.tensor_copy(
                            stream[:, co:co + 512], ufin[:, dc * 512:dc * 512 + 512]))
        s.barrier()

    if upto >= 8:
        _inner_layers()

    if upto == 5:
        dq = p.dout("dbg_q", [D, T], BF16)
        dk = p.dout("dbg_k", [D, T], BF16)
        dv = p.dout("dbg_v", [T, D], BF16)
        with contextlib.ExitStack() as esd:
            tmp = _sb(nc, esd, "dtmp", [128, T], BF16)
            for dc in range(NCH):
                s.dma('gpsimd', tmp[:, :], ex_q1[dc * 128:(dc + 1) * 128, :])
                s.dma('sync', dq[dc * 128:(dc + 1) * 128, :], tmp[:, :])
                s.dma('gpsimd', tmp[:, :], ex_k1[dc * 128:(dc + 1) * 128, :])
                s.dma('sync', dk[dc * 128:(dc + 1) * 128, :], tmp[:, :])
            for jt in range(T // 128):
                s.dma('gpsimd', tmp[:, 0:D], ex_v1[jt * 128:(jt + 1) * 128, :])
                s.dma('sync', dv[jt * 128:(jt + 1) * 128, :], tmp[:, 0:D])

    s.dma('sync', y_d.rearrange("(c p) t -> p c t", p=128),
          stream[:, :].rearrange("p (c t) -> p c t", c=NCH))
    p.finish()
    return p


def _meta_vals(c):
    jj = c % 4
    m = np.zeros((32, 1), np.int32)
    m[0] = jj * 2048
    m[1] = (jj - 1) * 128 if jj > 0 else 4 * 128
    m[2] = jj * 128
    m[3] = jj * 256
    return m


def kernel_fused(I, upto=99):
    x = np.asarray(I["L0_attn_x"])
    xt = x.reshape(NC, TPC, D)
    pF = cached_prog("fused%d" % upto, lambda p: None) if False else None
    key = "fused%d" % upto
    if key not in _PROG_CACHE:
        _PROG_CACHE[key] = build_fused(upto)
    pF = _PROG_CACHE[key]

    bq = np.ascontiguousarray(I["L0_attn_bq"].reshape(4, 128).T)
    bk = np.ascontiguousarray(I["L0_attn_bk"].reshape(4, 128).T)
    k_ = np.arange(128)[:, None]
    m384 = (k_ <= np.arange(384)[None, :] - 128).astype(BF)
    m896 = (k_ <= np.arange(896)[None, :] - 384).astype(BF)
    kq = np.arange(128)[:, None]; qq = np.arange(128)[None, :]
    mlow = np.where(kq <= qq, 0.0, -1e30).astype(np.float32)
    mupp = (kq >= qq).astype(BF)
    bpat = np.zeros((128, 2), np.float32)
    bpat[0:64, 0] = 1.0; bpat[64:128, 1] = 1.0
    bpatTf = np.ascontiguousarray(bpat.T.astype(np.float32))
    bpatb = bpat.astype(BF)
    DI = 2730
    qsc6 = np.zeros((128, 6), np.float32); qscw6 = np.zeros((128, 6), np.float32)
    ksc6 = np.zeros((128, 6), np.float32); kscw6 = np.zeros((128, 6), np.float32)
    qsr6 = np.zeros((6, 64), np.float32); ksr6 = np.zeros((6, 64), np.float32)
    b1c6 = np.zeros((128, 6 * 44), np.float32)
    b2c6 = np.zeros((128, 6 * NCH), np.float32)
    gc6 = np.zeros((128, 6 * NCH), np.float32); bc6 = np.zeros((128, 6 * NCH), np.float32)
    for li in range(6):
        qs_i = np.asarray(I["L1_ffn_q_scale"][li]); ks_i = np.asarray(I["L1_ffn_k_scale"][li])
        qsc6[:, li] = np.tile(qs_i, 2); ksc6[:, li] = np.tile(ks_i, 2)
        qscw6[:, li] = np.tile(np.concatenate([qs_i[32:], qs_i[:32]]), 2)
        kscw6[:, li] = np.tile(np.concatenate([ks_i[32:], ks_i[:32]]), 2)
        qsr6[li] = qs_i; ksr6[li] = ks_i
        b1_i = np.asarray(I["L1_ffn_b1"][li])
        col = 0
        for part0 in (0, DI):
            for half in range(2):
                for jbi in range(11):
                    jo = jbi * 128
                    jw = min(128, 1365 - jo)
                    b1c6[0:jw, li * 44 + col] = b1_i[part0 + half * 1365 + jo: part0 + half * 1365 + jo + jw]
                    col += 1
        b2c6[:, li * NCH:(li + 1) * NCH] = I["L1_ffn_b2"][li].reshape(NCH, 128).T
        gc6[:, li * NCH:(li + 1) * NCH] = I["L1_ffn_ln_g"][li].reshape(NCH, 128).T
        bc6[:, li * NCH:(li + 1) * NCH] = I["L1_ffn_ln_b"][li].reshape(NCH, 128).T

    ims = []
    for c in range(NC):
        t0c = (c % 4) * TPC
        cosT, ssinT = _module_rope(np.arange(t0c, t0c + TPC))
        cq, sq, _, _ = _inner_rope(np.arange(t0c, t0c + TPC))
        _, _, ck, sk = _inner_rope(np.arange(t0c - 256, t0c + TPC))
        halom = np.zeros((128, 2), np.float32) if c % 4 == 0 else np.ones((128, 2), np.float32)
        ims.append({
            "xfm": _fm(xt[c]), "meta": _meta_vals(c),
            "wq": I["L0_attn_wq"], "wk": I["L0_attn_wk"], "wv": I["L0_attn_wv"],
            "bqc": bq, "bkc": bk,
            "bkr": I["L0_attn_bk"].reshape(1, 512), "bvr": I["L0_attn_bv"].reshape(1, D),
            "wo0": I["L0_attn_wo"], "boc": _colpack(I["L0_attn_bo"]),
            "mask384": m384,
            "wrT": _fm(I["L0_ffn_W_r"]), "wkT": _fm(I["L0_ffn_W_k"]), "wvT": _fm(I["L0_ffn_W_v"]),
            "murc": _colpack(I["L0_ffn_mu_r"]), "mukc": _colpack(I["L0_ffn_mu_k"]),
            "qkvT": _fm(I["L1_attn_qkv"]), "cosT": cosT, "ssinT": ssinT,
            "m896": m896, "woT1": _fm(I["L1_attn_wo"]),
            "wqkv6": I["L1_ffn_Wqkv"], "wo6": I["L1_ffn_Wo"],
            "w16": I["L1_ffn_W1"], "w26": I["L1_ffn_W2"],
            "cq": cq, "sq": sq, "ck": ck, "sk": sk,
            "qsc6": qsc6, "qscw6": qscw6, "ksc6": ksc6, "kscw6": kscw6,
            "qsr6": qsr6, "ksr6": ksr6,
            "blockpat": bpatb, "blockpatT": bpatTf,
            "mlow": mlow, "mupp": mupp, "halom": halom,
            "b1c6": b1c6, "b2c6": b2c6, "gc6": gc6, "bc6": bc6,
            "zc": np.zeros((128, NCH), np.float32),
        })
    r = run(pF, ims)
    y = [np.asarray(r[c]["y"]) for c in range(NC)]
    out = np.concatenate([a.T for a in y], axis=0).reshape(B, N, D)
    out = np.ascontiguousarray(out.astype(np.float32, copy=False))
    if _RAW:
        return out, r
    return out
